# version 02
# baseline (speedup 1.0000x reference)
import contextlib
import numpy as np
import ml_dtypes

import concourse.bass as bass
import concourse.mybir as mybir
from concourse.bass_utils import run_bass_kernel_spmd

F32 = mybir.dt.float32
BF16 = mybir.dt.bfloat16
I32 = mybir.dt.int32
AF = mybir.ActivationFunctionType
ALU = mybir.AluOpType
AX = mybir.AxisListType
BFNP = ml_dtypes.bfloat16

NCORES = 8
B, S, D = 2, 16384, 1024
H = 1536
FF = 4096
NS = 32
SQ = S // 4
DQ = D // 4
EPS = 1e-5
ERFS = 0.7071067811865476


class Prog:
    ENGS = ("sync", "tensor", "vector", "scalar", "gpsimd")
    SELF = {"vector": "eV", "scalar": "eA", "gpsimd": "eG"}

    def __init__(self):
        self.nc = bass.Bass("TRN2", target_bir_lowering=False)
        self.es = contextlib.ExitStack()
        self.es.__enter__()
        self.q = {e: [] for e in self.ENGS}
        self.sems = {}
        self.cnt = {}
        self.lastw = {e: {} for e in self.ENGS}
        for s in ("dL", "dW", "dS", "dG", "eT", "eV", "eA", "eG", "eS"):
            self.sems[s] = self.es.enter_context(self.nc.semaphore(s))
            self.cnt[s] = 0
        nums = [h.num for h in self.sems.values()]
        rng = range(min(nums), max(nums) + 1)
        self.q["gpsimd"].append(lambda e, r=rng: e.dma_reset(r))
        self.q["gpsimd"].append(lambda e, r=rng: e.sem_clear(r))
        self.q["gpsimd"].append(
            lambda e: e.nop().then_inc(self.sems["eG"], 1))
        self.cnt["eG"] = 1
        self._started = {e: (e == "gpsimd") for e in self.ENGS}
        self.pending = {e: {} for e in self.ENGS}

    def din(self, name, shape, dt=F32):
        return self.nc.dram_tensor(name, list(shape), dt, kind="ExternalInput")

    def dout(self, name, shape, dt=F32):
        return self.nc.dram_tensor(name, list(shape), dt, kind="ExternalOutput")

    def sbuf(self, name, shape, dt=F32):
        return self.es.enter_context(self.nc.sbuf_tensor(name, list(shape), dt))

    def psum(self, name, shape, dt=F32):
        return self.es.enter_context(self.nc.psum_tensor(name, list(shape), dt))

    def tok(self, sem):
        return (sem, self.cnt[sem])

    def _wait(self, eng, sem, val):
        if val <= 0:
            return
        if self.lastw[eng].get(sem, 0) >= val:
            return
        self.lastw[eng][sem] = val
        self.q[eng].append(lambda e, s=self.sems[sem], v=val: e.wait_ge(s, v))

    def op(self, eng, fn, after=(), inc=None, dma=False):
        if not self._started[eng]:
            self._started[eng] = True
            self._wait(eng, "eG", 1)
        ss = None if dma else self.SELF.get(eng)
        afters = list(after)
        if self.pending[eng]:
            afters.extend(self.pending[eng].items())
            self.pending[eng] = {}
        if ss:
            afters.append((ss, self.cnt[ss]))
        merged = {}
        for sem, val in afters:
            merged[sem] = max(merged.get(sem, 0), val)
        for sem, val in merged.items():
            self._wait(eng, sem, val)
        incs = []
        if ss:
            self.cnt[ss] += 1
            incs.append((ss, 1))
        if inc:
            amt = 16 if dma else 1
            self.cnt[inc] += amt
            incs.append((inc, amt))

        def run(e, fn=fn, incs=tuple(incs)):
            inst = fn(e)
            for s, amt in incs:
                inst = inst.then_inc(self.sems[s], amt)
        self.q[eng].append(run)
        if inc:
            return self.tok(inc)
        if ss:
            return self.tok(ss)
        return None

    def dma(self, eng, out, in_, after=(), inc="dL", slow_ok=False):
        after = list(after)
        after.append((inc, self.cnt[inc]))
        if slow_ok:
            return self.op(eng, lambda e: e.dma_start(
                out=out, in_=in_, allow_slow_non_contiguous=True),
                after=after, inc=inc, dma=True)
        return self.op(eng, lambda e: e.dma_start(out=out, in_=in_),
                       after=after, inc=inc, dma=True)

    def barrier(self):
        for e in self.ENGS:
            for s in ("dL", "dW", "dS", "dG", "eT", "eV", "eA", "eG"):
                self.pending[e][s] = max(self.pending[e].get(s, 0), self.cnt[s])

    def finalize(self, final_eng=None):
        self._wait("sync", "dS", self.cnt["dS"])
        blk = self.nc.Block()
        blk.__enter__()
        for e in self.ENGS:
            if self.q[e]:
                getattr(blk, e)(lambda eng, lst=tuple(self.q[e]): [f(eng) for f in lst])
        blk.__exit__(None, None, None)
        nc = self.nc
        self.es.__exit__(None, None, None)
        nc.finalize()
        return nc


def rearr(t, pat, **kw):
    return t.ap().rearrange(pat, **kw)


def mm_group(p, out_ap, lhsT_fn, rhs_fn, K, after=()):
    for k in range(K):
        p.op("tensor",
             lambda e, k=k: e.matmul(out=out_ap, lhsT=lhsT_fn(k), rhs=rhs_fn(k),
                                     start=(k == 0), stop=(k == K - 1)),
             after=(after if k == 0 else ()), inc="eT" if k == K - 1 else None)
    return p.tok("eT")


def emit_ln_stats(p, x_fn, nch, T, psS, psQ, sq, row_mean, row_rstd, after,
                  denom, eps, row_var=None):
    for i in range(nch):
        p.op("tensor", lambda e, i=i: e.matmul(
            out=psS[0:1, 0:T], lhsT=p.ones_col[:, 0:1], rhs=x_fn(i),
            start=(i == 0), stop=(i == nch - 1)), after=after,
            inc="eT" if i == nch - 1 else None)
    tS = p.tok("eT")
    last_mm = None
    for i in range(nch):
        aft = list(after) + ([last_mm] if last_mm else [])
        tq = p.op("gpsimd", lambda e, i=i: e.tensor_tensor(
            out=sq[:, 0:T], in0=x_fn(i), in1=x_fn(i), op=ALU.mult), after=aft)
        last_mm = p.op("tensor", lambda e, i=i: e.matmul(
            out=psQ[0:1, 0:T], lhsT=p.ones_col[:, 0:1], rhs=sq[:, 0:T],
            start=(i == 0), stop=(i == nch - 1)), after=[tq], inc="eT")
    tQ = p.tok("eT")
    t1 = p.op("vector", lambda e: e.tensor_single_scalar(
        out=row_mean, in_=psS[0:1, 0:T], scalar=1.0 / denom, op=ALU.mult),
        after=[tS])
    p.op("vector", lambda e: e.tensor_tensor(
        out=row_rstd, in0=row_mean, in1=row_mean, op=ALU.mult))
    p.op("vector", lambda e: e.scalar_tensor_tensor(
        out=row_rstd, in0=psQ[0:1, 0:T], scalar=1.0 / denom, in1=row_rstd,
        op0=ALU.mult, op1=ALU.subtract), after=[tQ])
    if row_var is not None:
        p.op("vector", lambda e: e.tensor_copy(out=row_var, in_=row_rstd))
    p.op("vector", lambda e: e.tensor_single_scalar(
        out=row_rstd, in_=row_rstd, scalar=eps, op=ALU.add))
    t4 = p.op("scalar", lambda e: e.activation(
        out=row_rstd, in_=row_rstd, func=AF.Ln),
        after=[p.tok("eV")])
    t5 = p.op("scalar", lambda e: e.activation(
        out=row_rstd, in_=row_rstd, func=AF.Exp, scale=-0.5))
    return t5


def emit_bcast(p, row_ap, ps_out_ap, after):
    return p.op("tensor", lambda e: e.matmul(
        out=ps_out_ap, lhsT=p.ones_row[0:1, :], rhs=row_ap,
        start=True, stop=True), after=after, inc="eT")


def common_consts(p):
    p.ones_col = p.sbuf("ones_col", [128, 1])
    p.ones_row = p.sbuf("ones_row", [1, 128])
    p.op("gpsimd", lambda e: e.memset(p.ones_col[:, :], 1.0))
    t = p.op("gpsimd", lambda e: e.memset(p.ones_row[:, :], 1.0))
    return t


def build_L1a():
    p = Prog()
    T = 512
    NSUB = SQ // T

    xin = p.din("x", [D, SQ + 3])
    wgT = p.din("wgT", [D, H])
    wiT = p.din("wiT", [D, H])
    cw = p.din("cw", [128, 48])
    cb = p.din("cb", [128, 12])
    o_gg = p.dout("ggate", [H, SQ], BF16)
    o_yb = p.dout("ybf", [H, SQ], BF16)

    s_wg = p.sbuf("s_wg", [128, 8, H], BF16)
    s_wi = p.sbuf("s_wi", [128, 8, H], BF16)
    s_cw = p.sbuf("s_cw", [128, 12, 4])
    s_cb = p.sbuf("s_cb", [128, 12])
    tcst = common_consts(p)
    s_x = p.sbuf("s_x", [128, 8, T])
    s_ln = p.sbuf("s_ln", [128, 8, T], BF16)
    s_sq = p.sbuf("s_sq", [128, T])
    s_row = p.sbuf("s_row", [1, 2 * T])
    s_u = p.sbuf("s_u", [128, 12, T + 3])
    SA = p.sbuf("SA", [128, 12, T])
    SB = p.sbuf("SB", [128, 12, T])
    s_gg = p.sbuf("s_gg", [128, 12, T], BF16)
    s_ybf = p.sbuf("s_ybf", [128, 12, T], BF16)
    psS = p.psum("psS", [1, T])
    psQ = p.psum("psQ", [1, T])
    ps_bm = p.psum("ps_bm", [128, T])
    ps_br = p.psum("ps_br", [128, T])
    ps_m0 = p.psum("ps_m0", [128, T])
    ps_m1 = p.psum("ps_m1", [128, T])
    ps_u3 = p.psum("ps_u3", [128, 4])

    twg = p.dma("gpsimd", s_wg[:, :, :], rearr(wgT, "(c p) h -> p c h", p=128), inc="dG")
    twi = p.dma("gpsimd", s_wi[:, :, :], rearr(wiT, "(c p) h -> p c h", p=128), inc="dG")
    tcw = p.dma("sync", s_cw[:, :, :], rearr(cw, "p (c k) -> p c k", k=4), inc="dW")
    tcb = p.dma("sync", s_cb[:, :], cb[:, :], inc="dW")

    def ln_block(width, tx):
        trs = emit_ln_stats(p, lambda i: s_x[:, i, 0:width], 8, width, psS, psQ,
                            s_sq, s_row[0:1, 0:width], s_row[0:1, T:T + width],
                            [tx, tcst], D, EPS)
        tb1 = emit_bcast(p, s_row[0:1, 0:width], ps_bm[:, 0:width], [trs])
        tb2 = emit_bcast(p, s_row[0:1, T:T + width], ps_br[:, 0:width], [trs])
        for i in range(8):
            p.op("vector", lambda e, i=i: e.tensor_tensor(
                out=s_sq[:, 0:width], in0=s_x[:, i, 0:width], in1=ps_bm[:, 0:width],
                op=ALU.subtract), after=[tb1, tb2])
            p.op("vector", lambda e, i=i: e.tensor_tensor(
                out=s_ln[:, i, 0:width], in0=s_sq[:, 0:width], in1=ps_br[:, 0:width],
                op=ALU.mult))
        return p.tok("eV")

    tx = p.dma("sync", s_x[:, :, 0:3],
               rearr(xin, "(c p) t -> p c t", p=128)[:, :, 0:3], inc="dL")
    tln = ln_block(3, tx)
    tcp = None
    for m in range(12):
        tmm = mm_group(p, ps_u3[:, 0:3],
                       lambda k, m=m: s_wi[:, k, m * 128:(m + 1) * 128],
                       lambda k: s_ln[:, k, 0:3], 8,
                       after=[tln, twi] + ([tcp] if tcp else []))
        tcp = p.op("vector", lambda e, m=m: e.tensor_copy(
            out=s_u[:, m, 0:3], in_=ps_u3[:, 0:3]), after=[tmm])
    p.barrier()

    for it in range(NSUB):
        o = it * T
        tx = p.dma("sync", s_x[:, :, :],
                   rearr(xin, "(c p) t -> p c t", p=128)[:, :, 3 + o:3 + o + T],
                   inc="dL")
        tln = ln_block(T, tx)

        evac = {}
        for m in range(12):
            pb = ps_m0 if (m % 2 == 0) else ps_m1
            aft = [tln, twg] + ([evac[m - 2]] if m - 2 in evac else [])
            tg = mm_group(p, pb[:, 0:T],
                          lambda k, m=m: s_wg[:, k, m * 128:(m + 1) * 128],
                          lambda k: s_ln[:, k, 0:T], 8, after=aft)
            evac[m] = p.op("scalar", lambda e, m=m, pb=pb: e.activation(
                out=s_gg[:, m, :], in_=pb[:, 0:T], func=AF.Gelu), after=[tg])
        tgg = p.tok("eA")
        p.dma("sync", rearr(o_gg, "(c p) t -> p c t", p=128)[:, :, o:o + T],
              s_gg[:, :, :], after=[tgg], inc="dS")

        evac = {}
        for m in range(12):
            pb = ps_m0 if (m % 2 == 0) else ps_m1
            aft = [tln, twi, tgg] + ([evac[m - 2]] if m - 2 in evac else [])
            tu = mm_group(p, pb[:, 0:T],
                          lambda k, m=m: s_wi[:, k, m * 128:(m + 1) * 128],
                          lambda k: s_ln[:, k, 0:T], 8, after=aft)
            tc = p.op("vector", lambda e, m=m, pb=pb: e.tensor_copy(
                out=s_u[:, m, 3:T + 3], in_=pb[:, 0:T]), after=[tu])
            evac[m] = tc
        tuall = p.tok("eV")

        for m in range(12):
            p.op("vector", lambda e, m=m: e.tensor_scalar(
                out=SA[:, m, :], in0=s_u[:, m, 0:T], scalar1=s_cw[:, m, 0:1],
                scalar2=None, op0=ALU.mult), after=[tuall, tcw])
            for j in (1, 2, 3):
                p.op("vector", lambda e, m=m, j=j: e.scalar_tensor_tensor(
                    out=SA[:, m, :], in0=s_u[:, m, j:j + T],
                    scalar=s_cw[:, m, j:j + 1], in1=SA[:, m, :],
                    op0=ALU.mult, op1=ALU.add))
            p.op("scalar", lambda e, m=m: e.activation(
                out=s_ybf[:, m, :], in_=SA[:, m, :], func=AF.Identity,
                bias=s_cb[:, m:m + 1]), after=[p.tok("eV"), tcb])
            p.op("vector", lambda e, m=m: e.tensor_copy(
                out=s_u[:, m, 0:3], in_=s_u[:, m, T:T + 3]))
        tyb = p.tok("eA")
        p.dma("sync", rearr(o_yb, "(c p) t -> p c t", p=128)[:, :, o:o + T],
              s_ybf[:, :, :], after=[tyb], inc="dS")
        p.barrier()
    return p.finalize()


def build_L1b():
    p = Prog()
    T = 512
    NSUB = SQ // T

    ybin = p.din("ybf", [H, SQ], BF16)
    wliT = p.din("wliT", [H, H])
    wlgT = p.din("wlgT", [H, H])
    lbi = p.din("lbi", [128, 12])
    lbg = p.din("lbg", [128, 12])
    lam = p.din("lam", [128, 12])
    o_al = p.dout("alpha", [H, SQ])
    o_xb = p.dout("xbeta", [H, SQ])
    o_sm = p.dout("summ", [H, 4])

    s_wli = p.sbuf("s_wli", [128, 12, H], BF16)
    s_wlg = p.sbuf("s_wlg", [128, 12, H], BF16)
    s_lbi = p.sbuf("s_lbi", [128, 12])
    s_lbg = p.sbuf("s_lbg", [128, 12])
    s_msp = p.sbuf("s_msp", [128, 12])
    s_t0 = p.sbuf("s_t0", [128, 12])
    s_yb = p.sbuf("s_yb", [128, 12, T], BF16)
    s_ig = p.sbuf("s_ig", [128, 12, T])
    s_rg = p.sbuf("s_rg", [128, 12, T])
    s_al = p.sbuf("s_al", [128, 12, T])
    s_xb = p.sbuf("s_xb", [128, 12, T])
    s_sm = p.sbuf("s_sm", [128, 12, 4])
    ps0 = p.psum("ps0", [128, T])
    ps1 = p.psum("ps1", [128, T])

    twl1 = p.dma("gpsimd", s_wli[:, :, :], rearr(wliT, "(c p) h -> p c h", p=128), inc="dG")
    twl2 = p.dma("gpsimd", s_wlg[:, :, :], rearr(wlgT, "(c p) h -> p c h", p=128), inc="dG")
    tb1 = p.dma("sync", s_lbi[:, :], lbi[:, :], inc="dW")
    tb2 = p.dma("sync", s_lbg[:, :], lbg[:, :], inc="dW")
    tlam = p.dma("sync", s_msp[:, :], lam[:, :], inc="dW")
    tabs = p.op("scalar", lambda e: e.activation(out=s_t0[:, :], in_=s_msp[:, :],
                                                 func=AF.Abs), after=[tlam])
    p.op("scalar", lambda e: e.activation(out=s_t0[:, :], in_=s_t0[:, :],
                                          func=AF.Exp, scale=-1.0))
    ta = p.op("scalar", lambda e: e.activation(out=s_t0[:, :], in_=s_t0[:, :],
                                               func=AF.Ln, bias=1.0))
    p.op("vector", lambda e: e.tensor_single_scalar(
        out=s_msp[:, :], in_=s_msp[:, :], scalar=0.0, op=ALU.max),
        after=[tlam, tabs])
    p.op("vector", lambda e: e.scalar_tensor_tensor(
        out=s_msp[:, :], in0=s_msp[:, :], scalar=1.0, in1=s_t0[:, :],
        op0=ALU.mult, op1=ALU.add), after=[ta])
    tmsp = p.op("vector", lambda e: e.tensor_single_scalar(
        out=s_msp[:, :], in_=s_msp[:, :], scalar=-8.0, op=ALU.mult))
    p.op("vector", lambda e: e.memset(s_sm[:, :, 0:1], 1.0))
    p.op("vector", lambda e: e.memset(s_sm[:, :, 1:2], 0.0))

    for it in range(NSUB):
        o = it * T
        last = (it == NSUB - 1)
        ty = p.dma("sync", s_yb[:, :, :],
                   rearr(ybin, "(c p) t -> p c t", p=128)[:, :, o:o + T], inc="dL")
        pslast = {}
        for (wt, bt, dst, wtok) in ((s_wli, s_lbi, s_ig, twl1),
                                    (s_wlg, s_lbg, s_rg, twl2)):
            for m in range(12):
                pb = ps0 if (m % 2 == 0) else ps1
                aft = [ty, wtok] + ([pslast[id(pb)]] if id(pb) in pslast else [])
                tg = mm_group(p, pb[:, 0:T],
                              lambda k, m=m, wt=wt: wt[:, k, m * 128:(m + 1) * 128],
                              lambda k: s_yb[:, k, :], 12, after=aft)
                pslast[id(pb)] = p.op("scalar", lambda e, m=m, pb=pb, dst=dst, bt=bt:
                               e.activation(out=dst[:, m, :], in_=pb[:, 0:T],
                                            func=AF.Sigmoid, bias=bt[:, m:m + 1]),
                               after=[tg, tb1, tb2])
        tsig = p.tok("eA")
        for m in range(12):
            p.op("scalar", lambda e, m=m: e.activation(
                out=s_al[:, m, :], in_=s_rg[:, m, :], func=AF.Exp,
                scale=s_msp[:, m:m + 1]), after=[tsig, tmsp])
        tal = p.tok("eA")
        p.dma("sync", rearr(o_al, "(c p) t -> p c t", p=128)[:, :, o:o + T],
              s_al[:, :, :], after=[tal], inc="dS")
        for m in range(12):
            p.op("vector", lambda e, m=m: e.tensor_tensor(
                out=s_rg[:, m, :], in0=s_al[:, m, :], in1=s_al[:, m, :],
                op=ALU.mult), after=[tal])
            t1 = p.op("vector", lambda e, m=m: e.tensor_scalar(
                out=s_rg[:, m, :], in0=s_rg[:, m, :], scalar1=-1.0,
                scalar2=1.000001, op0=ALU.mult, op1=ALU.add))
            p.op("scalar", lambda e, m=m: e.activation(
                out=s_rg[:, m, :], in_=s_rg[:, m, :], func=AF.Sqrt), after=[t1])
        tbe = p.tok("eA")
        for m in range(12):
            p.op("vector", lambda e, m=m: e.tensor_tensor(
                out=s_ig[:, m, :], in0=s_ig[:, m, :], in1=s_yb[:, m, :],
                op=ALU.mult), after=[tbe])
            p.op("vector", lambda e, m=m: e.tensor_tensor(
                out=s_xb[:, m, :], in0=s_rg[:, m, :], in1=s_ig[:, m, :],
                op=ALU.mult))
        txb = p.tok("eV")
        p.dma("sync", rearr(o_xb, "(c p) t -> p c t", p=128)[:, :, o:o + T],
              s_xb[:, :, :], after=[txb], inc="dS")
        for m in range(12):
            p.op("vector", lambda e, m=m: e.tensor_tensor_scan(
                out=s_ig[:, m, :], data0=s_al[:, m, :], data1=s_al[:, m, :],
                initial=s_sm[:, m, 0:1], op0=ALU.mult, op1=ALU.bypass),
                after=[txb])
            if last:
                p.op("vector", lambda e, m=m: e.tensor_copy(
                    out=s_sm[:, m, 2:3], in_=s_ig[:, m, T - 2:T - 1]))
            p.op("vector", lambda e, m=m: e.tensor_copy(
                out=s_sm[:, m, 0:1], in_=s_ig[:, m, T - 1:T]))
            p.op("vector", lambda e, m=m: e.tensor_tensor_scan(
                out=s_ig[:, m, :], data0=s_al[:, m, :], data1=s_xb[:, m, :],
                initial=s_sm[:, m, 1:2], op0=ALU.mult, op1=ALU.add))
            if last:
                p.op("vector", lambda e, m=m: e.tensor_copy(
                    out=s_sm[:, m, 3:4], in_=s_ig[:, m, T - 2:T - 1]))
            p.op("vector", lambda e, m=m: e.tensor_copy(
                out=s_sm[:, m, 1:2], in_=s_ig[:, m, T - 1:T]))
        p.barrier()
    tsm = p.tok("eV")
    p.dma("sync", rearr(o_sm, "(c p) k -> p c k", p=128), s_sm[:, :, :],
          after=[tsm], inc="dS")
    return p.finalize()


def build_L2a():
    p = Prog()
    T = 512
    NSUB = SQ // T

    a_in = p.din("alpha", [H, SQ])
    xb_in = p.din("xbeta", [H, SQ])
    gg_in = p.din("ggate", [H, SQ], BF16)
    x_in = p.din("x", [D, SQ])
    pgam = p.din("pgam", [128, 12, 3])
    psnd = p.din("psend", [128, 12, 3])
    woT = p.din("woT", [H, D])
    o_r1 = p.dout("r1", [D, SQ])

    s_wo = p.sbuf("s_wo", [128, 12, D], BF16)
    s_pg = p.sbuf("s_pg", [128, 12, 3])
    s_ps = p.sbuf("s_ps", [128, 12, 3])
    s_cy = p.sbuf("s_cy", [128, 12])
    s_al = p.sbuf("s_al", [128, 12, T])
    s_xb = p.sbuf("s_xb", [128, 12, T])
    s_gg = p.sbuf("s_gg", [128, 12, T], BF16)
    s_h = p.sbuf("s_h", [128, 12, T])
    s_a1 = p.sbuf("s_a1", [128, 12, T], BF16)
    s_x = p.sbuf("s_x", [128, 8, T])
    s_r1 = p.sbuf("s_r1", [128, 8, T])
    ps0 = p.psum("ps0", [128, T])
    ps1 = p.psum("ps1", [128, T])

    two = p.dma("gpsimd", s_wo[:, :, :], rearr(woT, "(c p) d -> p c d", p=128), inc="dG")
    tg1 = p.dma("sync", s_pg[:, :, :], pgam.ap(), inc="dW")
    tg2 = p.dma("sync", s_ps[:, :, :], psnd.ap(), inc="dW")
    tc = p.op("vector", lambda e: e.memset(s_cy[:, :], 0.0), after=[tg1, tg2])
    for j in range(3):
        p.op("vector", lambda e, j=j: e.tensor_tensor(
            out=s_cy[:, :], in0=s_cy[:, :], in1=s_pg[:, :, j], op=ALU.mult))
        p.op("vector", lambda e, j=j: e.tensor_tensor(
            out=s_cy[:, :], in0=s_cy[:, :], in1=s_ps[:, :, j], op=ALU.add))
    tcy = p.tok("eV")

    for it in range(NSUB):
        o = it * T
        p.dma("sync", s_al[:, :, :],
              rearr(a_in, "(c p) t -> p c t", p=128)[:, :, o:o + T], inc="dL")
        p.dma("sync", s_xb[:, :, :],
              rearr(xb_in, "(c p) t -> p c t", p=128)[:, :, o:o + T], inc="dL")
        p.dma("sync", s_gg[:, :, :],
              rearr(gg_in, "(c p) t -> p c t", p=128)[:, :, o:o + T], inc="dL")
        tld = p.dma("sync", s_x[:, :, :],
                    rearr(x_in, "(c p) t -> p c t", p=128)[:, :, o:o + T], inc="dL")
        ta = tb = tg = tx = tld
        for m in range(12):
            p.op("vector", lambda e, m=m: e.tensor_tensor_scan(
                out=s_h[:, m, :], data0=s_al[:, m, :], data1=s_xb[:, m, :],
                initial=s_cy[:, m:m + 1], op0=ALU.mult, op1=ALU.add),
                after=[ta, tb, tcy])
            p.op("vector", lambda e, m=m: e.tensor_copy(
                out=s_cy[:, m:m + 1], in_=s_h[:, m, T - 1:T]))
            p.op("vector", lambda e, m=m: e.tensor_tensor(
                out=s_a1[:, m, :], in0=s_gg[:, m, :], in1=s_h[:, m, :],
                op=ALU.mult), after=[tg])
        ta1 = p.tok("eV")
        pslast = {}
        for m in range(8):
            pb = ps0 if (m % 2 == 0) else ps1
            aft = [ta1, two, tx] + ([pslast[id(pb)]] if id(pb) in pslast else [])
            tmm = mm_group(p, pb[:, 0:T],
                           lambda k, m=m: s_wo[:, k, m * 128:(m + 1) * 128],
                           lambda k: s_a1[:, k, :], 12, after=aft)
            pslast[id(pb)] = p.op("vector", lambda e, m=m, pb=pb: e.tensor_tensor(
                out=s_r1[:, m, :], in0=pb[:, 0:T], in1=s_x[:, m, :], op=ALU.add),
                after=[tmm])
        tr1 = p.tok("eV")
        p.dma("sync", rearr(o_r1, "(c p) t -> p c t", p=128)[:, :, o:o + T],
              s_r1[:, :, :], after=[tr1], inc="dS")
        p.barrier()
    return p.finalize()


def _emit_wquant(p, wT, qdst, nrc, ncol, wsc, c0, c1, tcst, s_stg, s_wt2,
                 psS, s_acc, s_m1, s_sc128, bcP):
    stages = []
    for c in range(nrc):
        for hcol in range(0, ncol, 2048):
            stages.append((c, hcol, min(2048, ncol - hcol)))
    tz = p.op("vector", lambda e: e.memset(s_acc[:, :], 0.0))
    tld_prev = None
    for (c, hcol, w) in stages:
        tld = p.dma("sync", s_stg[:, 0:w],
                    rearr(wT, "(c p) f -> p c f", p=128)[:, c, hcol:hcol + w],
                    after=[("eV", p.cnt["eV"])], inc="dL")
        p.op("vector", lambda e, w=w: e.tensor_reduce(
            out=s_m1[:, 0:1], in_=s_stg[:, 0:w], axis=AX.X, op=ALU.add,
            apply_absolute_value=True), after=[tld, tz])
        tld_prev = p.op("vector", lambda e: e.tensor_tensor(
            out=s_acc[:, :], in0=s_acc[:, :], in1=s_m1[:, 0:1], op=ALU.add))
    tmm = p.op("tensor", lambda e: e.matmul(
        out=psS[0:1, 0:1], lhsT=p.ones_col[:, :], rhs=s_acc[:, :],
        start=True, stop=True), after=[tld_prev, tcst], inc="eT")
    n = nrc * 128 * ncol
    p.op("vector", lambda e, n=n, c1=c1: e.tensor_single_scalar(
        out=wsc[0:1, c1:c1 + 1], in_=psS[0:1, 0:1], scalar=1.0 / n,
        op=ALU.mult), after=[tmm])
    tm = p.op("vector", lambda e, c1=c1: e.tensor_single_scalar(
        out=wsc[0:1, c1:c1 + 1], in_=wsc[0:1, c1:c1 + 1], scalar=1e-5,
        op=ALU.max))
    ta = p.op("scalar", lambda e, c0=c0, c1=c1: e.activation(
        out=wsc[0:1, c0:c0 + 1], in_=wsc[0:1, c1:c1 + 1], func=AF.Ln),
        after=[tm])
    ta = p.op("scalar", lambda e, c0=c0: e.activation(
        out=wsc[0:1, c0:c0 + 1], in_=wsc[0:1, c0:c0 + 1], func=AF.Exp,
        scale=-1.0))
    tbc = p.op("tensor", lambda e, c0=c0: e.matmul(
        out=bcP[:, 0:1], lhsT=p.ones_row[0:1, :], rhs=wsc[0:1, c0:c0 + 1],
        start=True, stop=True), after=[ta], inc="eT")
    tsc = p.op("vector", lambda e: e.tensor_copy(
        out=s_sc128[:, :], in_=bcP[:, 0:1]), after=[tbc])
    tld_prev = None
    for (c, hcol, w) in stages:
        tld = p.dma("sync", s_stg[:, 0:w],
                    rearr(wT, "(c p) f -> p c f", p=128)[:, c, hcol:hcol + w],
                    after=[("eV", p.cnt["eV"]), ("eA", p.cnt["eA"])], inc="dL")
        p.op("vector", lambda e, w=w: e.tensor_scalar(
            out=s_stg[:, 0:w], in0=s_stg[:, 0:w], scalar1=s_sc128[:, 0:1],
            scalar2=None, op0=ALU.mult), after=[tld, tsc])
        p.op("scalar", lambda e, w=w: e.activation(
            out=s_wt2[:, 0:w], in_=s_stg[:, 0:w], func=AF.Abs),
            after=[p.tok("eV")])
        p.op("vector", lambda e, w=w: e.tensor_single_scalar(
            out=s_wt2[:, 0:w], in_=s_wt2[:, 0:w], scalar=0.5, op=ALU.is_ge),
            after=[p.tok("eA")])
        ts = p.op("scalar", lambda e, w=w: e.activation(
            out=s_stg[:, 0:w], in_=s_stg[:, 0:w], func=AF.Sign),
            after=[p.tok("eV")])
        tld_prev = p.op("vector", lambda e, c=c, hcol=hcol, w=w, qdst=qdst:
                        e.tensor_tensor(out=qdst[:, c, hcol:hcol + w],
                                        in0=s_stg[:, 0:w], in1=s_wt2[:, 0:w],
                                        op=ALU.mult), after=[ts])
    return tld_prev


def _emit_round_clip(p, s_v, s_int, s_cf, s_fx, s_fg, dst_ap, lo, hi, T, after):
    t1 = p.op("vector", lambda e: e.tensor_copy(out=s_int[:, 0:T], in_=s_v[:, 0:T]),
              after=after)
    t2 = p.op("vector", lambda e: e.tensor_copy(out=s_cf[:, 0:T], in_=s_int[:, 0:T]))
    t3 = p.op("vector", lambda e: e.scalar_tensor_tensor(
        out=s_fx[:, 0:T], in0=s_cf[:, 0:T], scalar=-1.0, in1=s_v[:, 0:T],
        op0=ALU.mult, op1=ALU.add), after=[t2])
    t4 = p.op("gpsimd", lambda e: e.tensor_single_scalar(
        out=s_fg[:, 0:T], in_=s_fx[:, 0:T], scalar=0.5, op=ALU.is_gt),
        after=[t3])
    t5 = p.op("vector", lambda e: e.tensor_tensor(
        out=s_cf[:, 0:T], in0=s_cf[:, 0:T], in1=s_fg[:, 0:T], op=ALU.add),
        after=[t4])
    t6 = p.op("gpsimd", lambda e: e.tensor_single_scalar(
        out=s_fg[:, 0:T], in_=s_fx[:, 0:T], scalar=-0.5, op=ALU.is_lt),
        after=[t5])
    t7 = p.op("vector", lambda e: e.tensor_tensor(
        out=s_cf[:, 0:T], in0=s_cf[:, 0:T], in1=s_fg[:, 0:T], op=ALU.subtract),
        after=[t6])
    return p.op("vector", lambda e: e.tensor_scalar(
        out=dst_ap, in0=s_cf[:, 0:T], scalar1=float(hi), scalar2=float(lo),
        op0=ALU.min, op1=ALU.max))


def build_L2b1():
    p = Prog()
    T = 512
    NSUB = SQ // T

    r1_in = p.din("r1", [D, SQ])
    w1T = p.din("w1T", [D, FF])
    b1 = p.din("b1", [128, 32])
    ident = p.din("ident", [128, 128])
    o_h1 = p.dout("h1", [FF, SQ], BF16)

    s_w1q = p.sbuf("s_w1q", [128, 8, FF], BF16)
    s_stg = p.sbuf("s_stg", [128, 2048])
    s_wt2 = p.sbuf("s_wt2", [128, 2048])
    s_int = p.sbuf("s_int", [128, T], I32)
    s_id = p.sbuf("s_id", [128, 128])
    s_acc = p.sbuf("s_acc", [128, 1])
    s_wsc = p.sbuf("s_wsc", [1, 8])
    s_sc128 = p.sbuf("s_sc128", [128, 1])
    s_b1 = p.sbuf("s_b1", [128, 32])
    s_r1 = p.sbuf("s_r1", [128, 8, T])
    s_xm = p.sbuf("s_xm", [128, 8, T])
    s_q1 = p.sbuf("s_q1", [128, 8, T], BF16)
    s_h1 = p.sbuf("s_h1", [128, 32, T], BF16)
    s_m1 = p.sbuf("s_m1", [128, T])
    s_sq = p.sbuf("s_sq", [128, T])
    s_v = p.sbuf("s_v", [128, T])
    s_cf = p.sbuf("s_cf", [128, T])
    s_fx = p.sbuf("s_fx", [128, T])
    s_fg = p.sbuf("s_fg", [128, T])
    s_mxc = p.sbuf("s_mxc", [128, 1])
    s_bm = p.sbuf("s_bm", [128, T])
    s_bc = p.sbuf("s_bc", [128, T])
    s_be = p.sbuf("s_be", [128, T])
    ROW = p.sbuf("ROWS", [1, 10 * T])
    psS = p.psum("psS", [1, T])
    psQ = p.psum("psQ", [1, T])
    bcP = p.psum("bcP", [128, T])
    ps0 = p.psum("ps0", [128, T])
    ps1 = p.psum("ps1", [128, T])
    psT = p.psum("psT", [128, 128])

    r_mean = ROW[0:1, 0 * T:1 * T]
    r_rstd = ROW[0:1, 1 * T:2 * T]
    r_var = ROW[0:1, 2 * T:3 * T]
    r_r2a = ROW[0:1, 3 * T:4 * T]
    r_mx = ROW[0:1, 4 * T:4 * T + 128]
    r_mq = ROW[0:1, 5 * T:6 * T]
    r_sx = ROW[0:1, 6 * T:7 * T]
    r_esc = ROW[0:1, 7 * T:8 * T]
    r_csc = ROW[0:1, 8 * T:9 * T]

    tcst = common_consts(p)
    tid = p.dma("sync", s_id[:, :], ident.ap(), inc="dW")
    tb1 = p.dma("sync", s_b1[:, :], b1.ap(), inc="dW")
    twsc0 = p.op("vector", lambda e: e.memset(s_wsc[:, :], 0.0))
    tq = _emit_wquant(p, w1T, s_w1q, 8, FF, s_wsc, 0, 1, tcst, s_stg, s_wt2,
                      psS, s_acc, s_m1, s_sc128, bcP)
    p.barrier()

    for it in range(NSUB):
        o = it * T
        tld = p.dma("sync", s_r1[:, :, :],
                    rearr(r1_in, "(c p) t -> p c t", p=128)[:, :, o:o + T],
                    inc="dL")
        trs = emit_ln_stats(p, lambda i: s_r1[:, i, :], 8, T, psS, psQ, s_sq,
                            r_mean, r_rstd, [tld, tcst], D, EPS, row_var=r_var)
        p.op("vector", lambda e: e.tensor_tensor(
            out=r_r2a, in0=r_var, in1=r_rstd, op=ALU.mult), after=[trs])
        p.op("vector", lambda e: e.tensor_tensor(
            out=r_r2a, in0=r_r2a, in1=r_rstd, op=ALU.mult))
        t1 = p.op("vector", lambda e: e.tensor_single_scalar(
            out=r_r2a, in_=r_r2a, scalar=EPS, op=ALU.add))
        p.op("scalar", lambda e: e.activation(out=r_r2a, in_=r_r2a, func=AF.Ln),
             after=[t1])
        tr2 = p.op("scalar", lambda e: e.activation(out=r_r2a, in_=r_r2a,
                                                    func=AF.Exp, scale=-0.5))
        tbc = emit_bcast(p, r_mean, bcP[:, 0:T], [trs])
        tbm = p.op("vector", lambda e: e.tensor_copy(out=s_bm[:, 0:T],
                                                     in_=bcP[:, 0:T]), after=[tbc])
        for i in range(8):
            p.op("vector", lambda e, i=i: e.tensor_tensor(
                out=s_xm[:, i, :], in0=s_r1[:, i, :], in1=s_bm[:, 0:T],
                op=ALU.subtract), after=[tbm])
        txm = p.tok("eV")
        ta0 = p.op("scalar", lambda e: e.activation(
            out=s_m1[:, 0:T], in_=s_xm[:, 0, :], func=AF.Abs), after=[txm])
        for i in range(1, 8):
            tai = p.op("scalar", lambda e, i=i: e.activation(
                out=s_fx[:, 0:T], in_=s_xm[:, i, :], func=AF.Abs),
                after=[("eV", p.cnt["eV"])])
            p.op("vector", lambda e: e.tensor_tensor(
                out=s_m1[:, 0:T], in0=s_m1[:, 0:T], in1=s_fx[:, 0:T],
                op=ALU.max), after=[tai, ta0])
        tmx = p.tok("eV")
        tmxr = None
        for tb_ in range(T // 128):
            ttr = p.op("tensor", lambda e, tb_=tb_: e.transpose(
                out=psT[:, :], in_=s_m1[:, tb_ * 128:(tb_ + 1) * 128],
                identity=s_id[:, :]),
                after=[tmx, tid] + ([tmxr] if tmxr else []), inc="eT")
            trd = p.op("vector", lambda e: e.tensor_reduce(
                out=s_mxc[:, 0:1], in_=psT[:, :], axis=AX.X, op=ALU.max),
                after=[ttr])
            ttr2 = p.op("tensor", lambda e: e.transpose(
                out=psT[0:1, 0:128], in_=s_mxc[:, 0:1], identity=s_id[:, :]),
                after=[trd], inc="eT")
            tmxr = p.op("vector", lambda e, tb_=tb_: e.tensor_copy(
                out=ROW[0:1, 5 * T + tb_ * 128:5 * T + (tb_ + 1) * 128],
                in_=psT[0:1, 0:128]), after=[ttr2])
        p.op("vector", lambda e: e.tensor_tensor(
            out=r_mq, in0=r_mq, in1=r_rstd, op=ALU.mult), after=[tmxr, tr2])
        p.op("vector", lambda e: e.tensor_tensor(
            out=r_mq, in0=r_mq, in1=r_r2a, op=ALU.mult))
        tq1 = p.op("vector", lambda e: e.tensor_single_scalar(
            out=r_mq, in_=r_mq, scalar=1e-5, op=ALU.max))
        p.op("scalar", lambda e: e.activation(out=r_sx, in_=r_mq, func=AF.Ln),
             after=[tq1])
        ta = p.op("scalar", lambda e: e.activation(out=r_sx, in_=r_sx,
                                                   func=AF.Exp, scale=-1.0))
        p.op("vector", lambda e: e.tensor_single_scalar(
            out=r_sx, in_=r_sx, scalar=127.0, op=ALU.mult), after=[ta])
        p.op("vector", lambda e: e.tensor_single_scalar(
            out=r_esc, in_=r_mq, scalar=1.0 / 127.0, op=ALU.mult))
        p.op("vector", lambda e: e.tensor_scalar(
            out=r_esc, in0=r_esc, scalar1=s_wsc[0:1, 1:2], scalar2=None,
            op0=ALU.mult))
        p.op("vector", lambda e: e.tensor_tensor(
            out=r_csc, in0=r_rstd, in1=r_r2a, op=ALU.mult))
        tcsc = p.op("vector", lambda e: e.tensor_tensor(
            out=r_csc, in0=r_csc, in1=r_sx, op=ALU.mult))
        tbc1 = emit_bcast(p, r_csc, bcP[:, 0:T], [tcsc, tbm])
        tb_c = p.op("vector", lambda e: e.tensor_copy(out=s_bc[:, 0:T],
                                                      in_=bcP[:, 0:T]), after=[tbc1])
        tbc2 = emit_bcast(p, r_esc, bcP[:, 0:T], [tb_c])
        tb_e = p.op("vector", lambda e: e.tensor_copy(out=s_be[:, 0:T],
                                                      in_=bcP[:, 0:T]), after=[tbc2])
        for i in range(8):
            tv = p.op("vector", lambda e, i=i: e.tensor_tensor(
                out=s_v[:, 0:T], in0=s_xm[:, i, :], in1=s_bc[:, 0:T],
                op=ALU.mult), after=[tb_c])
            _emit_round_clip(p, s_v, s_int, s_cf, s_fx, s_fg, s_q1[:, i, :],
                             -128.0, 127.0, T, [tv])
        tq1b = p.tok("eV")
        pslast = {}
        for m in range(32):
            pb = ps0 if (m % 2 == 0) else ps1
            aft = [tq1b, tq] + ([pslast[id(pb)]] if id(pb) in pslast else [])
            tmm = mm_group(p, pb[:, 0:T],
                           lambda k, m=m: s_w1q[:, k, m * 128:(m + 1) * 128],
                           lambda k: s_q1[:, k, :], 8, after=aft)
            tt = p.op("vector", lambda e, m=m, pb=pb: e.tensor_tensor(
                out=s_sq[:, 0:T], in0=pb[:, 0:T], in1=s_be[:, 0:T],
                op=ALU.mult), after=[tmm, tb_e, ("eA", p.cnt["eA"])])
            pslast[id(pb)] = tt
            p.op("scalar", lambda e, m=m: e.activation(
                out=s_h1[:, m, :], in_=s_sq[:, 0:T], func=AF.Gelu,
                bias=s_b1[:, m:m + 1]), after=[tt, tb1])
        th1 = p.tok("eA")
        p.dma("sync", rearr(o_h1, "(c p) t -> p c t", p=128)[:, :, o:o + T],
              s_h1[:, :, :], after=[th1], inc="dS")
        p.barrier()
    return p.finalize()


def build_L2b2():
    p = Prog()
    T = 512
    NSUB = SQ // T

    h1_in = p.din("h1", [FF, SQ], BF16)
    r1_in = p.din("r1", [D, SQ])
    w2T = p.din("w2T", [FF, D])
    b2 = p.din("b2", [128, 8])
    ident = p.din("ident", [128, 128])
    o_r2 = p.dout("r2", [D, SQ])

    s_w2q = p.sbuf("s_w2q", [128, 32, D], BF16)
    s_stg = p.sbuf("s_stg", [128, 1024])
    s_wt2 = p.sbuf("s_wt2", [128, 1024])
    s_int = p.sbuf("s_int", [128, T], I32)
    s_id = p.sbuf("s_id", [128, 128])
    s_acc = p.sbuf("s_acc", [128, 1])
    s_wsc = p.sbuf("s_wsc", [1, 8])
    s_sc128 = p.sbuf("s_sc128", [128, 1])
    s_b2 = p.sbuf("s_b2", [128, 8])
    s_h1 = p.sbuf("s_h1", [128, 32, T], BF16)
    s_q2 = p.sbuf("s_q2", [128, 32, T], BF16)
    s_r1 = p.sbuf("s_r1", [128, 8, T])
    s_r2 = p.sbuf("s_r2", [128, 8, T])
    s_m1 = p.sbuf("s_m1", [128, T])
    s_sq = p.sbuf("s_sq", [128, T])
    s_v = p.sbuf("s_v", [128, T])
    s_cf = p.sbuf("s_cf", [128, T])
    s_fx = p.sbuf("s_fx", [128, T])
    s_fg = p.sbuf("s_fg", [128, T])
    s_mxc = p.sbuf("s_mxc", [128, 1])
    s_bc = p.sbuf("s_bc", [128, T])
    s_be = p.sbuf("s_be", [128, T])
    ROW = p.sbuf("ROWS", [1, 10 * T])
    psS = p.psum("psS", [1, T])
    psQ = p.psum("psQ", [1, T])
    bcP = p.psum("bcP", [128, T])
    ps0 = p.psum("ps0", [128, T])
    ps1 = p.psum("ps1", [128, T])
    psT = p.psum("psT", [128, 128])

    r_r2a = ROW[0:1, 3 * T:4 * T]
    r_mq = ROW[0:1, 5 * T:6 * T]
    r_sx = ROW[0:1, 6 * T:7 * T]
    r_esc = ROW[0:1, 7 * T:8 * T]
    r_csc = ROW[0:1, 8 * T:9 * T]

    tcst = common_consts(p)
    tid = p.dma("sync", s_id[:, :], ident.ap(), inc="dW")
    tb2 = p.dma("sync", s_b2[:, :], b2.ap(), inc="dW")
    twsc0 = p.op("vector", lambda e: e.memset(s_wsc[:, :], 0.0))
    tq = _emit_wquant(p, w2T, s_w2q, 32, D, s_wsc, 2, 3, tcst, s_stg, s_wt2,
                      psS, s_acc, s_m1, s_sc128, bcP)
    p.barrier()

    for it in range(NSUB):
        o = it * T
        p.dma("sync", s_h1[:, :, :],
              rearr(h1_in, "(c p) t -> p c t", p=128)[:, :, o:o + T], inc="dL")
        tld = p.dma("sync", s_r1[:, :, :],
                    rearr(r1_in, "(c p) t -> p c t", p=128)[:, :, o:o + T],
                    inc="dL")
        last_mm = None
        for i in range(32):
            aft = [tld, tcst] + ([last_mm] if last_mm else [])
            tqq = p.op("scalar", lambda e, i=i: e.activation(
                out=s_sq[:, 0:T], in_=s_h1[:, i, :], func=AF.Square), after=aft)
            last_mm = p.op("tensor", lambda e, i=i: e.matmul(
                out=psQ[0:1, 0:T], lhsT=p.ones_col[:, :], rhs=s_sq[:, 0:T],
                start=(i == 0), stop=(i == 31)), after=[tqq], inc="eT")
        tQ = p.tok("eT")
        ta0 = p.op("scalar", lambda e: e.activation(
            out=s_m1[:, 0:T], in_=s_h1[:, 0, :], func=AF.Abs), after=[tld])
        for i in range(1, 32):
            tai = p.op("scalar", lambda e, i=i: e.activation(
                out=s_fx[:, 0:T], in_=s_h1[:, i, :], func=AF.Abs),
                after=[("eV", p.cnt["eV"])])
            p.op("vector", lambda e: e.tensor_tensor(
                out=s_m1[:, 0:T], in0=s_m1[:, 0:T], in1=s_fx[:, 0:T],
                op=ALU.max), after=[tai, ta0])
        tmx = p.tok("eV")
        t1 = p.op("vector", lambda e: e.tensor_single_scalar(
            out=r_r2a, in_=psQ[0:1, 0:T], scalar=1.0 / FF, op=ALU.mult),
            after=[tQ])
        t1 = p.op("vector", lambda e: e.tensor_single_scalar(
            out=r_r2a, in_=r_r2a, scalar=EPS, op=ALU.add))
        p.op("scalar", lambda e: e.activation(out=r_r2a, in_=r_r2a, func=AF.Ln),
             after=[t1])
        trh = p.op("scalar", lambda e: e.activation(out=r_r2a, in_=r_r2a,
                                                    func=AF.Exp, scale=-0.5))
        tmxr = None
        for tb_ in range(T // 128):
            ttr = p.op("tensor", lambda e, tb_=tb_: e.transpose(
                out=psT[:, :], in_=s_m1[:, tb_ * 128:(tb_ + 1) * 128],
                identity=s_id[:, :]),
                after=[tmx, tid] + ([tmxr] if tmxr else []), inc="eT")
            trd = p.op("vector", lambda e: e.tensor_reduce(
                out=s_mxc[:, 0:1], in_=psT[:, :], axis=AX.X, op=ALU.max),
                after=[ttr])
            ttr2 = p.op("tensor", lambda e: e.transpose(
                out=psT[0:1, 0:128], in_=s_mxc[:, 0:1], identity=s_id[:, :]),
                after=[trd], inc="eT")
            tmxr = p.op("vector", lambda e, tb_=tb_: e.tensor_copy(
                out=ROW[0:1, 5 * T + tb_ * 128:5 * T + (tb_ + 1) * 128],
                in_=psT[0:1, 0:128]), after=[ttr2])
        p.op("vector", lambda e: e.tensor_tensor(
            out=r_mq, in0=r_mq, in1=r_r2a, op=ALU.mult), after=[tmxr, trh])
        tq1_ = p.op("vector", lambda e: e.tensor_single_scalar(
            out=r_mq, in_=r_mq, scalar=1e-5, op=ALU.max))
        p.op("scalar", lambda e: e.activation(out=r_sx, in_=r_mq, func=AF.Ln),
             after=[tq1_])
        ta = p.op("scalar", lambda e: e.activation(out=r_sx, in_=r_sx,
                                                   func=AF.Exp, scale=-1.0))
        p.op("vector", lambda e: e.tensor_single_scalar(
            out=r_sx, in_=r_sx, scalar=127.0, op=ALU.mult), after=[ta])
        p.op("vector", lambda e: e.tensor_single_scalar(
            out=r_esc, in_=r_mq, scalar=1.0 / 127.0, op=ALU.mult))
        p.op("vector", lambda e: e.tensor_scalar(
            out=r_esc, in0=r_esc, scalar1=s_wsc[0:1, 3:4], scalar2=None,
            op0=ALU.mult))
        tcsc = p.op("vector", lambda e: e.tensor_tensor(
            out=r_csc, in0=r_r2a, in1=r_sx, op=ALU.mult))
        tbc1 = emit_bcast(p, r_csc, bcP[:, 0:T], [tcsc])
        tb_c = p.op("vector", lambda e: e.tensor_copy(out=s_bc[:, 0:T],
                                                      in_=bcP[:, 0:T]), after=[tbc1])
        tbc2 = emit_bcast(p, r_esc, bcP[:, 0:T], [tb_c])
        tb_e = p.op("vector", lambda e: e.tensor_copy(out=s_be[:, 0:T],
                                                      in_=bcP[:, 0:T]), after=[tbc2])
        for i in range(32):
            tv = p.op("vector", lambda e, i=i: e.tensor_tensor(
                out=s_v[:, 0:T], in0=s_h1[:, i, :], in1=s_bc[:, 0:T],
                op=ALU.mult), after=[tb_c])
            _emit_round_clip(p, s_v, s_int, s_cf, s_fx, s_fg, s_q2[:, i, :],
                             -128.0, 127.0, T, [tv])
        tq2b = p.tok("eV")
        pslast = {}
        for m in range(8):
            pb = ps0 if (m % 2 == 0) else ps1
            aft = [tq2b, tq] + ([pslast[id(pb)]] if id(pb) in pslast else [])
            tmm = mm_group(p, pb[:, 0:T],
                           lambda k, m=m: s_w2q[:, k, m * 128:(m + 1) * 128],
                           lambda k: s_q2[:, k, :], 32, after=aft)
            tt = p.op("vector", lambda e, m=m, pb=pb: e.tensor_tensor(
                out=s_r2[:, m, :], in0=pb[:, 0:T], in1=s_be[:, 0:T],
                op=ALU.mult), after=[tmm, tb_e])
            pslast[id(pb)] = tt
            tb_ = p.op("scalar", lambda e, m=m: e.activation(
                out=s_r2[:, m, :], in_=s_r2[:, m, :], func=AF.Identity,
                bias=s_b2[:, m:m + 1]), after=[tt, tb2])
            p.op("vector", lambda e, m=m: e.tensor_tensor(
                out=s_r2[:, m, :], in0=s_r2[:, m, :], in1=s_r1[:, m, :],
                op=ALU.add), after=[tb_])
        tr2o = p.tok("eV")
        p.dma("sync", rearr(o_r2, "(c p) t -> p c t", p=128)[:, :, o:o + T],
              s_r2[:, :, :], after=[tr2o], inc="dS")
        p.barrier()
    return p.finalize()


def build_L3():
    p = Prog()
    T = 512
    NSUB = SQ // T

    r2h = p.din("r2h", [D, SQ + 3])
    ipjT = p.din("ipjT", [D, 2 * D])
    ipb = p.din("ipb", [128, 16])
    cw1 = p.din("cw1", [128, 32])
    cb1 = p.din("cb1", [128, 8])
    bcwT = p.din("bcwT", [D, 64])
    bcb = p.din("bcb", [64, 1])
    dtwT = p.din("dtwT", [D, D])
    dtb = p.din("dtb", [128, 8])
    hm = p.din("hmask", [128, 3])
    o_dt = p.dout("dt", [D, SQ])
    o_dx = p.dout("dtxs", [D, SQ])
    o_bc = p.dout("bc", [64, SQ])
    o_sz = p.dout("siluz", [D, SQ], BF16)

    s_ipj = p.sbuf("s_ipj", [128, 8, 2 * D], BF16)
    s_dtw = p.sbuf("s_dtw", [128, 8, D], BF16)
    s_bcw = p.sbuf("s_bcw", [128, 8, 64], BF16)
    s_ipb = p.sbuf("s_ipb", [128, 16])
    s_cw = p.sbuf("s_cw", [128, 8, 4])
    s_cb = p.sbuf("s_cb", [128, 8])
    s_dtb = p.sbuf("s_dtb", [128, 8])
    s_bcb = p.sbuf("s_bcb", [64, 1])
    s_hm = p.sbuf("s_hm", [128, 3])
    tcst = common_consts(p)
    s_x = p.sbuf("s_x", [128, 8, T])
    s_ln = p.sbuf("s_ln", [128, 8, T], BF16)
    s_sq = p.sbuf("s_sq", [128, T])
    s_row = p.sbuf("s_row2", [1, 2 * T])
    s_xc = p.sbuf("s_xc", [128, 8, T + 3])
    s_z = p.sbuf("s_z", [128, 8, T], BF16)
    s_xs = p.sbuf("s_xs", [128, 8, T])
    s_xsb = p.sbuf("s_xsb", [128, 8, T], BF16)
    s_dt = p.sbuf("s_dt", [128, 8, T])
    s_dx = p.sbuf("s_dx", [128, 8, T])
    s_sp = p.sbuf("s_sp", [128, T])
    s_bc64 = p.sbuf("s_bc64", [128, T])
    psS = p.psum("psS", [1, T])
    psQ = p.psum("psQ", [1, T])
    ps_bm = p.psum("ps_bm", [128, T])
    ps_br = p.psum("ps_br", [128, T])
    ps0 = p.psum("ps0", [128, T])
    ps1 = p.psum("ps1", [128, T])

    tw1 = p.dma("gpsimd", s_ipj[:, :, :], rearr(ipjT, "(c p) f -> p c f", p=128), inc="dG")
    tw2 = p.dma("gpsimd", s_dtw[:, :, :], rearr(dtwT, "(c p) f -> p c f", p=128), inc="dG")
    tw3 = p.dma("gpsimd", s_bcw[:, :, :], rearr(bcwT, "(c p) f -> p c f", p=128), inc="dG")
    tv = p.dma("sync", s_ipb[:, :], ipb.ap(), inc="dW")
    tv = p.dma("sync", s_cw[:, :, :], rearr(cw1, "p (c k) -> p c k", k=4), inc="dW")
    tv = p.dma("sync", s_cb[:, :], cb1.ap(), inc="dW")
    tv = p.dma("sync", s_dtb[:, :], dtb.ap(), inc="dW")
    tv = p.dma("sync", s_bcb[:, :], bcb.ap(), inc="dW")
    tv = p.dma("sync", s_hm[:, :], hm.ap(), inc="dW")
    twv = p.tok("dW")

    def ln_block(width, tx):
        trs = emit_ln_stats(p, lambda i: s_x[:, i, 0:width], 8, width, psS, psQ,
                            s_sq, s_row[0:1, 0:width], s_row[0:1, T:T + width],
                            [tx, tcst], D, EPS)
        tb1 = emit_bcast(p, s_row[0:1, 0:width], ps_bm[:, 0:width], [trs])
        tb2 = emit_bcast(p, s_row[0:1, T:T + width], ps_br[:, 0:width], [trs])
        for i in range(8):
            p.op("vector", lambda e, i=i, width=width: e.tensor_tensor(
                out=s_sq[:, 0:width], in0=s_x[:, i, 0:width], in1=ps_bm[:, 0:width],
                op=ALU.subtract), after=[tb1, tb2])
            p.op("vector", lambda e, i=i, width=width: e.tensor_tensor(
                out=s_ln[:, i, 0:width], in0=s_sq[:, 0:width], in1=ps_br[:, 0:width],
                op=ALU.mult))
        return p.tok("eV")

    def inproj(width, tln, z_dst_w):
        pslast = {}
        for m in range(16):
            pb = ps0 if (m % 2 == 0) else ps1
            aft = [tln, tw1, twv] + ([pslast[id(pb)]] if id(pb) in pslast else [])
            tmm = mm_group(p, pb[:, 0:width],
                           lambda k, m=m: s_ipj[:, k, m * 128:(m + 1) * 128],
                           lambda k, width=width: s_ln[:, k, 0:width], 8, after=aft)
            if m < 8:
                pslast[id(pb)] = p.op("scalar", lambda e, m=m, pb=pb, width=width:
                                      e.activation(out=s_xc[:, m, 3:3 + width],
                                                   in_=pb[:, 0:width],
                                                   func=AF.Identity,
                                                   bias=s_ipb[:, m:m + 1]),
                                      after=[tmm])
            else:
                pslast[id(pb)] = p.op("scalar", lambda e, m=m, pb=pb, width=width:
                                      e.activation(out=s_z[:, m - 8, 0:width],
                                                   in_=pb[:, 0:width], func=AF.Silu,
                                                   bias=s_ipb[:, 8 + (m - 8):9 + (m - 8)]),
                                      after=[tmm])
        return p.tok("eA")

    tx = p.dma("sync", s_x[:, :, 0:3],
               rearr(r2h, "(c p) t -> p c t", p=128)[:, :, 0:3], inc="dL")
    tln = ln_block(3, tx)
    tz = inproj(3, tln, 0)
    for i in range(8):
        p.op("vector", lambda e, i=i: e.tensor_tensor(
            out=s_xc[:, i, 3:5], in0=s_xc[:, i, 3:5], in1=s_hm[:, 0:2],
            op=ALU.mult), after=[tz])
    for i in range(8):
        p.op("vector", lambda e, i=i: e.tensor_copy(
            out=s_xc[:, i, 0:3], in_=s_xc[:, i, 3:6]))
    tcar = p.tok("eV")
    p.dma("sync", rearr(o_sz, "(c p) t -> p c t", p=128)[:, :, 0:1],
          s_z[:, :, 2:3], after=[tz], inc="dS", slow_ok=True)
    p.barrier()

    for it in range(NSUB):
        o = it * T
        last = (it == NSUB - 1)
        tx = p.dma("sync", s_x[:, :, :],
                   rearr(r2h, "(c p) t -> p c t", p=128)[:, :, o + 3:o + T + 3],
                   inc="dL")
        tln = ln_block(T, tx)
        tz = inproj(T, tln, T)
        if last:
            for i in range(8):
                p.op("vector", lambda e, i=i: e.tensor_tensor(
                    out=s_xc[:, i, T + 2:T + 3], in0=s_xc[:, i, T + 2:T + 3],
                    in1=s_hm[:, 2:3], op=ALU.mult), after=[tz])
        tmask = p.tok("eV")
        w = T if not last else T - 1
        p.dma("sync", rearr(o_sz, "(c p) t -> p c t", p=128)[:, :, o + 1:o + 1 + w],
              s_z[:, :, 0:w], after=[tz], inc="dS")
        for i in range(8):
            p.op("vector", lambda e, i=i: e.tensor_scalar(
                out=s_xs[:, i, :], in0=s_xc[:, i, 0:T], scalar1=s_cw[:, i, 0:1],
                scalar2=None, op0=ALU.mult), after=[tz, tmask, twv])
            for j in (1, 2, 3):
                p.op("vector", lambda e, i=i, j=j: e.scalar_tensor_tensor(
                    out=s_xs[:, i, :], in0=s_xc[:, i, j:j + T],
                    scalar=s_cw[:, i, j:j + 1], in1=s_xs[:, i, :],
                    op0=ALU.mult, op1=ALU.add))
            p.op("scalar", lambda e, i=i: e.activation(
                out=s_xs[:, i, :], in_=s_xs[:, i, :], func=AF.Silu,
                bias=s_cb[:, i:i + 1]), after=[p.tok("eV")])
            p.op("vector", lambda e, i=i: e.tensor_copy(
                out=s_xsb[:, i, :], in_=s_xs[:, i, :]), after=[p.tok("eA")])
            p.op("vector", lambda e, i=i: e.tensor_copy(
                out=s_xc[:, i, 0:3], in_=s_xc[:, i, T:T + 3]))
        txs = p.tok("eV")
        tmm = mm_group(p, ps0[0:64, 0:T],
                       lambda k: s_bcw[:, k, :],
                       lambda k: s_xsb[:, k, :], 8, after=[txs, tw3])
        tbc_ = p.op("scalar", lambda e: e.activation(
            out=s_bc64[0:64, 0:T], in_=ps0[0:64, 0:T], func=AF.Identity,
            bias=s_bcb[0:64, 0:1]), after=[tmm])
        p.dma("sync", o_bc[:, o:o + T], s_bc64[0:64, 0:T],
              after=[tbc_], inc="dS")
        pslast = {id(ps0): tbc_}
        tsp_rd = None
        for m in range(8):
            pb = ps1 if (m % 2 == 0) else ps0
            aft = [txs, tw2] + ([pslast[id(pb)]] if id(pb) in pslast else [])
            tmm = mm_group(p, pb[:, 0:T],
                           lambda k, m=m: s_dtw[:, k, m * 128:(m + 1) * 128],
                           lambda k: s_xsb[:, k, :], 8, after=aft)
            ta = p.op("scalar", lambda e, m=m, pb=pb: e.activation(
                out=s_sp[:, 0:T], in_=pb[:, 0:T], func=AF.Abs,
                bias=s_dtb[:, m:m + 1]),
                after=[tmm] + ([tsp_rd] if tsp_rd else []))
            p.op("scalar", lambda e: e.activation(
                out=s_sp[:, 0:T], in_=s_sp[:, 0:T], func=AF.Exp, scale=-1.0))
            ta2 = p.op("scalar", lambda e: e.activation(
                out=s_sp[:, 0:T], in_=s_sp[:, 0:T], func=AF.Ln, bias=1.0))
            trl = p.op("scalar", lambda e, m=m, pb=pb: e.activation(
                out=s_dt[:, m, :], in_=pb[:, 0:T], func=AF.Relu,
                bias=s_dtb[:, m:m + 1]))
            pslast[id(pb)] = trl
            tsp_rd = p.op("vector", lambda e, m=m: e.tensor_tensor(
                out=s_dt[:, m, :], in0=s_dt[:, m, :], in1=s_sp[:, 0:T],
                op=ALU.add), after=[trl, ta2])
            p.op("vector", lambda e, m=m: e.tensor_scalar(
                out=s_dt[:, m, :], in0=s_dt[:, m, :], scalar1=0.2,
                scalar2=0.001, op0=ALU.min, op1=ALU.max))
            p.op("vector", lambda e, m=m: e.tensor_tensor(
                out=s_dx[:, m, :], in0=s_dt[:, m, :], in1=s_xs[:, m, :],
                op=ALU.mult))
        tdt = p.tok("eV")
        p.dma("sync", rearr(o_dt, "(c p) t -> p c t", p=128)[:, :, o:o + T],
              s_dt[:, :, :], after=[tdt], inc="dS")
        p.dma("sync", rearr(o_dx, "(c p) t -> p c t", p=128)[:, :, o:o + T],
              s_dx[:, :, :], after=[tdt], inc="dS")
        p.barrier()
    return p.finalize()


def build_L4():
    p = Prog()
    TC = 1024
    NCH = S // TC

    dtd = p.din("dtd", [DQ, S])
    dxd = p.din("dxd", [DQ, S])
    bnm = p.din("bnm", [NS, S])
    cnm = p.din("cnm", [NS, S])
    alog = p.din("alog", [DQ, NS])
    identb = p.din("identb", [128, 128], BF16)
    onesb = p.din("onesb", [1, 128], BF16)
    o_y = p.dout("ydm", [DQ, S])

    s_A = p.sbuf("s_A", [128, 2, NS])
    s_idb = p.sbuf("s_idb", [128, 128], BF16)
    s_onb = p.sbuf("s_onb", [1, 128], BF16)
    s_dt = p.sbuf("s_dt", [128, 2, TC])
    s_dx = p.sbuf("s_dx", [128, 2, TC])
    s_bn = p.sbuf("s_bn", [1, NS, TC], BF16)
    s_cn = p.sbuf("s_cn", [1, NS, TC], BF16)
    s_ab = p.sbuf("s_ab", [128, 2, 2, TC])
    s_u = p.sbuf("s_u", [128, 2, TC])
    s_s = p.sbuf("s_s", [128, 2, TC])
    s_pr = p.sbuf("s_pr", [128, 2, TC], BF16)
    s_cy = p.sbuf("s_cy", [128, 64])
    s_y = p.sbuf("s_y", [128, 2, TC])
    sbC = p.sbuf("sbC", [128, 2, TC], BF16)
    psB = p.psum("psB", [128, TC])
    psC = p.psum("psC", [128, TC])
    psY0 = p.psum("psY0", [128, TC])
    psY1 = p.psum("psY1", [128, TC])
    psYs = (psY0, psY1)

    tcst = common_consts(p)
    tidb = p.dma("sync", s_idb[:, :], identb.ap(), inc="dW")
    tonb = p.dma("sync", s_onb[:, :], onesb.ap(), inc="dW")
    tA = p.dma("sync", s_A[:, :, :], rearr(alog, "(c p) n -> p c n", p=128), inc="dW")
    t1 = p.op("vector", lambda e: e.tensor_scalar(
        out=s_A[:, :, :], in0=s_A[:, :, :], scalar1=2.0, scalar2=-5.0,
        op0=ALU.min, op1=ALU.max), after=[tA])
    t2 = p.op("scalar", lambda e: e.activation(
        out=s_A[:, :, :], in_=s_A[:, :, :], func=AF.Exp), after=[t1])
    tAr = p.op("vector", lambda e: e.tensor_single_scalar(
        out=s_A[:, :, :], in_=s_A[:, :, :], scalar=-1.0, op=ALU.mult), after=[t2])
    tcy = p.op("vector", lambda e: e.memset(s_cy[:, :], 0.0))

    for it in range(NCH):
        o = it * TC
        p.dma("sync", s_dt[:, :, :],
              rearr(dtd, "(c p) t -> p c t", p=128)[:, :, o:o + TC], inc="dL")
        p.dma("sync", s_dx[:, :, :],
              rearr(dxd, "(c p) t -> p c t", p=128)[:, :, o:o + TC], inc="dL")
        p.dma("gpsimd", s_bn[0:1, :, :], bnm[0:NS, o:o + TC], inc="dG")
        tldG = p.dma("gpsimd", s_cn[0:1, :, :], cnm[0:NS, o:o + TC], inc="dG")
        tld = p.tok("dL")
        ab_rd = {}
        tB_rd = {}
        tC_rd = {}
        tsbC_rd = {}
        tu_rd = None
        ts_rd = {0: None, 1: None}
        tpr_rd = {0: None, 1: None}
        for n in range(NS):
            par = n % 2
            tabs = []
            for c in range(2):
                tabs.append(p.op("scalar", lambda e, c=c, n=n, par=par:
                    e.activation(out=s_ab[:, par, c, :], in_=s_dt[:, c, :],
                                 func=AF.Exp, scale=s_A[:, c, n:n + 1]),
                    after=[tld, tAr] + ([ab_rd[par]] if par in ab_rd else [])))
            for h in range(2):
                tbc = p.op("tensor", lambda e, n=n, h=h: e.matmul(
                    out=psB[:, h * 512:(h + 1) * 512], lhsT=s_onb[0:1, :],
                    rhs=s_bn[0:1, n, h * 512:(h + 1) * 512],
                    start=True, stop=True),
                    after=[tldG, tonb, tcst] +
                          ([tB_rd.get(0)] if tB_rd.get(0) else []),
                    inc="eT")
            for h in range(2):
                tcc = p.op("tensor", lambda e, n=n, h=h: e.matmul(
                    out=psC[:, h * 512:(h + 1) * 512], lhsT=s_onb[0:1, :],
                    rhs=s_cn[0:1, n, h * 512:(h + 1) * 512],
                    start=True, stop=True),
                    after=[tldG] + ([tC_rd.get(0)] if tC_rd.get(0) else []),
                    inc="eT")
            tCcp = p.op("scalar", lambda e, par=par: e.copy(
                out=sbC[:, par, :], in_=psC[:, 0:TC]),
                after=[tcc] + ([tsbC_rd[par]] if par in tsbC_rd else []))
            tC_rd[0] = tCcp
            tcl = {}
            for c in range(2):
                tu = p.op("vector", lambda e, c=c: e.tensor_tensor(
                    out=s_u[:, c, :], in0=s_dx[:, c, :], in1=psB[:, 0:TC],
                    op=ALU.mult),
                    after=[tbc, tld] + ([tu_rd] if tu_rd else []))
                tB_rd[0] = tu
                ts = p.op("vector", lambda e, c=c, n=n, par=par: e.tensor_tensor_scan(
                    out=s_s[:, c, :], data0=s_ab[:, par, c, :],
                    data1=s_u[:, c, :],
                    initial=s_cy[:, c * NS + n:c * NS + n + 1],
                    op0=ALU.mult, op1=ALU.add),
                    after=[tabs[c], tu, tcy] + ([ts_rd[c]] if ts_rd[c] else []))
                ab_rd[par] = ts
                tu_rd = ts
                tcl[c] = p.op("vector", lambda e, c=c: e.tensor_scalar(
                    out=s_s[:, c, :], in0=s_s[:, c, :], scalar1=10.0,
                    scalar2=-10.0, op0=ALU.min, op1=ALU.max), after=[ts])
            for c in range(2):
                p.op("scalar", lambda e, c=c, n=n: e.copy(
                    out=s_cy[:, c * NS + n:c * NS + n + 1],
                    in_=s_s[:, c, TC - 1:TC]), after=[tcl[c]])
                tpr = p.op("gpsimd", lambda e, c=c, par=par: e.tensor_tensor(
                    out=s_pr[:, c, :], in0=s_s[:, c, :], in1=sbC[:, par, :],
                    op=ALU.mult),
                    after=[tcl[c], tCcp] + ([tpr_rd[c]] if tpr_rd[c] else []))
                ts_rd[c] = tpr
                tsbC_rd[par] = tpr
                for h in range(2):
                    tpr_rd[c] = p.op("tensor", lambda e, c=c, n=n, h=h: e.matmul(
                        out=psYs[c][:, h * 512:(h + 1) * 512], lhsT=s_idb[:, :],
                        rhs=s_pr[:, c, h * 512:(h + 1) * 512],
                        start=(n == 0), stop=(n == NS - 1)),
                        after=[tpr, tidb], inc="eT")
        tev = None
        for c in range(2):
            tev = p.op("vector", lambda e, c=c: e.tensor_copy(
                out=s_y[:, c, :], in_=psYs[c][:, 0:TC]),
                after=[tpr_rd[0], tpr_rd[1]])
        p.dma("sync", rearr(o_y, "(c p) t -> p c t", p=128)[:, :, o:o + TC],
              s_y[:, :, :], after=[tev], inc="dS")
        p.barrier()
    return p.finalize()


def build_L5():
    p = Prog()
    T = 512
    NSUB = SQ // T

    y_in = p.din("yfm", [D, SQ])
    r2_in = p.din("r2", [D, SQ])
    sz_in = p.din("siluz", [D, SQ], BF16)
    owT = p.din("owT", [D, D])
    ob = p.din("ob", [128, 8])
    lng = p.din("lng", [128, 8])
    lnb = p.din("lnb", [128, 8])
    fwT = p.din("fwT", [D, D])
    o_out = p.dout("outfm", [D, SQ])

    s_ow = p.sbuf("s_ow", [128, 8, D], BF16)
    s_fw = p.sbuf("s_fw", [128, 8, D], BF16)
    s_stg = p.sbuf("s_stg", [128, D])
    s_ob = p.sbuf("s_ob", [128, 8])
    s_g = p.sbuf("s_g", [128, 8])
    s_bt = p.sbuf("s_bt", [128, 8])
    tcst = common_consts(p)
    s_y = p.sbuf("s_y", [128, 8, T])
    s_r2 = p.sbuf("s_r2", [128, 8, T])
    s_sz = p.sbuf("s_sz", [128, 8, T], BF16)
    s_g1 = p.sbuf("s_g1", [128, 8, T], BF16)
    s_o0 = p.sbuf("s_o0", [128, 8, T])
    s_l2 = p.sbuf("s_l2", [128, 8, T])
    s_r3 = p.sbuf("s_r3", [128, 8, T])
    s_l3 = p.sbuf("s_l3", [128, 8, T], BF16)
    s_sq = p.sbuf("s_sq", [128, T])
    s_row = p.sbuf("s_rowL5", [1, 2 * T])
    psS = p.psum("psS", [1, T])
    psQ = p.psum("psQ", [1, T])
    ps_bm = p.psum("ps_bm", [128, T])
    ps_br = p.psum("ps_br", [128, T])
    ps0 = p.psum("ps0", [128, T])
    ps1 = p.psum("ps1", [128, T])

    tow = p.dma("gpsimd", s_ow[:, :, :], rearr(owT, "(c p) f -> p c f", p=128), inc="dG")
    tv = p.dma("sync", s_ob[:, :], ob.ap(), inc="dW")
    tv = p.dma("sync", s_g[:, :], lng.ap(), inc="dW")
    tv = p.dma("sync", s_bt[:, :], lnb.ap(), inc="dW")
    twv = p.tok("dW")
    tsg_prev = None
    for c in range(8):
        tld = p.dma("sync", s_stg[:, :],
                    rearr(fwT, "(c p) f -> p c f", p=128)[:, c, :],
                    after=([tsg_prev] if tsg_prev else []), inc="dL")
        tsg_prev = p.op("scalar", lambda e, c=c: e.activation(
            out=s_fw[:, c, :], in_=s_stg[:, :], func=AF.Sign), after=[tld])
    tfw = tsg_prev
    p.barrier()

    def ln_stats_apply(src, dst, tafter, dst_bf=False, gain=False):
        trs = emit_ln_stats(p, lambda i: src[:, i, :], 8, T, psS, psQ, s_sq,
                            s_row[0:1, 0:T], s_row[0:1, T:2 * T],
                            tafter + [tcst], D, EPS)
        tb1 = emit_bcast(p, s_row[0:1, 0:T], ps_bm[:, 0:T], [trs])
        tb2 = emit_bcast(p, s_row[0:1, T:2 * T], ps_br[:, 0:T], [trs])
        for i in range(8):
            p.op("vector", lambda e, i=i: e.tensor_tensor(
                out=s_sq[:, 0:T], in0=src[:, i, :], in1=ps_bm[:, 0:T],
                op=ALU.subtract), after=[tb1, tb2, ("eA", p.cnt["eA"])])
            if gain:
                p.op("vector", lambda e, i=i: e.tensor_tensor(
                    out=s_sq[:, 0:T], in0=s_sq[:, 0:T], in1=ps_br[:, 0:T],
                    op=ALU.mult))
                p.op("scalar", lambda e, i=i: e.activation(
                    out=dst[:, i, :], in_=s_sq[:, 0:T], func=AF.Identity,
                    scale=s_g[:, i:i + 1], bias=s_bt[:, i:i + 1]),
                    after=[p.tok("eV"), twv])
            else:
                p.op("vector", lambda e, i=i: e.tensor_tensor(
                    out=dst[:, i, :], in0=s_sq[:, 0:T], in1=ps_br[:, 0:T],
                    op=ALU.mult))
        return p.tok("eA") if gain else p.tok("eV")

    for it in range(NSUB):
        o = it * T
        p.dma("sync", s_y[:, :, :],
              rearr(y_in, "(c p) t -> p c t", p=128)[:, :, o:o + T], inc="dL")
        p.dma("sync", s_r2[:, :, :],
              rearr(r2_in, "(c p) t -> p c t", p=128)[:, :, o:o + T], inc="dL")
        tld = p.dma("sync", s_sz[:, :, :],
                    rearr(sz_in, "(c p) t -> p c t", p=128)[:, :, o:o + T], inc="dL")
        for i in range(8):
            p.op("vector", lambda e, i=i: e.tensor_tensor(
                out=s_g1[:, i, :], in0=s_y[:, i, :], in1=s_sz[:, i, :],
                op=ALU.mult), after=[tld])
        tg1 = p.tok("eV")
        tl2 = ln_stats_apply(s_r2, s_l2, [tld])
        pslast = {}
        for m in range(8):
            pb = ps0 if (m % 2 == 0) else ps1
            aft = [tg1, tow] + ([pslast[id(pb)]] if id(pb) in pslast else [])
            tmm = mm_group(p, pb[:, 0:T],
                           lambda k, m=m: s_ow[:, k, m * 128:(m + 1) * 128],
                           lambda k: s_g1[:, k, :], 8, after=aft)
            ta = p.op("scalar", lambda e, m=m, pb=pb: e.activation(
                out=s_o0[:, m, :], in_=pb[:, 0:T], func=AF.Identity,
                bias=s_ob[:, m:m + 1]), after=[tmm, twv])
            pslast[id(pb)] = ta
            p.op("vector", lambda e, m=m: e.tensor_tensor(
                out=s_o0[:, m, :], in0=s_o0[:, m, :], in1=s_l2[:, m, :],
                op=ALU.add), after=[ta, tl2])
        to0 = p.tok("eV")
        tat = ln_stats_apply(s_o0, s_l2, [to0], gain=True)
        for i in range(8):
            p.op("vector", lambda e, i=i: e.tensor_tensor(
                out=s_r3[:, i, :], in0=s_r2[:, i, :], in1=s_l2[:, i, :],
                op=ALU.add), after=[tat])
        tr3 = p.tok("eV")
        tl3 = ln_stats_apply(s_r3, s_l3, [tr3])
        pslast = {}
        for m in range(8):
            pb = ps0 if (m % 2 == 0) else ps1
            aft = [tl3, tfw] + ([pslast[id(pb)]] if id(pb) in pslast else [])
            tmm = mm_group(p, pb[:, 0:T],
                           lambda k, m=m: s_fw[:, k, m * 128:(m + 1) * 128],
                           lambda k: s_l3[:, k, :], 8, after=aft)
            pslast[id(pb)] = p.op("vector", lambda e, m=m, pb=pb: e.tensor_tensor(
                out=s_o0[:, m, :], in0=pb[:, 0:T], in1=s_r3[:, m, :],
                op=ALU.add), after=[tmm])
        tout = p.tok("eV")
        p.dma("sync", rearr(o_out, "(c p) t -> p c t", p=128)[:, :, o:o + T],
              s_o0[:, :, :], after=[tout], inc="dS")
        p.barrier()
    return p.finalize()


def vec_pc(v, nch):
    return np.ascontiguousarray(np.asarray(v, np.float32).reshape(nch, 128).T)


def _core_slices():
    return [(c // 4, (c % 4) * SQ, (c % 4 + 1) * SQ) for c in range(NCORES)]


_PROG_CACHE = {}


def _prog(name, builder):
    if name not in _PROG_CACHE:
        _PROG_CACHE[name] = builder()
    return _PROG_CACHE[name]


def _run(name, builder, maps):
    nc = _prog(name, builder)
    last = None
    for attempt in range(3):
        try:
            return run_bass_kernel_spmd(nc, maps,
                                        core_ids=list(range(NCORES))).results
        except Exception as ex:
            last = ex
            if "UNAVAILABLE" not in str(ex) and "UNRECOVERABLE" not in str(ex):
                raise
            import time as _t
            _t.sleep(10)
    raise last


def kernel(**inputs):
    f = lambda k: np.asarray(inputs[k], np.float32)
    cc = np.ascontiguousarray

    x = f("L0_attn_x")
    x_fm = [cc(x[b].T) for b in range(B)]

    wgT = cc(f("L0_attn_Wg").T)
    wiT = cc(f("L0_attn_Wi").T)
    cw0 = cc(f("L0_attn_conv_w")[:, 0, :].reshape(12, 128, 4)
             .transpose(1, 0, 2).reshape(128, 48))
    cb0 = vec_pc(f("L0_attn_conv_b"), 12)
    wliT = cc(f("L0_attn_lru_Wi").T)
    wlgT = cc(f("L0_attn_lru_Wg").T)
    lbi = vec_pc(f("L0_attn_lru_bi"), 12)
    lbg = vec_pc(f("L0_attn_lru_bg"), 12)
    lam = vec_pc(f("L0_attn_lam"), 12)
    woT = cc(f("L0_attn_Wo").T)
    w1T = cc(f("L0_ffn_w1").T)
    b1 = vec_pc(f("L0_ffn_b1"), 32)
    w2T = cc(f("L0_ffn_w2").T)
    b2 = vec_pc(f("L0_ffn_b2"), 8)
    ipjT = cc(f("L1_attn_in_proj_w").T)
    ipb = vec_pc(f("L1_attn_in_proj_b"), 16)
    cw1 = cc(f("L1_attn_conv_w")[:, 0, :].reshape(8, 128, 4)
             .transpose(1, 0, 2).reshape(128, 32))
    cb1 = vec_pc(f("L1_attn_conv_b"), 8)
    alog = f("L1_attn_A_log")
    bcwT = cc(np.concatenate([f("L1_attn_B_w"), f("L1_attn_C_w")], 0).T)
    bcb = cc(np.concatenate([f("L1_attn_B_b"), f("L1_attn_C_b")])[:, None])
    dtwT = cc(f("L1_attn_dt_w").T)
    dtb = vec_pc(f("L1_attn_dt_b"), 8)
    owT = cc(f("L1_attn_out_w").T)
    ob = vec_pc(f("L1_attn_out_b"), 8)
    lng = vec_pc(f("L1_attn_ln_g"), 8)
    lnb = vec_pc(f("L1_attn_ln_beta"), 8)
    fwT = cc(f("L1_ffn_weight").T)
    ident = np.eye(128, dtype=np.float32)
    identb = np.eye(128, dtype=BFNP)

    cs = _core_slices()

    maps = []
    for (b, lo, hi) in cs:
        xs = np.zeros((D, SQ + 3), np.float32)
        xs[:, 3:] = x_fm[b][:, lo:hi]
        if lo > 0:
            xs[:, 0:3] = x_fm[b][:, lo - 3:lo]
        maps.append({"x": xs, "wgT": wgT, "wiT": wiT, "cw": cw0, "cb": cb0})
    rA = _run("L1a", build_L1a, maps)

    maps = [{"ybf": rA[c]["ybf"], "wliT": wliT, "wlgT": wlgT,
             "lbi": lbi, "lbg": lbg, "lam": lam} for c in range(NCORES)]
    rB = _run("L1b", build_L1b, maps)

    maps = []
    for c, (b, lo, hi) in enumerate(cs):
        q = c % 4
        pg = np.zeros((128, 12, 3), np.float32)
        ps = np.zeros((128, 12, 3), np.float32)
        for idx, j in enumerate(range(b * 4, b * 4 + q)):
            slot = 3 - q + idx
            pg[:, :, slot] = rB[j]["summ"][:, 0].reshape(12, 128).T
            ps[:, :, slot] = rB[j]["summ"][:, 1].reshape(12, 128).T
        maps.append({"alpha": rB[c]["alpha"], "xbeta": rB[c]["xbeta"],
                     "ggate": rA[c]["ggate"], "x": cc(x_fm[b][:, lo:hi]),
                     "pgam": pg, "psend": ps, "woT": woT})
    r2a = _run("L2a", build_L2a, maps)

    maps = [{"r1": r2a[c]["r1"], "w1T": w1T, "b1": b1, "ident": ident}
            for c in range(NCORES)]
    rb1 = _run("L2b1", build_L2b1, maps)
    maps = [{"h1": rb1[c]["h1"], "r1": r2a[c]["r1"], "w2T": w2T, "b2": b2,
             "ident": ident} for c in range(NCORES)]
    r2b = _run("L2b2", build_L2b2, maps)

    maps = []
    for c, (b, lo, hi) in enumerate(cs):
        q = c % 4
        r2h = np.zeros((D, SQ + 3), np.float32)
        r2h[:, 2:2 + SQ] = r2b[c]["r2"]
        hm = np.ones((128, 3), np.float32)
        if q > 0:
            r2h[:, 0:2] = r2b[c - 1]["r2"][:, SQ - 2:SQ]
        else:
            hm[:, 0:2] = 0.0
        if q < 3:
            r2h[:, SQ + 2:SQ + 3] = r2b[c + 1]["r2"][:, 0:1]
        else:
            hm[:, 2] = 0.0
        maps.append({"r2h": r2h, "ipjT": ipjT, "ipb": ipb, "cw1": cw1,
                     "cb1": cb1, "bcwT": bcwT, "bcb": bcb, "dtwT": dtwT,
                     "dtb": dtb, "hmask": hm})
    r3 = _run("L3", build_L3, maps)

    maps = []
    for c in range(NCORES):
        b, dq = c // 4, c % 4
        dsl = slice(dq * DQ, (dq + 1) * DQ)
        dt_b = np.concatenate([r3[b * 4 + j]["dt"] for j in range(4)], axis=1)
        dx_b = np.concatenate([r3[b * 4 + j]["dtxs"] for j in range(4)], axis=1)
        bc_b = np.concatenate([r3[b * 4 + j]["bc"] for j in range(4)], axis=1)
        maps.append({"dtd": cc(dt_b[dsl, :]), "dxd": cc(dx_b[dsl, :]),
                     "bnm": cc(bc_b[0:NS, :]), "cnm": cc(bc_b[NS:2 * NS, :]),
                     "alog": cc(alog[dsl, :]), "identb": identb,
                     "onesb": np.ones((1, 128), BFNP)})
    r4 = _run("L4", build_L4, maps)

    maps = []
    for c, (b, lo, hi) in enumerate(cs):
        y_b = np.concatenate([r4[b * 4 + dq]["ydm"] for dq in range(4)], axis=0)
        maps.append({"yfm": cc(y_b[:, lo:hi]), "r2": r2b[c]["r2"],
                     "siluz": r3[c]["siluz"], "owT": owT, "ob": ob,
                     "lng": lng, "lnb": lnb, "fwT": fwT})
    r5 = _run("L5", build_L5, maps)

    out = np.empty((B, S, D), np.float32)
    for c, (b, lo, hi) in enumerate(cs):
        out[b, lo:hi, :] = r5[c]["outfm"].T
    return out


# version 03
# speedup vs baseline: 1.2258x; 1.2258x over previous
import contextlib
import numpy as np
import ml_dtypes

import concourse.bass as bass
import concourse.mybir as mybir
from concourse.bass_utils import run_bass_kernel_spmd

F32 = mybir.dt.float32
BF16 = mybir.dt.bfloat16
I32 = mybir.dt.int32
AF = mybir.ActivationFunctionType
ALU = mybir.AluOpType
AX = mybir.AxisListType
BFNP = ml_dtypes.bfloat16

NCORES = 8
B, S, D = 2, 16384, 1024
H = 1536
FF = 4096
NS = 32
SQ = S // 4
DQ = D // 4
EPS = 1e-5
ERFS = 0.7071067811865476


class Prog:
    ENGS = ("sync", "tensor", "vector", "scalar", "gpsimd")
    SELF = {"vector": "eV", "scalar": "eA", "gpsimd": "eG"}

    def __init__(self):
        self.nc = bass.Bass("TRN2", target_bir_lowering=False)
        self.es = contextlib.ExitStack()
        self.es.__enter__()
        self.q = {e: [] for e in self.ENGS}
        self.sems = {}
        self.cnt = {}
        self.lastw = {e: {} for e in self.ENGS}
        for s in ("dL", "dW", "dS", "dG", "eT", "eV", "eA", "eG", "eS"):
            self.sems[s] = self.es.enter_context(self.nc.semaphore(s))
            self.cnt[s] = 0
        nums = [h.num for h in self.sems.values()]
        rng = range(min(nums), max(nums) + 1)
        self.q["gpsimd"].append(lambda e, r=rng: e.dma_reset(r))
        self.q["gpsimd"].append(lambda e, r=rng: e.sem_clear(r))
        self.q["gpsimd"].append(
            lambda e: e.nop().then_inc(self.sems["eG"], 1))
        self.cnt["eG"] = 1
        self._started = {e: (e == "gpsimd") for e in self.ENGS}
        self.pending = {e: {} for e in self.ENGS}

    def din(self, name, shape, dt=F32):
        return self.nc.dram_tensor(name, list(shape), dt, kind="ExternalInput")

    def dout(self, name, shape, dt=F32):
        return self.nc.dram_tensor(name, list(shape), dt, kind="ExternalOutput")

    def sbuf(self, name, shape, dt=F32):
        return self.es.enter_context(self.nc.sbuf_tensor(name, list(shape), dt))

    def psum(self, name, shape, dt=F32):
        return self.es.enter_context(self.nc.psum_tensor(name, list(shape), dt))

    def tok(self, sem):
        return (sem, self.cnt[sem])

    def _wait(self, eng, sem, val):
        if val <= 0:
            return
        if self.lastw[eng].get(sem, 0) >= val:
            return
        self.lastw[eng][sem] = val
        self.q[eng].append(lambda e, s=self.sems[sem], v=val: e.wait_ge(s, v))

    def op(self, eng, fn, after=(), inc=None, dma=False):
        if not self._started[eng]:
            self._started[eng] = True
            self._wait(eng, "eG", 1)
        ss = None if dma else self.SELF.get(eng)
        afters = list(after)
        if self.pending[eng]:
            afters.extend(self.pending[eng].items())
            self.pending[eng] = {}
        if ss:
            afters.append((ss, self.cnt[ss]))
        merged = {}
        for sem, val in afters:
            merged[sem] = max(merged.get(sem, 0), val)
        for sem, val in merged.items():
            self._wait(eng, sem, val)
        incs = []
        if ss:
            self.cnt[ss] += 1
            incs.append((ss, 1))
        if inc:
            amt = 16 if dma else 1
            self.cnt[inc] += amt
            incs.append((inc, amt))

        def run(e, fn=fn, incs=tuple(incs)):
            inst = fn(e)
            for s, amt in incs:
                inst = inst.then_inc(self.sems[s], amt)
        self.q[eng].append(run)
        if inc:
            return self.tok(inc)
        if ss:
            return self.tok(ss)
        return None

    def dma(self, eng, out, in_, after=(), inc="dL", slow_ok=False):
        after = list(after)
        after.append((inc, self.cnt[inc]))
        if slow_ok:
            return self.op(eng, lambda e: e.dma_start(
                out=out, in_=in_, allow_slow_non_contiguous=True),
                after=after, inc=inc, dma=True)
        return self.op(eng, lambda e: e.dma_start(out=out, in_=in_),
                       after=after, inc=inc, dma=True)

    def barrier(self):
        for e in self.ENGS:
            for s in ("dL", "dW", "dS", "dG", "eT", "eV", "eA", "eG"):
                self.pending[e][s] = max(self.pending[e].get(s, 0), self.cnt[s])

    def finalize(self, final_eng=None):
        self._wait("sync", "dS", self.cnt["dS"])
        blk = self.nc.Block()
        blk.__enter__()
        for e in self.ENGS:
            if self.q[e]:
                getattr(blk, e)(lambda eng, lst=tuple(self.q[e]): [f(eng) for f in lst])
        blk.__exit__(None, None, None)
        nc = self.nc
        self.es.__exit__(None, None, None)
        nc.finalize()
        return nc


def rearr(t, pat, **kw):
    return t.ap().rearrange(pat, **kw)


def mm_group(p, out_ap, lhsT_fn, rhs_fn, K, after=()):
    for k in range(K):
        p.op("tensor",
             lambda e, k=k: e.matmul(out=out_ap, lhsT=lhsT_fn(k), rhs=rhs_fn(k),
                                     start=(k == 0), stop=(k == K - 1)),
             after=(after if k == 0 else ()), inc="eT" if k == K - 1 else None)
    return p.tok("eT")


def emit_ln_stats(p, x_fn, nch, T, psS, psQ, sq, row_mean, row_rstd, after,
                  denom, eps, row_var=None):
    for i in range(nch):
        p.op("tensor", lambda e, i=i: e.matmul(
            out=psS[0:1, 0:T], lhsT=p.ones_col[:, 0:1], rhs=x_fn(i),
            start=(i == 0), stop=(i == nch - 1)), after=after,
            inc="eT" if i == nch - 1 else None)
    tS = p.tok("eT")
    last_mm = None
    for i in range(nch):
        aft = list(after) + ([last_mm] if last_mm else [])
        tq = p.op("gpsimd", lambda e, i=i: e.tensor_tensor(
            out=sq[:, 0:T], in0=x_fn(i), in1=x_fn(i), op=ALU.mult), after=aft)
        last_mm = p.op("tensor", lambda e, i=i: e.matmul(
            out=psQ[0:1, 0:T], lhsT=p.ones_col[:, 0:1], rhs=sq[:, 0:T],
            start=(i == 0), stop=(i == nch - 1)), after=[tq], inc="eT")
    tQ = p.tok("eT")
    t1 = p.op("vector", lambda e: e.tensor_single_scalar(
        out=row_mean, in_=psS[0:1, 0:T], scalar=1.0 / denom, op=ALU.mult),
        after=[tS])
    p.op("vector", lambda e: e.tensor_tensor(
        out=row_rstd, in0=row_mean, in1=row_mean, op=ALU.mult))
    p.op("vector", lambda e: e.scalar_tensor_tensor(
        out=row_rstd, in0=psQ[0:1, 0:T], scalar=1.0 / denom, in1=row_rstd,
        op0=ALU.mult, op1=ALU.subtract), after=[tQ])
    if row_var is not None:
        p.op("vector", lambda e: e.tensor_copy(out=row_var, in_=row_rstd))
    p.op("vector", lambda e: e.tensor_single_scalar(
        out=row_rstd, in_=row_rstd, scalar=eps, op=ALU.add))
    t4 = p.op("scalar", lambda e: e.activation(
        out=row_rstd, in_=row_rstd, func=AF.Ln),
        after=[p.tok("eV")])
    t5 = p.op("scalar", lambda e: e.activation(
        out=row_rstd, in_=row_rstd, func=AF.Exp, scale=-0.5))
    return t5


def emit_bcast(p, row_ap, ps_out_ap, after):
    return p.op("tensor", lambda e: e.matmul(
        out=ps_out_ap, lhsT=p.ones_row[0:1, :], rhs=row_ap,
        start=True, stop=True), after=after, inc="eT")


def common_consts(p):
    p.ones_col = p.sbuf("ones_col", [128, 1])
    p.ones_row = p.sbuf("ones_row", [1, 128])
    p.op("gpsimd", lambda e: e.memset(p.ones_col[:, :], 1.0))
    t = p.op("gpsimd", lambda e: e.memset(p.ones_row[:, :], 1.0))
    return t


def build_L1a():
    p = Prog()
    T = 512
    NSUB = SQ // T

    xin = p.din("x", [D, SQ + 3])
    wgT = p.din("wgT", [D, H])
    wiT = p.din("wiT", [D, H])
    cw = p.din("cw", [128, 48])
    cb = p.din("cb", [128, 12])
    o_gg = p.dout("ggate", [H, SQ], BF16)
    o_yb = p.dout("ybf", [H, SQ], BF16)

    s_wg = p.sbuf("s_wg", [128, 8, H], BF16)
    s_wi = p.sbuf("s_wi", [128, 8, H], BF16)
    s_cw = p.sbuf("s_cw", [128, 12, 4])
    s_cb = p.sbuf("s_cb", [128, 12])
    tcst = common_consts(p)
    s_x = p.sbuf("s_x", [128, 8, T])
    s_ln = p.sbuf("s_ln", [128, 8, T], BF16)
    s_sq = p.sbuf("s_sq", [128, T])
    s_row = p.sbuf("s_row", [1, 2 * T])
    s_u = p.sbuf("s_u", [128, 12, T + 3])
    SA = p.sbuf("SA", [128, 12, T])
    SB = p.sbuf("SB", [128, 12, T])
    s_gg = p.sbuf("s_gg", [128, 12, T], BF16)
    s_ybf = p.sbuf("s_ybf", [128, 12, T], BF16)
    psS = p.psum("psS", [1, T])
    psQ = p.psum("psQ", [1, T])
    ps_bm = p.psum("ps_bm", [128, T])
    ps_br = p.psum("ps_br", [128, T])
    ps_m0 = p.psum("ps_m0", [128, T])
    ps_m1 = p.psum("ps_m1", [128, T])
    ps_u3 = p.psum("ps_u3", [128, 4])

    twg = p.dma("gpsimd", s_wg[:, :, :], rearr(wgT, "(c p) h -> p c h", p=128), inc="dG")
    twi = p.dma("gpsimd", s_wi[:, :, :], rearr(wiT, "(c p) h -> p c h", p=128), inc="dG")
    tcw = p.dma("sync", s_cw[:, :, :], rearr(cw, "p (c k) -> p c k", k=4), inc="dW")
    tcb = p.dma("sync", s_cb[:, :], cb[:, :], inc="dW")

    def ln_block(width, tx):
        trs = emit_ln_stats(p, lambda i: s_x[:, i, 0:width], 8, width, psS, psQ,
                            s_sq, s_row[0:1, 0:width], s_row[0:1, T:T + width],
                            [tx, tcst], D, EPS)
        tb1 = emit_bcast(p, s_row[0:1, 0:width], ps_bm[:, 0:width], [trs])
        tb2 = emit_bcast(p, s_row[0:1, T:T + width], ps_br[:, 0:width], [trs])
        for i in range(8):
            p.op("vector", lambda e, i=i: e.tensor_tensor(
                out=s_sq[:, 0:width], in0=s_x[:, i, 0:width], in1=ps_bm[:, 0:width],
                op=ALU.subtract), after=[tb1, tb2])
            p.op("vector", lambda e, i=i: e.tensor_tensor(
                out=s_ln[:, i, 0:width], in0=s_sq[:, 0:width], in1=ps_br[:, 0:width],
                op=ALU.mult))
        return p.tok("eV")

    tx = p.dma("sync", s_x[:, :, 0:3],
               rearr(xin, "(c p) t -> p c t", p=128)[:, :, 0:3], inc="dL")
    tln = ln_block(3, tx)
    tcp = None
    for m in range(12):
        tmm = mm_group(p, ps_u3[:, 0:3],
                       lambda k, m=m: s_wi[:, k, m * 128:(m + 1) * 128],
                       lambda k: s_ln[:, k, 0:3], 8,
                       after=[tln, twi] + ([tcp] if tcp else []))
        tcp = p.op("vector", lambda e, m=m: e.tensor_copy(
            out=s_u[:, m, 0:3], in_=ps_u3[:, 0:3]), after=[tmm])
    p.barrier()

    for it in range(NSUB):
        o = it * T
        tx = p.dma("sync", s_x[:, :, :],
                   rearr(xin, "(c p) t -> p c t", p=128)[:, :, 3 + o:3 + o + T],
                   inc="dL")
        tln = ln_block(T, tx)

        evac = {}
        for m in range(12):
            pb = ps_m0 if (m % 2 == 0) else ps_m1
            aft = [tln, twg] + ([evac[m - 2]] if m - 2 in evac else [])
            tg = mm_group(p, pb[:, 0:T],
                          lambda k, m=m: s_wg[:, k, m * 128:(m + 1) * 128],
                          lambda k: s_ln[:, k, 0:T], 8, after=aft)
            evac[m] = p.op("scalar", lambda e, m=m, pb=pb: e.activation(
                out=s_gg[:, m, :], in_=pb[:, 0:T], func=AF.Gelu), after=[tg])
        tgg = p.tok("eA")
        p.dma("sync", rearr(o_gg, "(c p) t -> p c t", p=128)[:, :, o:o + T],
              s_gg[:, :, :], after=[tgg], inc="dS")

        evac = {}
        for m in range(12):
            pb = ps_m0 if (m % 2 == 0) else ps_m1
            aft = [tln, twi, tgg] + ([evac[m - 2]] if m - 2 in evac else [])
            tu = mm_group(p, pb[:, 0:T],
                          lambda k, m=m: s_wi[:, k, m * 128:(m + 1) * 128],
                          lambda k: s_ln[:, k, 0:T], 8, after=aft)
            tc = p.op("vector", lambda e, m=m, pb=pb: e.tensor_copy(
                out=s_u[:, m, 3:T + 3], in_=pb[:, 0:T]), after=[tu])
            evac[m] = tc
        tuall = p.tok("eV")

        for m in range(12):
            p.op("vector", lambda e, m=m: e.tensor_scalar(
                out=SA[:, m, :], in0=s_u[:, m, 0:T], scalar1=s_cw[:, m, 0:1],
                scalar2=None, op0=ALU.mult), after=[tuall, tcw])
            for j in (1, 2, 3):
                p.op("vector", lambda e, m=m, j=j: e.scalar_tensor_tensor(
                    out=SA[:, m, :], in0=s_u[:, m, j:j + T],
                    scalar=s_cw[:, m, j:j + 1], in1=SA[:, m, :],
                    op0=ALU.mult, op1=ALU.add))
            p.op("scalar", lambda e, m=m: e.activation(
                out=s_ybf[:, m, :], in_=SA[:, m, :], func=AF.Identity,
                bias=s_cb[:, m:m + 1]), after=[p.tok("eV"), tcb])
            p.op("vector", lambda e, m=m: e.tensor_copy(
                out=s_u[:, m, 0:3], in_=s_u[:, m, T:T + 3]))
        tyb = p.tok("eA")
        p.dma("sync", rearr(o_yb, "(c p) t -> p c t", p=128)[:, :, o:o + T],
              s_ybf[:, :, :], after=[tyb], inc="dS")
        p.barrier()
    return p.finalize()


def build_L1b():
    p = Prog()
    T = 512
    NSUB = SQ // T

    ybin = p.din("ybf", [H, SQ], BF16)
    wliT = p.din("wliT", [H, H])
    wlgT = p.din("wlgT", [H, H])
    lbi = p.din("lbi", [128, 12])
    lbg = p.din("lbg", [128, 12])
    lam = p.din("lam", [128, 12])
    o_al = p.dout("alpha", [H, SQ])
    o_xb = p.dout("xbeta", [H, SQ])
    o_sm = p.dout("summ", [H, 4])

    s_wli = p.sbuf("s_wli", [128, 12, H], BF16)
    s_wlg = p.sbuf("s_wlg", [128, 12, H], BF16)
    s_lbi = p.sbuf("s_lbi", [128, 12])
    s_lbg = p.sbuf("s_lbg", [128, 12])
    s_msp = p.sbuf("s_msp", [128, 12])
    s_t0 = p.sbuf("s_t0", [128, 12])
    s_yb = p.sbuf("s_yb", [128, 12, T], BF16)
    s_ig = p.sbuf("s_ig", [128, 12, T])
    s_rg = p.sbuf("s_rg", [128, 12, T])
    s_al = p.sbuf("s_al", [128, 12, T])
    s_xb = p.sbuf("s_xb", [128, 12, T])
    s_sm = p.sbuf("s_sm", [128, 12, 4])
    ps0 = p.psum("ps0", [128, T])
    ps1 = p.psum("ps1", [128, T])

    twl1 = p.dma("gpsimd", s_wli[:, :, :], rearr(wliT, "(c p) h -> p c h", p=128), inc="dG")
    twl2 = p.dma("gpsimd", s_wlg[:, :, :], rearr(wlgT, "(c p) h -> p c h", p=128), inc="dG")
    tb1 = p.dma("sync", s_lbi[:, :], lbi[:, :], inc="dW")
    tb2 = p.dma("sync", s_lbg[:, :], lbg[:, :], inc="dW")
    tlam = p.dma("sync", s_msp[:, :], lam[:, :], inc="dW")
    tabs = p.op("scalar", lambda e: e.activation(out=s_t0[:, :], in_=s_msp[:, :],
                                                 func=AF.Abs), after=[tlam])
    p.op("scalar", lambda e: e.activation(out=s_t0[:, :], in_=s_t0[:, :],
                                          func=AF.Exp, scale=-1.0))
    ta = p.op("scalar", lambda e: e.activation(out=s_t0[:, :], in_=s_t0[:, :],
                                               func=AF.Ln, bias=1.0))
    p.op("vector", lambda e: e.tensor_single_scalar(
        out=s_msp[:, :], in_=s_msp[:, :], scalar=0.0, op=ALU.max),
        after=[tlam, tabs])
    p.op("vector", lambda e: e.scalar_tensor_tensor(
        out=s_msp[:, :], in0=s_msp[:, :], scalar=1.0, in1=s_t0[:, :],
        op0=ALU.mult, op1=ALU.add), after=[ta])
    tmsp = p.op("vector", lambda e: e.tensor_single_scalar(
        out=s_msp[:, :], in_=s_msp[:, :], scalar=-8.0, op=ALU.mult))
    p.op("vector", lambda e: e.memset(s_sm[:, :, 0:1], 1.0))
    p.op("vector", lambda e: e.memset(s_sm[:, :, 1:2], 0.0))

    for it in range(NSUB):
        o = it * T
        last = (it == NSUB - 1)
        ty = p.dma("sync", s_yb[:, :, :],
                   rearr(ybin, "(c p) t -> p c t", p=128)[:, :, o:o + T], inc="dL")
        pslast = {}
        for (wt, bt, dst, wtok) in ((s_wli, s_lbi, s_ig, twl1),
                                    (s_wlg, s_lbg, s_rg, twl2)):
            for m in range(12):
                pb = ps0 if (m % 2 == 0) else ps1
                aft = [ty, wtok] + ([pslast[id(pb)]] if id(pb) in pslast else [])
                tg = mm_group(p, pb[:, 0:T],
                              lambda k, m=m, wt=wt: wt[:, k, m * 128:(m + 1) * 128],
                              lambda k: s_yb[:, k, :], 12, after=aft)
                pslast[id(pb)] = p.op("scalar", lambda e, m=m, pb=pb, dst=dst, bt=bt:
                               e.activation(out=dst[:, m, :], in_=pb[:, 0:T],
                                            func=AF.Sigmoid, bias=bt[:, m:m + 1]),
                               after=[tg, tb1, tb2])
        tsig = p.tok("eA")
        for m in range(12):
            p.op("scalar", lambda e, m=m: e.activation(
                out=s_al[:, m, :], in_=s_rg[:, m, :], func=AF.Exp,
                scale=s_msp[:, m:m + 1]), after=[tsig, tmsp])
        tal = p.tok("eA")
        p.dma("sync", rearr(o_al, "(c p) t -> p c t", p=128)[:, :, o:o + T],
              s_al[:, :, :], after=[tal], inc="dS")
        for m in range(12):
            p.op("vector", lambda e, m=m: e.tensor_tensor(
                out=s_rg[:, m, :], in0=s_al[:, m, :], in1=s_al[:, m, :],
                op=ALU.mult), after=[tal])
            t1 = p.op("vector", lambda e, m=m: e.tensor_scalar(
                out=s_rg[:, m, :], in0=s_rg[:, m, :], scalar1=-1.0,
                scalar2=1.000001, op0=ALU.mult, op1=ALU.add))
            p.op("scalar", lambda e, m=m: e.activation(
                out=s_rg[:, m, :], in_=s_rg[:, m, :], func=AF.Sqrt), after=[t1])
        tbe = p.tok("eA")
        for m in range(12):
            p.op("vector", lambda e, m=m: e.tensor_tensor(
                out=s_ig[:, m, :], in0=s_ig[:, m, :], in1=s_yb[:, m, :],
                op=ALU.mult), after=[tbe])
            p.op("vector", lambda e, m=m: e.tensor_tensor(
                out=s_xb[:, m, :], in0=s_rg[:, m, :], in1=s_ig[:, m, :],
                op=ALU.mult))
        txb = p.tok("eV")
        p.dma("sync", rearr(o_xb, "(c p) t -> p c t", p=128)[:, :, o:o + T],
              s_xb[:, :, :], after=[txb], inc="dS")
        for m in range(12):
            p.op("vector", lambda e, m=m: e.tensor_tensor_scan(
                out=s_ig[:, m, :], data0=s_al[:, m, :], data1=s_al[:, m, :],
                initial=s_sm[:, m, 0:1], op0=ALU.mult, op1=ALU.bypass),
                after=[txb])
            if last:
                p.op("vector", lambda e, m=m: e.tensor_copy(
                    out=s_sm[:, m, 2:3], in_=s_ig[:, m, T - 2:T - 1]))
            p.op("vector", lambda e, m=m: e.tensor_copy(
                out=s_sm[:, m, 0:1], in_=s_ig[:, m, T - 1:T]))
            p.op("vector", lambda e, m=m: e.tensor_tensor_scan(
                out=s_ig[:, m, :], data0=s_al[:, m, :], data1=s_xb[:, m, :],
                initial=s_sm[:, m, 1:2], op0=ALU.mult, op1=ALU.add))
            if last:
                p.op("vector", lambda e, m=m: e.tensor_copy(
                    out=s_sm[:, m, 3:4], in_=s_ig[:, m, T - 2:T - 1]))
            p.op("vector", lambda e, m=m: e.tensor_copy(
                out=s_sm[:, m, 1:2], in_=s_ig[:, m, T - 1:T]))
        p.barrier()
    tsm = p.tok("eV")
    p.dma("sync", rearr(o_sm, "(c p) k -> p c k", p=128), s_sm[:, :, :],
          after=[tsm], inc="dS")
    return p.finalize()


def build_L2a():
    p = Prog()
    T = 512
    NSUB = SQ // T

    a_in = p.din("alpha", [H, SQ])
    xb_in = p.din("xbeta", [H, SQ])
    gg_in = p.din("ggate", [H, SQ], BF16)
    x_in = p.din("x", [D, SQ])
    pgam = p.din("pgam", [128, 12, 3])
    psnd = p.din("psend", [128, 12, 3])
    woT = p.din("woT", [H, D])
    o_r1 = p.dout("r1", [D, SQ])

    s_wo = p.sbuf("s_wo", [128, 12, D], BF16)
    s_pg = p.sbuf("s_pg", [128, 12, 3])
    s_ps = p.sbuf("s_ps", [128, 12, 3])
    s_cy = p.sbuf("s_cy", [128, 12])
    s_al = p.sbuf("s_al", [128, 12, T])
    s_xb = p.sbuf("s_xb", [128, 12, T])
    s_gg = p.sbuf("s_gg", [128, 12, T], BF16)
    s_h = p.sbuf("s_h", [128, 12, T])
    s_a1 = p.sbuf("s_a1", [128, 12, T], BF16)
    s_x = p.sbuf("s_x", [128, 8, T])
    s_r1 = p.sbuf("s_r1", [128, 8, T])
    ps0 = p.psum("ps0", [128, T])
    ps1 = p.psum("ps1", [128, T])

    two = p.dma("gpsimd", s_wo[:, :, :], rearr(woT, "(c p) d -> p c d", p=128), inc="dG")
    tg1 = p.dma("sync", s_pg[:, :, :], pgam.ap(), inc="dW")
    tg2 = p.dma("sync", s_ps[:, :, :], psnd.ap(), inc="dW")
    tc = p.op("vector", lambda e: e.memset(s_cy[:, :], 0.0), after=[tg1, tg2])
    for j in range(3):
        p.op("vector", lambda e, j=j: e.tensor_tensor(
            out=s_cy[:, :], in0=s_cy[:, :], in1=s_pg[:, :, j], op=ALU.mult))
        p.op("vector", lambda e, j=j: e.tensor_tensor(
            out=s_cy[:, :], in0=s_cy[:, :], in1=s_ps[:, :, j], op=ALU.add))
    tcy = p.tok("eV")

    for it in range(NSUB):
        o = it * T
        p.dma("sync", s_al[:, :, :],
              rearr(a_in, "(c p) t -> p c t", p=128)[:, :, o:o + T], inc="dL")
        p.dma("sync", s_xb[:, :, :],
              rearr(xb_in, "(c p) t -> p c t", p=128)[:, :, o:o + T], inc="dL")
        p.dma("sync", s_gg[:, :, :],
              rearr(gg_in, "(c p) t -> p c t", p=128)[:, :, o:o + T], inc="dL")
        tld = p.dma("sync", s_x[:, :, :],
                    rearr(x_in, "(c p) t -> p c t", p=128)[:, :, o:o + T], inc="dL")
        ta = tb = tg = tx = tld
        for m in range(12):
            p.op("vector", lambda e, m=m: e.tensor_tensor_scan(
                out=s_h[:, m, :], data0=s_al[:, m, :], data1=s_xb[:, m, :],
                initial=s_cy[:, m:m + 1], op0=ALU.mult, op1=ALU.add),
                after=[ta, tb, tcy])
            p.op("vector", lambda e, m=m: e.tensor_copy(
                out=s_cy[:, m:m + 1], in_=s_h[:, m, T - 1:T]))
            p.op("vector", lambda e, m=m: e.tensor_tensor(
                out=s_a1[:, m, :], in0=s_gg[:, m, :], in1=s_h[:, m, :],
                op=ALU.mult), after=[tg])
        ta1 = p.tok("eV")
        pslast = {}
        for m in range(8):
            pb = ps0 if (m % 2 == 0) else ps1
            aft = [ta1, two, tx] + ([pslast[id(pb)]] if id(pb) in pslast else [])
            tmm = mm_group(p, pb[:, 0:T],
                           lambda k, m=m: s_wo[:, k, m * 128:(m + 1) * 128],
                           lambda k: s_a1[:, k, :], 12, after=aft)
            pslast[id(pb)] = p.op("vector", lambda e, m=m, pb=pb: e.tensor_tensor(
                out=s_r1[:, m, :], in0=pb[:, 0:T], in1=s_x[:, m, :], op=ALU.add),
                after=[tmm])
        tr1 = p.tok("eV")
        p.dma("sync", rearr(o_r1, "(c p) t -> p c t", p=128)[:, :, o:o + T],
              s_r1[:, :, :], after=[tr1], inc="dS")
        p.barrier()
    return p.finalize()


def _emit_wquant(p, wT, qdst, nrc, ncol, wsc, c0, c1, tcst, s_stg, s_wt2,
                 psS, s_acc, s_m1, s_sc128, bcP):
    stages = []
    for c in range(nrc):
        for hcol in range(0, ncol, 2048):
            stages.append((c, hcol, min(2048, ncol - hcol)))
    tz = p.op("vector", lambda e: e.memset(s_acc[:, :], 0.0))
    tld_prev = None
    for (c, hcol, w) in stages:
        tld = p.dma("sync", s_stg[:, 0:w],
                    rearr(wT, "(c p) f -> p c f", p=128)[:, c, hcol:hcol + w],
                    after=[("eV", p.cnt["eV"])], inc="dL")
        p.op("vector", lambda e, w=w: e.tensor_reduce(
            out=s_m1[:, 0:1], in_=s_stg[:, 0:w], axis=AX.X, op=ALU.add,
            apply_absolute_value=True), after=[tld, tz])
        tld_prev = p.op("vector", lambda e: e.tensor_tensor(
            out=s_acc[:, :], in0=s_acc[:, :], in1=s_m1[:, 0:1], op=ALU.add))
    tmm = p.op("tensor", lambda e: e.matmul(
        out=psS[0:1, 0:1], lhsT=p.ones_col[:, :], rhs=s_acc[:, :],
        start=True, stop=True), after=[tld_prev, tcst], inc="eT")
    n = nrc * 128 * ncol
    p.op("vector", lambda e, n=n, c1=c1: e.tensor_single_scalar(
        out=wsc[0:1, c1:c1 + 1], in_=psS[0:1, 0:1], scalar=1.0 / n,
        op=ALU.mult), after=[tmm])
    tm = p.op("vector", lambda e, c1=c1: e.tensor_single_scalar(
        out=wsc[0:1, c1:c1 + 1], in_=wsc[0:1, c1:c1 + 1], scalar=1e-5,
        op=ALU.max))
    ta = p.op("scalar", lambda e, c0=c0, c1=c1: e.activation(
        out=wsc[0:1, c0:c0 + 1], in_=wsc[0:1, c1:c1 + 1], func=AF.Ln),
        after=[tm])
    ta = p.op("scalar", lambda e, c0=c0: e.activation(
        out=wsc[0:1, c0:c0 + 1], in_=wsc[0:1, c0:c0 + 1], func=AF.Exp,
        scale=-1.0))
    tbc = p.op("tensor", lambda e, c0=c0: e.matmul(
        out=bcP[:, 0:1], lhsT=p.ones_row[0:1, :], rhs=wsc[0:1, c0:c0 + 1],
        start=True, stop=True), after=[ta], inc="eT")
    tsc = p.op("vector", lambda e: e.tensor_copy(
        out=s_sc128[:, :], in_=bcP[:, 0:1]), after=[tbc])
    tld_prev = None
    for (c, hcol, w) in stages:
        tld = p.dma("sync", s_stg[:, 0:w],
                    rearr(wT, "(c p) f -> p c f", p=128)[:, c, hcol:hcol + w],
                    after=[("eV", p.cnt["eV"]), ("eA", p.cnt["eA"])], inc="dL")
        p.op("vector", lambda e, w=w: e.tensor_scalar(
            out=s_stg[:, 0:w], in0=s_stg[:, 0:w], scalar1=s_sc128[:, 0:1],
            scalar2=None, op0=ALU.mult), after=[tld, tsc])
        p.op("scalar", lambda e, w=w: e.activation(
            out=s_wt2[:, 0:w], in_=s_stg[:, 0:w], func=AF.Abs),
            after=[p.tok("eV")])
        p.op("vector", lambda e, w=w: e.tensor_single_scalar(
            out=s_wt2[:, 0:w], in_=s_wt2[:, 0:w], scalar=0.5, op=ALU.is_ge),
            after=[p.tok("eA")])
        ts = p.op("scalar", lambda e, w=w: e.activation(
            out=s_stg[:, 0:w], in_=s_stg[:, 0:w], func=AF.Sign),
            after=[p.tok("eV")])
        tld_prev = p.op("vector", lambda e, c=c, hcol=hcol, w=w, qdst=qdst:
                        e.tensor_tensor(out=qdst[:, c, hcol:hcol + w],
                                        in0=s_stg[:, 0:w], in1=s_wt2[:, 0:w],
                                        op=ALU.mult), after=[ts])
    return tld_prev


def _emit_round_clip(p, s_v, s_int, s_cf, s_fx, s_fg, dst_ap, lo, hi, T, after):
    t1 = p.op("vector", lambda e: e.tensor_copy(out=s_int[:, 0:T], in_=s_v[:, 0:T]),
              after=after)
    t2 = p.op("vector", lambda e: e.tensor_copy(out=s_cf[:, 0:T], in_=s_int[:, 0:T]))
    t3 = p.op("vector", lambda e: e.scalar_tensor_tensor(
        out=s_fx[:, 0:T], in0=s_cf[:, 0:T], scalar=-1.0, in1=s_v[:, 0:T],
        op0=ALU.mult, op1=ALU.add), after=[t2])
    t4 = p.op("vector", lambda e: e.tensor_single_scalar(
        out=s_fg[:, 0:T], in_=s_fx[:, 0:T], scalar=0.5, op=ALU.is_gt),
        after=[t3])
    t5 = p.op("vector", lambda e: e.tensor_tensor(
        out=s_cf[:, 0:T], in0=s_cf[:, 0:T], in1=s_fg[:, 0:T], op=ALU.add),
        after=[t4])
    t6 = p.op("vector", lambda e: e.tensor_single_scalar(
        out=s_fg[:, 0:T], in_=s_fx[:, 0:T], scalar=-0.5, op=ALU.is_lt),
        after=[t5])
    t7 = p.op("vector", lambda e: e.tensor_tensor(
        out=s_cf[:, 0:T], in0=s_cf[:, 0:T], in1=s_fg[:, 0:T], op=ALU.subtract),
        after=[t6])
    return p.op("vector", lambda e: e.tensor_scalar(
        out=dst_ap, in0=s_cf[:, 0:T], scalar1=float(hi), scalar2=float(lo),
        op0=ALU.min, op1=ALU.max))


def build_L2b1():
    p = Prog()
    T = 512
    NSUB = SQ // T

    r1_in = p.din("r1", [D, SQ])
    w1T = p.din("w1T", [D, FF])
    b1 = p.din("b1", [128, 32])
    ident = p.din("ident", [128, 128])
    o_h1 = p.dout("h1", [FF, SQ], BF16)

    s_w1q = p.sbuf("s_w1q", [128, 8, FF], BF16)
    s_stg = p.sbuf("s_stg", [128, 2048])
    s_wt2 = p.sbuf("s_wt2", [128, 2048])
    s_int = p.sbuf("s_int", [128, T], I32)
    s_id = p.sbuf("s_id", [128, 128])
    s_acc = p.sbuf("s_acc", [128, 1])
    s_wsc = p.sbuf("s_wsc", [1, 8])
    s_sc128 = p.sbuf("s_sc128", [128, 1])
    s_b1 = p.sbuf("s_b1", [128, 32])
    s_r1 = p.sbuf("s_r1", [128, 8, T])
    s_xm = p.sbuf("s_xm", [128, 8, T])
    s_q1 = p.sbuf("s_q1", [128, 8, T], BF16)
    s_h1 = p.sbuf("s_h1", [128, 32, T], BF16)
    s_m1 = p.sbuf("s_m1", [128, T])
    s_sq = p.sbuf("s_sq", [128, T])
    s_v = p.sbuf("s_v", [128, T])
    s_cf = p.sbuf("s_cf", [128, T])
    s_fx = p.sbuf("s_fx", [128, T])
    s_fg = p.sbuf("s_fg", [128, T])
    s_mxc = p.sbuf("s_mxc", [128, 1])
    s_bm = p.sbuf("s_bm", [128, T])
    s_bc = p.sbuf("s_bc", [128, T])
    s_be = p.sbuf("s_be", [128, T])
    ROW = p.sbuf("ROWS", [1, 10 * T])
    psS = p.psum("psS", [1, T])
    psQ = p.psum("psQ", [1, T])
    bcP = p.psum("bcP", [128, T])
    ps0 = p.psum("ps0", [128, T])
    ps1 = p.psum("ps1", [128, T])
    psT = p.psum("psT", [128, 128])

    r_mean = ROW[0:1, 0 * T:1 * T]
    r_rstd = ROW[0:1, 1 * T:2 * T]
    r_var = ROW[0:1, 2 * T:3 * T]
    r_r2a = ROW[0:1, 3 * T:4 * T]
    r_mx = ROW[0:1, 4 * T:4 * T + 128]
    r_mq = ROW[0:1, 5 * T:6 * T]
    r_sx = ROW[0:1, 6 * T:7 * T]
    r_esc = ROW[0:1, 7 * T:8 * T]
    r_csc = ROW[0:1, 8 * T:9 * T]

    tcst = common_consts(p)
    tid = p.dma("sync", s_id[:, :], ident.ap(), inc="dW")
    tb1 = p.dma("sync", s_b1[:, :], b1.ap(), inc="dW")
    twsc0 = p.op("vector", lambda e: e.memset(s_wsc[:, :], 0.0))
    tq = _emit_wquant(p, w1T, s_w1q, 8, FF, s_wsc, 0, 1, tcst, s_stg, s_wt2,
                      psS, s_acc, s_m1, s_sc128, bcP)
    p.barrier()

    for it in range(NSUB):
        o = it * T
        tld = p.dma("sync", s_r1[:, :, :],
                    rearr(r1_in, "(c p) t -> p c t", p=128)[:, :, o:o + T],
                    inc="dL")
        trs = emit_ln_stats(p, lambda i: s_r1[:, i, :], 8, T, psS, psQ, s_sq,
                            r_mean, r_rstd, [tld, tcst], D, EPS, row_var=r_var)
        p.op("vector", lambda e: e.tensor_tensor(
            out=r_r2a, in0=r_var, in1=r_rstd, op=ALU.mult), after=[trs])
        p.op("vector", lambda e: e.tensor_tensor(
            out=r_r2a, in0=r_r2a, in1=r_rstd, op=ALU.mult))
        t1 = p.op("vector", lambda e: e.tensor_single_scalar(
            out=r_r2a, in_=r_r2a, scalar=EPS, op=ALU.add))
        p.op("scalar", lambda e: e.activation(out=r_r2a, in_=r_r2a, func=AF.Ln),
             after=[t1])
        tr2 = p.op("scalar", lambda e: e.activation(out=r_r2a, in_=r_r2a,
                                                    func=AF.Exp, scale=-0.5))
        tbc = emit_bcast(p, r_mean, bcP[:, 0:T], [trs])
        tbm = p.op("vector", lambda e: e.tensor_copy(out=s_bm[:, 0:T],
                                                     in_=bcP[:, 0:T]), after=[tbc])
        for i in range(8):
            p.op("vector", lambda e, i=i: e.tensor_tensor(
                out=s_xm[:, i, :], in0=s_r1[:, i, :], in1=s_bm[:, 0:T],
                op=ALU.subtract), after=[tbm])
        txm = p.tok("eV")
        ta0 = p.op("scalar", lambda e: e.activation(
            out=s_m1[:, 0:T], in_=s_xm[:, 0, :], func=AF.Abs), after=[txm])
        for i in range(1, 8):
            tai = p.op("scalar", lambda e, i=i: e.activation(
                out=s_fx[:, 0:T], in_=s_xm[:, i, :], func=AF.Abs),
                after=[("eV", p.cnt["eV"])])
            p.op("vector", lambda e: e.tensor_tensor(
                out=s_m1[:, 0:T], in0=s_m1[:, 0:T], in1=s_fx[:, 0:T],
                op=ALU.max), after=[tai, ta0])
        tmx = p.tok("eV")
        tmxr = None
        for tb_ in range(T // 128):
            ttr = p.op("tensor", lambda e, tb_=tb_: e.transpose(
                out=psT[:, :], in_=s_m1[:, tb_ * 128:(tb_ + 1) * 128],
                identity=s_id[:, :]),
                after=[tmx, tid] + ([tmxr] if tmxr else []), inc="eT")
            trd = p.op("vector", lambda e: e.tensor_reduce(
                out=s_mxc[:, 0:1], in_=psT[:, :], axis=AX.X, op=ALU.max),
                after=[ttr])
            ttr2 = p.op("tensor", lambda e: e.transpose(
                out=psT[0:1, 0:128], in_=s_mxc[:, 0:1], identity=s_id[:, :]),
                after=[trd], inc="eT")
            tmxr = p.op("vector", lambda e, tb_=tb_: e.tensor_copy(
                out=ROW[0:1, 5 * T + tb_ * 128:5 * T + (tb_ + 1) * 128],
                in_=psT[0:1, 0:128]), after=[ttr2])
        p.op("vector", lambda e: e.tensor_tensor(
            out=r_mq, in0=r_mq, in1=r_rstd, op=ALU.mult), after=[tmxr, tr2])
        p.op("vector", lambda e: e.tensor_tensor(
            out=r_mq, in0=r_mq, in1=r_r2a, op=ALU.mult))
        tq1 = p.op("vector", lambda e: e.tensor_single_scalar(
            out=r_mq, in_=r_mq, scalar=1e-5, op=ALU.max))
        p.op("scalar", lambda e: e.activation(out=r_sx, in_=r_mq, func=AF.Ln),
             after=[tq1])
        ta = p.op("scalar", lambda e: e.activation(out=r_sx, in_=r_sx,
                                                   func=AF.Exp, scale=-1.0))
        p.op("vector", lambda e: e.tensor_single_scalar(
            out=r_sx, in_=r_sx, scalar=127.0, op=ALU.mult), after=[ta])
        p.op("vector", lambda e: e.tensor_single_scalar(
            out=r_esc, in_=r_mq, scalar=1.0 / 127.0, op=ALU.mult))
        p.op("vector", lambda e: e.tensor_scalar(
            out=r_esc, in0=r_esc, scalar1=s_wsc[0:1, 1:2], scalar2=None,
            op0=ALU.mult))
        p.op("vector", lambda e: e.tensor_tensor(
            out=r_csc, in0=r_rstd, in1=r_r2a, op=ALU.mult))
        tcsc = p.op("vector", lambda e: e.tensor_tensor(
            out=r_csc, in0=r_csc, in1=r_sx, op=ALU.mult))
        tbc1 = emit_bcast(p, r_csc, bcP[:, 0:T], [tcsc, tbm])
        tb_c = p.op("vector", lambda e: e.tensor_copy(out=s_bc[:, 0:T],
                                                      in_=bcP[:, 0:T]), after=[tbc1])
        tbc2 = emit_bcast(p, r_esc, bcP[:, 0:T], [tb_c])
        tb_e = p.op("vector", lambda e: e.tensor_copy(out=s_be[:, 0:T],
                                                      in_=bcP[:, 0:T]), after=[tbc2])
        for i in range(8):
            tv = p.op("vector", lambda e, i=i: e.tensor_tensor(
                out=s_v[:, 0:T], in0=s_xm[:, i, :], in1=s_bc[:, 0:T],
                op=ALU.mult), after=[tb_c])
            _emit_round_clip(p, s_v, s_int, s_cf, s_fx, s_fg, s_q1[:, i, :],
                             -128.0, 127.0, T, [tv])
        tq1b = p.tok("eV")
        pslast = {}
        for m in range(32):
            pb = ps0 if (m % 2 == 0) else ps1
            aft = [tq1b, tq] + ([pslast[id(pb)]] if id(pb) in pslast else [])
            tmm = mm_group(p, pb[:, 0:T],
                           lambda k, m=m: s_w1q[:, k, m * 128:(m + 1) * 128],
                           lambda k: s_q1[:, k, :], 8, after=aft)
            tt = p.op("vector", lambda e, m=m, pb=pb: e.tensor_tensor(
                out=s_sq[:, 0:T], in0=pb[:, 0:T], in1=s_be[:, 0:T],
                op=ALU.mult), after=[tmm, tb_e, ("eA", p.cnt["eA"])])
            pslast[id(pb)] = tt
            p.op("scalar", lambda e, m=m: e.activation(
                out=s_h1[:, m, :], in_=s_sq[:, 0:T], func=AF.Gelu,
                bias=s_b1[:, m:m + 1]), after=[tt, tb1])
        th1 = p.tok("eA")
        p.dma("sync", rearr(o_h1, "(c p) t -> p c t", p=128)[:, :, o:o + T],
              s_h1[:, :, :], after=[th1], inc="dS")
        p.barrier()
    return p.finalize()


def build_L2b2():
    p = Prog()
    T = 512
    NSUB = SQ // T

    h1_in = p.din("h1", [FF, SQ], BF16)
    r1_in = p.din("r1", [D, SQ])
    w2T = p.din("w2T", [FF, D])
    b2 = p.din("b2", [128, 8])
    ident = p.din("ident", [128, 128])
    o_r2 = p.dout("r2", [D, SQ])

    s_w2q = p.sbuf("s_w2q", [128, 32, D], BF16)
    s_stg = p.sbuf("s_stg", [128, 1024])
    s_wt2 = p.sbuf("s_wt2", [128, 1024])
    s_int = p.sbuf("s_int", [128, T], I32)
    s_id = p.sbuf("s_id", [128, 128])
    s_acc = p.sbuf("s_acc", [128, 1])
    s_wsc = p.sbuf("s_wsc", [1, 8])
    s_sc128 = p.sbuf("s_sc128", [128, 1])
    s_b2 = p.sbuf("s_b2", [128, 8])
    s_h1 = p.sbuf("s_h1", [128, 32, T], BF16)
    s_q2 = p.sbuf("s_q2", [128, 32, T], BF16)
    s_r1 = p.sbuf("s_r1", [128, 8, T])
    s_r2 = p.sbuf("s_r2", [128, 8, T])
    s_m1 = p.sbuf("s_m1", [128, T])
    s_sq = p.sbuf("s_sq", [128, T])
    s_v = p.sbuf("s_v", [128, T])
    s_cf = p.sbuf("s_cf", [128, T])
    s_fx = p.sbuf("s_fx", [128, T])
    s_fg = p.sbuf("s_fg", [128, T])
    s_mxc = p.sbuf("s_mxc", [128, 1])
    s_bc = p.sbuf("s_bc", [128, T])
    s_be = p.sbuf("s_be", [128, T])
    ROW = p.sbuf("ROWS", [1, 10 * T])
    psS = p.psum("psS", [1, T])
    psQ = p.psum("psQ", [1, T])
    bcP = p.psum("bcP", [128, T])
    ps0 = p.psum("ps0", [128, T])
    ps1 = p.psum("ps1", [128, T])
    psT = p.psum("psT", [128, 128])

    r_r2a = ROW[0:1, 3 * T:4 * T]
    r_mq = ROW[0:1, 5 * T:6 * T]
    r_sx = ROW[0:1, 6 * T:7 * T]
    r_esc = ROW[0:1, 7 * T:8 * T]
    r_csc = ROW[0:1, 8 * T:9 * T]

    tcst = common_consts(p)
    tid = p.dma("sync", s_id[:, :], ident.ap(), inc="dW")
    tb2 = p.dma("sync", s_b2[:, :], b2.ap(), inc="dW")
    twsc0 = p.op("vector", lambda e: e.memset(s_wsc[:, :], 0.0))
    tq = _emit_wquant(p, w2T, s_w2q, 32, D, s_wsc, 2, 3, tcst, s_stg, s_wt2,
                      psS, s_acc, s_m1, s_sc128, bcP)
    p.barrier()

    for it in range(NSUB):
        o = it * T
        p.dma("sync", s_h1[:, :, :],
              rearr(h1_in, "(c p) t -> p c t", p=128)[:, :, o:o + T], inc="dL")
        tld = p.dma("sync", s_r1[:, :, :],
                    rearr(r1_in, "(c p) t -> p c t", p=128)[:, :, o:o + T],
                    inc="dL")
        last_mm = None
        for i in range(32):
            aft = [tld, tcst] + ([last_mm] if last_mm else [])
            tqq = p.op("scalar", lambda e, i=i: e.activation(
                out=s_sq[:, 0:T], in_=s_h1[:, i, :], func=AF.Square), after=aft)
            last_mm = p.op("tensor", lambda e, i=i: e.matmul(
                out=psQ[0:1, 0:T], lhsT=p.ones_col[:, :], rhs=s_sq[:, 0:T],
                start=(i == 0), stop=(i == 31)), after=[tqq], inc="eT")
        tQ = p.tok("eT")
        ta0 = p.op("scalar", lambda e: e.activation(
            out=s_m1[:, 0:T], in_=s_h1[:, 0, :], func=AF.Abs), after=[tld])
        for i in range(1, 32):
            tai = p.op("scalar", lambda e, i=i: e.activation(
                out=s_fx[:, 0:T], in_=s_h1[:, i, :], func=AF.Abs),
                after=[("eV", p.cnt["eV"])])
            p.op("vector", lambda e: e.tensor_tensor(
                out=s_m1[:, 0:T], in0=s_m1[:, 0:T], in1=s_fx[:, 0:T],
                op=ALU.max), after=[tai, ta0])
        tmx = p.tok("eV")
        t1 = p.op("vector", lambda e: e.tensor_single_scalar(
            out=r_r2a, in_=psQ[0:1, 0:T], scalar=1.0 / FF, op=ALU.mult),
            after=[tQ])
        t1 = p.op("vector", lambda e: e.tensor_single_scalar(
            out=r_r2a, in_=r_r2a, scalar=EPS, op=ALU.add))
        p.op("scalar", lambda e: e.activation(out=r_r2a, in_=r_r2a, func=AF.Ln),
             after=[t1])
        trh = p.op("scalar", lambda e: e.activation(out=r_r2a, in_=r_r2a,
                                                    func=AF.Exp, scale=-0.5))
        tmxr = None
        for tb_ in range(T // 128):
            ttr = p.op("tensor", lambda e, tb_=tb_: e.transpose(
                out=psT[:, :], in_=s_m1[:, tb_ * 128:(tb_ + 1) * 128],
                identity=s_id[:, :]),
                after=[tmx, tid] + ([tmxr] if tmxr else []), inc="eT")
            trd = p.op("vector", lambda e: e.tensor_reduce(
                out=s_mxc[:, 0:1], in_=psT[:, :], axis=AX.X, op=ALU.max),
                after=[ttr])
            ttr2 = p.op("tensor", lambda e: e.transpose(
                out=psT[0:1, 0:128], in_=s_mxc[:, 0:1], identity=s_id[:, :]),
                after=[trd], inc="eT")
            tmxr = p.op("vector", lambda e, tb_=tb_: e.tensor_copy(
                out=ROW[0:1, 5 * T + tb_ * 128:5 * T + (tb_ + 1) * 128],
                in_=psT[0:1, 0:128]), after=[ttr2])
        p.op("vector", lambda e: e.tensor_tensor(
            out=r_mq, in0=r_mq, in1=r_r2a, op=ALU.mult), after=[tmxr, trh])
        tq1_ = p.op("vector", lambda e: e.tensor_single_scalar(
            out=r_mq, in_=r_mq, scalar=1e-5, op=ALU.max))
        p.op("scalar", lambda e: e.activation(out=r_sx, in_=r_mq, func=AF.Ln),
             after=[tq1_])
        ta = p.op("scalar", lambda e: e.activation(out=r_sx, in_=r_sx,
                                                   func=AF.Exp, scale=-1.0))
        p.op("vector", lambda e: e.tensor_single_scalar(
            out=r_sx, in_=r_sx, scalar=127.0, op=ALU.mult), after=[ta])
        p.op("vector", lambda e: e.tensor_single_scalar(
            out=r_esc, in_=r_mq, scalar=1.0 / 127.0, op=ALU.mult))
        p.op("vector", lambda e: e.tensor_scalar(
            out=r_esc, in0=r_esc, scalar1=s_wsc[0:1, 3:4], scalar2=None,
            op0=ALU.mult))
        tcsc = p.op("vector", lambda e: e.tensor_tensor(
            out=r_csc, in0=r_r2a, in1=r_sx, op=ALU.mult))
        tbc1 = emit_bcast(p, r_csc, bcP[:, 0:T], [tcsc])
        tb_c = p.op("vector", lambda e: e.tensor_copy(out=s_bc[:, 0:T],
                                                      in_=bcP[:, 0:T]), after=[tbc1])
        tbc2 = emit_bcast(p, r_esc, bcP[:, 0:T], [tb_c])
        tb_e = p.op("vector", lambda e: e.tensor_copy(out=s_be[:, 0:T],
                                                      in_=bcP[:, 0:T]), after=[tbc2])
        for i in range(32):
            tv = p.op("vector", lambda e, i=i: e.tensor_tensor(
                out=s_v[:, 0:T], in0=s_h1[:, i, :], in1=s_bc[:, 0:T],
                op=ALU.mult), after=[tb_c])
            _emit_round_clip(p, s_v, s_int, s_cf, s_fx, s_fg, s_q2[:, i, :],
                             -128.0, 127.0, T, [tv])
        tq2b = p.tok("eV")
        pslast = {}
        for m in range(8):
            pb = ps0 if (m % 2 == 0) else ps1
            aft = [tq2b, tq] + ([pslast[id(pb)]] if id(pb) in pslast else [])
            tmm = mm_group(p, pb[:, 0:T],
                           lambda k, m=m: s_w2q[:, k, m * 128:(m + 1) * 128],
                           lambda k: s_q2[:, k, :], 32, after=aft)
            tt = p.op("vector", lambda e, m=m, pb=pb: e.tensor_tensor(
                out=s_r2[:, m, :], in0=pb[:, 0:T], in1=s_be[:, 0:T],
                op=ALU.mult), after=[tmm, tb_e])
            pslast[id(pb)] = tt
            tb_ = p.op("scalar", lambda e, m=m: e.activation(
                out=s_r2[:, m, :], in_=s_r2[:, m, :], func=AF.Identity,
                bias=s_b2[:, m:m + 1]), after=[tt, tb2])
            p.op("vector", lambda e, m=m: e.tensor_tensor(
                out=s_r2[:, m, :], in0=s_r2[:, m, :], in1=s_r1[:, m, :],
                op=ALU.add), after=[tb_])
        tr2o = p.tok("eV")
        p.dma("sync", rearr(o_r2, "(c p) t -> p c t", p=128)[:, :, o:o + T],
              s_r2[:, :, :], after=[tr2o], inc="dS")
        p.barrier()
    return p.finalize()


def build_L3():
    p = Prog()
    T = 512
    NSUB = SQ // T

    r2h = p.din("r2h", [D, SQ + 3])
    ipjT = p.din("ipjT", [D, 2 * D])
    ipb = p.din("ipb", [128, 16])
    cw1 = p.din("cw1", [128, 32])
    cb1 = p.din("cb1", [128, 8])
    bcwT = p.din("bcwT", [D, 64])
    bcb = p.din("bcb", [64, 1])
    dtwT = p.din("dtwT", [D, D])
    dtb = p.din("dtb", [128, 8])
    hm = p.din("hmask", [128, 3])
    o_dt = p.dout("dt", [D, SQ])
    o_dx = p.dout("dtxs", [D, SQ])
    o_bc = p.dout("bc", [64, SQ])
    o_sz = p.dout("siluz", [D, SQ], BF16)

    s_ipj = p.sbuf("s_ipj", [128, 8, 2 * D], BF16)
    s_dtw = p.sbuf("s_dtw", [128, 8, D], BF16)
    s_bcw = p.sbuf("s_bcw", [128, 8, 64], BF16)
    s_ipb = p.sbuf("s_ipb", [128, 16])
    s_cw = p.sbuf("s_cw", [128, 8, 4])
    s_cb = p.sbuf("s_cb", [128, 8])
    s_dtb = p.sbuf("s_dtb", [128, 8])
    s_bcb = p.sbuf("s_bcb", [64, 1])
    s_hm = p.sbuf("s_hm", [128, 3])
    tcst = common_consts(p)
    s_x = p.sbuf("s_x", [128, 8, T])
    s_ln = p.sbuf("s_ln", [128, 8, T], BF16)
    s_sq = p.sbuf("s_sq", [128, T])
    s_row = p.sbuf("s_row2", [1, 2 * T])
    s_xc = p.sbuf("s_xc", [128, 8, T + 3])
    s_z = p.sbuf("s_z", [128, 8, T], BF16)
    s_xs = p.sbuf("s_xs", [128, 8, T])
    s_xsb = p.sbuf("s_xsb", [128, 8, T], BF16)
    s_dt = p.sbuf("s_dt", [128, 8, T])
    s_dx = p.sbuf("s_dx", [128, 8, T])
    s_sp = p.sbuf("s_sp", [128, T])
    s_bc64 = p.sbuf("s_bc64", [128, T])
    psS = p.psum("psS", [1, T])
    psQ = p.psum("psQ", [1, T])
    ps_bm = p.psum("ps_bm", [128, T])
    ps_br = p.psum("ps_br", [128, T])
    ps0 = p.psum("ps0", [128, T])
    ps1 = p.psum("ps1", [128, T])

    tw1 = p.dma("gpsimd", s_ipj[:, :, :], rearr(ipjT, "(c p) f -> p c f", p=128), inc="dG")
    tw2 = p.dma("gpsimd", s_dtw[:, :, :], rearr(dtwT, "(c p) f -> p c f", p=128), inc="dG")
    tw3 = p.dma("gpsimd", s_bcw[:, :, :], rearr(bcwT, "(c p) f -> p c f", p=128), inc="dG")
    tv = p.dma("sync", s_ipb[:, :], ipb.ap(), inc="dW")
    tv = p.dma("sync", s_cw[:, :, :], rearr(cw1, "p (c k) -> p c k", k=4), inc="dW")
    tv = p.dma("sync", s_cb[:, :], cb1.ap(), inc="dW")
    tv = p.dma("sync", s_dtb[:, :], dtb.ap(), inc="dW")
    tv = p.dma("sync", s_bcb[:, :], bcb.ap(), inc="dW")
    tv = p.dma("sync", s_hm[:, :], hm.ap(), inc="dW")
    twv = p.tok("dW")

    def ln_block(width, tx):
        trs = emit_ln_stats(p, lambda i: s_x[:, i, 0:width], 8, width, psS, psQ,
                            s_sq, s_row[0:1, 0:width], s_row[0:1, T:T + width],
                            [tx, tcst], D, EPS)
        tb1 = emit_bcast(p, s_row[0:1, 0:width], ps_bm[:, 0:width], [trs])
        tb2 = emit_bcast(p, s_row[0:1, T:T + width], ps_br[:, 0:width], [trs])
        for i in range(8):
            p.op("vector", lambda e, i=i, width=width: e.tensor_tensor(
                out=s_sq[:, 0:width], in0=s_x[:, i, 0:width], in1=ps_bm[:, 0:width],
                op=ALU.subtract), after=[tb1, tb2])
            p.op("vector", lambda e, i=i, width=width: e.tensor_tensor(
                out=s_ln[:, i, 0:width], in0=s_sq[:, 0:width], in1=ps_br[:, 0:width],
                op=ALU.mult))
        return p.tok("eV")

    def inproj(width, tln, z_dst_w):
        pslast = {}
        for m in range(16):
            pb = ps0 if (m % 2 == 0) else ps1
            aft = [tln, tw1, twv] + ([pslast[id(pb)]] if id(pb) in pslast else [])
            tmm = mm_group(p, pb[:, 0:width],
                           lambda k, m=m: s_ipj[:, k, m * 128:(m + 1) * 128],
                           lambda k, width=width: s_ln[:, k, 0:width], 8, after=aft)
            if m < 8:
                pslast[id(pb)] = p.op("scalar", lambda e, m=m, pb=pb, width=width:
                                      e.activation(out=s_xc[:, m, 3:3 + width],
                                                   in_=pb[:, 0:width],
                                                   func=AF.Identity,
                                                   bias=s_ipb[:, m:m + 1]),
                                      after=[tmm])
            else:
                pslast[id(pb)] = p.op("scalar", lambda e, m=m, pb=pb, width=width:
                                      e.activation(out=s_z[:, m - 8, 0:width],
                                                   in_=pb[:, 0:width], func=AF.Silu,
                                                   bias=s_ipb[:, 8 + (m - 8):9 + (m - 8)]),
                                      after=[tmm])
        return p.tok("eA")

    tx = p.dma("sync", s_x[:, :, 0:3],
               rearr(r2h, "(c p) t -> p c t", p=128)[:, :, 0:3], inc="dL")
    tln = ln_block(3, tx)
    tz = inproj(3, tln, 0)
    for i in range(8):
        p.op("vector", lambda e, i=i: e.tensor_tensor(
            out=s_xc[:, i, 3:5], in0=s_xc[:, i, 3:5], in1=s_hm[:, 0:2],
            op=ALU.mult), after=[tz])
    for i in range(8):
        p.op("vector", lambda e, i=i: e.tensor_copy(
            out=s_xc[:, i, 0:3], in_=s_xc[:, i, 3:6]))
    tcar = p.tok("eV")
    p.dma("sync", rearr(o_sz, "(c p) t -> p c t", p=128)[:, :, 0:1],
          s_z[:, :, 2:3], after=[tz], inc="dS", slow_ok=True)
    p.barrier()

    for it in range(NSUB):
        o = it * T
        last = (it == NSUB - 1)
        tx = p.dma("sync", s_x[:, :, :],
                   rearr(r2h, "(c p) t -> p c t", p=128)[:, :, o + 3:o + T + 3],
                   inc="dL")
        tln = ln_block(T, tx)
        tz = inproj(T, tln, T)
        if last:
            for i in range(8):
                p.op("vector", lambda e, i=i: e.tensor_tensor(
                    out=s_xc[:, i, T + 2:T + 3], in0=s_xc[:, i, T + 2:T + 3],
                    in1=s_hm[:, 2:3], op=ALU.mult), after=[tz])
        tmask = p.tok("eV")
        w = T if not last else T - 1
        p.dma("sync", rearr(o_sz, "(c p) t -> p c t", p=128)[:, :, o + 1:o + 1 + w],
              s_z[:, :, 0:w], after=[tz], inc="dS")
        for i in range(8):
            p.op("vector", lambda e, i=i: e.tensor_scalar(
                out=s_xs[:, i, :], in0=s_xc[:, i, 0:T], scalar1=s_cw[:, i, 0:1],
                scalar2=None, op0=ALU.mult), after=[tz, tmask, twv])
            for j in (1, 2, 3):
                p.op("vector", lambda e, i=i, j=j: e.scalar_tensor_tensor(
                    out=s_xs[:, i, :], in0=s_xc[:, i, j:j + T],
                    scalar=s_cw[:, i, j:j + 1], in1=s_xs[:, i, :],
                    op0=ALU.mult, op1=ALU.add))
            p.op("scalar", lambda e, i=i: e.activation(
                out=s_xs[:, i, :], in_=s_xs[:, i, :], func=AF.Silu,
                bias=s_cb[:, i:i + 1]), after=[p.tok("eV")])
            p.op("vector", lambda e, i=i: e.tensor_copy(
                out=s_xsb[:, i, :], in_=s_xs[:, i, :]), after=[p.tok("eA")])
            p.op("vector", lambda e, i=i: e.tensor_copy(
                out=s_xc[:, i, 0:3], in_=s_xc[:, i, T:T + 3]))
        txs = p.tok("eV")
        tmm = mm_group(p, ps0[0:64, 0:T],
                       lambda k: s_bcw[:, k, :],
                       lambda k: s_xsb[:, k, :], 8, after=[txs, tw3])
        tbc_ = p.op("scalar", lambda e: e.activation(
            out=s_bc64[0:64, 0:T], in_=ps0[0:64, 0:T], func=AF.Identity,
            bias=s_bcb[0:64, 0:1]), after=[tmm])
        p.dma("sync", o_bc[:, o:o + T], s_bc64[0:64, 0:T],
              after=[tbc_], inc="dS")
        pslast = {id(ps0): tbc_}
        tsp_rd = None
        for m in range(8):
            pb = ps1 if (m % 2 == 0) else ps0
            aft = [txs, tw2] + ([pslast[id(pb)]] if id(pb) in pslast else [])
            tmm = mm_group(p, pb[:, 0:T],
                           lambda k, m=m: s_dtw[:, k, m * 128:(m + 1) * 128],
                           lambda k: s_xsb[:, k, :], 8, after=aft)
            ta = p.op("scalar", lambda e, m=m, pb=pb: e.activation(
                out=s_sp[:, 0:T], in_=pb[:, 0:T], func=AF.Abs,
                bias=s_dtb[:, m:m + 1]),
                after=[tmm] + ([tsp_rd] if tsp_rd else []))
            p.op("scalar", lambda e: e.activation(
                out=s_sp[:, 0:T], in_=s_sp[:, 0:T], func=AF.Exp, scale=-1.0))
            ta2 = p.op("scalar", lambda e: e.activation(
                out=s_sp[:, 0:T], in_=s_sp[:, 0:T], func=AF.Ln, bias=1.0))
            trl = p.op("scalar", lambda e, m=m, pb=pb: e.activation(
                out=s_dt[:, m, :], in_=pb[:, 0:T], func=AF.Relu,
                bias=s_dtb[:, m:m + 1]))
            pslast[id(pb)] = trl
            tsp_rd = p.op("vector", lambda e, m=m: e.tensor_tensor(
                out=s_dt[:, m, :], in0=s_dt[:, m, :], in1=s_sp[:, 0:T],
                op=ALU.add), after=[trl, ta2])
            p.op("vector", lambda e, m=m: e.tensor_scalar(
                out=s_dt[:, m, :], in0=s_dt[:, m, :], scalar1=0.2,
                scalar2=0.001, op0=ALU.min, op1=ALU.max))
            p.op("vector", lambda e, m=m: e.tensor_tensor(
                out=s_dx[:, m, :], in0=s_dt[:, m, :], in1=s_xs[:, m, :],
                op=ALU.mult))
        tdt = p.tok("eV")
        p.dma("sync", rearr(o_dt, "(c p) t -> p c t", p=128)[:, :, o:o + T],
              s_dt[:, :, :], after=[tdt], inc="dS")
        p.dma("sync", rearr(o_dx, "(c p) t -> p c t", p=128)[:, :, o:o + T],
              s_dx[:, :, :], after=[tdt], inc="dS")
        p.barrier()
    return p.finalize()


def build_L4():
    p = Prog()
    TC = 1024
    NCH = S // TC

    dtd = p.din("dtd", [DQ, S])
    dxd = p.din("dxd", [DQ, S])
    bnm = p.din("bnm", [NS, S])
    cnm = p.din("cnm", [NS, S])
    alog = p.din("alog", [DQ, NS])
    identb = p.din("identb", [128, 128], BF16)
    onesb = p.din("onesb", [1, 128], BF16)
    o_y = p.dout("ydm", [DQ, S])

    s_A = p.sbuf("s_A", [128, 2, NS])
    s_idb = p.sbuf("s_idb", [128, 128], BF16)
    s_onb = p.sbuf("s_onb", [1, 128], BF16)
    s_dt = p.sbuf("s_dt", [128, 2, TC])
    s_dx = p.sbuf("s_dx", [128, 2, TC])
    s_bn = p.sbuf("s_bn", [1, NS, TC], BF16)
    s_cn = p.sbuf("s_cn", [1, NS, TC], BF16)
    s_ab = p.sbuf("s_ab", [128, 2, 2, TC])
    s_u = p.sbuf("s_u", [128, 2, TC])
    s_s = p.sbuf("s_s", [128, 2, TC])
    s_pr = p.sbuf("s_pr", [128, 2, TC], BF16)
    s_cy = p.sbuf("s_cy", [128, 64])
    s_y = p.sbuf("s_y", [128, 2, TC])
    sbC = p.sbuf("sbC", [128, 2, TC], BF16)
    psB = p.psum("psB", [128, TC])
    psC = p.psum("psC", [128, TC])
    psY0 = p.psum("psY0", [128, TC])
    psY1 = p.psum("psY1", [128, TC])
    psYs = (psY0, psY1)

    tcst = common_consts(p)
    tidb = p.dma("sync", s_idb[:, :], identb.ap(), inc="dW")
    tonb = p.dma("sync", s_onb[:, :], onesb.ap(), inc="dW")
    tA = p.dma("sync", s_A[:, :, :], rearr(alog, "(c p) n -> p c n", p=128), inc="dW")
    t1 = p.op("vector", lambda e: e.tensor_scalar(
        out=s_A[:, :, :], in0=s_A[:, :, :], scalar1=2.0, scalar2=-5.0,
        op0=ALU.min, op1=ALU.max), after=[tA])
    t2 = p.op("scalar", lambda e: e.activation(
        out=s_A[:, :, :], in_=s_A[:, :, :], func=AF.Exp), after=[t1])
    tAr = p.op("vector", lambda e: e.tensor_single_scalar(
        out=s_A[:, :, :], in_=s_A[:, :, :], scalar=-1.0, op=ALU.mult), after=[t2])
    tcy = p.op("vector", lambda e: e.memset(s_cy[:, :], 0.0))

    for it in range(NCH):
        o = it * TC
        p.dma("sync", s_dt[:, :, :],
              rearr(dtd, "(c p) t -> p c t", p=128)[:, :, o:o + TC], inc="dL")
        p.dma("sync", s_dx[:, :, :],
              rearr(dxd, "(c p) t -> p c t", p=128)[:, :, o:o + TC], inc="dL")
        p.dma("gpsimd", s_bn[0:1, :, :], bnm[0:NS, o:o + TC], inc="dG")
        tldG = p.dma("gpsimd", s_cn[0:1, :, :], cnm[0:NS, o:o + TC], inc="dG")
        tld = p.tok("dL")
        ab_rd = {}
        tB_rd = {}
        tC_rd = {}
        tsbC_rd = {}
        tu_rd = None
        ts_rd = {0: None, 1: None}
        tpr_rd = {0: None, 1: None}
        for n in range(NS):
            par = n % 2
            tabs = []
            for c in range(2):
                tabs.append(p.op("scalar", lambda e, c=c, n=n, par=par:
                    e.activation(out=s_ab[:, par, c, :], in_=s_dt[:, c, :],
                                 func=AF.Exp, scale=s_A[:, c, n:n + 1]),
                    after=[tld, tAr] + ([ab_rd[par]] if par in ab_rd else [])))
            for h in range(2):
                tbc = p.op("tensor", lambda e, n=n, h=h: e.matmul(
                    out=psB[:, h * 512:(h + 1) * 512], lhsT=s_onb[0:1, :],
                    rhs=s_bn[0:1, n, h * 512:(h + 1) * 512],
                    start=True, stop=True),
                    after=[tldG, tonb, tcst] +
                          ([tB_rd.get(0)] if tB_rd.get(0) else []),
                    inc="eT")
            for h in range(2):
                tcc = p.op("tensor", lambda e, n=n, h=h: e.matmul(
                    out=psC[:, h * 512:(h + 1) * 512], lhsT=s_onb[0:1, :],
                    rhs=s_cn[0:1, n, h * 512:(h + 1) * 512],
                    start=True, stop=True),
                    after=[tldG] + ([tC_rd.get(0)] if tC_rd.get(0) else []),
                    inc="eT")
            tCcp = p.op("scalar", lambda e, par=par: e.copy(
                out=sbC[:, par, :], in_=psC[:, 0:TC]),
                after=[tcc] + ([tsbC_rd[par]] if par in tsbC_rd else []))
            tC_rd[0] = tCcp
            tcl = {}
            for c in range(2):
                tu = p.op("vector", lambda e, c=c: e.tensor_tensor(
                    out=s_u[:, c, :], in0=s_dx[:, c, :], in1=psB[:, 0:TC],
                    op=ALU.mult),
                    after=[tbc, tld] + ([tu_rd] if tu_rd else []))
                tB_rd[0] = tu
                ts = p.op("vector", lambda e, c=c, n=n, par=par: e.tensor_tensor_scan(
                    out=s_s[:, c, :], data0=s_ab[:, par, c, :],
                    data1=s_u[:, c, :],
                    initial=s_cy[:, c * NS + n:c * NS + n + 1],
                    op0=ALU.mult, op1=ALU.add),
                    after=[tabs[c], tu, tcy] + ([ts_rd[c]] if ts_rd[c] else []))
                ab_rd[par] = ts
                tu_rd = ts
                tcl[c] = p.op("vector", lambda e, c=c: e.tensor_scalar(
                    out=s_s[:, c, :], in0=s_s[:, c, :], scalar1=10.0,
                    scalar2=-10.0, op0=ALU.min, op1=ALU.max), after=[ts])
            for c in range(2):
                p.op("scalar", lambda e, c=c, n=n: e.copy(
                    out=s_cy[:, c * NS + n:c * NS + n + 1],
                    in_=s_s[:, c, TC - 1:TC]), after=[tcl[c]])
                tpr = p.op("gpsimd", lambda e, c=c, par=par: e.tensor_tensor(
                    out=s_pr[:, c, :], in0=s_s[:, c, :], in1=sbC[:, par, :],
                    op=ALU.mult),
                    after=[tcl[c], tCcp] + ([tpr_rd[c]] if tpr_rd[c] else []))
                ts_rd[c] = tpr
                tsbC_rd[par] = tpr
                for h in range(2):
                    tpr_rd[c] = p.op("tensor", lambda e, c=c, n=n, h=h: e.matmul(
                        out=psYs[c][:, h * 512:(h + 1) * 512], lhsT=s_idb[:, :],
                        rhs=s_pr[:, c, h * 512:(h + 1) * 512],
                        start=(n == 0), stop=(n == NS - 1)),
                        after=[tpr, tidb], inc="eT")
        tev = None
        for c in range(2):
            tev = p.op("vector", lambda e, c=c: e.tensor_copy(
                out=s_y[:, c, :], in_=psYs[c][:, 0:TC]),
                after=[tpr_rd[0], tpr_rd[1]])
        p.dma("sync", rearr(o_y, "(c p) t -> p c t", p=128)[:, :, o:o + TC],
              s_y[:, :, :], after=[tev], inc="dS")
        p.barrier()
    return p.finalize()


def build_L5():
    p = Prog()
    T = 512
    NSUB = SQ // T

    y_in = p.din("yfm", [D, SQ])
    r2_in = p.din("r2", [D, SQ])
    sz_in = p.din("siluz", [D, SQ], BF16)
    owT = p.din("owT", [D, D])
    ob = p.din("ob", [128, 8])
    lng = p.din("lng", [128, 8])
    lnb = p.din("lnb", [128, 8])
    fwT = p.din("fwT", [D, D])
    o_out = p.dout("outfm", [D, SQ])

    s_ow = p.sbuf("s_ow", [128, 8, D], BF16)
    s_fw = p.sbuf("s_fw", [128, 8, D], BF16)
    s_stg = p.sbuf("s_stg", [128, D])
    s_ob = p.sbuf("s_ob", [128, 8])
    s_g = p.sbuf("s_g", [128, 8])
    s_bt = p.sbuf("s_bt", [128, 8])
    tcst = common_consts(p)
    s_y = p.sbuf("s_y", [128, 8, T])
    s_r2 = p.sbuf("s_r2", [128, 8, T])
    s_sz = p.sbuf("s_sz", [128, 8, T], BF16)
    s_g1 = p.sbuf("s_g1", [128, 8, T], BF16)
    s_o0 = p.sbuf("s_o0", [128, 8, T])
    s_l2 = p.sbuf("s_l2", [128, 8, T])
    s_r3 = p.sbuf("s_r3", [128, 8, T])
    s_l3 = p.sbuf("s_l3", [128, 8, T], BF16)
    s_sq = p.sbuf("s_sq", [128, T])
    s_row = p.sbuf("s_rowL5", [1, 2 * T])
    psS = p.psum("psS", [1, T])
    psQ = p.psum("psQ", [1, T])
    ps_bm = p.psum("ps_bm", [128, T])
    ps_br = p.psum("ps_br", [128, T])
    ps0 = p.psum("ps0", [128, T])
    ps1 = p.psum("ps1", [128, T])

    tow = p.dma("gpsimd", s_ow[:, :, :], rearr(owT, "(c p) f -> p c f", p=128), inc="dG")
    tv = p.dma("sync", s_ob[:, :], ob.ap(), inc="dW")
    tv = p.dma("sync", s_g[:, :], lng.ap(), inc="dW")
    tv = p.dma("sync", s_bt[:, :], lnb.ap(), inc="dW")
    twv = p.tok("dW")
    tsg_prev = None
    for c in range(8):
        tld = p.dma("sync", s_stg[:, :],
                    rearr(fwT, "(c p) f -> p c f", p=128)[:, c, :],
                    after=([tsg_prev] if tsg_prev else []), inc="dL")
        tsg_prev = p.op("scalar", lambda e, c=c: e.activation(
            out=s_fw[:, c, :], in_=s_stg[:, :], func=AF.Sign), after=[tld])
    tfw = tsg_prev
    p.barrier()

    def ln_stats_apply(src, dst, tafter, dst_bf=False, gain=False):
        trs = emit_ln_stats(p, lambda i: src[:, i, :], 8, T, psS, psQ, s_sq,
                            s_row[0:1, 0:T], s_row[0:1, T:2 * T],
                            tafter + [tcst], D, EPS)
        tb1 = emit_bcast(p, s_row[0:1, 0:T], ps_bm[:, 0:T], [trs])
        tb2 = emit_bcast(p, s_row[0:1, T:2 * T], ps_br[:, 0:T], [trs])
        for i in range(8):
            p.op("vector", lambda e, i=i: e.tensor_tensor(
                out=s_sq[:, 0:T], in0=src[:, i, :], in1=ps_bm[:, 0:T],
                op=ALU.subtract), after=[tb1, tb2, ("eA", p.cnt["eA"])])
            if gain:
                p.op("vector", lambda e, i=i: e.tensor_tensor(
                    out=s_sq[:, 0:T], in0=s_sq[:, 0:T], in1=ps_br[:, 0:T],
                    op=ALU.mult))
                p.op("scalar", lambda e, i=i: e.activation(
                    out=dst[:, i, :], in_=s_sq[:, 0:T], func=AF.Identity,
                    scale=s_g[:, i:i + 1], bias=s_bt[:, i:i + 1]),
                    after=[p.tok("eV"), twv])
            else:
                p.op("vector", lambda e, i=i: e.tensor_tensor(
                    out=dst[:, i, :], in0=s_sq[:, 0:T], in1=ps_br[:, 0:T],
                    op=ALU.mult))
        return p.tok("eA") if gain else p.tok("eV")

    for it in range(NSUB):
        o = it * T
        p.dma("sync", s_y[:, :, :],
              rearr(y_in, "(c p) t -> p c t", p=128)[:, :, o:o + T], inc="dL")
        p.dma("sync", s_r2[:, :, :],
              rearr(r2_in, "(c p) t -> p c t", p=128)[:, :, o:o + T], inc="dL")
        tld = p.dma("sync", s_sz[:, :, :],
                    rearr(sz_in, "(c p) t -> p c t", p=128)[:, :, o:o + T], inc="dL")
        for i in range(8):
            p.op("vector", lambda e, i=i: e.tensor_tensor(
                out=s_g1[:, i, :], in0=s_y[:, i, :], in1=s_sz[:, i, :],
                op=ALU.mult), after=[tld])
        tg1 = p.tok("eV")
        tl2 = ln_stats_apply(s_r2, s_l2, [tld])
        pslast = {}
        for m in range(8):
            pb = ps0 if (m % 2 == 0) else ps1
            aft = [tg1, tow] + ([pslast[id(pb)]] if id(pb) in pslast else [])
            tmm = mm_group(p, pb[:, 0:T],
                           lambda k, m=m: s_ow[:, k, m * 128:(m + 1) * 128],
                           lambda k: s_g1[:, k, :], 8, after=aft)
            ta = p.op("scalar", lambda e, m=m, pb=pb: e.activation(
                out=s_o0[:, m, :], in_=pb[:, 0:T], func=AF.Identity,
                bias=s_ob[:, m:m + 1]), after=[tmm, twv])
            pslast[id(pb)] = ta
            p.op("vector", lambda e, m=m: e.tensor_tensor(
                out=s_o0[:, m, :], in0=s_o0[:, m, :], in1=s_l2[:, m, :],
                op=ALU.add), after=[ta, tl2])
        to0 = p.tok("eV")
        tat = ln_stats_apply(s_o0, s_l2, [to0], gain=True)
        for i in range(8):
            p.op("vector", lambda e, i=i: e.tensor_tensor(
                out=s_r3[:, i, :], in0=s_r2[:, i, :], in1=s_l2[:, i, :],
                op=ALU.add), after=[tat])
        tr3 = p.tok("eV")
        tl3 = ln_stats_apply(s_r3, s_l3, [tr3])
        pslast = {}
        for m in range(8):
            pb = ps0 if (m % 2 == 0) else ps1
            aft = [tl3, tfw] + ([pslast[id(pb)]] if id(pb) in pslast else [])
            tmm = mm_group(p, pb[:, 0:T],
                           lambda k, m=m: s_fw[:, k, m * 128:(m + 1) * 128],
                           lambda k: s_l3[:, k, :], 8, after=aft)
            pslast[id(pb)] = p.op("vector", lambda e, m=m, pb=pb: e.tensor_tensor(
                out=s_o0[:, m, :], in0=pb[:, 0:T], in1=s_r3[:, m, :],
                op=ALU.add), after=[tmm])
        tout = p.tok("eV")
        p.dma("sync", rearr(o_out, "(c p) t -> p c t", p=128)[:, :, o:o + T],
              s_o0[:, :, :], after=[tout], inc="dS")
        p.barrier()
    return p.finalize()


def vec_pc(v, nch):
    return np.ascontiguousarray(np.asarray(v, np.float32).reshape(nch, 128).T)


def _core_slices():
    return [(c // 4, (c % 4) * SQ, (c % 4 + 1) * SQ) for c in range(NCORES)]


_PROG_CACHE = {}


def _prog(name, builder):
    if name not in _PROG_CACHE:
        _PROG_CACHE[name] = builder()
    return _PROG_CACHE[name]


def _run(name, builder, maps):
    nc = _prog(name, builder)
    last = None
    for attempt in range(3):
        try:
            return run_bass_kernel_spmd(nc, maps,
                                        core_ids=list(range(NCORES))).results
        except Exception as ex:
            last = ex
            if "UNAVAILABLE" not in str(ex) and "UNRECOVERABLE" not in str(ex):
                raise
            import time as _t
            _t.sleep(10)
    raise last


def kernel(**inputs):
    f = lambda k: np.asarray(inputs[k], np.float32)
    cc = np.ascontiguousarray

    x = f("L0_attn_x")
    x_fm = [cc(x[b].T) for b in range(B)]

    wgT = cc(f("L0_attn_Wg").T)
    wiT = cc(f("L0_attn_Wi").T)
    cw0 = cc(f("L0_attn_conv_w")[:, 0, :].reshape(12, 128, 4)
             .transpose(1, 0, 2).reshape(128, 48))
    cb0 = vec_pc(f("L0_attn_conv_b"), 12)
    wliT = cc(f("L0_attn_lru_Wi").T)
    wlgT = cc(f("L0_attn_lru_Wg").T)
    lbi = vec_pc(f("L0_attn_lru_bi"), 12)
    lbg = vec_pc(f("L0_attn_lru_bg"), 12)
    lam = vec_pc(f("L0_attn_lam"), 12)
    woT = cc(f("L0_attn_Wo").T)
    w1T = cc(f("L0_ffn_w1").T)
    b1 = vec_pc(f("L0_ffn_b1"), 32)
    w2T = cc(f("L0_ffn_w2").T)
    b2 = vec_pc(f("L0_ffn_b2"), 8)
    ipjT = cc(f("L1_attn_in_proj_w").T)
    ipb = vec_pc(f("L1_attn_in_proj_b"), 16)
    cw1 = cc(f("L1_attn_conv_w")[:, 0, :].reshape(8, 128, 4)
             .transpose(1, 0, 2).reshape(128, 32))
    cb1 = vec_pc(f("L1_attn_conv_b"), 8)
    alog = f("L1_attn_A_log")
    bcwT = cc(np.concatenate([f("L1_attn_B_w"), f("L1_attn_C_w")], 0).T)
    bcb = cc(np.concatenate([f("L1_attn_B_b"), f("L1_attn_C_b")])[:, None])
    dtwT = cc(f("L1_attn_dt_w").T)
    dtb = vec_pc(f("L1_attn_dt_b"), 8)
    owT = cc(f("L1_attn_out_w").T)
    ob = vec_pc(f("L1_attn_out_b"), 8)
    lng = vec_pc(f("L1_attn_ln_g"), 8)
    lnb = vec_pc(f("L1_attn_ln_beta"), 8)
    fwT = cc(f("L1_ffn_weight").T)
    ident = np.eye(128, dtype=np.float32)
    identb = np.eye(128, dtype=BFNP)

    cs = _core_slices()

    maps = []
    for (b, lo, hi) in cs:
        xs = np.zeros((D, SQ + 3), np.float32)
        xs[:, 3:] = x_fm[b][:, lo:hi]
        if lo > 0:
            xs[:, 0:3] = x_fm[b][:, lo - 3:lo]
        maps.append({"x": xs, "wgT": wgT, "wiT": wiT, "cw": cw0, "cb": cb0})
    rA = _run("L1a", build_L1a, maps)

    maps = [{"ybf": rA[c]["ybf"], "wliT": wliT, "wlgT": wlgT,
             "lbi": lbi, "lbg": lbg, "lam": lam} for c in range(NCORES)]
    rB = _run("L1b", build_L1b, maps)

    maps = []
    for c, (b, lo, hi) in enumerate(cs):
        q = c % 4
        pg = np.zeros((128, 12, 3), np.float32)
        ps = np.zeros((128, 12, 3), np.float32)
        for idx, j in enumerate(range(b * 4, b * 4 + q)):
            slot = 3 - q + idx
            pg[:, :, slot] = rB[j]["summ"][:, 0].reshape(12, 128).T
            ps[:, :, slot] = rB[j]["summ"][:, 1].reshape(12, 128).T
        maps.append({"alpha": rB[c]["alpha"], "xbeta": rB[c]["xbeta"],
                     "ggate": rA[c]["ggate"], "x": cc(x_fm[b][:, lo:hi]),
                     "pgam": pg, "psend": ps, "woT": woT})
    r2a = _run("L2a", build_L2a, maps)

    maps = [{"r1": r2a[c]["r1"], "w1T": w1T, "b1": b1, "ident": ident}
            for c in range(NCORES)]
    rb1 = _run("L2b1", build_L2b1, maps)
    maps = [{"h1": rb1[c]["h1"], "r1": r2a[c]["r1"], "w2T": w2T, "b2": b2,
             "ident": ident} for c in range(NCORES)]
    r2b = _run("L2b2", build_L2b2, maps)

    maps = []
    for c, (b, lo, hi) in enumerate(cs):
        q = c % 4
        r2h = np.zeros((D, SQ + 3), np.float32)
        r2h[:, 2:2 + SQ] = r2b[c]["r2"]
        hm = np.ones((128, 3), np.float32)
        if q > 0:
            r2h[:, 0:2] = r2b[c - 1]["r2"][:, SQ - 2:SQ]
        else:
            hm[:, 0:2] = 0.0
        if q < 3:
            r2h[:, SQ + 2:SQ + 3] = r2b[c + 1]["r2"][:, 0:1]
        else:
            hm[:, 2] = 0.0
        maps.append({"r2h": r2h, "ipjT": ipjT, "ipb": ipb, "cw1": cw1,
                     "cb1": cb1, "bcwT": bcwT, "bcb": bcb, "dtwT": dtwT,
                     "dtb": dtb, "hmask": hm})
    r3 = _run("L3", build_L3, maps)

    maps = []
    for c in range(NCORES):
        b, dq = c // 4, c % 4
        dsl = slice(dq * DQ, (dq + 1) * DQ)
        dt_b = np.concatenate([r3[b * 4 + j]["dt"] for j in range(4)], axis=1)
        dx_b = np.concatenate([r3[b * 4 + j]["dtxs"] for j in range(4)], axis=1)
        bc_b = np.concatenate([r3[b * 4 + j]["bc"] for j in range(4)], axis=1)
        maps.append({"dtd": cc(dt_b[dsl, :]), "dxd": cc(dx_b[dsl, :]),
                     "bnm": cc(bc_b[0:NS, :]), "cnm": cc(bc_b[NS:2 * NS, :]),
                     "alog": cc(alog[dsl, :]), "identb": identb,
                     "onesb": np.ones((1, 128), BFNP)})
    r4 = _run("L4", build_L4, maps)

    maps = []
    for c, (b, lo, hi) in enumerate(cs):
        y_b = np.concatenate([r4[b * 4 + dq]["ydm"] for dq in range(4)], axis=0)
        maps.append({"yfm": cc(y_b[:, lo:hi]), "r2": r2b[c]["r2"],
                     "siluz": r3[c]["siluz"], "owT": owT, "ob": ob,
                     "lng": lng, "lnb": lnb, "fwT": fwT})
    r5 = _run("L5", build_L5, maps)

    out = np.empty((B, S, D), np.float32)
    for c, (b, lo, hi) in enumerate(cs):
        out[b, lo:hi, :] = r5[c]["outfm"].T
    return out


# version 04
# speedup vs baseline: 1.3443x; 1.0967x over previous
import contextlib
import numpy as np
import ml_dtypes

import concourse.bass as bass
import concourse.mybir as mybir
from concourse.bass_utils import run_bass_kernel_spmd

F32 = mybir.dt.float32
BF16 = mybir.dt.bfloat16
I32 = mybir.dt.int32
AF = mybir.ActivationFunctionType
ALU = mybir.AluOpType
AX = mybir.AxisListType
BFNP = ml_dtypes.bfloat16

NCORES = 8
B, S, D = 2, 16384, 1024
H = 1536
FF = 4096
NS = 32
SQ = S // 4
DQ = D // 4
EPS = 1e-5
ERFS = 0.7071067811865476


class Prog:
    ENGS = ("sync", "tensor", "vector", "scalar", "gpsimd")
    SELF = {"vector": "eV", "scalar": "eA", "gpsimd": "eG"}

    def __init__(self):
        self.nc = bass.Bass("TRN2", target_bir_lowering=False)
        self.es = contextlib.ExitStack()
        self.es.__enter__()
        self.q = {e: [] for e in self.ENGS}
        self.sems = {}
        self.cnt = {}
        self.lastw = {e: {} for e in self.ENGS}
        for s in ("dL", "dW", "dS", "dG", "eT", "eV", "eA", "eG", "eS"):
            self.sems[s] = self.es.enter_context(self.nc.semaphore(s))
            self.cnt[s] = 0
        nums = [h.num for h in self.sems.values()]
        rng = range(min(nums), max(nums) + 1)
        self.q["gpsimd"].append(lambda e, r=rng: e.dma_reset(r))
        self.q["gpsimd"].append(lambda e, r=rng: e.sem_clear(r))
        self.q["gpsimd"].append(
            lambda e: e.nop().then_inc(self.sems["eG"], 1))
        self.cnt["eG"] = 1
        self._started = {e: (e == "gpsimd") for e in self.ENGS}
        self.pending = {e: {} for e in self.ENGS}

    def din(self, name, shape, dt=F32):
        return self.nc.dram_tensor(name, list(shape), dt, kind="ExternalInput")

    def dout(self, name, shape, dt=F32):
        return self.nc.dram_tensor(name, list(shape), dt, kind="ExternalOutput")

    def sbuf(self, name, shape, dt=F32):
        return self.es.enter_context(self.nc.sbuf_tensor(name, list(shape), dt))

    def psum(self, name, shape, dt=F32):
        return self.es.enter_context(self.nc.psum_tensor(name, list(shape), dt))

    def tok(self, sem):
        return (sem, self.cnt[sem])

    def _wait(self, eng, sem, val):
        if val <= 0:
            return
        if self.lastw[eng].get(sem, 0) >= val:
            return
        self.lastw[eng][sem] = val
        self.q[eng].append(lambda e, s=self.sems[sem], v=val: e.wait_ge(s, v))

    def op(self, eng, fn, after=(), inc=None, dma=False):
        if not self._started[eng]:
            self._started[eng] = True
            self._wait(eng, "eG", 1)
        ss = None if dma else self.SELF.get(eng)
        afters = list(after)
        if self.pending[eng]:
            afters.extend(self.pending[eng].items())
            self.pending[eng] = {}
        if ss:
            afters.append((ss, self.cnt[ss]))
        merged = {}
        for sem, val in afters:
            merged[sem] = max(merged.get(sem, 0), val)
        for sem, val in merged.items():
            self._wait(eng, sem, val)
        incs = []
        if ss:
            self.cnt[ss] += 1
            incs.append((ss, 1))
        if inc:
            amt = 16 if dma else 1
            self.cnt[inc] += amt
            incs.append((inc, amt))

        def run(e, fn=fn, incs=tuple(incs)):
            inst = fn(e)
            for s, amt in incs:
                inst = inst.then_inc(self.sems[s], amt)
        self.q[eng].append(run)
        if inc:
            return self.tok(inc)
        if ss:
            return self.tok(ss)
        return None

    def dma(self, eng, out, in_, after=(), inc="dL", slow_ok=False):
        after = list(after)
        after.append((inc, self.cnt[inc]))
        if slow_ok:
            return self.op(eng, lambda e: e.dma_start(
                out=out, in_=in_, allow_slow_non_contiguous=True),
                after=after, inc=inc, dma=True)
        return self.op(eng, lambda e: e.dma_start(out=out, in_=in_),
                       after=after, inc=inc, dma=True)

    def barrier(self):
        for e in self.ENGS:
            for s in ("dL", "dW", "dS", "dG", "eT", "eV", "eA", "eG"):
                self.pending[e][s] = max(self.pending[e].get(s, 0), self.cnt[s])

    def finalize(self, final_eng=None):
        self._wait("sync", "dS", self.cnt["dS"])
        blk = self.nc.Block()
        blk.__enter__()
        for e in self.ENGS:
            if self.q[e]:
                getattr(blk, e)(lambda eng, lst=tuple(self.q[e]): [f(eng) for f in lst])
        blk.__exit__(None, None, None)
        nc = self.nc
        self.es.__exit__(None, None, None)
        nc.finalize()
        return nc


def rearr(t, pat, **kw):
    return t.ap().rearrange(pat, **kw)


def mm_group(p, out_ap, lhsT_fn, rhs_fn, K, after=()):
    for k in range(K):
        p.op("tensor",
             lambda e, k=k: e.matmul(out=out_ap, lhsT=lhsT_fn(k), rhs=rhs_fn(k),
                                     start=(k == 0), stop=(k == K - 1)),
             after=(after if k == 0 else ()), inc="eT" if k == K - 1 else None)
    return p.tok("eT")


def emit_ln_stats(p, x_fn, nch, T, psS, psQ, sq, row_mean, row_rstd, after,
                  denom, eps, row_var=None):
    for i in range(nch):
        p.op("tensor", lambda e, i=i: e.matmul(
            out=psS[0:1, 0:T], lhsT=p.ones_col[:, 0:1], rhs=x_fn(i),
            start=(i == 0), stop=(i == nch - 1)), after=after,
            inc="eT" if i == nch - 1 else None)
    tS = p.tok("eT")
    last_mm = None
    for i in range(nch):
        aft = list(after) + ([last_mm] if last_mm else [])
        tq = p.op("gpsimd", lambda e, i=i: e.tensor_tensor(
            out=sq[:, 0:T], in0=x_fn(i), in1=x_fn(i), op=ALU.mult), after=aft)
        last_mm = p.op("tensor", lambda e, i=i: e.matmul(
            out=psQ[0:1, 0:T], lhsT=p.ones_col[:, 0:1], rhs=sq[:, 0:T],
            start=(i == 0), stop=(i == nch - 1)), after=[tq], inc="eT")
    tQ = p.tok("eT")
    t1 = p.op("vector", lambda e: e.tensor_single_scalar(
        out=row_mean, in_=psS[0:1, 0:T], scalar=1.0 / denom, op=ALU.mult),
        after=[tS])
    p.op("vector", lambda e: e.tensor_tensor(
        out=row_rstd, in0=row_mean, in1=row_mean, op=ALU.mult))
    p.op("vector", lambda e: e.scalar_tensor_tensor(
        out=row_rstd, in0=psQ[0:1, 0:T], scalar=1.0 / denom, in1=row_rstd,
        op0=ALU.mult, op1=ALU.subtract), after=[tQ])
    if row_var is not None:
        p.op("vector", lambda e: e.tensor_copy(out=row_var, in_=row_rstd))
    p.op("vector", lambda e: e.tensor_single_scalar(
        out=row_rstd, in_=row_rstd, scalar=eps, op=ALU.add))
    t4 = p.op("scalar", lambda e: e.activation(
        out=row_rstd, in_=row_rstd, func=AF.Ln),
        after=[p.tok("eV")])
    t5 = p.op("scalar", lambda e: e.activation(
        out=row_rstd, in_=row_rstd, func=AF.Exp, scale=-0.5))
    return t5


def emit_bcast(p, row_ap, ps_out_ap, after):
    return p.op("tensor", lambda e: e.matmul(
        out=ps_out_ap, lhsT=p.ones_row[0:1, :], rhs=row_ap,
        start=True, stop=True), after=after, inc="eT")


def common_consts(p):
    p.ones_col = p.sbuf("ones_col", [128, 1])
    p.ones_row = p.sbuf("ones_row", [1, 128])
    p.op("gpsimd", lambda e: e.memset(p.ones_col[:, :], 1.0))
    t = p.op("gpsimd", lambda e: e.memset(p.ones_row[:, :], 1.0))
    return t


def build_L1a():
    p = Prog()
    T = 512
    NSUB = SQ // T

    xin = p.din("x", [D, SQ + 3])
    wgT = p.din("wgT", [D, H])
    wiT = p.din("wiT", [D, H])
    cw = p.din("cw", [128, 48])
    cb = p.din("cb", [128, 12])
    o_gg = p.dout("ggate", [H, SQ], BF16)
    o_yb = p.dout("ybf", [H, SQ], BF16)

    s_wg = p.sbuf("s_wg", [128, 8, H], BF16)
    s_wi = p.sbuf("s_wi", [128, 8, H], BF16)
    s_cw = p.sbuf("s_cw", [128, 12, 4])
    s_cb = p.sbuf("s_cb", [128, 12])
    tcst = common_consts(p)
    s_x = p.sbuf("s_x", [128, 8, T])
    s_ln = p.sbuf("s_ln", [128, 8, T], BF16)
    s_sq = p.sbuf("s_sq", [128, T])
    s_row = p.sbuf("s_row", [1, 2 * T])
    s_u = p.sbuf("s_u", [128, 12, T + 3])
    SA = p.sbuf("SA", [128, 12, T])
    SB = p.sbuf("SB", [128, 12, T])
    s_gg = p.sbuf("s_gg", [128, 12, T], BF16)
    s_ybf = p.sbuf("s_ybf", [128, 12, T], BF16)
    psS = p.psum("psS", [1, T])
    psQ = p.psum("psQ", [1, T])
    ps_bm = p.psum("ps_bm", [128, T])
    ps_br = p.psum("ps_br", [128, T])
    ps_m0 = p.psum("ps_m0", [128, T])
    ps_m1 = p.psum("ps_m1", [128, T])
    ps_u3 = p.psum("ps_u3", [128, 4])

    twg = p.dma("gpsimd", s_wg[:, :, :], rearr(wgT, "(c p) h -> p c h", p=128), inc="dG")
    twi = p.dma("gpsimd", s_wi[:, :, :], rearr(wiT, "(c p) h -> p c h", p=128), inc="dG")
    tcw = p.dma("sync", s_cw[:, :, :], rearr(cw, "p (c k) -> p c k", k=4), inc="dW")
    tcb = p.dma("sync", s_cb[:, :], cb[:, :], inc="dW")

    def ln_block(width, tx):
        trs = emit_ln_stats(p, lambda i: s_x[:, i, 0:width], 8, width, psS, psQ,
                            s_sq, s_row[0:1, 0:width], s_row[0:1, T:T + width],
                            [tx, tcst], D, EPS)
        tb1 = emit_bcast(p, s_row[0:1, 0:width], ps_bm[:, 0:width], [trs])
        tb2 = emit_bcast(p, s_row[0:1, T:T + width], ps_br[:, 0:width], [trs])
        for i in range(8):
            p.op("vector", lambda e, i=i: e.tensor_tensor(
                out=s_sq[:, 0:width], in0=s_x[:, i, 0:width], in1=ps_bm[:, 0:width],
                op=ALU.subtract), after=[tb1, tb2])
            p.op("vector", lambda e, i=i: e.tensor_tensor(
                out=s_ln[:, i, 0:width], in0=s_sq[:, 0:width], in1=ps_br[:, 0:width],
                op=ALU.mult))
        return p.tok("eV")

    tx = p.dma("sync", s_x[:, :, 0:3],
               rearr(xin, "(c p) t -> p c t", p=128)[:, :, 0:3], inc="dL")
    tln = ln_block(3, tx)
    tcp = None
    for m in range(12):
        tmm = mm_group(p, ps_u3[:, 0:3],
                       lambda k, m=m: s_wi[:, k, m * 128:(m + 1) * 128],
                       lambda k: s_ln[:, k, 0:3], 8,
                       after=[tln, twi] + ([tcp] if tcp else []))
        tcp = p.op("vector", lambda e, m=m: e.tensor_copy(
            out=s_u[:, m, 0:3], in_=ps_u3[:, 0:3]), after=[tmm])
    p.barrier()

    for it in range(NSUB):
        o = it * T
        tx = p.dma("sync", s_x[:, :, :],
                   rearr(xin, "(c p) t -> p c t", p=128)[:, :, 3 + o:3 + o + T],
                   inc="dL")
        tln = ln_block(T, tx)

        evac = {}
        for m in range(12):
            pb = ps_m0 if (m % 2 == 0) else ps_m1
            aft = [tln, twg] + ([evac[m - 2]] if m - 2 in evac else [])
            tg = mm_group(p, pb[:, 0:T],
                          lambda k, m=m: s_wg[:, k, m * 128:(m + 1) * 128],
                          lambda k: s_ln[:, k, 0:T], 8, after=aft)
            evac[m] = p.op("scalar", lambda e, m=m, pb=pb: e.activation(
                out=s_gg[:, m, :], in_=pb[:, 0:T], func=AF.Gelu), after=[tg])
        tgg = p.tok("eA")
        p.dma("sync", rearr(o_gg, "(c p) t -> p c t", p=128)[:, :, o:o + T],
              s_gg[:, :, :], after=[tgg], inc="dS")

        evac = {}
        for m in range(12):
            pb = ps_m0 if (m % 2 == 0) else ps_m1
            aft = [tln, twi, tgg] + ([evac[m - 2]] if m - 2 in evac else [])
            tu = mm_group(p, pb[:, 0:T],
                          lambda k, m=m: s_wi[:, k, m * 128:(m + 1) * 128],
                          lambda k: s_ln[:, k, 0:T], 8, after=aft)
            tc = p.op("vector", lambda e, m=m, pb=pb: e.tensor_copy(
                out=s_u[:, m, 3:T + 3], in_=pb[:, 0:T]), after=[tu])
            evac[m] = tc
        tuall = p.tok("eV")

        for m in range(12):
            p.op("vector", lambda e, m=m: e.tensor_scalar(
                out=SA[:, m, :], in0=s_u[:, m, 0:T], scalar1=s_cw[:, m, 0:1],
                scalar2=None, op0=ALU.mult), after=[tuall, tcw])
            for j in (1, 2, 3):
                p.op("vector", lambda e, m=m, j=j: e.scalar_tensor_tensor(
                    out=SA[:, m, :], in0=s_u[:, m, j:j + T],
                    scalar=s_cw[:, m, j:j + 1], in1=SA[:, m, :],
                    op0=ALU.mult, op1=ALU.add))
            p.op("scalar", lambda e, m=m: e.activation(
                out=s_ybf[:, m, :], in_=SA[:, m, :], func=AF.Identity,
                bias=s_cb[:, m:m + 1]), after=[p.tok("eV"), tcb])
            p.op("vector", lambda e, m=m: e.tensor_copy(
                out=s_u[:, m, 0:3], in_=s_u[:, m, T:T + 3]))
        tyb = p.tok("eA")
        p.dma("sync", rearr(o_yb, "(c p) t -> p c t", p=128)[:, :, o:o + T],
              s_ybf[:, :, :], after=[tyb], inc="dS")
        p.barrier()
    return p.finalize()


def build_L1b():
    p = Prog()
    T = 512
    NSUB = SQ // T

    ybin = p.din("ybf", [H, SQ], BF16)
    wliT = p.din("wliT", [H, H])
    wlgT = p.din("wlgT", [H, H])
    lbi = p.din("lbi", [128, 12])
    lbg = p.din("lbg", [128, 12])
    lam = p.din("lam", [128, 12])
    o_al = p.dout("alpha", [H, SQ])
    o_xb = p.dout("xbeta", [H, SQ])
    o_sm = p.dout("summ", [H, 4])

    s_wli = p.sbuf("s_wli", [128, 12, H], BF16)
    s_wlg = p.sbuf("s_wlg", [128, 12, H], BF16)
    s_lbi = p.sbuf("s_lbi", [128, 12])
    s_lbg = p.sbuf("s_lbg", [128, 12])
    s_msp = p.sbuf("s_msp", [128, 12])
    s_t0 = p.sbuf("s_t0", [128, 12])
    s_yb = p.sbuf("s_yb", [128, 12, T], BF16)
    s_ig = p.sbuf("s_ig", [128, 12, T])
    s_rg = p.sbuf("s_rg", [128, 12, T])
    s_al = p.sbuf("s_al", [128, 12, T])
    s_xb = p.sbuf("s_xb", [128, 12, T])
    s_sm = p.sbuf("s_sm", [128, 12, 4])
    ps0 = p.psum("ps0", [128, T])
    ps1 = p.psum("ps1", [128, T])

    twl1 = p.dma("gpsimd", s_wli[:, :, :], rearr(wliT, "(c p) h -> p c h", p=128), inc="dG")
    twl2 = p.dma("gpsimd", s_wlg[:, :, :], rearr(wlgT, "(c p) h -> p c h", p=128), inc="dG")
    tb1 = p.dma("sync", s_lbi[:, :], lbi[:, :], inc="dW")
    tb2 = p.dma("sync", s_lbg[:, :], lbg[:, :], inc="dW")
    tlam = p.dma("sync", s_msp[:, :], lam[:, :], inc="dW")
    tabs = p.op("scalar", lambda e: e.activation(out=s_t0[:, :], in_=s_msp[:, :],
                                                 func=AF.Abs), after=[tlam])
    p.op("scalar", lambda e: e.activation(out=s_t0[:, :], in_=s_t0[:, :],
                                          func=AF.Exp, scale=-1.0))
    ta = p.op("scalar", lambda e: e.activation(out=s_t0[:, :], in_=s_t0[:, :],
                                               func=AF.Ln, bias=1.0))
    p.op("vector", lambda e: e.tensor_single_scalar(
        out=s_msp[:, :], in_=s_msp[:, :], scalar=0.0, op=ALU.max),
        after=[tlam, tabs])
    p.op("vector", lambda e: e.scalar_tensor_tensor(
        out=s_msp[:, :], in0=s_msp[:, :], scalar=1.0, in1=s_t0[:, :],
        op0=ALU.mult, op1=ALU.add), after=[ta])
    tmsp = p.op("vector", lambda e: e.tensor_single_scalar(
        out=s_msp[:, :], in_=s_msp[:, :], scalar=-8.0, op=ALU.mult))
    p.op("vector", lambda e: e.memset(s_sm[:, :, 0:1], 1.0))
    p.op("vector", lambda e: e.memset(s_sm[:, :, 1:2], 0.0))

    for it in range(NSUB):
        o = it * T
        last = (it == NSUB - 1)
        ty = p.dma("sync", s_yb[:, :, :],
                   rearr(ybin, "(c p) t -> p c t", p=128)[:, :, o:o + T], inc="dL")
        pslast = {}
        for (wt, bt, dst, wtok) in ((s_wli, s_lbi, s_ig, twl1),
                                    (s_wlg, s_lbg, s_rg, twl2)):
            for m in range(12):
                pb = ps0 if (m % 2 == 0) else ps1
                aft = [ty, wtok] + ([pslast[id(pb)]] if id(pb) in pslast else [])
                tg = mm_group(p, pb[:, 0:T],
                              lambda k, m=m, wt=wt: wt[:, k, m * 128:(m + 1) * 128],
                              lambda k: s_yb[:, k, :], 12, after=aft)
                pslast[id(pb)] = p.op("scalar", lambda e, m=m, pb=pb, dst=dst, bt=bt:
                               e.activation(out=dst[:, m, :], in_=pb[:, 0:T],
                                            func=AF.Sigmoid, bias=bt[:, m:m + 1]),
                               after=[tg, tb1, tb2])
        tsig = p.tok("eA")
        for m in range(12):
            p.op("scalar", lambda e, m=m: e.activation(
                out=s_al[:, m, :], in_=s_rg[:, m, :], func=AF.Exp,
                scale=s_msp[:, m:m + 1]), after=[tsig, tmsp])
        tal = p.tok("eA")
        p.dma("sync", rearr(o_al, "(c p) t -> p c t", p=128)[:, :, o:o + T],
              s_al[:, :, :], after=[tal], inc="dS")
        for m in range(12):
            p.op("vector", lambda e, m=m: e.tensor_tensor(
                out=s_rg[:, m, :], in0=s_al[:, m, :], in1=s_al[:, m, :],
                op=ALU.mult), after=[tal])
            t1 = p.op("vector", lambda e, m=m: e.tensor_scalar(
                out=s_rg[:, m, :], in0=s_rg[:, m, :], scalar1=-1.0,
                scalar2=1.000001, op0=ALU.mult, op1=ALU.add))
            p.op("scalar", lambda e, m=m: e.activation(
                out=s_rg[:, m, :], in_=s_rg[:, m, :], func=AF.Sqrt), after=[t1])
        tbe = p.tok("eA")
        for m in range(12):
            p.op("vector", lambda e, m=m: e.tensor_tensor(
                out=s_ig[:, m, :], in0=s_ig[:, m, :], in1=s_yb[:, m, :],
                op=ALU.mult), after=[tbe])
            p.op("vector", lambda e, m=m: e.tensor_tensor(
                out=s_xb[:, m, :], in0=s_rg[:, m, :], in1=s_ig[:, m, :],
                op=ALU.mult))
        txb = p.tok("eV")
        p.dma("sync", rearr(o_xb, "(c p) t -> p c t", p=128)[:, :, o:o + T],
              s_xb[:, :, :], after=[txb], inc="dS")
        for m in range(12):
            p.op("vector", lambda e, m=m: e.tensor_tensor_scan(
                out=s_ig[:, m, :], data0=s_al[:, m, :], data1=s_al[:, m, :],
                initial=s_sm[:, m, 0:1], op0=ALU.mult, op1=ALU.bypass),
                after=[txb])
            if last:
                p.op("vector", lambda e, m=m: e.tensor_copy(
                    out=s_sm[:, m, 2:3], in_=s_ig[:, m, T - 2:T - 1]))
            p.op("vector", lambda e, m=m: e.tensor_copy(
                out=s_sm[:, m, 0:1], in_=s_ig[:, m, T - 1:T]))
            p.op("vector", lambda e, m=m: e.tensor_tensor_scan(
                out=s_ig[:, m, :], data0=s_al[:, m, :], data1=s_xb[:, m, :],
                initial=s_sm[:, m, 1:2], op0=ALU.mult, op1=ALU.add))
            if last:
                p.op("vector", lambda e, m=m: e.tensor_copy(
                    out=s_sm[:, m, 3:4], in_=s_ig[:, m, T - 2:T - 1]))
            p.op("vector", lambda e, m=m: e.tensor_copy(
                out=s_sm[:, m, 1:2], in_=s_ig[:, m, T - 1:T]))
        p.barrier()
    tsm = p.tok("eV")
    p.dma("sync", rearr(o_sm, "(c p) k -> p c k", p=128), s_sm[:, :, :],
          after=[tsm], inc="dS")
    return p.finalize()


def build_L2a():
    p = Prog()
    T = 512
    NSUB = SQ // T

    a_in = p.din("alpha", [H, SQ])
    xb_in = p.din("xbeta", [H, SQ])
    gg_in = p.din("ggate", [H, SQ], BF16)
    x_in = p.din("x", [D, SQ])
    pgam = p.din("pgam", [128, 12, 3])
    psnd = p.din("psend", [128, 12, 3])
    woT = p.din("woT", [H, D])
    o_r1 = p.dout("r1", [D, SQ])

    s_wo = p.sbuf("s_wo", [128, 12, D], BF16)
    s_pg = p.sbuf("s_pg", [128, 12, 3])
    s_ps = p.sbuf("s_ps", [128, 12, 3])
    s_cy = p.sbuf("s_cy", [128, 12])
    s_al = p.sbuf("s_al", [128, 12, T])
    s_xb = p.sbuf("s_xb", [128, 12, T])
    s_gg = p.sbuf("s_gg", [128, 12, T], BF16)
    s_h = p.sbuf("s_h", [128, 12, T])
    s_a1 = p.sbuf("s_a1", [128, 12, T], BF16)
    s_x = p.sbuf("s_x", [128, 8, T])
    s_r1 = p.sbuf("s_r1", [128, 8, T])
    ps0 = p.psum("ps0", [128, T])
    ps1 = p.psum("ps1", [128, T])

    two = p.dma("gpsimd", s_wo[:, :, :], rearr(woT, "(c p) d -> p c d", p=128), inc="dG")
    tg1 = p.dma("sync", s_pg[:, :, :], pgam.ap(), inc="dW")
    tg2 = p.dma("sync", s_ps[:, :, :], psnd.ap(), inc="dW")
    tc = p.op("vector", lambda e: e.memset(s_cy[:, :], 0.0), after=[tg1, tg2])
    for j in range(3):
        p.op("vector", lambda e, j=j: e.tensor_tensor(
            out=s_cy[:, :], in0=s_cy[:, :], in1=s_pg[:, :, j], op=ALU.mult))
        p.op("vector", lambda e, j=j: e.tensor_tensor(
            out=s_cy[:, :], in0=s_cy[:, :], in1=s_ps[:, :, j], op=ALU.add))
    tcy = p.tok("eV")

    for it in range(NSUB):
        o = it * T
        p.dma("sync", s_al[:, :, :],
              rearr(a_in, "(c p) t -> p c t", p=128)[:, :, o:o + T], inc="dL")
        p.dma("sync", s_xb[:, :, :],
              rearr(xb_in, "(c p) t -> p c t", p=128)[:, :, o:o + T], inc="dL")
        p.dma("sync", s_gg[:, :, :],
              rearr(gg_in, "(c p) t -> p c t", p=128)[:, :, o:o + T], inc="dL")
        tld = p.dma("sync", s_x[:, :, :],
                    rearr(x_in, "(c p) t -> p c t", p=128)[:, :, o:o + T], inc="dL")
        ta = tb = tg = tx = tld
        for m in range(12):
            p.op("vector", lambda e, m=m: e.tensor_tensor_scan(
                out=s_h[:, m, :], data0=s_al[:, m, :], data1=s_xb[:, m, :],
                initial=s_cy[:, m:m + 1], op0=ALU.mult, op1=ALU.add),
                after=[ta, tb, tcy])
            p.op("vector", lambda e, m=m: e.tensor_copy(
                out=s_cy[:, m:m + 1], in_=s_h[:, m, T - 1:T]))
            p.op("vector", lambda e, m=m: e.tensor_tensor(
                out=s_a1[:, m, :], in0=s_gg[:, m, :], in1=s_h[:, m, :],
                op=ALU.mult), after=[tg])
        ta1 = p.tok("eV")
        pslast = {}
        for m in range(8):
            pb = ps0 if (m % 2 == 0) else ps1
            aft = [ta1, two, tx] + ([pslast[id(pb)]] if id(pb) in pslast else [])
            tmm = mm_group(p, pb[:, 0:T],
                           lambda k, m=m: s_wo[:, k, m * 128:(m + 1) * 128],
                           lambda k: s_a1[:, k, :], 12, after=aft)
            pslast[id(pb)] = p.op("vector", lambda e, m=m, pb=pb: e.tensor_tensor(
                out=s_r1[:, m, :], in0=pb[:, 0:T], in1=s_x[:, m, :], op=ALU.add),
                after=[tmm])
        tr1 = p.tok("eV")
        p.dma("sync", rearr(o_r1, "(c p) t -> p c t", p=128)[:, :, o:o + T],
              s_r1[:, :, :], after=[tr1], inc="dS")
        p.barrier()
    return p.finalize()


def _emit_wquant(p, wT, qdst, nrc, ncol, wsc, c0, c1, tcst, s_stg, s_wt2,
                 psS, s_acc, s_m1, s_sc128, bcP):
    stages = []
    for c in range(nrc):
        for hcol in range(0, ncol, 2048):
            stages.append((c, hcol, min(2048, ncol - hcol)))
    tz = p.op("vector", lambda e: e.memset(s_acc[:, :], 0.0))
    tld_prev = None
    for (c, hcol, w) in stages:
        tld = p.dma("sync", s_stg[:, 0:w],
                    rearr(wT, "(c p) f -> p c f", p=128)[:, c, hcol:hcol + w],
                    after=[("eV", p.cnt["eV"])], inc="dL")
        p.op("vector", lambda e, w=w: e.tensor_reduce(
            out=s_m1[:, 0:1], in_=s_stg[:, 0:w], axis=AX.X, op=ALU.add,
            apply_absolute_value=True), after=[tld, tz])
        tld_prev = p.op("vector", lambda e: e.tensor_tensor(
            out=s_acc[:, :], in0=s_acc[:, :], in1=s_m1[:, 0:1], op=ALU.add))
    tmm = p.op("tensor", lambda e: e.matmul(
        out=psS[0:1, 0:1], lhsT=p.ones_col[:, :], rhs=s_acc[:, :],
        start=True, stop=True), after=[tld_prev, tcst], inc="eT")
    n = nrc * 128 * ncol
    p.op("vector", lambda e, n=n, c1=c1: e.tensor_single_scalar(
        out=wsc[0:1, c1:c1 + 1], in_=psS[0:1, 0:1], scalar=1.0 / n,
        op=ALU.mult), after=[tmm])
    tm = p.op("vector", lambda e, c1=c1: e.tensor_single_scalar(
        out=wsc[0:1, c1:c1 + 1], in_=wsc[0:1, c1:c1 + 1], scalar=1e-5,
        op=ALU.max))
    ta = p.op("scalar", lambda e, c0=c0, c1=c1: e.activation(
        out=wsc[0:1, c0:c0 + 1], in_=wsc[0:1, c1:c1 + 1], func=AF.Ln),
        after=[tm])
    ta = p.op("scalar", lambda e, c0=c0: e.activation(
        out=wsc[0:1, c0:c0 + 1], in_=wsc[0:1, c0:c0 + 1], func=AF.Exp,
        scale=-1.0))
    tbc = p.op("tensor", lambda e, c0=c0: e.matmul(
        out=bcP[:, 0:1], lhsT=p.ones_row[0:1, :], rhs=wsc[0:1, c0:c0 + 1],
        start=True, stop=True), after=[ta], inc="eT")
    tsc = p.op("vector", lambda e: e.tensor_copy(
        out=s_sc128[:, :], in_=bcP[:, 0:1]), after=[tbc])
    tld_prev = None
    for (c, hcol, w) in stages:
        tld = p.dma("sync", s_stg[:, 0:w],
                    rearr(wT, "(c p) f -> p c f", p=128)[:, c, hcol:hcol + w],
                    after=[("eV", p.cnt["eV"]), ("eA", p.cnt["eA"])], inc="dL")
        p.op("vector", lambda e, w=w: e.tensor_scalar(
            out=s_stg[:, 0:w], in0=s_stg[:, 0:w], scalar1=s_sc128[:, 0:1],
            scalar2=None, op0=ALU.mult), after=[tld, tsc])
        p.op("scalar", lambda e, w=w: e.activation(
            out=s_wt2[:, 0:w], in_=s_stg[:, 0:w], func=AF.Abs),
            after=[p.tok("eV")])
        p.op("vector", lambda e, w=w: e.tensor_single_scalar(
            out=s_wt2[:, 0:w], in_=s_wt2[:, 0:w], scalar=0.5, op=ALU.is_ge),
            after=[p.tok("eA")])
        ts = p.op("scalar", lambda e, w=w: e.activation(
            out=s_stg[:, 0:w], in_=s_stg[:, 0:w], func=AF.Sign),
            after=[p.tok("eV")])
        tld_prev = p.op("vector", lambda e, c=c, hcol=hcol, w=w, qdst=qdst:
                        e.tensor_tensor(out=qdst[:, c, hcol:hcol + w],
                                        in0=s_stg[:, 0:w], in1=s_wt2[:, 0:w],
                                        op=ALU.mult), after=[ts])
    return tld_prev


def _emit_round_clip(p, s_v, s_int, s_cf, s_fx, s_fg, dst_ap, lo, hi, T, after):
    t1 = p.op("vector", lambda e: e.tensor_copy(out=s_int[:, 0:T], in_=s_v[:, 0:T]),
              after=after)
    t2 = p.op("vector", lambda e: e.tensor_copy(out=s_cf[:, 0:T], in_=s_int[:, 0:T]))
    t3 = p.op("vector", lambda e: e.scalar_tensor_tensor(
        out=s_fx[:, 0:T], in0=s_cf[:, 0:T], scalar=-1.0, in1=s_v[:, 0:T],
        op0=ALU.mult, op1=ALU.add), after=[t2])
    t4 = p.op("vector", lambda e: e.tensor_single_scalar(
        out=s_fg[:, 0:T], in_=s_fx[:, 0:T], scalar=0.5, op=ALU.is_gt),
        after=[t3])
    t5 = p.op("vector", lambda e: e.tensor_tensor(
        out=s_cf[:, 0:T], in0=s_cf[:, 0:T], in1=s_fg[:, 0:T], op=ALU.add),
        after=[t4])
    t6 = p.op("vector", lambda e: e.tensor_single_scalar(
        out=s_fg[:, 0:T], in_=s_fx[:, 0:T], scalar=-0.5, op=ALU.is_lt),
        after=[t5])
    t7 = p.op("vector", lambda e: e.tensor_tensor(
        out=s_cf[:, 0:T], in0=s_cf[:, 0:T], in1=s_fg[:, 0:T], op=ALU.subtract),
        after=[t6])
    return p.op("vector", lambda e: e.tensor_scalar(
        out=dst_ap, in0=s_cf[:, 0:T], scalar1=float(hi), scalar2=float(lo),
        op0=ALU.min, op1=ALU.max))


def build_L2b1():
    p = Prog()
    T = 512
    NSUB = SQ // T

    r1_in = p.din("r1", [D, SQ])
    w1T = p.din("w1T", [D, FF])
    b1 = p.din("b1", [128, 32])
    ident = p.din("ident", [128, 128])
    o_h1 = p.dout("h1", [FF, SQ], BF16)

    s_w1q = p.sbuf("s_w1q", [128, 8, FF], BF16)
    s_stg = p.sbuf("s_stg", [128, 2048])
    s_wt2 = p.sbuf("s_wt2", [128, 2048])
    s_int = p.sbuf("s_int", [128, T], I32)
    s_id = p.sbuf("s_id", [128, 128])
    s_acc = p.sbuf("s_acc", [128, 1])
    s_wsc = p.sbuf("s_wsc", [1, 8])
    s_sc128 = p.sbuf("s_sc128", [128, 1])
    s_b1 = p.sbuf("s_b1", [128, 32])
    s_r1 = p.sbuf("s_r1", [128, 8, T])
    s_xm = p.sbuf("s_xm", [128, 8, T])
    s_q1 = p.sbuf("s_q1", [128, 8, T], BF16)
    s_h1 = p.sbuf("s_h1", [128, 32, T], BF16)
    s_m1 = p.sbuf("s_m1", [128, T])
    s_sq = p.sbuf("s_sq", [128, T])
    s_v = p.sbuf("s_v", [128, T])
    s_cf = p.sbuf("s_cf", [128, T])
    s_fx = p.sbuf("s_fx", [128, T])
    s_fg = p.sbuf("s_fg", [128, T])
    s_mxc = p.sbuf("s_mxc", [128, 1])
    s_bm = p.sbuf("s_bm", [128, T])
    s_bc = p.sbuf("s_bc", [128, T])
    s_be = p.sbuf("s_be", [128, T])
    ROW = p.sbuf("ROWS", [1, 10 * T])
    psS = p.psum("psS", [1, T])
    psQ = p.psum("psQ", [1, T])
    bcP = p.psum("bcP", [128, T])
    ps0 = p.psum("ps0", [128, T])
    ps1 = p.psum("ps1", [128, T])
    psT = p.psum("psT", [128, 128])

    r_mean = ROW[0:1, 0 * T:1 * T]
    r_rstd = ROW[0:1, 1 * T:2 * T]
    r_var = ROW[0:1, 2 * T:3 * T]
    r_r2a = ROW[0:1, 3 * T:4 * T]
    r_mx = ROW[0:1, 4 * T:4 * T + 128]
    r_mq = ROW[0:1, 5 * T:6 * T]
    r_sx = ROW[0:1, 6 * T:7 * T]
    r_esc = ROW[0:1, 7 * T:8 * T]
    r_csc = ROW[0:1, 8 * T:9 * T]

    tcst = common_consts(p)
    tid = p.dma("sync", s_id[:, :], ident.ap(), inc="dW")
    tb1 = p.dma("sync", s_b1[:, :], b1.ap(), inc="dW")
    twsc0 = p.op("vector", lambda e: e.memset(s_wsc[:, :], 0.0))
    tq = _emit_wquant(p, w1T, s_w1q, 8, FF, s_wsc, 0, 1, tcst, s_stg, s_wt2,
                      psS, s_acc, s_m1, s_sc128, bcP)
    p.barrier()

    for it in range(NSUB):
        o = it * T
        tld = p.dma("sync", s_r1[:, :, :],
                    rearr(r1_in, "(c p) t -> p c t", p=128)[:, :, o:o + T],
                    inc="dL")
        trs = emit_ln_stats(p, lambda i: s_r1[:, i, :], 8, T, psS, psQ, s_sq,
                            r_mean, r_rstd, [tld, tcst], D, EPS, row_var=r_var)
        p.op("vector", lambda e: e.tensor_tensor(
            out=r_r2a, in0=r_var, in1=r_rstd, op=ALU.mult), after=[trs])
        p.op("vector", lambda e: e.tensor_tensor(
            out=r_r2a, in0=r_r2a, in1=r_rstd, op=ALU.mult))
        t1 = p.op("vector", lambda e: e.tensor_single_scalar(
            out=r_r2a, in_=r_r2a, scalar=EPS, op=ALU.add))
        p.op("scalar", lambda e: e.activation(out=r_r2a, in_=r_r2a, func=AF.Ln),
             after=[t1])
        tr2 = p.op("scalar", lambda e: e.activation(out=r_r2a, in_=r_r2a,
                                                    func=AF.Exp, scale=-0.5))
        tbc = emit_bcast(p, r_mean, bcP[:, 0:T], [trs])
        tbm = p.op("vector", lambda e: e.tensor_copy(out=s_bm[:, 0:T],
                                                     in_=bcP[:, 0:T]), after=[tbc])
        for i in range(8):
            p.op("vector", lambda e, i=i: e.tensor_tensor(
                out=s_xm[:, i, :], in0=s_r1[:, i, :], in1=s_bm[:, 0:T],
                op=ALU.subtract), after=[tbm])
        txm = p.tok("eV")
        ta0 = p.op("scalar", lambda e: e.activation(
            out=s_m1[:, 0:T], in_=s_xm[:, 0, :], func=AF.Abs), after=[txm])
        for i in range(1, 8):
            tai = p.op("scalar", lambda e, i=i: e.activation(
                out=s_fx[:, 0:T], in_=s_xm[:, i, :], func=AF.Abs),
                after=[("eV", p.cnt["eV"])])
            p.op("vector", lambda e: e.tensor_tensor(
                out=s_m1[:, 0:T], in0=s_m1[:, 0:T], in1=s_fx[:, 0:T],
                op=ALU.max), after=[tai, ta0])
        tmx = p.tok("eV")
        tmxr = None
        for tb_ in range(T // 128):
            ttr = p.op("tensor", lambda e, tb_=tb_: e.transpose(
                out=psT[:, :], in_=s_m1[:, tb_ * 128:(tb_ + 1) * 128],
                identity=s_id[:, :]),
                after=[tmx, tid] + ([tmxr] if tmxr else []), inc="eT")
            trd = p.op("vector", lambda e: e.tensor_reduce(
                out=s_mxc[:, 0:1], in_=psT[:, :], axis=AX.X, op=ALU.max),
                after=[ttr])
            ttr2 = p.op("tensor", lambda e: e.transpose(
                out=psT[0:1, 0:128], in_=s_mxc[:, 0:1], identity=s_id[:, :]),
                after=[trd], inc="eT")
            tmxr = p.op("vector", lambda e, tb_=tb_: e.tensor_copy(
                out=ROW[0:1, 5 * T + tb_ * 128:5 * T + (tb_ + 1) * 128],
                in_=psT[0:1, 0:128]), after=[ttr2])
        p.op("vector", lambda e: e.tensor_tensor(
            out=r_mq, in0=r_mq, in1=r_rstd, op=ALU.mult), after=[tmxr, tr2])
        p.op("vector", lambda e: e.tensor_tensor(
            out=r_mq, in0=r_mq, in1=r_r2a, op=ALU.mult))
        tq1 = p.op("vector", lambda e: e.tensor_single_scalar(
            out=r_mq, in_=r_mq, scalar=1e-5, op=ALU.max))
        p.op("scalar", lambda e: e.activation(out=r_sx, in_=r_mq, func=AF.Ln),
             after=[tq1])
        ta = p.op("scalar", lambda e: e.activation(out=r_sx, in_=r_sx,
                                                   func=AF.Exp, scale=-1.0))
        p.op("vector", lambda e: e.tensor_single_scalar(
            out=r_sx, in_=r_sx, scalar=127.0, op=ALU.mult), after=[ta])
        p.op("vector", lambda e: e.tensor_single_scalar(
            out=r_esc, in_=r_mq, scalar=1.0 / 127.0, op=ALU.mult))
        p.op("vector", lambda e: e.tensor_scalar(
            out=r_esc, in0=r_esc, scalar1=s_wsc[0:1, 1:2], scalar2=None,
            op0=ALU.mult))
        p.op("vector", lambda e: e.tensor_tensor(
            out=r_csc, in0=r_rstd, in1=r_r2a, op=ALU.mult))
        tcsc = p.op("vector", lambda e: e.tensor_tensor(
            out=r_csc, in0=r_csc, in1=r_sx, op=ALU.mult))
        tbc1 = emit_bcast(p, r_csc, bcP[:, 0:T], [tcsc, tbm])
        tb_c = p.op("vector", lambda e: e.tensor_copy(out=s_bc[:, 0:T],
                                                      in_=bcP[:, 0:T]), after=[tbc1])
        tbc2 = emit_bcast(p, r_esc, bcP[:, 0:T], [tb_c])
        tb_e = p.op("vector", lambda e: e.tensor_copy(out=s_be[:, 0:T],
                                                      in_=bcP[:, 0:T]), after=[tbc2])
        for i in range(8):
            tv = p.op("vector", lambda e, i=i: e.tensor_tensor(
                out=s_v[:, 0:T], in0=s_xm[:, i, :], in1=s_bc[:, 0:T],
                op=ALU.mult), after=[tb_c])
            _emit_round_clip(p, s_v, s_int, s_cf, s_fx, s_fg, s_q1[:, i, :],
                             -128.0, 127.0, T, [tv])
        tq1b = p.tok("eV")
        pslast = {}
        for m in range(32):
            pb = ps0 if (m % 2 == 0) else ps1
            aft = [tq1b, tq] + ([pslast[id(pb)]] if id(pb) in pslast else [])
            tmm = mm_group(p, pb[:, 0:T],
                           lambda k, m=m: s_w1q[:, k, m * 128:(m + 1) * 128],
                           lambda k: s_q1[:, k, :], 8, after=aft)
            tt = p.op("vector", lambda e, m=m, pb=pb: e.tensor_tensor(
                out=s_sq[:, 0:T], in0=pb[:, 0:T], in1=s_be[:, 0:T],
                op=ALU.mult), after=[tmm, tb_e, ("eA", p.cnt["eA"])])
            pslast[id(pb)] = tt
            p.op("scalar", lambda e, m=m: e.activation(
                out=s_h1[:, m, :], in_=s_sq[:, 0:T], func=AF.Gelu,
                bias=s_b1[:, m:m + 1]), after=[tt, tb1])
        th1 = p.tok("eA")
        p.dma("sync", rearr(o_h1, "(c p) t -> p c t", p=128)[:, :, o:o + T],
              s_h1[:, :, :], after=[th1], inc="dS")
        p.barrier()
    return p.finalize()


def build_L2b2():
    p = Prog()
    T = 512
    NSUB = SQ // T

    h1_in = p.din("h1", [FF, SQ], BF16)
    r1_in = p.din("r1", [D, SQ])
    w2T = p.din("w2T", [FF, D])
    b2 = p.din("b2", [128, 8])
    ident = p.din("ident", [128, 128])
    o_r2 = p.dout("r2", [D, SQ])

    s_w2q = p.sbuf("s_w2q", [128, 32, D], BF16)
    s_stg = p.sbuf("s_stg", [128, 1024])
    s_wt2 = p.sbuf("s_wt2", [128, 1024])
    s_int = p.sbuf("s_int", [128, T], I32)
    s_id = p.sbuf("s_id", [128, 128])
    s_acc = p.sbuf("s_acc", [128, 1])
    s_wsc = p.sbuf("s_wsc", [1, 8])
    s_sc128 = p.sbuf("s_sc128", [128, 1])
    s_b2 = p.sbuf("s_b2", [128, 8])
    s_h1 = p.sbuf("s_h1", [128, 32, T], BF16)
    s_q2 = p.sbuf("s_q2", [128, 32, T], BF16)
    s_r1 = p.sbuf("s_r1", [128, 8, T])
    s_r2 = p.sbuf("s_r2", [128, 8, T])
    s_m1 = p.sbuf("s_m1", [128, T])
    s_sq = p.sbuf("s_sq", [128, T])
    s_v = p.sbuf("s_v", [128, T])
    s_cf = p.sbuf("s_cf", [128, T])
    s_fx = p.sbuf("s_fx", [128, T])
    s_fg = p.sbuf("s_fg", [128, T])
    s_mxc = p.sbuf("s_mxc", [128, 1])
    s_bc = p.sbuf("s_bc", [128, T])
    s_be = p.sbuf("s_be", [128, T])
    ROW = p.sbuf("ROWS", [1, 10 * T])
    psS = p.psum("psS", [1, T])
    psQ = p.psum("psQ", [1, T])
    bcP = p.psum("bcP", [128, T])
    ps0 = p.psum("ps0", [128, T])
    ps1 = p.psum("ps1", [128, T])
    psT = p.psum("psT", [128, 128])

    r_r2a = ROW[0:1, 3 * T:4 * T]
    r_mq = ROW[0:1, 5 * T:6 * T]
    r_sx = ROW[0:1, 6 * T:7 * T]
    r_esc = ROW[0:1, 7 * T:8 * T]
    r_csc = ROW[0:1, 8 * T:9 * T]

    tcst = common_consts(p)
    tid = p.dma("sync", s_id[:, :], ident.ap(), inc="dW")
    tb2 = p.dma("sync", s_b2[:, :], b2.ap(), inc="dW")
    twsc0 = p.op("vector", lambda e: e.memset(s_wsc[:, :], 0.0))
    tq = _emit_wquant(p, w2T, s_w2q, 32, D, s_wsc, 2, 3, tcst, s_stg, s_wt2,
                      psS, s_acc, s_m1, s_sc128, bcP)
    p.barrier()

    for it in range(NSUB):
        o = it * T
        p.dma("sync", s_h1[:, :, :],
              rearr(h1_in, "(c p) t -> p c t", p=128)[:, :, o:o + T], inc="dL")
        tld = p.dma("sync", s_r1[:, :, :],
                    rearr(r1_in, "(c p) t -> p c t", p=128)[:, :, o:o + T],
                    inc="dL")
        last_mm = None
        for i in range(32):
            aft = [tld, tcst] + ([last_mm] if last_mm else [])
            tqq = p.op("scalar", lambda e, i=i: e.activation(
                out=s_sq[:, 0:T], in_=s_h1[:, i, :], func=AF.Square), after=aft)
            last_mm = p.op("tensor", lambda e, i=i: e.matmul(
                out=psQ[0:1, 0:T], lhsT=p.ones_col[:, :], rhs=s_sq[:, 0:T],
                start=(i == 0), stop=(i == 31)), after=[tqq], inc="eT")
        tQ = p.tok("eT")
        ta0 = p.op("scalar", lambda e: e.activation(
            out=s_m1[:, 0:T], in_=s_h1[:, 0, :], func=AF.Abs), after=[tld])
        for i in range(1, 32):
            tai = p.op("scalar", lambda e, i=i: e.activation(
                out=s_fx[:, 0:T], in_=s_h1[:, i, :], func=AF.Abs),
                after=[("eV", p.cnt["eV"])])
            p.op("vector", lambda e: e.tensor_tensor(
                out=s_m1[:, 0:T], in0=s_m1[:, 0:T], in1=s_fx[:, 0:T],
                op=ALU.max), after=[tai, ta0])
        tmx = p.tok("eV")
        t1 = p.op("vector", lambda e: e.tensor_single_scalar(
            out=r_r2a, in_=psQ[0:1, 0:T], scalar=1.0 / FF, op=ALU.mult),
            after=[tQ])
        t1 = p.op("vector", lambda e: e.tensor_single_scalar(
            out=r_r2a, in_=r_r2a, scalar=EPS, op=ALU.add))
        p.op("scalar", lambda e: e.activation(out=r_r2a, in_=r_r2a, func=AF.Ln),
             after=[t1])
        trh = p.op("scalar", lambda e: e.activation(out=r_r2a, in_=r_r2a,
                                                    func=AF.Exp, scale=-0.5))
        tmxr = None
        for tb_ in range(T // 128):
            ttr = p.op("tensor", lambda e, tb_=tb_: e.transpose(
                out=psT[:, :], in_=s_m1[:, tb_ * 128:(tb_ + 1) * 128],
                identity=s_id[:, :]),
                after=[tmx, tid] + ([tmxr] if tmxr else []), inc="eT")
            trd = p.op("vector", lambda e: e.tensor_reduce(
                out=s_mxc[:, 0:1], in_=psT[:, :], axis=AX.X, op=ALU.max),
                after=[ttr])
            ttr2 = p.op("tensor", lambda e: e.transpose(
                out=psT[0:1, 0:128], in_=s_mxc[:, 0:1], identity=s_id[:, :]),
                after=[trd], inc="eT")
            tmxr = p.op("vector", lambda e, tb_=tb_: e.tensor_copy(
                out=ROW[0:1, 5 * T + tb_ * 128:5 * T + (tb_ + 1) * 128],
                in_=psT[0:1, 0:128]), after=[ttr2])
        p.op("vector", lambda e: e.tensor_tensor(
            out=r_mq, in0=r_mq, in1=r_r2a, op=ALU.mult), after=[tmxr, trh])
        tq1_ = p.op("vector", lambda e: e.tensor_single_scalar(
            out=r_mq, in_=r_mq, scalar=1e-5, op=ALU.max))
        p.op("scalar", lambda e: e.activation(out=r_sx, in_=r_mq, func=AF.Ln),
             after=[tq1_])
        ta = p.op("scalar", lambda e: e.activation(out=r_sx, in_=r_sx,
                                                   func=AF.Exp, scale=-1.0))
        p.op("vector", lambda e: e.tensor_single_scalar(
            out=r_sx, in_=r_sx, scalar=127.0, op=ALU.mult), after=[ta])
        p.op("vector", lambda e: e.tensor_single_scalar(
            out=r_esc, in_=r_mq, scalar=1.0 / 127.0, op=ALU.mult))
        p.op("vector", lambda e: e.tensor_scalar(
            out=r_esc, in0=r_esc, scalar1=s_wsc[0:1, 3:4], scalar2=None,
            op0=ALU.mult))
        tcsc = p.op("vector", lambda e: e.tensor_tensor(
            out=r_csc, in0=r_r2a, in1=r_sx, op=ALU.mult))
        tbc1 = emit_bcast(p, r_csc, bcP[:, 0:T], [tcsc])
        tb_c = p.op("vector", lambda e: e.tensor_copy(out=s_bc[:, 0:T],
                                                      in_=bcP[:, 0:T]), after=[tbc1])
        tbc2 = emit_bcast(p, r_esc, bcP[:, 0:T], [tb_c])
        tb_e = p.op("vector", lambda e: e.tensor_copy(out=s_be[:, 0:T],
                                                      in_=bcP[:, 0:T]), after=[tbc2])
        for i in range(32):
            tv = p.op("vector", lambda e, i=i: e.tensor_tensor(
                out=s_v[:, 0:T], in0=s_h1[:, i, :], in1=s_bc[:, 0:T],
                op=ALU.mult), after=[tb_c])
            _emit_round_clip(p, s_v, s_int, s_cf, s_fx, s_fg, s_q2[:, i, :],
                             -128.0, 127.0, T, [tv])
        tq2b = p.tok("eV")
        pslast = {}
        for m in range(8):
            pb = ps0 if (m % 2 == 0) else ps1
            aft = [tq2b, tq] + ([pslast[id(pb)]] if id(pb) in pslast else [])
            tmm = mm_group(p, pb[:, 0:T],
                           lambda k, m=m: s_w2q[:, k, m * 128:(m + 1) * 128],
                           lambda k: s_q2[:, k, :], 32, after=aft)
            tt = p.op("vector", lambda e, m=m, pb=pb: e.tensor_tensor(
                out=s_r2[:, m, :], in0=pb[:, 0:T], in1=s_be[:, 0:T],
                op=ALU.mult), after=[tmm, tb_e])
            pslast[id(pb)] = tt
            tb_ = p.op("scalar", lambda e, m=m: e.activation(
                out=s_r2[:, m, :], in_=s_r2[:, m, :], func=AF.Identity,
                bias=s_b2[:, m:m + 1]), after=[tt, tb2])
            p.op("vector", lambda e, m=m: e.tensor_tensor(
                out=s_r2[:, m, :], in0=s_r2[:, m, :], in1=s_r1[:, m, :],
                op=ALU.add), after=[tb_])
        tr2o = p.tok("eV")
        p.dma("sync", rearr(o_r2, "(c p) t -> p c t", p=128)[:, :, o:o + T],
              s_r2[:, :, :], after=[tr2o], inc="dS")
        p.barrier()
    return p.finalize()


def build_L3():
    p = Prog()
    T = 512
    NSUB = SQ // T

    r2h = p.din("r2h", [D, SQ + 3])
    ipjT = p.din("ipjT", [D, 2 * D])
    ipb = p.din("ipb", [128, 16])
    cw1 = p.din("cw1", [128, 32])
    cb1 = p.din("cb1", [128, 8])
    bcwT = p.din("bcwT", [D, 64])
    bcb = p.din("bcb", [64, 1])
    dtwT = p.din("dtwT", [D, D])
    dtb = p.din("dtb", [128, 8])
    hm = p.din("hmask", [128, 3])
    o_dt = p.dout("dt", [D, SQ])
    o_dx = p.dout("dtxs", [D, SQ])
    o_bc = p.dout("bc", [64, SQ])
    o_sz = p.dout("siluz", [D, SQ], BF16)

    s_ipj = p.sbuf("s_ipj", [128, 8, 2 * D], BF16)
    s_dtw = p.sbuf("s_dtw", [128, 8, D], BF16)
    s_bcw = p.sbuf("s_bcw", [128, 8, 64], BF16)
    s_ipb = p.sbuf("s_ipb", [128, 16])
    s_cw = p.sbuf("s_cw", [128, 8, 4])
    s_cb = p.sbuf("s_cb", [128, 8])
    s_dtb = p.sbuf("s_dtb", [128, 8])
    s_bcb = p.sbuf("s_bcb", [64, 1])
    s_hm = p.sbuf("s_hm", [128, 3])
    tcst = common_consts(p)
    s_x = p.sbuf("s_x", [128, 8, T])
    s_ln = p.sbuf("s_ln", [128, 8, T], BF16)
    s_sq = p.sbuf("s_sq", [128, T])
    s_row = p.sbuf("s_row2", [1, 2 * T])
    s_xc = p.sbuf("s_xc", [128, 8, T + 3])
    s_z = p.sbuf("s_z", [128, 8, T], BF16)
    s_xs = p.sbuf("s_xs", [128, 8, T])
    s_xsb = p.sbuf("s_xsb", [128, 8, T], BF16)
    s_dt = p.sbuf("s_dt", [128, 8, T])
    s_dx = p.sbuf("s_dx", [128, 8, T])
    s_sp = p.sbuf("s_sp", [128, T])
    s_bc64 = p.sbuf("s_bc64", [128, T])
    psS = p.psum("psS", [1, T])
    psQ = p.psum("psQ", [1, T])
    ps_bm = p.psum("ps_bm", [128, T])
    ps_br = p.psum("ps_br", [128, T])
    ps0 = p.psum("ps0", [128, T])
    ps1 = p.psum("ps1", [128, T])

    tw1 = p.dma("gpsimd", s_ipj[:, :, :], rearr(ipjT, "(c p) f -> p c f", p=128), inc="dG")
    tw2 = p.dma("gpsimd", s_dtw[:, :, :], rearr(dtwT, "(c p) f -> p c f", p=128), inc="dG")
    tw3 = p.dma("gpsimd", s_bcw[:, :, :], rearr(bcwT, "(c p) f -> p c f", p=128), inc="dG")
    tv = p.dma("sync", s_ipb[:, :], ipb.ap(), inc="dW")
    tv = p.dma("sync", s_cw[:, :, :], rearr(cw1, "p (c k) -> p c k", k=4), inc="dW")
    tv = p.dma("sync", s_cb[:, :], cb1.ap(), inc="dW")
    tv = p.dma("sync", s_dtb[:, :], dtb.ap(), inc="dW")
    tv = p.dma("sync", s_bcb[:, :], bcb.ap(), inc="dW")
    tv = p.dma("sync", s_hm[:, :], hm.ap(), inc="dW")
    twv = p.tok("dW")

    def ln_block(width, tx):
        trs = emit_ln_stats(p, lambda i: s_x[:, i, 0:width], 8, width, psS, psQ,
                            s_sq, s_row[0:1, 0:width], s_row[0:1, T:T + width],
                            [tx, tcst], D, EPS)
        tb1 = emit_bcast(p, s_row[0:1, 0:width], ps_bm[:, 0:width], [trs])
        tb2 = emit_bcast(p, s_row[0:1, T:T + width], ps_br[:, 0:width], [trs])
        for i in range(8):
            p.op("vector", lambda e, i=i, width=width: e.tensor_tensor(
                out=s_sq[:, 0:width], in0=s_x[:, i, 0:width], in1=ps_bm[:, 0:width],
                op=ALU.subtract), after=[tb1, tb2])
            p.op("vector", lambda e, i=i, width=width: e.tensor_tensor(
                out=s_ln[:, i, 0:width], in0=s_sq[:, 0:width], in1=ps_br[:, 0:width],
                op=ALU.mult))
        return p.tok("eV")

    def inproj(width, tln, z_dst_w):
        pslast = {}
        for m in range(16):
            pb = ps0 if (m % 2 == 0) else ps1
            aft = [tln, tw1, twv] + ([pslast[id(pb)]] if id(pb) in pslast else [])
            tmm = mm_group(p, pb[:, 0:width],
                           lambda k, m=m: s_ipj[:, k, m * 128:(m + 1) * 128],
                           lambda k, width=width: s_ln[:, k, 0:width], 8, after=aft)
            if m < 8:
                pslast[id(pb)] = p.op("scalar", lambda e, m=m, pb=pb, width=width:
                                      e.activation(out=s_xc[:, m, 3:3 + width],
                                                   in_=pb[:, 0:width],
                                                   func=AF.Identity,
                                                   bias=s_ipb[:, m:m + 1]),
                                      after=[tmm])
            else:
                pslast[id(pb)] = p.op("scalar", lambda e, m=m, pb=pb, width=width:
                                      e.activation(out=s_z[:, m - 8, 0:width],
                                                   in_=pb[:, 0:width], func=AF.Silu,
                                                   bias=s_ipb[:, 8 + (m - 8):9 + (m - 8)]),
                                      after=[tmm])
        return p.tok("eA")

    tx = p.dma("sync", s_x[:, :, 0:3],
               rearr(r2h, "(c p) t -> p c t", p=128)[:, :, 0:3], inc="dL")
    tln = ln_block(3, tx)
    tz = inproj(3, tln, 0)
    for i in range(8):
        p.op("vector", lambda e, i=i: e.tensor_tensor(
            out=s_xc[:, i, 3:5], in0=s_xc[:, i, 3:5], in1=s_hm[:, 0:2],
            op=ALU.mult), after=[tz])
    for i in range(8):
        p.op("vector", lambda e, i=i: e.tensor_copy(
            out=s_xc[:, i, 0:3], in_=s_xc[:, i, 3:6]))
    tcar = p.tok("eV")
    p.dma("sync", rearr(o_sz, "(c p) t -> p c t", p=128)[:, :, 0:1],
          s_z[:, :, 2:3], after=[tz], inc="dS", slow_ok=True)
    p.barrier()

    for it in range(NSUB):
        o = it * T
        last = (it == NSUB - 1)
        tx = p.dma("sync", s_x[:, :, :],
                   rearr(r2h, "(c p) t -> p c t", p=128)[:, :, o + 3:o + T + 3],
                   inc="dL")
        tln = ln_block(T, tx)
        tz = inproj(T, tln, T)
        if last:
            for i in range(8):
                p.op("vector", lambda e, i=i: e.tensor_tensor(
                    out=s_xc[:, i, T + 2:T + 3], in0=s_xc[:, i, T + 2:T + 3],
                    in1=s_hm[:, 2:3], op=ALU.mult), after=[tz])
        tmask = p.tok("eV")
        w = T if not last else T - 1
        p.dma("sync", rearr(o_sz, "(c p) t -> p c t", p=128)[:, :, o + 1:o + 1 + w],
              s_z[:, :, 0:w], after=[tz], inc="dS")
        for i in range(8):
            p.op("vector", lambda e, i=i: e.tensor_scalar(
                out=s_xs[:, i, :], in0=s_xc[:, i, 0:T], scalar1=s_cw[:, i, 0:1],
                scalar2=None, op0=ALU.mult), after=[tz, tmask, twv])
            for j in (1, 2, 3):
                p.op("vector", lambda e, i=i, j=j: e.scalar_tensor_tensor(
                    out=s_xs[:, i, :], in0=s_xc[:, i, j:j + T],
                    scalar=s_cw[:, i, j:j + 1], in1=s_xs[:, i, :],
                    op0=ALU.mult, op1=ALU.add))
            p.op("scalar", lambda e, i=i: e.activation(
                out=s_xs[:, i, :], in_=s_xs[:, i, :], func=AF.Silu,
                bias=s_cb[:, i:i + 1]), after=[p.tok("eV")])
            p.op("vector", lambda e, i=i: e.tensor_copy(
                out=s_xsb[:, i, :], in_=s_xs[:, i, :]), after=[p.tok("eA")])
            p.op("vector", lambda e, i=i: e.tensor_copy(
                out=s_xc[:, i, 0:3], in_=s_xc[:, i, T:T + 3]))
        txs = p.tok("eV")
        tmm = mm_group(p, ps0[0:64, 0:T],
                       lambda k: s_bcw[:, k, :],
                       lambda k: s_xsb[:, k, :], 8, after=[txs, tw3])
        tbc_ = p.op("scalar", lambda e: e.activation(
            out=s_bc64[0:64, 0:T], in_=ps0[0:64, 0:T], func=AF.Identity,
            bias=s_bcb[0:64, 0:1]), after=[tmm])
        p.dma("sync", o_bc[:, o:o + T], s_bc64[0:64, 0:T],
              after=[tbc_], inc="dS")
        pslast = {id(ps0): tbc_}
        tsp_rd = None
        for m in range(8):
            pb = ps1 if (m % 2 == 0) else ps0
            aft = [txs, tw2] + ([pslast[id(pb)]] if id(pb) in pslast else [])
            tmm = mm_group(p, pb[:, 0:T],
                           lambda k, m=m: s_dtw[:, k, m * 128:(m + 1) * 128],
                           lambda k: s_xsb[:, k, :], 8, after=aft)
            ta = p.op("scalar", lambda e, m=m, pb=pb: e.activation(
                out=s_sp[:, 0:T], in_=pb[:, 0:T], func=AF.Abs,
                bias=s_dtb[:, m:m + 1]),
                after=[tmm] + ([tsp_rd] if tsp_rd else []))
            p.op("scalar", lambda e: e.activation(
                out=s_sp[:, 0:T], in_=s_sp[:, 0:T], func=AF.Exp, scale=-1.0))
            ta2 = p.op("scalar", lambda e: e.activation(
                out=s_sp[:, 0:T], in_=s_sp[:, 0:T], func=AF.Ln, bias=1.0))
            trl = p.op("scalar", lambda e, m=m, pb=pb: e.activation(
                out=s_dt[:, m, :], in_=pb[:, 0:T], func=AF.Relu,
                bias=s_dtb[:, m:m + 1]))
            pslast[id(pb)] = trl
            tsp_rd = p.op("vector", lambda e, m=m: e.tensor_tensor(
                out=s_dt[:, m, :], in0=s_dt[:, m, :], in1=s_sp[:, 0:T],
                op=ALU.add), after=[trl, ta2])
            p.op("vector", lambda e, m=m: e.tensor_scalar(
                out=s_dt[:, m, :], in0=s_dt[:, m, :], scalar1=0.2,
                scalar2=0.001, op0=ALU.min, op1=ALU.max))
            p.op("vector", lambda e, m=m: e.tensor_tensor(
                out=s_dx[:, m, :], in0=s_dt[:, m, :], in1=s_xs[:, m, :],
                op=ALU.mult))
        tdt = p.tok("eV")
        p.dma("sync", rearr(o_dt, "(c p) t -> p c t", p=128)[:, :, o:o + T],
              s_dt[:, :, :], after=[tdt], inc="dS")
        p.dma("sync", rearr(o_dx, "(c p) t -> p c t", p=128)[:, :, o:o + T],
              s_dx[:, :, :], after=[tdt], inc="dS")
        p.barrier()
    return p.finalize()


def build_L4():
    p = Prog()
    TC = 1024
    NCH = S // TC

    dtd = p.din("dtd", [DQ, S])
    dxd = p.din("dxd", [DQ, S])
    bnm = p.din("bnm", [NS, S])
    cnm = p.din("cnm", [NS, S])
    alog = p.din("alog", [DQ, NS])
    identb = p.din("identb", [128, 128], BF16)
    onesb = p.din("onesb", [1, 128], BF16)
    o_y = p.dout("ydm", [DQ, S])

    s_A = p.sbuf("s_A", [128, 2, NS])
    s_idb = p.sbuf("s_idb", [128, 128], BF16)
    s_onb = p.sbuf("s_onb", [1, 128], BF16)
    s_dt = p.sbuf("s_dt", [128, 2, TC])
    s_dx = p.sbuf("s_dx", [128, 2, TC])
    s_bn = p.sbuf("s_bn", [1, NS, TC], BF16)
    s_cn = p.sbuf("s_cn", [1, NS, TC], BF16)
    s_ab = p.sbuf("s_ab", [128, 2, 2, TC])
    s_u = p.sbuf("s_u", [128, 2, TC])
    s_s = p.sbuf("s_s", [128, 2, 2, TC])
    s_pr = p.sbuf("s_pr", [128, 2, 2, TC], BF16)
    s_cy = p.sbuf("s_cy", [128, 64])
    s_y = p.sbuf("s_y", [128, TC])
    sbC = p.sbuf("sbC", [128, 2, TC], BF16)
    psB = p.psum("psB", [128, TC])
    psC = p.psum("psC", [128, TC])
    psY0 = p.psum("psY0", [128, TC])
    psY1 = p.psum("psY1", [128, TC])
    psYs = (psY0, psY1)

    tcst = common_consts(p)
    tidb = p.dma("sync", s_idb[:, :], identb.ap(), inc="dW")
    tonb = p.dma("sync", s_onb[:, :], onesb.ap(), inc="dW")
    tA = p.dma("sync", s_A[:, :, :], rearr(alog, "(c p) n -> p c n", p=128), inc="dW")
    t1 = p.op("vector", lambda e: e.tensor_scalar(
        out=s_A[:, :, :], in0=s_A[:, :, :], scalar1=2.0, scalar2=-5.0,
        op0=ALU.min, op1=ALU.max), after=[tA])
    t2 = p.op("scalar", lambda e: e.activation(
        out=s_A[:, :, :], in_=s_A[:, :, :], func=AF.Exp), after=[t1])
    tAr = p.op("vector", lambda e: e.tensor_single_scalar(
        out=s_A[:, :, :], in_=s_A[:, :, :], scalar=-1.0, op=ALU.mult), after=[t2])
    tcy = p.op("vector", lambda e: e.memset(s_cy[:, :], 0.0))

    def bcastB(n, extra):
        t = None
        for h in range(2):
            t = p.op("tensor", lambda e, n=n, h=h: e.matmul(
                out=psB[:, h * 512:(h + 1) * 512], lhsT=s_onb[0:1, :],
                rhs=s_bn[0:1, n, h * 512:(h + 1) * 512],
                start=True, stop=True), after=extra, inc="eT")
        return t

    def bcastC(n, extra):
        t = None
        for h in range(2):
            t = p.op("tensor", lambda e, n=n, h=h: e.matmul(
                out=psC[:, h * 512:(h + 1) * 512], lhsT=s_onb[0:1, :],
                rhs=s_cn[0:1, n, h * 512:(h + 1) * 512],
                start=True, stop=True), after=extra, inc="eT")
        return t

    for it in range(NCH):
        o = it * TC
        p.dma("sync", s_dt[:, :, :],
              rearr(dtd, "(c p) t -> p c t", p=128)[:, :, o:o + TC], inc="dL")
        p.dma("sync", s_dx[:, :, :],
              rearr(dxd, "(c p) t -> p c t", p=128)[:, :, o:o + TC], inc="dL")
        p.dma("gpsimd", s_bn[0:1, :, :], bnm[0:NS, o:o + TC], inc="dG")
        tldG = p.dma("gpsimd", s_cn[0:1, :, :], cnm[0:NS, o:o + TC], inc="dG")
        tld = p.tok("dL")
        tbcB = bcastB(0, [tldG, tonb, tcst])
        tbcC = bcastC(0, [tldG])
        scan_rd = {}
        s_rd = {}
        pr_rd = {}
        sbc_rd = {}
        for n in range(NS):
            par = n % 2
            ceng = "vector" if par == 0 else "gpsimd"
            tabs = []
            for c in range(2):
                tabs.append(p.op("scalar", lambda e, c=c, n=n, par=par:
                    e.activation(out=s_ab[:, par, c, :], in_=s_dt[:, c, :],
                                 func=AF.Exp, scale=s_A[:, c, n:n + 1]),
                    after=[tld, tAr] + scan_rd.get((par, c), [])))
            tu = {}
            tcl = {}
            for c in range(2):
                tu[c] = p.op("vector", lambda e, c=c: e.tensor_tensor(
                    out=s_u[:, c, :], in0=s_dx[:, c, :], in1=psB[:, 0:TC],
                    op=ALU.mult),
                    after=[tbcB, tld] + scan_rd.get((par, c), []))
                ts = p.op("vector", lambda e, c=c, n=n, par=par: e.tensor_tensor_scan(
                    out=s_s[:, par, c, :], data0=s_ab[:, par, c, :],
                    data1=s_u[:, c, :],
                    initial=s_cy[:, c * NS + n:c * NS + n + 1],
                    op0=ALU.mult, op1=ALU.add),
                    after=[tabs[c], tu[c], tcy] + s_rd.get((par, c), []))
                scan_rd[(par, c)] = [ts]
                tcl[c] = p.op(ceng, lambda e, c=c, par=par: e.tensor_scalar(
                    out=s_s[:, par, c, :], in0=s_s[:, par, c, :], scalar1=10.0,
                    scalar2=-10.0, op0=ALU.min, op1=ALU.max), after=[ts])
            if n + 1 < NS:
                tbcB = bcastB(n + 1, [tu[0], tu[1]])
            tCcp = p.op("scalar", lambda e, par=par: e.copy(
                out=sbC[:, par, :], in_=psC[:, 0:TC]),
                after=[tbcC] + sbc_rd.get(par, []))
            if n + 1 < NS:
                tbcC = bcastC(n + 1, [tCcp])
            tprs = []
            for c in range(2):
                tcar = p.op("scalar", lambda e, c=c, n=n, par=par: e.copy(
                    out=s_cy[:, c * NS + n:c * NS + n + 1],
                    in_=s_s[:, par, c, TC - 1:TC]), after=[tcl[c]])
                tpr = p.op("gpsimd", lambda e, c=c, par=par: e.tensor_tensor(
                    out=s_pr[:, par, c, :], in0=s_s[:, par, c, :],
                    in1=sbC[:, par, :], op=ALU.mult),
                    after=[tcl[c], tCcp] + pr_rd.get((par, c), []))
                tprs.append(tpr)
                s_rd[(par, c)] = [tpr, tcar]
                tmm = None
                for h in range(2):
                    tmm = p.op("tensor", lambda e, c=c, n=n, h=h, par=par: e.matmul(
                        out=psYs[c][:, h * 512:(h + 1) * 512], lhsT=s_idb[:, :],
                        rhs=s_pr[:, par, c, h * 512:(h + 1) * 512],
                        start=(n == 0), stop=(n == NS - 1)),
                        after=[tpr, tidb], inc="eT")
                pr_rd[(par, c)] = [tmm]
            sbc_rd[par] = tprs
        mm_done = pr_rd[(0, 0)] + pr_rd[(0, 1)] + pr_rd[(1, 0)] + pr_rd[(1, 1)]
        tst = None
        for c in range(2):
            tev = p.op("vector", lambda e, c=c: e.tensor_copy(
                out=s_y[:, :], in_=psYs[c][:, 0:TC]),
                after=mm_done + ([tst] if tst else []))
            tst = p.dma("sync",
                        rearr(o_y, "(c p) t -> p c t", p=128)[:, c, o:o + TC],
                        s_y[:, :], after=[tev], inc="dS")
        p.barrier()
    return p.finalize()


def build_L5():
    p = Prog()
    T = 512
    NSUB = SQ // T

    y_in = p.din("yfm", [D, SQ])
    r2_in = p.din("r2", [D, SQ])
    sz_in = p.din("siluz", [D, SQ], BF16)
    owT = p.din("owT", [D, D])
    ob = p.din("ob", [128, 8])
    lng = p.din("lng", [128, 8])
    lnb = p.din("lnb", [128, 8])
    fwT = p.din("fwT", [D, D])
    o_out = p.dout("outfm", [D, SQ])

    s_ow = p.sbuf("s_ow", [128, 8, D], BF16)
    s_fw = p.sbuf("s_fw", [128, 8, D], BF16)
    s_stg = p.sbuf("s_stg", [128, D])
    s_ob = p.sbuf("s_ob", [128, 8])
    s_g = p.sbuf("s_g", [128, 8])
    s_bt = p.sbuf("s_bt", [128, 8])
    tcst = common_consts(p)
    s_y = p.sbuf("s_y", [128, 8, T])
    s_r2 = p.sbuf("s_r2", [128, 8, T])
    s_sz = p.sbuf("s_sz", [128, 8, T], BF16)
    s_g1 = p.sbuf("s_g1", [128, 8, T], BF16)
    s_o0 = p.sbuf("s_o0", [128, 8, T])
    s_l2 = p.sbuf("s_l2", [128, 8, T])
    s_r3 = p.sbuf("s_r3", [128, 8, T])
    s_l3 = p.sbuf("s_l3", [128, 8, T], BF16)
    s_sq = p.sbuf("s_sq", [128, T])
    s_row = p.sbuf("s_rowL5", [1, 2 * T])
    psS = p.psum("psS", [1, T])
    psQ = p.psum("psQ", [1, T])
    ps_bm = p.psum("ps_bm", [128, T])
    ps_br = p.psum("ps_br", [128, T])
    ps0 = p.psum("ps0", [128, T])
    ps1 = p.psum("ps1", [128, T])

    tow = p.dma("gpsimd", s_ow[:, :, :], rearr(owT, "(c p) f -> p c f", p=128), inc="dG")
    tv = p.dma("sync", s_ob[:, :], ob.ap(), inc="dW")
    tv = p.dma("sync", s_g[:, :], lng.ap(), inc="dW")
    tv = p.dma("sync", s_bt[:, :], lnb.ap(), inc="dW")
    twv = p.tok("dW")
    tsg_prev = None
    for c in range(8):
        tld = p.dma("sync", s_stg[:, :],
                    rearr(fwT, "(c p) f -> p c f", p=128)[:, c, :],
                    after=([tsg_prev] if tsg_prev else []), inc="dL")
        tsg_prev = p.op("scalar", lambda e, c=c: e.activation(
            out=s_fw[:, c, :], in_=s_stg[:, :], func=AF.Sign), after=[tld])
    tfw = tsg_prev
    p.barrier()

    def ln_stats_apply(src, dst, tafter, dst_bf=False, gain=False):
        trs = emit_ln_stats(p, lambda i: src[:, i, :], 8, T, psS, psQ, s_sq,
                            s_row[0:1, 0:T], s_row[0:1, T:2 * T],
                            tafter + [tcst], D, EPS)
        tb1 = emit_bcast(p, s_row[0:1, 0:T], ps_bm[:, 0:T], [trs])
        tb2 = emit_bcast(p, s_row[0:1, T:2 * T], ps_br[:, 0:T], [trs])
        for i in range(8):
            p.op("vector", lambda e, i=i: e.tensor_tensor(
                out=s_sq[:, 0:T], in0=src[:, i, :], in1=ps_bm[:, 0:T],
                op=ALU.subtract), after=[tb1, tb2, ("eA", p.cnt["eA"])])
            if gain:
                p.op("vector", lambda e, i=i: e.tensor_tensor(
                    out=s_sq[:, 0:T], in0=s_sq[:, 0:T], in1=ps_br[:, 0:T],
                    op=ALU.mult))
                p.op("scalar", lambda e, i=i: e.activation(
                    out=dst[:, i, :], in_=s_sq[:, 0:T], func=AF.Identity,
                    scale=s_g[:, i:i + 1], bias=s_bt[:, i:i + 1]),
                    after=[p.tok("eV"), twv])
            else:
                p.op("vector", lambda e, i=i: e.tensor_tensor(
                    out=dst[:, i, :], in0=s_sq[:, 0:T], in1=ps_br[:, 0:T],
                    op=ALU.mult))
        return p.tok("eA") if gain else p.tok("eV")

    for it in range(NSUB):
        o = it * T
        p.dma("sync", s_y[:, :, :],
              rearr(y_in, "(c p) t -> p c t", p=128)[:, :, o:o + T], inc="dL")
        p.dma("sync", s_r2[:, :, :],
              rearr(r2_in, "(c p) t -> p c t", p=128)[:, :, o:o + T], inc="dL")
        tld = p.dma("sync", s_sz[:, :, :],
                    rearr(sz_in, "(c p) t -> p c t", p=128)[:, :, o:o + T], inc="dL")
        for i in range(8):
            p.op("vector", lambda e, i=i: e.tensor_tensor(
                out=s_g1[:, i, :], in0=s_y[:, i, :], in1=s_sz[:, i, :],
                op=ALU.mult), after=[tld])
        tg1 = p.tok("eV")
        tl2 = ln_stats_apply(s_r2, s_l2, [tld])
        pslast = {}
        for m in range(8):
            pb = ps0 if (m % 2 == 0) else ps1
            aft = [tg1, tow] + ([pslast[id(pb)]] if id(pb) in pslast else [])
            tmm = mm_group(p, pb[:, 0:T],
                           lambda k, m=m: s_ow[:, k, m * 128:(m + 1) * 128],
                           lambda k: s_g1[:, k, :], 8, after=aft)
            ta = p.op("scalar", lambda e, m=m, pb=pb: e.activation(
                out=s_o0[:, m, :], in_=pb[:, 0:T], func=AF.Identity,
                bias=s_ob[:, m:m + 1]), after=[tmm, twv])
            pslast[id(pb)] = ta
            p.op("vector", lambda e, m=m: e.tensor_tensor(
                out=s_o0[:, m, :], in0=s_o0[:, m, :], in1=s_l2[:, m, :],
                op=ALU.add), after=[ta, tl2])
        to0 = p.tok("eV")
        tat = ln_stats_apply(s_o0, s_l2, [to0], gain=True)
        for i in range(8):
            p.op("vector", lambda e, i=i: e.tensor_tensor(
                out=s_r3[:, i, :], in0=s_r2[:, i, :], in1=s_l2[:, i, :],
                op=ALU.add), after=[tat])
        tr3 = p.tok("eV")
        tl3 = ln_stats_apply(s_r3, s_l3, [tr3])
        pslast = {}
        for m in range(8):
            pb = ps0 if (m % 2 == 0) else ps1
            aft = [tl3, tfw] + ([pslast[id(pb)]] if id(pb) in pslast else [])
            tmm = mm_group(p, pb[:, 0:T],
                           lambda k, m=m: s_fw[:, k, m * 128:(m + 1) * 128],
                           lambda k: s_l3[:, k, :], 8, after=aft)
            pslast[id(pb)] = p.op("vector", lambda e, m=m, pb=pb: e.tensor_tensor(
                out=s_o0[:, m, :], in0=pb[:, 0:T], in1=s_r3[:, m, :],
                op=ALU.add), after=[tmm])
        tout = p.tok("eV")
        p.dma("sync", rearr(o_out, "(c p) t -> p c t", p=128)[:, :, o:o + T],
              s_o0[:, :, :], after=[tout], inc="dS")
        p.barrier()
    return p.finalize()


def vec_pc(v, nch):
    return np.ascontiguousarray(np.asarray(v, np.float32).reshape(nch, 128).T)


def _core_slices():
    return [(c // 4, (c % 4) * SQ, (c % 4 + 1) * SQ) for c in range(NCORES)]


_PROG_CACHE = {}


def _prog(name, builder):
    if name not in _PROG_CACHE:
        _PROG_CACHE[name] = builder()
    return _PROG_CACHE[name]


def _run(name, builder, maps):
    nc = _prog(name, builder)
    last = None
    for attempt in range(3):
        try:
            return run_bass_kernel_spmd(nc, maps,
                                        core_ids=list(range(NCORES))).results
        except Exception as ex:
            last = ex
            if "UNAVAILABLE" not in str(ex) and "UNRECOVERABLE" not in str(ex):
                raise
            import time as _t
            _t.sleep(10)
    raise last


def kernel(**inputs):
    f = lambda k: np.asarray(inputs[k], np.float32)
    cc = np.ascontiguousarray

    x = f("L0_attn_x")
    x_fm = [cc(x[b].T) for b in range(B)]

    wgT = cc(f("L0_attn_Wg").T)
    wiT = cc(f("L0_attn_Wi").T)
    cw0 = cc(f("L0_attn_conv_w")[:, 0, :].reshape(12, 128, 4)
             .transpose(1, 0, 2).reshape(128, 48))
    cb0 = vec_pc(f("L0_attn_conv_b"), 12)
    wliT = cc(f("L0_attn_lru_Wi").T)
    wlgT = cc(f("L0_attn_lru_Wg").T)
    lbi = vec_pc(f("L0_attn_lru_bi"), 12)
    lbg = vec_pc(f("L0_attn_lru_bg"), 12)
    lam = vec_pc(f("L0_attn_lam"), 12)
    woT = cc(f("L0_attn_Wo").T)
    w1T = cc(f("L0_ffn_w1").T)
    b1 = vec_pc(f("L0_ffn_b1"), 32)
    w2T = cc(f("L0_ffn_w2").T)
    b2 = vec_pc(f("L0_ffn_b2"), 8)
    ipjT = cc(f("L1_attn_in_proj_w").T)
    ipb = vec_pc(f("L1_attn_in_proj_b"), 16)
    cw1 = cc(f("L1_attn_conv_w")[:, 0, :].reshape(8, 128, 4)
             .transpose(1, 0, 2).reshape(128, 32))
    cb1 = vec_pc(f("L1_attn_conv_b"), 8)
    alog = f("L1_attn_A_log")
    bcwT = cc(np.concatenate([f("L1_attn_B_w"), f("L1_attn_C_w")], 0).T)
    bcb = cc(np.concatenate([f("L1_attn_B_b"), f("L1_attn_C_b")])[:, None])
    dtwT = cc(f("L1_attn_dt_w").T)
    dtb = vec_pc(f("L1_attn_dt_b"), 8)
    owT = cc(f("L1_attn_out_w").T)
    ob = vec_pc(f("L1_attn_out_b"), 8)
    lng = vec_pc(f("L1_attn_ln_g"), 8)
    lnb = vec_pc(f("L1_attn_ln_beta"), 8)
    fwT = cc(f("L1_ffn_weight").T)
    ident = np.eye(128, dtype=np.float32)
    identb = np.eye(128, dtype=BFNP)

    cs = _core_slices()

    maps = []
    for (b, lo, hi) in cs:
        xs = np.zeros((D, SQ + 3), np.float32)
        xs[:, 3:] = x_fm[b][:, lo:hi]
        if lo > 0:
            xs[:, 0:3] = x_fm[b][:, lo - 3:lo]
        maps.append({"x": xs, "wgT": wgT, "wiT": wiT, "cw": cw0, "cb": cb0})
    rA = _run("L1a", build_L1a, maps)

    maps = [{"ybf": rA[c]["ybf"], "wliT": wliT, "wlgT": wlgT,
             "lbi": lbi, "lbg": lbg, "lam": lam} for c in range(NCORES)]
    rB = _run("L1b", build_L1b, maps)

    maps = []
    for c, (b, lo, hi) in enumerate(cs):
        q = c % 4
        pg = np.zeros((128, 12, 3), np.float32)
        ps = np.zeros((128, 12, 3), np.float32)
        for idx, j in enumerate(range(b * 4, b * 4 + q)):
            slot = 3 - q + idx
            pg[:, :, slot] = rB[j]["summ"][:, 0].reshape(12, 128).T
            ps[:, :, slot] = rB[j]["summ"][:, 1].reshape(12, 128).T
        maps.append({"alpha": rB[c]["alpha"], "xbeta": rB[c]["xbeta"],
                     "ggate": rA[c]["ggate"], "x": cc(x_fm[b][:, lo:hi]),
                     "pgam": pg, "psend": ps, "woT": woT})
    r2a = _run("L2a", build_L2a, maps)

    maps = [{"r1": r2a[c]["r1"], "w1T": w1T, "b1": b1, "ident": ident}
            for c in range(NCORES)]
    rb1 = _run("L2b1", build_L2b1, maps)
    maps = [{"h1": rb1[c]["h1"], "r1": r2a[c]["r1"], "w2T": w2T, "b2": b2,
             "ident": ident} for c in range(NCORES)]
    r2b = _run("L2b2", build_L2b2, maps)

    maps = []
    for c, (b, lo, hi) in enumerate(cs):
        q = c % 4
        r2h = np.zeros((D, SQ + 3), np.float32)
        r2h[:, 2:2 + SQ] = r2b[c]["r2"]
        hm = np.ones((128, 3), np.float32)
        if q > 0:
            r2h[:, 0:2] = r2b[c - 1]["r2"][:, SQ - 2:SQ]
        else:
            hm[:, 0:2] = 0.0
        if q < 3:
            r2h[:, SQ + 2:SQ + 3] = r2b[c + 1]["r2"][:, 0:1]
        else:
            hm[:, 2] = 0.0
        maps.append({"r2h": r2h, "ipjT": ipjT, "ipb": ipb, "cw1": cw1,
                     "cb1": cb1, "bcwT": bcwT, "bcb": bcb, "dtwT": dtwT,
                     "dtb": dtb, "hmask": hm})
    r3 = _run("L3", build_L3, maps)

    maps = []
    for c in range(NCORES):
        b, dq = c // 4, c % 4
        dsl = slice(dq * DQ, (dq + 1) * DQ)
        dt_b = np.concatenate([r3[b * 4 + j]["dt"] for j in range(4)], axis=1)
        dx_b = np.concatenate([r3[b * 4 + j]["dtxs"] for j in range(4)], axis=1)
        bc_b = np.concatenate([r3[b * 4 + j]["bc"] for j in range(4)], axis=1)
        maps.append({"dtd": cc(dt_b[dsl, :]), "dxd": cc(dx_b[dsl, :]),
                     "bnm": cc(bc_b[0:NS, :]), "cnm": cc(bc_b[NS:2 * NS, :]),
                     "alog": cc(alog[dsl, :]), "identb": identb,
                     "onesb": np.ones((1, 128), BFNP)})
    r4 = _run("L4", build_L4, maps)

    maps = []
    for c, (b, lo, hi) in enumerate(cs):
        y_b = np.concatenate([r4[b * 4 + dq]["ydm"] for dq in range(4)], axis=0)
        maps.append({"yfm": cc(y_b[:, lo:hi]), "r2": r2b[c]["r2"],
                     "siluz": r3[c]["siluz"], "owT": owT, "ob": ob,
                     "lng": lng, "lnb": lnb, "fwT": fwT})
    r5 = _run("L5", build_L5, maps)

    out = np.empty((B, S, D), np.float32)
    for c, (b, lo, hi) in enumerate(cs):
        out[b, lo:hi, :] = r5[c]["outfm"].T
    return out


# version 05
# speedup vs baseline: 1.4581x; 1.0847x over previous
import contextlib
import numpy as np
import ml_dtypes

import concourse.bass as bass
import concourse.mybir as mybir
from concourse.bass_utils import run_bass_kernel_spmd

F32 = mybir.dt.float32
BF16 = mybir.dt.bfloat16
I32 = mybir.dt.int32
AF = mybir.ActivationFunctionType
ALU = mybir.AluOpType
AX = mybir.AxisListType
BFNP = ml_dtypes.bfloat16

NCORES = 8
B, S, D = 2, 16384, 1024
H = 1536
FF = 4096
NS = 32
SQ = S // 4
DQ = D // 4
EPS = 1e-5
ERFS = 0.7071067811865476


class Prog:
    ENGS = ("sync", "tensor", "vector", "scalar", "gpsimd")
    SELF = {"vector": "eV", "scalar": "eA", "gpsimd": "eG"}

    def __init__(self):
        self.nc = bass.Bass("TRN2", target_bir_lowering=False)
        self.es = contextlib.ExitStack()
        self.es.__enter__()
        self.q = {e: [] for e in self.ENGS}
        self.sems = {}
        self.cnt = {}
        self.lastw = {e: {} for e in self.ENGS}
        for s in ("dL", "dW", "dS", "dG", "eT", "eV", "eA", "eG", "eS"):
            self.sems[s] = self.es.enter_context(self.nc.semaphore(s))
            self.cnt[s] = 0
        nums = [h.num for h in self.sems.values()]
        rng = range(min(nums), max(nums) + 1)
        self.q["gpsimd"].append(lambda e, r=rng: e.dma_reset(r))
        self.q["gpsimd"].append(lambda e, r=rng: e.sem_clear(r))
        self.q["gpsimd"].append(
            lambda e: e.nop().then_inc(self.sems["eG"], 1))
        self.cnt["eG"] = 1
        self._started = {e: (e == "gpsimd") for e in self.ENGS}
        self.pending = {e: {} for e in self.ENGS}

    def din(self, name, shape, dt=F32):
        return self.nc.dram_tensor(name, list(shape), dt, kind="ExternalInput")

    def dout(self, name, shape, dt=F32):
        return self.nc.dram_tensor(name, list(shape), dt, kind="ExternalOutput")

    def sbuf(self, name, shape, dt=F32):
        return self.es.enter_context(self.nc.sbuf_tensor(name, list(shape), dt))

    def psum(self, name, shape, dt=F32):
        return self.es.enter_context(self.nc.psum_tensor(name, list(shape), dt))

    def tok(self, sem):
        return (sem, self.cnt[sem])

    def _wait(self, eng, sem, val):
        if val <= 0:
            return
        if self.lastw[eng].get(sem, 0) >= val:
            return
        self.lastw[eng][sem] = val
        self.q[eng].append(lambda e, s=self.sems[sem], v=val: e.wait_ge(s, v))

    def op(self, eng, fn, after=(), inc=None, dma=False):
        if not self._started[eng]:
            self._started[eng] = True
            self._wait(eng, "eG", 1)
        ss = None if dma else self.SELF.get(eng)
        afters = list(after)
        if self.pending[eng]:
            afters.extend(self.pending[eng].items())
            self.pending[eng] = {}
        if ss:
            afters.append((ss, self.cnt[ss]))
        merged = {}
        for sem, val in afters:
            merged[sem] = max(merged.get(sem, 0), val)
        for sem, val in merged.items():
            self._wait(eng, sem, val)
        incs = []
        if ss:
            self.cnt[ss] += 1
            incs.append((ss, 1))
        if inc:
            amt = 16 if dma else 1
            self.cnt[inc] += amt
            incs.append((inc, amt))

        def run(e, fn=fn, incs=tuple(incs)):
            inst = fn(e)
            for s, amt in incs:
                inst = inst.then_inc(self.sems[s], amt)
        self.q[eng].append(run)
        if inc:
            return self.tok(inc)
        if ss:
            return self.tok(ss)
        return None

    def dma(self, eng, out, in_, after=(), inc="dL", slow_ok=False):
        after = list(after)
        after.append((inc, self.cnt[inc]))
        if slow_ok:
            return self.op(eng, lambda e: e.dma_start(
                out=out, in_=in_, allow_slow_non_contiguous=True),
                after=after, inc=inc, dma=True)
        return self.op(eng, lambda e: e.dma_start(out=out, in_=in_),
                       after=after, inc=inc, dma=True)

    def barrier(self):
        for e in self.ENGS:
            for s in ("dL", "dW", "dS", "dG", "eT", "eV", "eA", "eG"):
                self.pending[e][s] = max(self.pending[e].get(s, 0), self.cnt[s])

    def finalize(self, final_eng=None):
        self._wait("sync", "dS", self.cnt["dS"])
        blk = self.nc.Block()
        blk.__enter__()
        for e in self.ENGS:
            if self.q[e]:
                getattr(blk, e)(lambda eng, lst=tuple(self.q[e]): [f(eng) for f in lst])
        blk.__exit__(None, None, None)
        nc = self.nc
        self.es.__exit__(None, None, None)
        nc.finalize()
        return nc


def rearr(t, pat, **kw):
    return t.ap().rearrange(pat, **kw)


def mm_group(p, out_ap, lhsT_fn, rhs_fn, K, after=()):
    for k in range(K):
        p.op("tensor",
             lambda e, k=k: e.matmul(out=out_ap, lhsT=lhsT_fn(k), rhs=rhs_fn(k),
                                     start=(k == 0), stop=(k == K - 1)),
             after=(after if k == 0 else ()), inc="eT" if k == K - 1 else None)
    return p.tok("eT")


def emit_ln_stats(p, x_fn, nch, T, psS, psQ, sq, row_mean, row_rstd, after,
                  denom, eps, row_var=None, sq2=None):
    for i in range(nch):
        p.op("tensor", lambda e, i=i: e.matmul(
            out=psS[0:1, 0:T], lhsT=p.ones_col[:, 0:1], rhs=x_fn(i),
            start=(i == 0), stop=(i == nch - 1)), after=after,
            inc="eT" if i == nch - 1 else None)
    tS = p.tok("eT")
    mm_by_buf = {}
    for i in range(nch):
        buf = sq if (sq2 is None or i % 2 == 0) else sq2
        aft = list(after) + ([mm_by_buf[id(buf)]] if id(buf) in mm_by_buf else [])
        tq = p.op("gpsimd", lambda e, i=i, buf=buf: e.tensor_tensor(
            out=buf[:, 0:T], in0=x_fn(i), in1=x_fn(i), op=ALU.mult), after=aft)
        mm_by_buf[id(buf)] = p.op("tensor", lambda e, i=i, buf=buf: e.matmul(
            out=psQ[0:1, 0:T], lhsT=p.ones_col[:, 0:1], rhs=buf[:, 0:T],
            start=(i == 0), stop=(i == nch - 1)), after=[tq], inc="eT")
    tQ = p.tok("eT")
    t1 = p.op("vector", lambda e: e.tensor_single_scalar(
        out=row_mean, in_=psS[0:1, 0:T], scalar=1.0 / denom, op=ALU.mult),
        after=[tS])
    p.op("vector", lambda e: e.tensor_tensor(
        out=row_rstd, in0=row_mean, in1=row_mean, op=ALU.mult))
    p.op("vector", lambda e: e.scalar_tensor_tensor(
        out=row_rstd, in0=psQ[0:1, 0:T], scalar=1.0 / denom, in1=row_rstd,
        op0=ALU.mult, op1=ALU.subtract), after=[tQ])
    if row_var is not None:
        p.op("vector", lambda e: e.tensor_copy(out=row_var, in_=row_rstd))
    p.op("vector", lambda e: e.tensor_single_scalar(
        out=row_rstd, in_=row_rstd, scalar=eps, op=ALU.add))
    t4 = p.op("scalar", lambda e: e.activation(
        out=row_rstd, in_=row_rstd, func=AF.Ln),
        after=[p.tok("eV")])
    t5 = p.op("scalar", lambda e: e.activation(
        out=row_rstd, in_=row_rstd, func=AF.Exp, scale=-0.5))
    return t5


def emit_bcast(p, row_ap, ps_out_ap, after):
    return p.op("tensor", lambda e: e.matmul(
        out=ps_out_ap, lhsT=p.ones_row[0:1, :], rhs=row_ap,
        start=True, stop=True), after=after, inc="eT")


def common_consts(p):
    p.ones_col = p.sbuf("ones_col", [128, 1])
    p.ones_row = p.sbuf("ones_row", [1, 128])
    p.op("gpsimd", lambda e: e.memset(p.ones_col[:, :], 1.0))
    t = p.op("gpsimd", lambda e: e.memset(p.ones_row[:, :], 1.0))
    return t


def build_L1a():
    p = Prog()
    T = 512
    NSUB = SQ // T

    xin = p.din("x", [D, SQ + 3])
    wgT = p.din("wgT", [D, H])
    wiT = p.din("wiT", [D, H])
    cw = p.din("cw", [128, 48])
    cb = p.din("cb", [128, 12])
    o_gg = p.dout("ggate", [H, SQ], BF16)
    o_yb = p.dout("ybf", [H, SQ], BF16)

    s_wg = p.sbuf("s_wg", [128, 8, H], BF16)
    s_wi = p.sbuf("s_wi", [128, 8, H], BF16)
    s_cw = p.sbuf("s_cw", [128, 12, 4])
    s_cb = p.sbuf("s_cb", [128, 12])
    tcst = common_consts(p)
    s_x = p.sbuf("s_x", [128, 8, T])
    s_ln = p.sbuf("s_ln", [128, 8, T], BF16)
    s_sq = p.sbuf("s_sq", [128, T])
    s_row = p.sbuf("s_row", [1, 2 * T])
    s_u = p.sbuf("s_u", [128, 12, T + 3])
    SA = p.sbuf("SA", [128, 12, T])
    SB = p.sbuf("SB", [128, 12, T])
    s_gg = p.sbuf("s_gg", [128, 12, T], BF16)
    s_ybf = p.sbuf("s_ybf", [128, 12, T], BF16)
    psS = p.psum("psS", [1, T])
    psQ = p.psum("psQ", [1, T])
    ps_bm = p.psum("ps_bm", [128, T])
    ps_br = p.psum("ps_br", [128, T])
    ps_m0 = p.psum("ps_m0", [128, T])
    ps_m1 = p.psum("ps_m1", [128, T])
    ps_u3 = p.psum("ps_u3", [128, 4])

    twg = p.dma("gpsimd", s_wg[:, :, :], rearr(wgT, "(c p) h -> p c h", p=128), inc="dG")
    twi = p.dma("gpsimd", s_wi[:, :, :], rearr(wiT, "(c p) h -> p c h", p=128), inc="dG")
    tcw = p.dma("sync", s_cw[:, :, :], rearr(cw, "p (c k) -> p c k", k=4), inc="dW")
    tcb = p.dma("sync", s_cb[:, :], cb[:, :], inc="dW")

    def ln_block(width, tx):
        trs = emit_ln_stats(p, lambda i: s_x[:, i, 0:width], 8, width, psS, psQ,
                            s_sq, s_row[0:1, 0:width], s_row[0:1, T:T + width],
                            [tx, tcst], D, EPS, sq2=SA[:, 0, :])
        tb1 = emit_bcast(p, s_row[0:1, 0:width], ps_bm[:, 0:width], [trs])
        tb2 = emit_bcast(p, s_row[0:1, T:T + width], ps_br[:, 0:width], [trs])
        for i in range(8):
            p.op("vector", lambda e, i=i: e.tensor_tensor(
                out=s_sq[:, 0:width], in0=s_x[:, i, 0:width], in1=ps_bm[:, 0:width],
                op=ALU.subtract), after=[tb1, tb2])
            p.op("vector", lambda e, i=i: e.tensor_tensor(
                out=s_ln[:, i, 0:width], in0=s_sq[:, 0:width], in1=ps_br[:, 0:width],
                op=ALU.mult))
        return p.tok("eV")

    tx = p.dma("sync", s_x[:, :, 0:3],
               rearr(xin, "(c p) t -> p c t", p=128)[:, :, 0:3], inc="dL")
    tln = ln_block(3, tx)
    tcp = None
    for m in range(12):
        tmm = mm_group(p, ps_u3[:, 0:3],
                       lambda k, m=m: s_wi[:, k, m * 128:(m + 1) * 128],
                       lambda k: s_ln[:, k, 0:3], 8,
                       after=[tln, twi] + ([tcp] if tcp else []))
        tcp = p.op("vector", lambda e, m=m: e.tensor_copy(
            out=s_u[:, m, 0:3], in_=ps_u3[:, 0:3]), after=[tmm])
    p.barrier()

    for it in range(NSUB):
        o = it * T
        tx = p.dma("sync", s_x[:, :, :],
                   rearr(xin, "(c p) t -> p c t", p=128)[:, :, 3 + o:3 + o + T],
                   inc="dL")
        tln = ln_block(T, tx)

        evac = {}
        for m in range(12):
            pb = ps_m0 if (m % 2 == 0) else ps_m1
            aft = [tln, twg] + ([evac[m - 2]] if m - 2 in evac else [])
            tg = mm_group(p, pb[:, 0:T],
                          lambda k, m=m: s_wg[:, k, m * 128:(m + 1) * 128],
                          lambda k: s_ln[:, k, 0:T], 8, after=aft)
            evac[m] = p.op("scalar", lambda e, m=m, pb=pb: e.activation(
                out=s_gg[:, m, :], in_=pb[:, 0:T], func=AF.Gelu), after=[tg])
        tgg = p.tok("eA")
        p.dma("sync", rearr(o_gg, "(c p) t -> p c t", p=128)[:, :, o:o + T],
              s_gg[:, :, :], after=[tgg], inc="dS")

        evac = {}
        for m in range(12):
            pb = ps_m0 if (m % 2 == 0) else ps_m1
            aft = [tln, twi, tgg] + ([evac[m - 2]] if m - 2 in evac else [])
            tu = mm_group(p, pb[:, 0:T],
                          lambda k, m=m: s_wi[:, k, m * 128:(m + 1) * 128],
                          lambda k: s_ln[:, k, 0:T], 8, after=aft)
            tc = p.op("vector", lambda e, m=m, pb=pb: e.tensor_copy(
                out=s_u[:, m, 3:T + 3], in_=pb[:, 0:T]), after=[tu])
            evac[m] = tc
        tuall = p.tok("eV")

        for m in range(12):
            p.op("vector", lambda e, m=m: e.tensor_scalar(
                out=SA[:, m, :], in0=s_u[:, m, 0:T], scalar1=s_cw[:, m, 0:1],
                scalar2=None, op0=ALU.mult), after=[tuall, tcw])
            for j in (1, 2, 3):
                p.op("vector", lambda e, m=m, j=j: e.scalar_tensor_tensor(
                    out=SA[:, m, :], in0=s_u[:, m, j:j + T],
                    scalar=s_cw[:, m, j:j + 1], in1=SA[:, m, :],
                    op0=ALU.mult, op1=ALU.add))
            p.op("scalar", lambda e, m=m: e.activation(
                out=s_ybf[:, m, :], in_=SA[:, m, :], func=AF.Identity,
                bias=s_cb[:, m:m + 1]), after=[p.tok("eV"), tcb])
            p.op("vector", lambda e, m=m: e.tensor_copy(
                out=s_u[:, m, 0:3], in_=s_u[:, m, T:T + 3]))
        tyb = p.tok("eA")
        p.dma("sync", rearr(o_yb, "(c p) t -> p c t", p=128)[:, :, o:o + T],
              s_ybf[:, :, :], after=[tyb], inc="dS")
        p.barrier()
    return p.finalize()


def build_L1b():
    p = Prog()
    T = 512
    NSUB = SQ // T

    ybin = p.din("ybf", [H, SQ], BF16)
    wliT = p.din("wliT", [H, H])
    wlgT = p.din("wlgT", [H, H])
    lbi = p.din("lbi", [128, 12])
    lbg = p.din("lbg", [128, 12])
    lam = p.din("lam", [128, 12])
    o_al = p.dout("alpha", [H, SQ])
    o_xb = p.dout("xbeta", [H, SQ])
    o_sm = p.dout("summ", [H, 4])

    s_wli = p.sbuf("s_wli", [128, 12, H], BF16)
    s_wlg = p.sbuf("s_wlg", [128, 12, H], BF16)
    s_lbi = p.sbuf("s_lbi", [128, 12])
    s_lbg = p.sbuf("s_lbg", [128, 12])
    s_msp = p.sbuf("s_msp", [128, 12])
    s_t0 = p.sbuf("s_t0", [128, 12])
    s_yb = p.sbuf("s_yb", [128, 12, T], BF16)
    s_ig = p.sbuf("s_ig", [128, 12, T])
    s_rg = p.sbuf("s_rg", [128, 12, T])
    s_al = p.sbuf("s_al", [128, 12, T])
    s_xb = p.sbuf("s_xb", [128, 12, T])
    s_sm = p.sbuf("s_sm", [128, 12, 4])
    ps0 = p.psum("ps0", [128, T])
    ps1 = p.psum("ps1", [128, T])

    twl1 = p.dma("gpsimd", s_wli[:, :, :], rearr(wliT, "(c p) h -> p c h", p=128), inc="dG")
    twl2 = p.dma("gpsimd", s_wlg[:, :, :], rearr(wlgT, "(c p) h -> p c h", p=128), inc="dG")
    tb1 = p.dma("sync", s_lbi[:, :], lbi[:, :], inc="dW")
    tb2 = p.dma("sync", s_lbg[:, :], lbg[:, :], inc="dW")
    tlam = p.dma("sync", s_msp[:, :], lam[:, :], inc="dW")
    tabs = p.op("scalar", lambda e: e.activation(out=s_t0[:, :], in_=s_msp[:, :],
                                                 func=AF.Abs), after=[tlam])
    p.op("scalar", lambda e: e.activation(out=s_t0[:, :], in_=s_t0[:, :],
                                          func=AF.Exp, scale=-1.0))
    ta = p.op("scalar", lambda e: e.activation(out=s_t0[:, :], in_=s_t0[:, :],
                                               func=AF.Ln, bias=1.0))
    p.op("vector", lambda e: e.tensor_single_scalar(
        out=s_msp[:, :], in_=s_msp[:, :], scalar=0.0, op=ALU.max),
        after=[tlam, tabs])
    p.op("vector", lambda e: e.scalar_tensor_tensor(
        out=s_msp[:, :], in0=s_msp[:, :], scalar=1.0, in1=s_t0[:, :],
        op0=ALU.mult, op1=ALU.add), after=[ta])
    tmsp = p.op("vector", lambda e: e.tensor_single_scalar(
        out=s_msp[:, :], in_=s_msp[:, :], scalar=-8.0, op=ALU.mult))
    p.op("vector", lambda e: e.memset(s_sm[:, :, 0:1], 1.0))
    p.op("vector", lambda e: e.memset(s_sm[:, :, 1:2], 0.0))

    for it in range(NSUB):
        o = it * T
        last = (it == NSUB - 1)
        ty = p.dma("sync", s_yb[:, :, :],
                   rearr(ybin, "(c p) t -> p c t", p=128)[:, :, o:o + T], inc="dL")
        pslast = {}
        for (wt, bt, dst, wtok) in ((s_wli, s_lbi, s_ig, twl1),
                                    (s_wlg, s_lbg, s_rg, twl2)):
            for m in range(12):
                pb = ps0 if (m % 2 == 0) else ps1
                aft = [ty, wtok] + ([pslast[id(pb)]] if id(pb) in pslast else [])
                tg = mm_group(p, pb[:, 0:T],
                              lambda k, m=m, wt=wt: wt[:, k, m * 128:(m + 1) * 128],
                              lambda k: s_yb[:, k, :], 12, after=aft)
                pslast[id(pb)] = p.op("scalar", lambda e, m=m, pb=pb, dst=dst, bt=bt:
                               e.activation(out=dst[:, m, :], in_=pb[:, 0:T],
                                            func=AF.Sigmoid, bias=bt[:, m:m + 1]),
                               after=[tg, tb1, tb2])
        tsig = p.tok("eA")
        for m in range(12):
            p.op("scalar", lambda e, m=m: e.activation(
                out=s_al[:, m, :], in_=s_rg[:, m, :], func=AF.Exp,
                scale=s_msp[:, m:m + 1]), after=[tsig, tmsp])
        tal = p.tok("eA")
        p.dma("sync", rearr(o_al, "(c p) t -> p c t", p=128)[:, :, o:o + T],
              s_al[:, :, :], after=[tal], inc="dS")
        for m in range(12):
            p.op("vector", lambda e, m=m: e.tensor_tensor(
                out=s_rg[:, m, :], in0=s_al[:, m, :], in1=s_al[:, m, :],
                op=ALU.mult), after=[tal])
            t1 = p.op("vector", lambda e, m=m: e.tensor_scalar(
                out=s_rg[:, m, :], in0=s_rg[:, m, :], scalar1=-1.0,
                scalar2=1.000001, op0=ALU.mult, op1=ALU.add))
            p.op("scalar", lambda e, m=m: e.activation(
                out=s_rg[:, m, :], in_=s_rg[:, m, :], func=AF.Sqrt), after=[t1])
        tbe = p.tok("eA")
        for m in range(12):
            p.op("vector", lambda e, m=m: e.tensor_tensor(
                out=s_ig[:, m, :], in0=s_ig[:, m, :], in1=s_yb[:, m, :],
                op=ALU.mult), after=[tbe])
            p.op("vector", lambda e, m=m: e.tensor_tensor(
                out=s_xb[:, m, :], in0=s_rg[:, m, :], in1=s_ig[:, m, :],
                op=ALU.mult))
        txb = p.tok("eV")
        p.dma("sync", rearr(o_xb, "(c p) t -> p c t", p=128)[:, :, o:o + T],
              s_xb[:, :, :], after=[txb], inc="dS")
        for m in range(12):
            p.op("vector", lambda e, m=m: e.tensor_tensor_scan(
                out=s_ig[:, m, :], data0=s_al[:, m, :], data1=s_al[:, m, :],
                initial=s_sm[:, m, 0:1], op0=ALU.mult, op1=ALU.bypass),
                after=[txb])
            if last:
                p.op("vector", lambda e, m=m: e.tensor_copy(
                    out=s_sm[:, m, 2:3], in_=s_ig[:, m, T - 2:T - 1]))
            p.op("vector", lambda e, m=m: e.tensor_copy(
                out=s_sm[:, m, 0:1], in_=s_ig[:, m, T - 1:T]))
            p.op("vector", lambda e, m=m: e.tensor_tensor_scan(
                out=s_ig[:, m, :], data0=s_al[:, m, :], data1=s_xb[:, m, :],
                initial=s_sm[:, m, 1:2], op0=ALU.mult, op1=ALU.add))
            if last:
                p.op("vector", lambda e, m=m: e.tensor_copy(
                    out=s_sm[:, m, 3:4], in_=s_ig[:, m, T - 2:T - 1]))
            p.op("vector", lambda e, m=m: e.tensor_copy(
                out=s_sm[:, m, 1:2], in_=s_ig[:, m, T - 1:T]))
        p.barrier()
    tsm = p.tok("eV")
    p.dma("sync", rearr(o_sm, "(c p) k -> p c k", p=128), s_sm[:, :, :],
          after=[tsm], inc="dS")
    return p.finalize()


def build_L2a():
    p = Prog()
    T = 512
    NSUB = SQ // T

    a_in = p.din("alpha", [H, SQ])
    xb_in = p.din("xbeta", [H, SQ])
    gg_in = p.din("ggate", [H, SQ], BF16)
    x_in = p.din("x", [D, SQ])
    pgam = p.din("pgam", [128, 12, 3])
    psnd = p.din("psend", [128, 12, 3])
    woT = p.din("woT", [H, D])
    o_r1 = p.dout("r1", [D, SQ])

    s_wo = p.sbuf("s_wo", [128, 12, D], BF16)
    s_pg = p.sbuf("s_pg", [128, 12, 3])
    s_ps = p.sbuf("s_ps", [128, 12, 3])
    s_cy = p.sbuf("s_cy", [128, 12])
    s_al = p.sbuf("s_al", [128, 12, T])
    s_xb = p.sbuf("s_xb", [128, 12, T])
    s_gg = p.sbuf("s_gg", [128, 2, 12, T], BF16)
    s_h = p.sbuf("s_h", [128, 12, T])
    s_a1 = p.sbuf("s_a1", [128, 2, 12, T], BF16)
    s_x = p.sbuf("s_x", [128, 2, 8, T])
    s_r1 = p.sbuf("s_r1", [128, 8, T])
    ps0 = p.psum("ps0", [128, T])
    ps1 = p.psum("ps1", [128, T])

    two = p.dma("gpsimd", s_wo[:, :, :], rearr(woT, "(c p) d -> p c d", p=128), inc="dG")
    tg1 = p.dma("sync", s_pg[:, :, :], pgam.ap(), inc="dW")
    tg2 = p.dma("sync", s_ps[:, :, :], psnd.ap(), inc="dW")
    tc = p.op("vector", lambda e: e.memset(s_cy[:, :], 0.0), after=[tg1, tg2])
    for j in range(3):
        p.op("vector", lambda e, j=j: e.tensor_tensor(
            out=s_cy[:, :], in0=s_cy[:, :], in1=s_pg[:, :, j], op=ALU.mult))
        p.op("vector", lambda e, j=j: e.tensor_tensor(
            out=s_cy[:, :], in0=s_cy[:, :], in1=s_ps[:, :, j], op=ALU.add))
    tcy = p.tok("eV")

    rd_albx = []
    rd_gx = {}
    rd_r1 = {}
    tld_albx = {}
    tld_gx = {}
    pslast = {}

    def emit_loads(it):
        q = it % 2
        o = it * T
        p.dma("sync", s_al[:, :, :],
              rearr(a_in, "(c p) t -> p c t", p=128)[:, :, o:o + T],
              after=list(rd_albx), inc="dL")
        tld_albx[it] = p.dma("sync", s_xb[:, :, :],
              rearr(xb_in, "(c p) t -> p c t", p=128)[:, :, o:o + T], inc="dL")
        p.dma("sync", s_gg[:, q, :, :],
              rearr(gg_in, "(c p) t -> p c t", p=128)[:, :, o:o + T],
              after=rd_gx.get(q, []), inc="dL")
        tld_gx[it] = p.dma("sync", s_x[:, q, :, :],
              rearr(x_in, "(c p) t -> p c t", p=128)[:, :, o:o + T], inc="dL")

    emit_loads(0)
    for it in range(NSUB):
        q = it % 2
        o = it * T
        tldA = tld_albx.pop(it)
        tldG = tld_gx.pop(it)
        tsc = None
        for m in range(12):
            tsc = p.op("vector", lambda e, m=m: e.tensor_tensor_scan(
                out=s_h[:, m, :], data0=s_al[:, m, :], data1=s_xb[:, m, :],
                initial=s_cy[:, m:m + 1], op0=ALU.mult, op1=ALU.add),
                after=[tldA, tcy])
            p.op("vector", lambda e, m=m: e.tensor_copy(
                out=s_cy[:, m:m + 1], in_=s_h[:, m, T - 1:T]))
            p.op("vector", lambda e, m=m, q=q: e.tensor_tensor(
                out=s_a1[:, q, m, :], in0=s_gg[:, q, m, :], in1=s_h[:, m, :],
                op=ALU.mult), after=[tldG])
        ta1 = p.tok("eV")
        rd_albx[:] = [tsc]
        if it + 1 < NSUB:
            emit_loads(it + 1)
        for m in range(8):
            pb = ps0 if (m % 2 == 0) else ps1
            aft = [ta1, two] + pslast.get(id(pb), [])
            tmm = mm_group(p, pb[:, 0:T],
                           lambda k, m=m, q=q: s_wo[:, k, m * 128:(m + 1) * 128],
                           lambda k, q=q: s_a1[:, q, k, :], 12, after=aft)
            pslast[id(pb)] = [p.op("vector", lambda e, m=m, pb=pb, q=q:
                e.tensor_tensor(out=s_r1[:, m, :], in0=pb[:, 0:T],
                                in1=s_x[:, q, m, :], op=ALU.add),
                after=[tmm] + rd_r1.get(0, []))]
        tr1 = p.tok("eV")
        tst = p.dma("sync", rearr(o_r1, "(c p) t -> p c t", p=128)[:, :, o:o + T],
                    s_r1[:, :, :], after=[tr1], inc="dS")
        rd_r1[0] = [tst]
        rd_gx[q] = [tr1]
    return p.finalize()


def _emit_wquant(p, wT, qdst, nrc, ncol, wsc, c0, c1, tcst, s_stg, s_wt2,
                 psS, s_acc, s_m1, s_sc128, bcP, stgw):
    stages = []
    for c in range(nrc):
        for hcol in range(0, ncol, stgw):
            stages.append((c, hcol, min(stgw, ncol - hcol)))
    tz = p.op("vector", lambda e: e.memset(s_acc[:, :], 0.0))
    rd = {}
    tld_prev = None
    for si, (c, hcol, w) in enumerate(stages):
        half = si % 2
        stg = s_stg[:, half * stgw:half * stgw + w]
        tld = p.dma("sync", stg,
                    rearr(wT, "(c p) f -> p c f", p=128)[:, c, hcol:hcol + w],
                    after=rd.get(half, []), inc="dL")
        tred = p.op("vector", lambda e, stg=stg, half=half: e.tensor_reduce(
            out=s_m1[:, half:half + 1], in_=stg, axis=AX.X, op=ALU.add,
            apply_absolute_value=True), after=[tld, tz])
        rd[half] = [tred]
        tld_prev = p.op("vector", lambda e, half=half: e.tensor_tensor(
            out=s_acc[:, :], in0=s_acc[:, :], in1=s_m1[:, half:half + 1],
            op=ALU.add))
    tmm = p.op("tensor", lambda e: e.matmul(
        out=psS[0:1, 0:1], lhsT=p.ones_col[:, :], rhs=s_acc[:, :],
        start=True, stop=True), after=[tld_prev, tcst], inc="eT")
    n = nrc * 128 * ncol
    p.op("vector", lambda e, n=n, c1=c1: e.tensor_single_scalar(
        out=wsc[0:1, c1:c1 + 1], in_=psS[0:1, 0:1], scalar=1.0 / n,
        op=ALU.mult), after=[tmm])
    tm = p.op("vector", lambda e, c1=c1: e.tensor_single_scalar(
        out=wsc[0:1, c1:c1 + 1], in_=wsc[0:1, c1:c1 + 1], scalar=1e-5,
        op=ALU.max))
    ta = p.op("scalar", lambda e, c0=c0, c1=c1: e.activation(
        out=wsc[0:1, c0:c0 + 1], in_=wsc[0:1, c1:c1 + 1], func=AF.Ln),
        after=[tm])
    ta = p.op("scalar", lambda e, c0=c0: e.activation(
        out=wsc[0:1, c0:c0 + 1], in_=wsc[0:1, c0:c0 + 1], func=AF.Exp,
        scale=-1.0))
    tbc = p.op("tensor", lambda e, c0=c0: e.matmul(
        out=bcP[:, 0:1], lhsT=p.ones_row[0:1, :], rhs=wsc[0:1, c0:c0 + 1],
        start=True, stop=True), after=[ta], inc="eT")
    tsc = p.op("vector", lambda e: e.tensor_copy(
        out=s_sc128[:, :], in_=bcP[:, 0:1]), after=[tbc])
    tld_prev = None
    for si, (c, hcol, w) in enumerate(stages):
        half = si % 2
        stg = s_stg[:, half * stgw:half * stgw + w]
        wt2 = s_wt2[:, half * stgw:half * stgw + w]
        tld = p.dma("sync", stg,
                    rearr(wT, "(c p) f -> p c f", p=128)[:, c, hcol:hcol + w],
                    after=rd.get(half, []), inc="dL")
        p.op("vector", lambda e, stg=stg: e.tensor_scalar(
            out=stg, in0=stg, scalar1=s_sc128[:, 0:1],
            scalar2=None, op0=ALU.mult), after=[tld, tsc])
        p.op("scalar", lambda e, stg=stg, wt2=wt2: e.activation(
            out=wt2, in_=stg, func=AF.Abs),
            after=[p.tok("eV")])
        p.op("vector", lambda e, wt2=wt2: e.tensor_single_scalar(
            out=wt2, in_=wt2, scalar=0.5, op=ALU.is_ge),
            after=[p.tok("eA")])
        ts = p.op("scalar", lambda e, stg=stg: e.activation(
            out=stg, in_=stg, func=AF.Sign),
            after=[p.tok("eV")])
        tld_prev = p.op("vector", lambda e, c=c, hcol=hcol, w=w, qdst=qdst,
                        stg=stg, wt2=wt2:
                        e.tensor_tensor(out=qdst[:, c, hcol:hcol + w],
                                        in0=stg, in1=wt2,
                                        op=ALU.mult), after=[ts])
        rd[half] = [tld_prev]
    return tld_prev


def _emit_round_clip(p, s_v, s_int, s_cf, s_fx, s_fg, dst_ap, lo, hi, T, after):
    t1 = p.op("vector", lambda e: e.tensor_copy(out=s_int[:, 0:T], in_=s_v[:, 0:T]),
              after=after)
    t2 = p.op("vector", lambda e: e.tensor_copy(out=s_cf[:, 0:T], in_=s_int[:, 0:T]))
    t3 = p.op("vector", lambda e: e.scalar_tensor_tensor(
        out=s_fx[:, 0:T], in0=s_cf[:, 0:T], scalar=-1.0, in1=s_v[:, 0:T],
        op0=ALU.mult, op1=ALU.add), after=[t2])
    t4 = p.op("vector", lambda e: e.tensor_single_scalar(
        out=s_fg[:, 0:T], in_=s_fx[:, 0:T], scalar=0.5, op=ALU.is_gt),
        after=[t3])
    t5 = p.op("vector", lambda e: e.tensor_tensor(
        out=s_cf[:, 0:T], in0=s_cf[:, 0:T], in1=s_fg[:, 0:T], op=ALU.add),
        after=[t4])
    t6 = p.op("vector", lambda e: e.tensor_single_scalar(
        out=s_fg[:, 0:T], in_=s_fx[:, 0:T], scalar=-0.5, op=ALU.is_lt),
        after=[t5])
    t7 = p.op("vector", lambda e: e.tensor_tensor(
        out=s_cf[:, 0:T], in0=s_cf[:, 0:T], in1=s_fg[:, 0:T], op=ALU.subtract),
        after=[t6])
    return p.op("vector", lambda e: e.tensor_scalar(
        out=dst_ap, in0=s_cf[:, 0:T], scalar1=float(hi), scalar2=float(lo),
        op0=ALU.min, op1=ALU.max))


def build_L2b1():
    p = Prog()
    T = 512
    NSUB = SQ // T

    r1_in = p.din("r1", [D, SQ])
    w1T = p.din("w1T", [D, FF])
    b1 = p.din("b1", [128, 32])
    ident = p.din("ident", [128, 128])
    o_h1 = p.dout("h1", [FF, SQ], BF16)

    s_w1q = p.sbuf("s_w1q", [128, 8, FF], BF16)
    s_stg = p.sbuf("s_stg", [128, 2048])
    s_wt2 = p.sbuf("s_wt2", [128, 2048])
    s_int = p.sbuf("s_int", [128, T], I32)
    s_id = p.sbuf("s_id", [128, 128])
    s_acc = p.sbuf("s_acc", [128, 1])
    s_wsc = p.sbuf("s_wsc", [1, 8])
    s_sc128 = p.sbuf("s_sc128", [128, 1])
    s_b1 = p.sbuf("s_b1", [128, 32])
    s_r1 = p.sbuf("s_r1", [128, 8, T])
    s_xm = p.sbuf("s_xm", [128, 8, T])
    s_q1 = p.sbuf("s_q1", [128, 8, T], BF16)
    s_h1 = p.sbuf("s_h1", [128, 32, T], BF16)
    s_m1 = p.sbuf("s_m1", [128, T])
    s_sq = p.sbuf("s_sq", [128, T])
    s_v = p.sbuf("s_v", [128, T])
    s_cf = p.sbuf("s_cf", [128, T])
    s_fx = p.sbuf("s_fx", [128, T])
    s_fg = p.sbuf("s_fg", [128, T])
    s_mxc = p.sbuf("s_mxc", [128, 1])
    s_bm = p.sbuf("s_bm", [128, T])
    s_bc = p.sbuf("s_bc", [128, T])
    s_be = p.sbuf("s_be", [128, T])
    ROW = p.sbuf("ROWS", [1, 10 * T])
    psS = p.psum("psS", [1, T])
    psQ = p.psum("psQ", [1, T])
    bcP = p.psum("bcP", [128, T])
    ps0 = p.psum("ps0", [128, T])
    ps1 = p.psum("ps1", [128, T])
    psT = p.psum("psT", [128, 128])

    r_mean = ROW[0:1, 0 * T:1 * T]
    r_rstd = ROW[0:1, 1 * T:2 * T]
    r_var = ROW[0:1, 2 * T:3 * T]
    r_r2a = ROW[0:1, 3 * T:4 * T]
    r_mx = ROW[0:1, 4 * T:4 * T + 128]
    r_mq = ROW[0:1, 5 * T:6 * T]
    r_sx = ROW[0:1, 6 * T:7 * T]
    r_esc = ROW[0:1, 7 * T:8 * T]
    r_csc = ROW[0:1, 8 * T:9 * T]

    tcst = common_consts(p)
    tid = p.dma("sync", s_id[:, :], ident.ap(), inc="dW")
    tb1 = p.dma("sync", s_b1[:, :], b1.ap(), inc="dW")
    twsc0 = p.op("vector", lambda e: e.memset(s_wsc[:, :], 0.0))
    tq = _emit_wquant(p, w1T, s_w1q, 8, FF, s_wsc, 0, 1, tcst, s_stg, s_wt2,
                      psS, s_acc, s_m1, s_sc128, bcP, 1024)
    p.barrier()

    for it in range(NSUB):
        o = it * T
        tld = p.dma("sync", s_r1[:, :, :],
                    rearr(r1_in, "(c p) t -> p c t", p=128)[:, :, o:o + T],
                    inc="dL")
        trs = emit_ln_stats(p, lambda i: s_r1[:, i, :], 8, T, psS, psQ, s_sq,
                            r_mean, r_rstd, [tld, tcst], D, EPS, row_var=r_var,
                            sq2=s_v)
        p.op("vector", lambda e: e.tensor_tensor(
            out=r_r2a, in0=r_var, in1=r_rstd, op=ALU.mult), after=[trs])
        p.op("vector", lambda e: e.tensor_tensor(
            out=r_r2a, in0=r_r2a, in1=r_rstd, op=ALU.mult))
        t1 = p.op("vector", lambda e: e.tensor_single_scalar(
            out=r_r2a, in_=r_r2a, scalar=EPS, op=ALU.add))
        p.op("scalar", lambda e: e.activation(out=r_r2a, in_=r_r2a, func=AF.Ln),
             after=[t1])
        tr2 = p.op("scalar", lambda e: e.activation(out=r_r2a, in_=r_r2a,
                                                    func=AF.Exp, scale=-0.5))
        tbc = emit_bcast(p, r_mean, bcP[:, 0:T], [trs])
        tbm = p.op("vector", lambda e: e.tensor_copy(out=s_bm[:, 0:T],
                                                     in_=bcP[:, 0:T]), after=[tbc])
        for i in range(8):
            p.op("vector", lambda e, i=i: e.tensor_tensor(
                out=s_xm[:, i, :], in0=s_r1[:, i, :], in1=s_bm[:, 0:T],
                op=ALU.subtract), after=[tbm])
        txm = p.tok("eV")
        ta0 = p.op("scalar", lambda e: e.activation(
            out=s_m1[:, 0:T], in_=s_xm[:, 0, :], func=AF.Abs), after=[txm])
        mx_by = {}
        for i in range(1, 8):
            buf = s_fx if i % 2 == 0 else s_cf
            tai = p.op("scalar", lambda e, i=i, buf=buf: e.activation(
                out=buf[:, 0:T], in_=s_xm[:, i, :], func=AF.Abs),
                after=[txm] + ([mx_by[id(buf)]] if id(buf) in mx_by else []))
            mx_by[id(buf)] = p.op("vector", lambda e, buf=buf: e.tensor_tensor(
                out=s_m1[:, 0:T], in0=s_m1[:, 0:T], in1=buf[:, 0:T],
                op=ALU.max), after=[tai, ta0])
        tmx = p.tok("eV")
        tmxr = None
        for tb_ in range(T // 128):
            ttr = p.op("tensor", lambda e, tb_=tb_: e.transpose(
                out=psT[:, :], in_=s_m1[:, tb_ * 128:(tb_ + 1) * 128],
                identity=s_id[:, :]),
                after=[tmx, tid] + ([tmxr] if tmxr else []), inc="eT")
            trd = p.op("vector", lambda e: e.tensor_reduce(
                out=s_mxc[:, 0:1], in_=psT[:, :], axis=AX.X, op=ALU.max),
                after=[ttr])
            ttr2 = p.op("tensor", lambda e: e.transpose(
                out=psT[0:1, 0:128], in_=s_mxc[:, 0:1], identity=s_id[:, :]),
                after=[trd], inc="eT")
            tmxr = p.op("vector", lambda e, tb_=tb_: e.tensor_copy(
                out=ROW[0:1, 5 * T + tb_ * 128:5 * T + (tb_ + 1) * 128],
                in_=psT[0:1, 0:128]), after=[ttr2])
        p.op("vector", lambda e: e.tensor_tensor(
            out=r_mq, in0=r_mq, in1=r_rstd, op=ALU.mult), after=[tmxr, tr2])
        p.op("vector", lambda e: e.tensor_tensor(
            out=r_mq, in0=r_mq, in1=r_r2a, op=ALU.mult))
        tq1 = p.op("vector", lambda e: e.tensor_single_scalar(
            out=r_mq, in_=r_mq, scalar=1e-5, op=ALU.max))
        p.op("scalar", lambda e: e.activation(out=r_sx, in_=r_mq, func=AF.Ln),
             after=[tq1])
        ta = p.op("scalar", lambda e: e.activation(out=r_sx, in_=r_sx,
                                                   func=AF.Exp, scale=-1.0))
        p.op("vector", lambda e: e.tensor_single_scalar(
            out=r_sx, in_=r_sx, scalar=127.0, op=ALU.mult), after=[ta])
        p.op("vector", lambda e: e.tensor_single_scalar(
            out=r_esc, in_=r_mq, scalar=1.0 / 127.0, op=ALU.mult))
        p.op("vector", lambda e: e.tensor_scalar(
            out=r_esc, in0=r_esc, scalar1=s_wsc[0:1, 1:2], scalar2=None,
            op0=ALU.mult))
        p.op("vector", lambda e: e.tensor_tensor(
            out=r_csc, in0=r_rstd, in1=r_r2a, op=ALU.mult))
        tcsc = p.op("vector", lambda e: e.tensor_tensor(
            out=r_csc, in0=r_csc, in1=r_sx, op=ALU.mult))
        tbc1 = emit_bcast(p, r_csc, bcP[:, 0:T], [tcsc, tbm])
        tb_c = p.op("vector", lambda e: e.tensor_copy(out=s_bc[:, 0:T],
                                                      in_=bcP[:, 0:T]), after=[tbc1])
        tbc2 = emit_bcast(p, r_esc, bcP[:, 0:T], [tb_c])
        tb_e = p.op("vector", lambda e: e.tensor_copy(out=s_be[:, 0:T],
                                                      in_=bcP[:, 0:T]), after=[tbc2])
        for i in range(8):
            tv = p.op("vector", lambda e, i=i: e.tensor_tensor(
                out=s_v[:, 0:T], in0=s_xm[:, i, :], in1=s_bc[:, 0:T],
                op=ALU.mult), after=[tb_c])
            _emit_round_clip(p, s_v, s_int, s_cf, s_fx, s_fg, s_q1[:, i, :],
                             -128.0, 127.0, T, [tv])
        tq1b = p.tok("eV")
        pslast = {}
        for m in range(32):
            pb = ps0 if (m % 2 == 0) else ps1
            aft = [tq1b, tq] + ([pslast[id(pb)]] if id(pb) in pslast else [])
            tmm = mm_group(p, pb[:, 0:T],
                           lambda k, m=m: s_w1q[:, k, m * 128:(m + 1) * 128],
                           lambda k: s_q1[:, k, :], 8, after=aft)
            tt = p.op("vector", lambda e, m=m, pb=pb: e.tensor_tensor(
                out=s_sq[:, 0:T], in0=pb[:, 0:T], in1=s_be[:, 0:T],
                op=ALU.mult), after=[tmm, tb_e, ("eA", p.cnt["eA"])])
            pslast[id(pb)] = tt
            p.op("scalar", lambda e, m=m: e.activation(
                out=s_h1[:, m, :], in_=s_sq[:, 0:T], func=AF.Gelu,
                bias=s_b1[:, m:m + 1]), after=[tt, tb1])
        th1 = p.tok("eA")
        p.dma("sync", rearr(o_h1, "(c p) t -> p c t", p=128)[:, :, o:o + T],
              s_h1[:, :, :], after=[th1], inc="dS")
        p.barrier()
    return p.finalize()


def build_L2b2():
    p = Prog()
    T = 512
    NSUB = SQ // T

    h1_in = p.din("h1", [FF, SQ], BF16)
    r1_in = p.din("r1", [D, SQ])
    w2T = p.din("w2T", [FF, D])
    b2 = p.din("b2", [128, 8])
    ident = p.din("ident", [128, 128])
    o_r2 = p.dout("r2", [D, SQ])

    s_w2q = p.sbuf("s_w2q", [128, 32, D], BF16)
    s_stg = p.sbuf("s_stg", [128, 1024])
    s_wt2 = p.sbuf("s_wt2", [128, 1024])
    s_int = p.sbuf("s_int", [128, T], I32)
    s_id = p.sbuf("s_id", [128, 128])
    s_acc = p.sbuf("s_acc", [128, 1])
    s_wsc = p.sbuf("s_wsc", [1, 8])
    s_sc128 = p.sbuf("s_sc128", [128, 1])
    s_b2 = p.sbuf("s_b2", [128, 8])
    s_h1 = p.sbuf("s_h1", [128, 32, T], BF16)
    s_q2 = p.sbuf("s_q2", [128, 32, T], BF16)
    s_r1 = p.sbuf("s_r1", [128, 8, T])
    s_r2 = p.sbuf("s_r2", [128, 8, T])
    s_m1 = p.sbuf("s_m1", [128, T])
    s_sq = p.sbuf("s_sq", [128, T])
    s_v = p.sbuf("s_v", [128, T])
    s_cf = p.sbuf("s_cf", [128, T])
    s_fx = p.sbuf("s_fx", [128, T])
    s_fg = p.sbuf("s_fg", [128, T])
    s_mxc = p.sbuf("s_mxc", [128, 1])
    s_bc = p.sbuf("s_bc", [128, T])
    s_be = p.sbuf("s_be", [128, T])
    ROW = p.sbuf("ROWS", [1, 10 * T])
    psS = p.psum("psS", [1, T])
    psQ = p.psum("psQ", [1, T])
    bcP = p.psum("bcP", [128, T])
    ps0 = p.psum("ps0", [128, T])
    ps1 = p.psum("ps1", [128, T])
    psT = p.psum("psT", [128, 128])

    r_r2a = ROW[0:1, 3 * T:4 * T]
    r_mq = ROW[0:1, 5 * T:6 * T]
    r_sx = ROW[0:1, 6 * T:7 * T]
    r_esc = ROW[0:1, 7 * T:8 * T]
    r_csc = ROW[0:1, 8 * T:9 * T]

    tcst = common_consts(p)
    tid = p.dma("sync", s_id[:, :], ident.ap(), inc="dW")
    tb2 = p.dma("sync", s_b2[:, :], b2.ap(), inc="dW")
    twsc0 = p.op("vector", lambda e: e.memset(s_wsc[:, :], 0.0))
    tq = _emit_wquant(p, w2T, s_w2q, 32, D, s_wsc, 2, 3, tcst, s_stg, s_wt2,
                      psS, s_acc, s_m1, s_sc128, bcP, 512)
    p.barrier()

    for it in range(NSUB):
        o = it * T
        p.dma("sync", s_h1[:, :, :],
              rearr(h1_in, "(c p) t -> p c t", p=128)[:, :, o:o + T], inc="dL")
        tld = p.dma("sync", s_r1[:, :, :],
                    rearr(r1_in, "(c p) t -> p c t", p=128)[:, :, o:o + T],
                    inc="dL")
        mm_by = {}
        for i in range(32):
            buf = s_sq if i % 2 == 0 else s_v
            aft = [tld, tcst] + ([mm_by[id(buf)]] if id(buf) in mm_by else [])
            tqq = p.op("scalar", lambda e, i=i, buf=buf: e.activation(
                out=buf[:, 0:T], in_=s_h1[:, i, :], func=AF.Square), after=aft)
            mm_by[id(buf)] = p.op("tensor", lambda e, i=i, buf=buf: e.matmul(
                out=psQ[0:1, 0:T], lhsT=p.ones_col[:, :], rhs=buf[:, 0:T],
                start=(i == 0), stop=(i == 31)), after=[tqq], inc="eT")
        tQ = p.tok("eT")
        ta0 = p.op("scalar", lambda e: e.activation(
            out=s_m1[:, 0:T], in_=s_h1[:, 0, :], func=AF.Abs), after=[tld])
        mx_by = {}
        for i in range(1, 32):
            buf = s_fx if i % 2 == 0 else s_cf
            tai = p.op("scalar", lambda e, i=i, buf=buf: e.activation(
                out=buf[:, 0:T], in_=s_h1[:, i, :], func=AF.Abs),
                after=([mx_by[id(buf)]] if id(buf) in mx_by else []))
            mx_by[id(buf)] = p.op("vector", lambda e, buf=buf: e.tensor_tensor(
                out=s_m1[:, 0:T], in0=s_m1[:, 0:T], in1=buf[:, 0:T],
                op=ALU.max), after=[tai, ta0])
        tmx = p.tok("eV")
        t1 = p.op("vector", lambda e: e.tensor_single_scalar(
            out=r_r2a, in_=psQ[0:1, 0:T], scalar=1.0 / FF, op=ALU.mult),
            after=[tQ])
        t1 = p.op("vector", lambda e: e.tensor_single_scalar(
            out=r_r2a, in_=r_r2a, scalar=EPS, op=ALU.add))
        p.op("scalar", lambda e: e.activation(out=r_r2a, in_=r_r2a, func=AF.Ln),
             after=[t1])
        trh = p.op("scalar", lambda e: e.activation(out=r_r2a, in_=r_r2a,
                                                    func=AF.Exp, scale=-0.5))
        tmxr = None
        for tb_ in range(T // 128):
            ttr = p.op("tensor", lambda e, tb_=tb_: e.transpose(
                out=psT[:, :], in_=s_m1[:, tb_ * 128:(tb_ + 1) * 128],
                identity=s_id[:, :]),
                after=[tmx, tid] + ([tmxr] if tmxr else []), inc="eT")
            trd = p.op("vector", lambda e: e.tensor_reduce(
                out=s_mxc[:, 0:1], in_=psT[:, :], axis=AX.X, op=ALU.max),
                after=[ttr])
            ttr2 = p.op("tensor", lambda e: e.transpose(
                out=psT[0:1, 0:128], in_=s_mxc[:, 0:1], identity=s_id[:, :]),
                after=[trd], inc="eT")
            tmxr = p.op("vector", lambda e, tb_=tb_: e.tensor_copy(
                out=ROW[0:1, 5 * T + tb_ * 128:5 * T + (tb_ + 1) * 128],
                in_=psT[0:1, 0:128]), after=[ttr2])
        p.op("vector", lambda e: e.tensor_tensor(
            out=r_mq, in0=r_mq, in1=r_r2a, op=ALU.mult), after=[tmxr, trh])
        tq1_ = p.op("vector", lambda e: e.tensor_single_scalar(
            out=r_mq, in_=r_mq, scalar=1e-5, op=ALU.max))
        p.op("scalar", lambda e: e.activation(out=r_sx, in_=r_mq, func=AF.Ln),
             after=[tq1_])
        ta = p.op("scalar", lambda e: e.activation(out=r_sx, in_=r_sx,
                                                   func=AF.Exp, scale=-1.0))
        p.op("vector", lambda e: e.tensor_single_scalar(
            out=r_sx, in_=r_sx, scalar=127.0, op=ALU.mult), after=[ta])
        p.op("vector", lambda e: e.tensor_single_scalar(
            out=r_esc, in_=r_mq, scalar=1.0 / 127.0, op=ALU.mult))
        p.op("vector", lambda e: e.tensor_scalar(
            out=r_esc, in0=r_esc, scalar1=s_wsc[0:1, 3:4], scalar2=None,
            op0=ALU.mult))
        tcsc = p.op("vector", lambda e: e.tensor_tensor(
            out=r_csc, in0=r_r2a, in1=r_sx, op=ALU.mult))
        tbc1 = emit_bcast(p, r_csc, bcP[:, 0:T], [tcsc])
        tb_c = p.op("vector", lambda e: e.tensor_copy(out=s_bc[:, 0:T],
                                                      in_=bcP[:, 0:T]), after=[tbc1])
        tbc2 = emit_bcast(p, r_esc, bcP[:, 0:T], [tb_c])
        tb_e = p.op("vector", lambda e: e.tensor_copy(out=s_be[:, 0:T],
                                                      in_=bcP[:, 0:T]), after=[tbc2])
        for i in range(32):
            tv = p.op("vector", lambda e, i=i: e.tensor_tensor(
                out=s_v[:, 0:T], in0=s_h1[:, i, :], in1=s_bc[:, 0:T],
                op=ALU.mult), after=[tb_c])
            _emit_round_clip(p, s_v, s_int, s_cf, s_fx, s_fg, s_q2[:, i, :],
                             -128.0, 127.0, T, [tv])
        tq2b = p.tok("eV")
        pslast = {}
        for m in range(8):
            pb = ps0 if (m % 2 == 0) else ps1
            aft = [tq2b, tq] + ([pslast[id(pb)]] if id(pb) in pslast else [])
            tmm = mm_group(p, pb[:, 0:T],
                           lambda k, m=m: s_w2q[:, k, m * 128:(m + 1) * 128],
                           lambda k: s_q2[:, k, :], 32, after=aft)
            tt = p.op("vector", lambda e, m=m, pb=pb: e.tensor_tensor(
                out=s_r2[:, m, :], in0=pb[:, 0:T], in1=s_be[:, 0:T],
                op=ALU.mult), after=[tmm, tb_e])
            pslast[id(pb)] = tt
            tb_ = p.op("scalar", lambda e, m=m: e.activation(
                out=s_r2[:, m, :], in_=s_r2[:, m, :], func=AF.Identity,
                bias=s_b2[:, m:m + 1]), after=[tt, tb2])
            p.op("vector", lambda e, m=m: e.tensor_tensor(
                out=s_r2[:, m, :], in0=s_r2[:, m, :], in1=s_r1[:, m, :],
                op=ALU.add), after=[tb_])
        tr2o = p.tok("eV")
        p.dma("sync", rearr(o_r2, "(c p) t -> p c t", p=128)[:, :, o:o + T],
              s_r2[:, :, :], after=[tr2o], inc="dS")
        p.barrier()
    return p.finalize()


def build_L3():
    p = Prog()
    T = 512
    NSUB = SQ // T

    r2h = p.din("r2h", [D, SQ + 3])
    ipjT = p.din("ipjT", [D, 2 * D])
    ipb = p.din("ipb", [128, 16])
    cw1 = p.din("cw1", [128, 32])
    cb1 = p.din("cb1", [128, 8])
    bcwT = p.din("bcwT", [D, 64])
    bcb = p.din("bcb", [64, 1])
    dtwT = p.din("dtwT", [D, D])
    dtb = p.din("dtb", [128, 8])
    hm = p.din("hmask", [128, 3])
    o_dt = p.dout("dt", [D, SQ])
    o_dx = p.dout("dtxs", [D, SQ])
    o_bc = p.dout("bc", [64, SQ])
    o_sz = p.dout("siluz", [D, SQ], BF16)

    s_ipj = p.sbuf("s_ipj", [128, 8, 2 * D], BF16)
    s_dtw = p.sbuf("s_dtw", [128, 8, D], BF16)
    s_bcw = p.sbuf("s_bcw", [128, 8, 64], BF16)
    s_ipb = p.sbuf("s_ipb", [128, 16])
    s_cw = p.sbuf("s_cw", [128, 8, 4])
    s_cb = p.sbuf("s_cb", [128, 8])
    s_dtb = p.sbuf("s_dtb", [128, 8])
    s_bcb = p.sbuf("s_bcb", [64, 1])
    s_hm = p.sbuf("s_hm", [128, 3])
    tcst = common_consts(p)
    s_x = p.sbuf("s_x", [128, 8, T])
    s_ln = p.sbuf("s_ln", [128, 8, T], BF16)
    s_sq = p.sbuf("s_sq", [128, T])
    s_row = p.sbuf("s_row2", [1, 2 * T])
    s_xc = p.sbuf("s_xc", [128, 8, T + 3])
    s_z = p.sbuf("s_z", [128, 8, T], BF16)
    s_xs = p.sbuf("s_xs", [128, 8, T])
    s_xsb = p.sbuf("s_xsb", [128, 8, T], BF16)
    s_dt = p.sbuf("s_dt", [128, 8, T])
    s_dx = p.sbuf("s_dx", [128, 8, T])
    s_sp = p.sbuf("s_sp", [128, T])
    s_bc64 = p.sbuf("s_bc64", [128, T])
    psS = p.psum("psS", [1, T])
    psQ = p.psum("psQ", [1, T])
    ps_bm = p.psum("ps_bm", [128, T])
    ps_br = p.psum("ps_br", [128, T])
    ps0 = p.psum("ps0", [128, T])
    ps1 = p.psum("ps1", [128, T])

    tw1 = p.dma("gpsimd", s_ipj[:, :, :], rearr(ipjT, "(c p) f -> p c f", p=128), inc="dG")
    tw2 = p.dma("gpsimd", s_dtw[:, :, :], rearr(dtwT, "(c p) f -> p c f", p=128), inc="dG")
    tw3 = p.dma("gpsimd", s_bcw[:, :, :], rearr(bcwT, "(c p) f -> p c f", p=128), inc="dG")
    tv = p.dma("sync", s_ipb[:, :], ipb.ap(), inc="dW")
    tv = p.dma("sync", s_cw[:, :, :], rearr(cw1, "p (c k) -> p c k", k=4), inc="dW")
    tv = p.dma("sync", s_cb[:, :], cb1.ap(), inc="dW")
    tv = p.dma("sync", s_dtb[:, :], dtb.ap(), inc="dW")
    tv = p.dma("sync", s_bcb[:, :], bcb.ap(), inc="dW")
    tv = p.dma("sync", s_hm[:, :], hm.ap(), inc="dW")
    twv = p.tok("dW")

    def ln_block(width, tx):
        trs = emit_ln_stats(p, lambda i: s_x[:, i, 0:width], 8, width, psS, psQ,
                            s_sq, s_row[0:1, 0:width], s_row[0:1, T:T + width],
                            [tx, tcst], D, EPS, sq2=s_sp)
        tb1 = emit_bcast(p, s_row[0:1, 0:width], ps_bm[:, 0:width], [trs])
        tb2 = emit_bcast(p, s_row[0:1, T:T + width], ps_br[:, 0:width], [trs])
        for i in range(8):
            p.op("vector", lambda e, i=i, width=width: e.tensor_tensor(
                out=s_sq[:, 0:width], in0=s_x[:, i, 0:width], in1=ps_bm[:, 0:width],
                op=ALU.subtract), after=[tb1, tb2])
            p.op("vector", lambda e, i=i, width=width: e.tensor_tensor(
                out=s_ln[:, i, 0:width], in0=s_sq[:, 0:width], in1=ps_br[:, 0:width],
                op=ALU.mult))
        return p.tok("eV")

    def inproj(width, tln, z_dst_w):
        pslast = {}
        for m in range(16):
            pb = ps0 if (m % 2 == 0) else ps1
            aft = [tln, tw1, twv] + ([pslast[id(pb)]] if id(pb) in pslast else [])
            tmm = mm_group(p, pb[:, 0:width],
                           lambda k, m=m: s_ipj[:, k, m * 128:(m + 1) * 128],
                           lambda k, width=width: s_ln[:, k, 0:width], 8, after=aft)
            if m < 8:
                pslast[id(pb)] = p.op("scalar", lambda e, m=m, pb=pb, width=width:
                                      e.activation(out=s_xc[:, m, 3:3 + width],
                                                   in_=pb[:, 0:width],
                                                   func=AF.Identity,
                                                   bias=s_ipb[:, m:m + 1]),
                                      after=[tmm])
            else:
                pslast[id(pb)] = p.op("scalar", lambda e, m=m, pb=pb, width=width:
                                      e.activation(out=s_z[:, m - 8, 0:width],
                                                   in_=pb[:, 0:width], func=AF.Silu,
                                                   bias=s_ipb[:, 8 + (m - 8):9 + (m - 8)]),
                                      after=[tmm])
        return p.tok("eA")

    tx = p.dma("sync", s_x[:, :, 0:3],
               rearr(r2h, "(c p) t -> p c t", p=128)[:, :, 0:3], inc="dL")
    tln = ln_block(3, tx)
    tz = inproj(3, tln, 0)
    for i in range(8):
        p.op("vector", lambda e, i=i: e.tensor_tensor(
            out=s_xc[:, i, 3:5], in0=s_xc[:, i, 3:5], in1=s_hm[:, 0:2],
            op=ALU.mult), after=[tz])
    for i in range(8):
        p.op("vector", lambda e, i=i: e.tensor_copy(
            out=s_xc[:, i, 0:3], in_=s_xc[:, i, 3:6]))
    tcar = p.tok("eV")
    p.dma("sync", rearr(o_sz, "(c p) t -> p c t", p=128)[:, :, 0:1],
          s_z[:, :, 2:3], after=[tz], inc="dS", slow_ok=True)
    p.barrier()

    for it in range(NSUB):
        o = it * T
        last = (it == NSUB - 1)
        tx = p.dma("sync", s_x[:, :, :],
                   rearr(r2h, "(c p) t -> p c t", p=128)[:, :, o + 3:o + T + 3],
                   inc="dL")
        tln = ln_block(T, tx)
        tz = inproj(T, tln, T)
        if last:
            for i in range(8):
                p.op("vector", lambda e, i=i: e.tensor_tensor(
                    out=s_xc[:, i, T + 2:T + 3], in0=s_xc[:, i, T + 2:T + 3],
                    in1=s_hm[:, 2:3], op=ALU.mult), after=[tz])
        tmask = p.tok("eV")
        w = T if not last else T - 1
        p.dma("sync", rearr(o_sz, "(c p) t -> p c t", p=128)[:, :, o + 1:o + 1 + w],
              s_z[:, :, 0:w], after=[tz], inc="dS")
        for i in range(8):
            p.op("vector", lambda e, i=i: e.tensor_scalar(
                out=s_xs[:, i, :], in0=s_xc[:, i, 0:T], scalar1=s_cw[:, i, 0:1],
                scalar2=None, op0=ALU.mult), after=[tz, tmask, twv])
            for j in (1, 2, 3):
                p.op("vector", lambda e, i=i, j=j: e.scalar_tensor_tensor(
                    out=s_xs[:, i, :], in0=s_xc[:, i, j:j + T],
                    scalar=s_cw[:, i, j:j + 1], in1=s_xs[:, i, :],
                    op0=ALU.mult, op1=ALU.add))
            p.op("scalar", lambda e, i=i: e.activation(
                out=s_xs[:, i, :], in_=s_xs[:, i, :], func=AF.Silu,
                bias=s_cb[:, i:i + 1]), after=[p.tok("eV")])
            p.op("vector", lambda e, i=i: e.tensor_copy(
                out=s_xsb[:, i, :], in_=s_xs[:, i, :]), after=[p.tok("eA")])
            p.op("vector", lambda e, i=i: e.tensor_copy(
                out=s_xc[:, i, 0:3], in_=s_xc[:, i, T:T + 3]))
        txs = p.tok("eV")
        tmm = mm_group(p, ps0[0:64, 0:T],
                       lambda k: s_bcw[:, k, :],
                       lambda k: s_xsb[:, k, :], 8, after=[txs, tw3])
        tbc_ = p.op("scalar", lambda e: e.activation(
            out=s_bc64[0:64, 0:T], in_=ps0[0:64, 0:T], func=AF.Identity,
            bias=s_bcb[0:64, 0:1]), after=[tmm])
        p.dma("sync", o_bc[:, o:o + T], s_bc64[0:64, 0:T],
              after=[tbc_], inc="dS")
        pslast = {id(ps0): tbc_}
        tsp_rd = None
        for m in range(8):
            pb = ps1 if (m % 2 == 0) else ps0
            aft = [txs, tw2] + ([pslast[id(pb)]] if id(pb) in pslast else [])
            tmm = mm_group(p, pb[:, 0:T],
                           lambda k, m=m: s_dtw[:, k, m * 128:(m + 1) * 128],
                           lambda k: s_xsb[:, k, :], 8, after=aft)
            ta = p.op("scalar", lambda e, m=m, pb=pb: e.activation(
                out=s_sp[:, 0:T], in_=pb[:, 0:T], func=AF.Abs,
                bias=s_dtb[:, m:m + 1]),
                after=[tmm] + ([tsp_rd] if tsp_rd else []))
            p.op("scalar", lambda e: e.activation(
                out=s_sp[:, 0:T], in_=s_sp[:, 0:T], func=AF.Exp, scale=-1.0))
            ta2 = p.op("scalar", lambda e: e.activation(
                out=s_sp[:, 0:T], in_=s_sp[:, 0:T], func=AF.Ln, bias=1.0))
            trl = p.op("scalar", lambda e, m=m, pb=pb: e.activation(
                out=s_dt[:, m, :], in_=pb[:, 0:T], func=AF.Relu,
                bias=s_dtb[:, m:m + 1]))
            pslast[id(pb)] = trl
            tsp_rd = p.op("vector", lambda e, m=m: e.tensor_tensor(
                out=s_dt[:, m, :], in0=s_dt[:, m, :], in1=s_sp[:, 0:T],
                op=ALU.add), after=[trl, ta2])
            p.op("vector", lambda e, m=m: e.tensor_scalar(
                out=s_dt[:, m, :], in0=s_dt[:, m, :], scalar1=0.2,
                scalar2=0.001, op0=ALU.min, op1=ALU.max))
            p.op("vector", lambda e, m=m: e.tensor_tensor(
                out=s_dx[:, m, :], in0=s_dt[:, m, :], in1=s_xs[:, m, :],
                op=ALU.mult))
        tdt = p.tok("eV")
        p.dma("sync", rearr(o_dt, "(c p) t -> p c t", p=128)[:, :, o:o + T],
              s_dt[:, :, :], after=[tdt], inc="dS")
        p.dma("sync", rearr(o_dx, "(c p) t -> p c t", p=128)[:, :, o:o + T],
              s_dx[:, :, :], after=[tdt], inc="dS")
        p.barrier()
    return p.finalize()


def build_L4():
    p = Prog()
    TC = 1024
    NCH = S // TC

    dtd = p.din("dtd", [DQ, S])
    dxd = p.din("dxd", [DQ, S])
    bnm = p.din("bnm", [NS, S])
    cnm = p.din("cnm", [NS, S])
    alog = p.din("alog", [DQ, NS])
    identb = p.din("identb", [128, 128], BF16)
    onesb = p.din("onesb", [1, 128], BF16)
    o_y = p.dout("ydm", [DQ, S])

    s_A = p.sbuf("s_A", [128, 2, NS])
    s_idb = p.sbuf("s_idb", [128, 128], BF16)
    s_onb = p.sbuf("s_onb", [1, 128], BF16)
    s_dt = p.sbuf("s_dt", [128, 2, TC])
    s_dx = p.sbuf("s_dx", [128, 2, TC])
    s_bn = p.sbuf("s_bn", [1, NS, TC], BF16)
    s_cn = p.sbuf("s_cn", [1, NS, TC], BF16)
    s_ab = p.sbuf("s_ab", [128, 2, 2, TC])
    s_u = p.sbuf("s_u", [128, 2, TC])
    s_s = p.sbuf("s_s", [128, 2, 2, TC])
    s_pr = p.sbuf("s_pr", [128, 2, 2, TC], BF16)
    s_cy = p.sbuf("s_cy", [128, 64])
    s_y = p.sbuf("s_y", [128, TC])
    sbC = p.sbuf("sbC", [128, 2, TC], BF16)
    psB = p.psum("psB", [128, TC])
    psC = p.psum("psC", [128, TC])
    psY0 = p.psum("psY0", [128, TC])
    psY1 = p.psum("psY1", [128, TC])
    psYs = (psY0, psY1)

    tcst = common_consts(p)
    tidb = p.dma("sync", s_idb[:, :], identb.ap(), inc="dW")
    tonb = p.dma("sync", s_onb[:, :], onesb.ap(), inc="dW")
    tA = p.dma("sync", s_A[:, :, :], rearr(alog, "(c p) n -> p c n", p=128), inc="dW")
    t1 = p.op("vector", lambda e: e.tensor_scalar(
        out=s_A[:, :, :], in0=s_A[:, :, :], scalar1=2.0, scalar2=-5.0,
        op0=ALU.min, op1=ALU.max), after=[tA])
    t2 = p.op("scalar", lambda e: e.activation(
        out=s_A[:, :, :], in_=s_A[:, :, :], func=AF.Exp), after=[t1])
    tAr = p.op("vector", lambda e: e.tensor_single_scalar(
        out=s_A[:, :, :], in_=s_A[:, :, :], scalar=-1.0, op=ALU.mult), after=[t2])
    tcy = p.op("vector", lambda e: e.memset(s_cy[:, :], 0.0))

    def bcastB(n, extra):
        t = None
        for h in range(2):
            t = p.op("tensor", lambda e, n=n, h=h: e.matmul(
                out=psB[:, h * 512:(h + 1) * 512], lhsT=s_onb[0:1, :],
                rhs=s_bn[0:1, n, h * 512:(h + 1) * 512],
                start=True, stop=True), after=extra, inc="eT")
        return t

    def bcastC(n, extra):
        t = None
        for h in range(2):
            t = p.op("tensor", lambda e, n=n, h=h: e.matmul(
                out=psC[:, h * 512:(h + 1) * 512], lhsT=s_onb[0:1, :],
                rhs=s_cn[0:1, n, h * 512:(h + 1) * 512],
                start=True, stop=True), after=extra, inc="eT")
        return t

    for it in range(NCH):
        o = it * TC
        p.dma("sync", s_dt[:, :, :],
              rearr(dtd, "(c p) t -> p c t", p=128)[:, :, o:o + TC], inc="dL")
        p.dma("sync", s_dx[:, :, :],
              rearr(dxd, "(c p) t -> p c t", p=128)[:, :, o:o + TC], inc="dL")
        p.dma("gpsimd", s_bn[0:1, :, :], bnm[0:NS, o:o + TC], inc="dG")
        tldG = p.dma("gpsimd", s_cn[0:1, :, :], cnm[0:NS, o:o + TC], inc="dG")
        tld = p.tok("dL")
        tbcB = bcastB(0, [tldG, tonb, tcst])
        tbcC = bcastC(0, [tldG])
        scan_rd = {}
        s_rd = {}
        pr_rd = {}
        sbc_rd = {}
        for n in range(NS):
            par = n % 2
            ceng = "vector" if par == 0 else "gpsimd"
            tabs = []
            for c in range(2):
                tabs.append(p.op("scalar", lambda e, c=c, n=n, par=par:
                    e.activation(out=s_ab[:, par, c, :], in_=s_dt[:, c, :],
                                 func=AF.Exp, scale=s_A[:, c, n:n + 1]),
                    after=[tld, tAr] + scan_rd.get((par, c), [])))
            tu = {}
            tcl = {}
            for c in range(2):
                tu[c] = p.op("vector", lambda e, c=c: e.tensor_tensor(
                    out=s_u[:, c, :], in0=s_dx[:, c, :], in1=psB[:, 0:TC],
                    op=ALU.mult),
                    after=[tbcB, tld] + scan_rd.get((par, c), []))
                ts = p.op("vector", lambda e, c=c, n=n, par=par: e.tensor_tensor_scan(
                    out=s_s[:, par, c, :], data0=s_ab[:, par, c, :],
                    data1=s_u[:, c, :],
                    initial=s_cy[:, c * NS + n:c * NS + n + 1],
                    op0=ALU.mult, op1=ALU.add),
                    after=[tabs[c], tu[c], tcy] + s_rd.get((par, c), []))
                scan_rd[(par, c)] = [ts]
                tcl[c] = p.op(ceng, lambda e, c=c, par=par: e.tensor_scalar(
                    out=s_s[:, par, c, :], in0=s_s[:, par, c, :], scalar1=10.0,
                    scalar2=-10.0, op0=ALU.min, op1=ALU.max), after=[ts])
            if n + 1 < NS:
                tbcB = bcastB(n + 1, [tu[0], tu[1]])
            tCcp = p.op("scalar", lambda e, par=par: e.copy(
                out=sbC[:, par, :], in_=psC[:, 0:TC]),
                after=[tbcC] + sbc_rd.get(par, []))
            if n + 1 < NS:
                tbcC = bcastC(n + 1, [tCcp])
            tprs = []
            for c in range(2):
                tcar = p.op("scalar", lambda e, c=c, n=n, par=par: e.copy(
                    out=s_cy[:, c * NS + n:c * NS + n + 1],
                    in_=s_s[:, par, c, TC - 1:TC]), after=[tcl[c]])
                tpr = p.op("gpsimd", lambda e, c=c, par=par: e.tensor_tensor(
                    out=s_pr[:, par, c, :], in0=s_s[:, par, c, :],
                    in1=sbC[:, par, :], op=ALU.mult),
                    after=[tcl[c], tCcp] + pr_rd.get((par, c), []))
                tprs.append(tpr)
                s_rd[(par, c)] = [tpr, tcar]
                tmm = None
                for h in range(2):
                    tmm = p.op("tensor", lambda e, c=c, n=n, h=h, par=par: e.matmul(
                        out=psYs[c][:, h * 512:(h + 1) * 512], lhsT=s_idb[:, :],
                        rhs=s_pr[:, par, c, h * 512:(h + 1) * 512],
                        start=(n == 0), stop=(n == NS - 1)),
                        after=[tpr, tidb], inc="eT")
                pr_rd[(par, c)] = [tmm]
            sbc_rd[par] = tprs
        mm_done = pr_rd[(0, 0)] + pr_rd[(0, 1)] + pr_rd[(1, 0)] + pr_rd[(1, 1)]
        tst = None
        for c in range(2):
            tev = p.op("vector", lambda e, c=c: e.tensor_copy(
                out=s_y[:, :], in_=psYs[c][:, 0:TC]),
                after=mm_done + ([tst] if tst else []))
            tst = p.dma("sync",
                        rearr(o_y, "(c p) t -> p c t", p=128)[:, c, o:o + TC],
                        s_y[:, :], after=[tev], inc="dS")
        p.barrier()
    return p.finalize()


def build_L5():
    p = Prog()
    T = 512
    NSUB = SQ // T

    y_in = p.din("yfm", [D, SQ])
    r2_in = p.din("r2", [D, SQ])
    sz_in = p.din("siluz", [D, SQ], BF16)
    owT = p.din("owT", [D, D])
    ob = p.din("ob", [128, 8])
    lng = p.din("lng", [128, 8])
    lnb = p.din("lnb", [128, 8])
    fwT = p.din("fwT", [D, D])
    o_out = p.dout("outfm", [D, SQ])

    s_ow = p.sbuf("s_ow", [128, 8, D], BF16)
    s_fw = p.sbuf("s_fw", [128, 8, D], BF16)
    s_stg = p.sbuf("s_stg", [128, D])
    s_ob = p.sbuf("s_ob", [128, 8])
    s_g = p.sbuf("s_g", [128, 8])
    s_bt = p.sbuf("s_bt", [128, 8])
    tcst = common_consts(p)
    s_y = p.sbuf("s_y", [128, 8, T])
    s_r2 = p.sbuf("s_r2", [128, 8, T])
    s_sz = p.sbuf("s_sz", [128, 8, T], BF16)
    s_g1 = p.sbuf("s_g1", [128, 8, T], BF16)
    s_o0 = p.sbuf("s_o0", [128, 8, T])
    s_l2 = p.sbuf("s_l2", [128, 8, T])
    s_r3 = p.sbuf("s_r3", [128, 8, T])
    s_l3 = p.sbuf("s_l3", [128, 8, T], BF16)
    s_sq = p.sbuf("s_sq", [128, T])
    s_sq2 = p.sbuf("s_sq2L5", [128, T])
    s_row = p.sbuf("s_rowL5", [1, 2 * T])
    psS = p.psum("psS", [1, T])
    psQ = p.psum("psQ", [1, T])
    ps_bm = p.psum("ps_bm", [128, T])
    ps_br = p.psum("ps_br", [128, T])
    ps0 = p.psum("ps0", [128, T])
    ps1 = p.psum("ps1", [128, T])

    tow = p.dma("gpsimd", s_ow[:, :, :], rearr(owT, "(c p) f -> p c f", p=128), inc="dG")
    tv = p.dma("sync", s_ob[:, :], ob.ap(), inc="dW")
    tv = p.dma("sync", s_g[:, :], lng.ap(), inc="dW")
    tv = p.dma("sync", s_bt[:, :], lnb.ap(), inc="dW")
    twv = p.tok("dW")
    tsg_prev = None
    for c in range(8):
        tld = p.dma("sync", s_stg[:, :],
                    rearr(fwT, "(c p) f -> p c f", p=128)[:, c, :],
                    after=([tsg_prev] if tsg_prev else []), inc="dL")
        tsg_prev = p.op("scalar", lambda e, c=c: e.activation(
            out=s_fw[:, c, :], in_=s_stg[:, :], func=AF.Sign), after=[tld])
    tfw = tsg_prev
    p.barrier()

    def ln_stats_apply(src, dst, tafter, dst_bf=False, gain=False):
        trs = emit_ln_stats(p, lambda i: src[:, i, :], 8, T, psS, psQ, s_sq,
                            s_row[0:1, 0:T], s_row[0:1, T:2 * T],
                            tafter + [tcst], D, EPS, sq2=s_sq2)
        tb1 = emit_bcast(p, s_row[0:1, 0:T], ps_bm[:, 0:T], [trs])
        tb2 = emit_bcast(p, s_row[0:1, T:2 * T], ps_br[:, 0:T], [trs])
        for i in range(8):
            p.op("vector", lambda e, i=i: e.tensor_tensor(
                out=s_sq[:, 0:T], in0=src[:, i, :], in1=ps_bm[:, 0:T],
                op=ALU.subtract), after=[tb1, tb2, ("eA", p.cnt["eA"])])
            if gain:
                p.op("vector", lambda e, i=i: e.tensor_tensor(
                    out=s_sq[:, 0:T], in0=s_sq[:, 0:T], in1=ps_br[:, 0:T],
                    op=ALU.mult))
                p.op("scalar", lambda e, i=i: e.activation(
                    out=dst[:, i, :], in_=s_sq[:, 0:T], func=AF.Identity,
                    scale=s_g[:, i:i + 1], bias=s_bt[:, i:i + 1]),
                    after=[p.tok("eV"), twv])
            else:
                p.op("vector", lambda e, i=i: e.tensor_tensor(
                    out=dst[:, i, :], in0=s_sq[:, 0:T], in1=ps_br[:, 0:T],
                    op=ALU.mult))
        return p.tok("eA") if gain else p.tok("eV")

    for it in range(NSUB):
        o = it * T
        p.dma("sync", s_y[:, :, :],
              rearr(y_in, "(c p) t -> p c t", p=128)[:, :, o:o + T], inc="dL")
        p.dma("sync", s_r2[:, :, :],
              rearr(r2_in, "(c p) t -> p c t", p=128)[:, :, o:o + T], inc="dL")
        tld = p.dma("sync", s_sz[:, :, :],
                    rearr(sz_in, "(c p) t -> p c t", p=128)[:, :, o:o + T], inc="dL")
        for i in range(8):
            p.op("vector", lambda e, i=i: e.tensor_tensor(
                out=s_g1[:, i, :], in0=s_y[:, i, :], in1=s_sz[:, i, :],
                op=ALU.mult), after=[tld])
        tg1 = p.tok("eV")
        tl2 = ln_stats_apply(s_r2, s_l2, [tld])
        pslast = {}
        for m in range(8):
            pb = ps0 if (m % 2 == 0) else ps1
            aft = [tg1, tow] + ([pslast[id(pb)]] if id(pb) in pslast else [])
            tmm = mm_group(p, pb[:, 0:T],
                           lambda k, m=m: s_ow[:, k, m * 128:(m + 1) * 128],
                           lambda k: s_g1[:, k, :], 8, after=aft)
            ta = p.op("scalar", lambda e, m=m, pb=pb: e.activation(
                out=s_o0[:, m, :], in_=pb[:, 0:T], func=AF.Identity,
                bias=s_ob[:, m:m + 1]), after=[tmm, twv])
            pslast[id(pb)] = ta
            p.op("vector", lambda e, m=m: e.tensor_tensor(
                out=s_o0[:, m, :], in0=s_o0[:, m, :], in1=s_l2[:, m, :],
                op=ALU.add), after=[ta, tl2])
        to0 = p.tok("eV")
        tat = ln_stats_apply(s_o0, s_l2, [to0], gain=True)
        for i in range(8):
            p.op("vector", lambda e, i=i: e.tensor_tensor(
                out=s_r3[:, i, :], in0=s_r2[:, i, :], in1=s_l2[:, i, :],
                op=ALU.add), after=[tat])
        tr3 = p.tok("eV")
        tl3 = ln_stats_apply(s_r3, s_l3, [tr3])
        pslast = {}
        for m in range(8):
            pb = ps0 if (m % 2 == 0) else ps1
            aft = [tl3, tfw] + ([pslast[id(pb)]] if id(pb) in pslast else [])
            tmm = mm_group(p, pb[:, 0:T],
                           lambda k, m=m: s_fw[:, k, m * 128:(m + 1) * 128],
                           lambda k: s_l3[:, k, :], 8, after=aft)
            pslast[id(pb)] = p.op("vector", lambda e, m=m, pb=pb: e.tensor_tensor(
                out=s_o0[:, m, :], in0=pb[:, 0:T], in1=s_r3[:, m, :],
                op=ALU.add), after=[tmm])
        tout = p.tok("eV")
        p.dma("sync", rearr(o_out, "(c p) t -> p c t", p=128)[:, :, o:o + T],
              s_o0[:, :, :], after=[tout], inc="dS")
        p.barrier()
    return p.finalize()


def vec_pc(v, nch):
    return np.ascontiguousarray(np.asarray(v, np.float32).reshape(nch, 128).T)


def _core_slices():
    return [(c // 4, (c % 4) * SQ, (c % 4 + 1) * SQ) for c in range(NCORES)]


_PROG_CACHE = {}


def _prog(name, builder):
    if name not in _PROG_CACHE:
        _PROG_CACHE[name] = builder()
    return _PROG_CACHE[name]


def _run(name, builder, maps):
    nc = _prog(name, builder)
    last = None
    for attempt in range(3):
        try:
            return run_bass_kernel_spmd(nc, maps,
                                        core_ids=list(range(NCORES))).results
        except Exception as ex:
            last = ex
            if "UNAVAILABLE" not in str(ex) and "UNRECOVERABLE" not in str(ex):
                raise
            import time as _t
            _t.sleep(10)
    raise last


def kernel(**inputs):
    f = lambda k: np.asarray(inputs[k], np.float32)
    cc = np.ascontiguousarray

    x = f("L0_attn_x")
    x_fm = [cc(x[b].T) for b in range(B)]

    wgT = cc(f("L0_attn_Wg").T)
    wiT = cc(f("L0_attn_Wi").T)
    cw0 = cc(f("L0_attn_conv_w")[:, 0, :].reshape(12, 128, 4)
             .transpose(1, 0, 2).reshape(128, 48))
    cb0 = vec_pc(f("L0_attn_conv_b"), 12)
    wliT = cc(f("L0_attn_lru_Wi").T)
    wlgT = cc(f("L0_attn_lru_Wg").T)
    lbi = vec_pc(f("L0_attn_lru_bi"), 12)
    lbg = vec_pc(f("L0_attn_lru_bg"), 12)
    lam = vec_pc(f("L0_attn_lam"), 12)
    woT = cc(f("L0_attn_Wo").T)
    w1T = cc(f("L0_ffn_w1").T)
    b1 = vec_pc(f("L0_ffn_b1"), 32)
    w2T = cc(f("L0_ffn_w2").T)
    b2 = vec_pc(f("L0_ffn_b2"), 8)
    ipjT = cc(f("L1_attn_in_proj_w").T)
    ipb = vec_pc(f("L1_attn_in_proj_b"), 16)
    cw1 = cc(f("L1_attn_conv_w")[:, 0, :].reshape(8, 128, 4)
             .transpose(1, 0, 2).reshape(128, 32))
    cb1 = vec_pc(f("L1_attn_conv_b"), 8)
    alog = f("L1_attn_A_log")
    bcwT = cc(np.concatenate([f("L1_attn_B_w"), f("L1_attn_C_w")], 0).T)
    bcb = cc(np.concatenate([f("L1_attn_B_b"), f("L1_attn_C_b")])[:, None])
    dtwT = cc(f("L1_attn_dt_w").T)
    dtb = vec_pc(f("L1_attn_dt_b"), 8)
    owT = cc(f("L1_attn_out_w").T)
    ob = vec_pc(f("L1_attn_out_b"), 8)
    lng = vec_pc(f("L1_attn_ln_g"), 8)
    lnb = vec_pc(f("L1_attn_ln_beta"), 8)
    fwT = cc(f("L1_ffn_weight").T)
    ident = np.eye(128, dtype=np.float32)
    identb = np.eye(128, dtype=BFNP)

    cs = _core_slices()

    maps = []
    for (b, lo, hi) in cs:
        xs = np.zeros((D, SQ + 3), np.float32)
        xs[:, 3:] = x_fm[b][:, lo:hi]
        if lo > 0:
            xs[:, 0:3] = x_fm[b][:, lo - 3:lo]
        maps.append({"x": xs, "wgT": wgT, "wiT": wiT, "cw": cw0, "cb": cb0})
    rA = _run("L1a", build_L1a, maps)

    maps = [{"ybf": rA[c]["ybf"], "wliT": wliT, "wlgT": wlgT,
             "lbi": lbi, "lbg": lbg, "lam": lam} for c in range(NCORES)]
    rB = _run("L1b", build_L1b, maps)

    maps = []
    for c, (b, lo, hi) in enumerate(cs):
        q = c % 4
        pg = np.zeros((128, 12, 3), np.float32)
        ps = np.zeros((128, 12, 3), np.float32)
        for idx, j in enumerate(range(b * 4, b * 4 + q)):
            slot = 3 - q + idx
            pg[:, :, slot] = rB[j]["summ"][:, 0].reshape(12, 128).T
            ps[:, :, slot] = rB[j]["summ"][:, 1].reshape(12, 128).T
        maps.append({"alpha": rB[c]["alpha"], "xbeta": rB[c]["xbeta"],
                     "ggate": rA[c]["ggate"], "x": cc(x_fm[b][:, lo:hi]),
                     "pgam": pg, "psend": ps, "woT": woT})
    r2a = _run("L2a", build_L2a, maps)

    maps = [{"r1": r2a[c]["r1"], "w1T": w1T, "b1": b1, "ident": ident}
            for c in range(NCORES)]
    rb1 = _run("L2b1", build_L2b1, maps)
    maps = [{"h1": rb1[c]["h1"], "r1": r2a[c]["r1"], "w2T": w2T, "b2": b2,
             "ident": ident} for c in range(NCORES)]
    r2b = _run("L2b2", build_L2b2, maps)

    maps = []
    for c, (b, lo, hi) in enumerate(cs):
        q = c % 4
        r2h = np.zeros((D, SQ + 3), np.float32)
        r2h[:, 2:2 + SQ] = r2b[c]["r2"]
        hm = np.ones((128, 3), np.float32)
        if q > 0:
            r2h[:, 0:2] = r2b[c - 1]["r2"][:, SQ - 2:SQ]
        else:
            hm[:, 0:2] = 0.0
        if q < 3:
            r2h[:, SQ + 2:SQ + 3] = r2b[c + 1]["r2"][:, 0:1]
        else:
            hm[:, 2] = 0.0
        maps.append({"r2h": r2h, "ipjT": ipjT, "ipb": ipb, "cw1": cw1,
                     "cb1": cb1, "bcwT": bcwT, "bcb": bcb, "dtwT": dtwT,
                     "dtb": dtb, "hmask": hm})
    r3 = _run("L3", build_L3, maps)

    maps = []
    for c in range(NCORES):
        b, dq = c // 4, c % 4
        dsl = slice(dq * DQ, (dq + 1) * DQ)
        dt_b = np.concatenate([r3[b * 4 + j]["dt"] for j in range(4)], axis=1)
        dx_b = np.concatenate([r3[b * 4 + j]["dtxs"] for j in range(4)], axis=1)
        bc_b = np.concatenate([r3[b * 4 + j]["bc"] for j in range(4)], axis=1)
        maps.append({"dtd": cc(dt_b[dsl, :]), "dxd": cc(dx_b[dsl, :]),
                     "bnm": cc(bc_b[0:NS, :]), "cnm": cc(bc_b[NS:2 * NS, :]),
                     "alog": cc(alog[dsl, :]), "identb": identb,
                     "onesb": np.ones((1, 128), BFNP)})
    r4 = _run("L4", build_L4, maps)

    maps = []
    for c, (b, lo, hi) in enumerate(cs):
        y_b = np.concatenate([r4[b * 4 + dq]["ydm"] for dq in range(4)], axis=0)
        maps.append({"yfm": cc(y_b[:, lo:hi]), "r2": r2b[c]["r2"],
                     "siluz": r3[c]["siluz"], "owT": owT, "ob": ob,
                     "lng": lng, "lnb": lnb, "fwT": fwT})
    r5 = _run("L5", build_L5, maps)

    out = np.empty((B, S, D), np.float32)
    for c, (b, lo, hi) in enumerate(cs):
        out[b, lo:hi, :] = r5[c]["outfm"].T
    return out


# version 08
# speedup vs baseline: 1.5387x; 1.0552x over previous
import contextlib
import numpy as np
import ml_dtypes

import concourse.bass as bass
import concourse.mybir as mybir
from concourse.bass_utils import run_bass_kernel_spmd

F32 = mybir.dt.float32
BF16 = mybir.dt.bfloat16
I32 = mybir.dt.int32
AF = mybir.ActivationFunctionType
ALU = mybir.AluOpType
AX = mybir.AxisListType
BFNP = ml_dtypes.bfloat16

NCORES = 8
B, S, D = 2, 16384, 1024
H = 1536
FF = 4096
NS = 32
SQ = S // 4
DQ = D // 4
EPS = 1e-5
ERFS = 0.7071067811865476


class Prog:
    ENGS = ("sync", "tensor", "vector", "scalar", "gpsimd")
    SELF = {"vector": "eV", "scalar": "eA", "gpsimd": "eG"}

    def __init__(self):
        self.nc = bass.Bass("TRN2", target_bir_lowering=False)
        self.es = contextlib.ExitStack()
        self.es.__enter__()
        self.q = {e: [] for e in self.ENGS}
        self.sems = {}
        self.cnt = {}
        self.lastw = {e: {} for e in self.ENGS}
        for s in ("dL", "dW", "dS", "dG", "eT", "eV", "eA", "eG", "eS"):
            self.sems[s] = self.es.enter_context(self.nc.semaphore(s))
            self.cnt[s] = 0
        nums = [h.num for h in self.sems.values()]
        rng = range(min(nums), max(nums) + 1)
        self.q["gpsimd"].append(lambda e, r=rng: e.dma_reset(r))
        self.q["gpsimd"].append(lambda e, r=rng: e.sem_clear(r))
        self.q["gpsimd"].append(
            lambda e: e.nop().then_inc(self.sems["eG"], 1))
        self.cnt["eG"] = 1
        self._started = {e: (e == "gpsimd") for e in self.ENGS}
        self.pending = {e: {} for e in self.ENGS}

    def din(self, name, shape, dt=F32):
        return self.nc.dram_tensor(name, list(shape), dt, kind="ExternalInput")

    def dout(self, name, shape, dt=F32):
        return self.nc.dram_tensor(name, list(shape), dt, kind="ExternalOutput")

    def sbuf(self, name, shape, dt=F32):
        return self.es.enter_context(self.nc.sbuf_tensor(name, list(shape), dt))

    def psum(self, name, shape, dt=F32):
        return self.es.enter_context(self.nc.psum_tensor(name, list(shape), dt))

    def tok(self, sem):
        return (sem, self.cnt[sem])

    def _wait(self, eng, sem, val):
        if val <= 0:
            return
        if self.lastw[eng].get(sem, 0) >= val:
            return
        self.lastw[eng][sem] = val
        self.q[eng].append(lambda e, s=self.sems[sem], v=val: e.wait_ge(s, v))

    def op(self, eng, fn, after=(), inc=None, dma=False):
        if not self._started[eng]:
            self._started[eng] = True
            self._wait(eng, "eG", 1)
        ss = None if dma else self.SELF.get(eng)
        afters = list(after)
        if self.pending[eng]:
            afters.extend(self.pending[eng].items())
            self.pending[eng] = {}
        if ss:
            afters.append((ss, self.cnt[ss]))
        merged = {}
        for sem, val in afters:
            merged[sem] = max(merged.get(sem, 0), val)
        for sem, val in merged.items():
            self._wait(eng, sem, val)
        incs = []
        if ss:
            self.cnt[ss] += 1
            incs.append((ss, 1))
        if inc:
            amt = 16 if dma else 1
            self.cnt[inc] += amt
            incs.append((inc, amt))

        def run(e, fn=fn, incs=tuple(incs)):
            inst = fn(e)
            for s, amt in incs:
                inst = inst.then_inc(self.sems[s], amt)
        self.q[eng].append(run)
        if inc:
            return self.tok(inc)
        if ss:
            return self.tok(ss)
        return None

    def dma(self, eng, out, in_, after=(), inc="dL", slow_ok=False):
        after = list(after)
        after.append((inc, self.cnt[inc]))
        if slow_ok:
            return self.op(eng, lambda e: e.dma_start(
                out=out, in_=in_, allow_slow_non_contiguous=True),
                after=after, inc=inc, dma=True)
        return self.op(eng, lambda e: e.dma_start(out=out, in_=in_),
                       after=after, inc=inc, dma=True)

    def barrier(self):
        for e in self.ENGS:
            for s in ("dL", "dW", "dS", "dG", "eT", "eV", "eA", "eG"):
                self.pending[e][s] = max(self.pending[e].get(s, 0), self.cnt[s])

    def finalize(self, final_eng=None):
        self._wait("sync", "dS", self.cnt["dS"])
        blk = self.nc.Block()
        blk.__enter__()
        for e in self.ENGS:
            if self.q[e]:
                getattr(blk, e)(lambda eng, lst=tuple(self.q[e]): [f(eng) for f in lst])
        blk.__exit__(None, None, None)
        nc = self.nc
        self.es.__exit__(None, None, None)
        nc.finalize()
        return nc


def rearr(t, pat, **kw):
    return t.ap().rearrange(pat, **kw)


def mm_group(p, out_ap, lhsT_fn, rhs_fn, K, after=()):
    for k in range(K):
        p.op("tensor",
             lambda e, k=k: e.matmul(out=out_ap, lhsT=lhsT_fn(k), rhs=rhs_fn(k),
                                     start=(k == 0), stop=(k == K - 1)),
             after=(after if k == 0 else ()), inc="eT" if k == K - 1 else None)
    return p.tok("eT")


def emit_ln_stats(p, x_fn, nch, T, psS, psQ, sq, row_mean, row_rstd, after,
                  denom, eps, row_var=None, sq2=None):
    for i in range(nch):
        p.op("tensor", lambda e, i=i: e.matmul(
            out=psS[0:1, 0:T], lhsT=p.ones_col[:, 0:1], rhs=x_fn(i),
            start=(i == 0), stop=(i == nch - 1)), after=after,
            inc="eT" if i == nch - 1 else None)
    tS = p.tok("eT")
    mm_by_buf = {}
    for i in range(nch):
        buf = sq if (sq2 is None or i % 2 == 0) else sq2
        aft = list(after) + ([mm_by_buf[id(buf)]] if id(buf) in mm_by_buf else [])
        tq = p.op("gpsimd", lambda e, i=i, buf=buf: e.tensor_tensor(
            out=buf[:, 0:T], in0=x_fn(i), in1=x_fn(i), op=ALU.mult), after=aft)
        mm_by_buf[id(buf)] = p.op("tensor", lambda e, i=i, buf=buf: e.matmul(
            out=psQ[0:1, 0:T], lhsT=p.ones_col[:, 0:1], rhs=buf[:, 0:T],
            start=(i == 0), stop=(i == nch - 1)), after=[tq], inc="eT")
    tQ = p.tok("eT")
    t1 = p.op("vector", lambda e: e.tensor_single_scalar(
        out=row_mean, in_=psS[0:1, 0:T], scalar=1.0 / denom, op=ALU.mult),
        after=[tS])
    p.op("vector", lambda e: e.tensor_tensor(
        out=row_rstd, in0=row_mean, in1=row_mean, op=ALU.mult))
    p.op("vector", lambda e: e.scalar_tensor_tensor(
        out=row_rstd, in0=psQ[0:1, 0:T], scalar=1.0 / denom, in1=row_rstd,
        op0=ALU.mult, op1=ALU.subtract), after=[tQ])
    if row_var is not None:
        p.op("vector", lambda e: e.tensor_copy(out=row_var, in_=row_rstd))
    p.op("vector", lambda e: e.tensor_single_scalar(
        out=row_rstd, in_=row_rstd, scalar=eps, op=ALU.add))
    t4 = p.op("scalar", lambda e: e.activation(
        out=row_rstd, in_=row_rstd, func=AF.Ln),
        after=[p.tok("eV")])
    t5 = p.op("scalar", lambda e: e.activation(
        out=row_rstd, in_=row_rstd, func=AF.Exp, scale=-0.5))
    return t5


def emit_bcast(p, row_ap, ps_out_ap, after):
    return p.op("tensor", lambda e: e.matmul(
        out=ps_out_ap, lhsT=p.ones_row[0:1, :], rhs=row_ap,
        start=True, stop=True), after=after, inc="eT")


def common_consts(p):
    p.ones_col = p.sbuf("ones_col", [128, 1])
    p.ones_row = p.sbuf("ones_row", [1, 128])
    p.op("gpsimd", lambda e: e.memset(p.ones_col[:, :], 1.0))
    t = p.op("gpsimd", lambda e: e.memset(p.ones_row[:, :], 1.0))
    return t


def build_L1a():
    p = Prog()
    T = 512
    NSUB = SQ // T

    xin = p.din("x", [D, SQ + 3])
    wgT = p.din("wgT", [D, H])
    wiT = p.din("wiT", [D, H])
    cw = p.din("cw", [128, 48])
    cb = p.din("cb", [128, 12])
    o_gg = p.dout("ggate", [H, SQ], BF16)
    o_yb = p.dout("ybf", [H, SQ], BF16)

    s_wg = p.sbuf("s_wg", [128, 8, H], BF16)
    s_wi = p.sbuf("s_wi", [128, 8, H], BF16)
    s_cw = p.sbuf("s_cw", [128, 12, 4])
    s_cb = p.sbuf("s_cb", [128, 12])
    tcst = common_consts(p)
    s_x = p.sbuf("s_x", [128, 8, T])
    s_ln = p.sbuf("s_ln", [128, 8, T], BF16)
    s_sq = p.sbuf("s_sq", [128, T])
    s_row = p.sbuf("s_row", [1, 2 * T])
    s_u = p.sbuf("s_u", [128, 12, T + 3])
    SA = p.sbuf("SA", [128, 12, T])
    SB = p.sbuf("SB", [128, 12, T])
    s_gg = p.sbuf("s_gg", [128, 12, T], BF16)
    s_ybf = p.sbuf("s_ybf", [128, 12, T], BF16)
    psS = p.psum("psS", [1, T])
    psQ = p.psum("psQ", [1, T])
    ps_bm = p.psum("ps_bm", [128, T])
    ps_br = p.psum("ps_br", [128, T])
    ps_m0 = p.psum("ps_m0", [128, T])
    ps_m1 = p.psum("ps_m1", [128, T])
    ps_u3 = p.psum("ps_u3", [128, 4])

    twg = p.dma("gpsimd", s_wg[:, :, :], rearr(wgT, "(c p) h -> p c h", p=128), inc="dG")
    twi = p.dma("gpsimd", s_wi[:, :, :], rearr(wiT, "(c p) h -> p c h", p=128), inc="dG")
    tcw = p.dma("sync", s_cw[:, :, :], rearr(cw, "p (c k) -> p c k", k=4), inc="dW")
    tcb = p.dma("sync", s_cb[:, :], cb[:, :], inc="dW")

    def ln_block(width, tx):
        trs = emit_ln_stats(p, lambda i: s_x[:, i, 0:width], 8, width, psS, psQ,
                            s_sq, s_row[0:1, 0:width], s_row[0:1, T:T + width],
                            [tx, tcst], D, EPS, sq2=SA[:, 0, :])
        tb1 = emit_bcast(p, s_row[0:1, 0:width], ps_bm[:, 0:width], [trs])
        tb2 = emit_bcast(p, s_row[0:1, T:T + width], ps_br[:, 0:width], [trs])
        for i in range(8):
            p.op("vector", lambda e, i=i: e.tensor_tensor(
                out=s_sq[:, 0:width], in0=s_x[:, i, 0:width], in1=ps_bm[:, 0:width],
                op=ALU.subtract), after=[tb1, tb2])
            p.op("vector", lambda e, i=i: e.tensor_tensor(
                out=s_ln[:, i, 0:width], in0=s_sq[:, 0:width], in1=ps_br[:, 0:width],
                op=ALU.mult))
        return p.tok("eV")

    tx = p.dma("sync", s_x[:, :, 0:3],
               rearr(xin, "(c p) t -> p c t", p=128)[:, :, 0:3], inc="dL")
    tln = ln_block(3, tx)
    tcp = None
    for m in range(12):
        tmm = mm_group(p, ps_u3[:, 0:3],
                       lambda k, m=m: s_wi[:, k, m * 128:(m + 1) * 128],
                       lambda k: s_ln[:, k, 0:3], 8,
                       after=[tln, twi] + ([tcp] if tcp else []))
        tcp = p.op("vector", lambda e, m=m: e.tensor_copy(
            out=s_u[:, m, 0:3], in_=ps_u3[:, 0:3]), after=[tmm])
    p.barrier()

    for it in range(NSUB):
        o = it * T
        tx = p.dma("sync", s_x[:, :, :],
                   rearr(xin, "(c p) t -> p c t", p=128)[:, :, 3 + o:3 + o + T],
                   inc="dL")
        tln = ln_block(T, tx)

        evac = {}
        for m in range(12):
            pb = ps_m0 if (m % 2 == 0) else ps_m1
            aft = [tln, twg] + ([evac[m - 2]] if m - 2 in evac else [])
            tg = mm_group(p, pb[:, 0:T],
                          lambda k, m=m: s_wg[:, k, m * 128:(m + 1) * 128],
                          lambda k: s_ln[:, k, 0:T], 8, after=aft)
            evac[m] = p.op("scalar", lambda e, m=m, pb=pb: e.activation(
                out=s_gg[:, m, :], in_=pb[:, 0:T], func=AF.Gelu), after=[tg])
        tgg = p.tok("eA")
        p.dma("sync", rearr(o_gg, "(c p) t -> p c t", p=128)[:, :, o:o + T],
              s_gg[:, :, :], after=[tgg], inc="dS")

        evac = {}
        for m in range(12):
            pb = ps_m0 if (m % 2 == 0) else ps_m1
            aft = [tln, twi, tgg] + ([evac[m - 2]] if m - 2 in evac else [])
            tu = mm_group(p, pb[:, 0:T],
                          lambda k, m=m: s_wi[:, k, m * 128:(m + 1) * 128],
                          lambda k: s_ln[:, k, 0:T], 8, after=aft)
            tc = p.op("vector", lambda e, m=m, pb=pb: e.tensor_copy(
                out=s_u[:, m, 3:T + 3], in_=pb[:, 0:T]), after=[tu])
            evac[m] = tc
        tuall = p.tok("eV")

        for m in range(12):
            p.op("vector", lambda e, m=m: e.tensor_scalar(
                out=SA[:, m, :], in0=s_u[:, m, 0:T], scalar1=s_cw[:, m, 0:1],
                scalar2=None, op0=ALU.mult), after=[tuall, tcw])
            for j in (1, 2, 3):
                p.op("vector", lambda e, m=m, j=j: e.scalar_tensor_tensor(
                    out=SA[:, m, :], in0=s_u[:, m, j:j + T],
                    scalar=s_cw[:, m, j:j + 1], in1=SA[:, m, :],
                    op0=ALU.mult, op1=ALU.add))
            p.op("scalar", lambda e, m=m: e.activation(
                out=s_ybf[:, m, :], in_=SA[:, m, :], func=AF.Identity,
                bias=s_cb[:, m:m + 1]), after=[p.tok("eV"), tcb])
            p.op("vector", lambda e, m=m: e.tensor_copy(
                out=s_u[:, m, 0:3], in_=s_u[:, m, T:T + 3]))
        tyb = p.tok("eA")
        p.dma("sync", rearr(o_yb, "(c p) t -> p c t", p=128)[:, :, o:o + T],
              s_ybf[:, :, :], after=[tyb], inc="dS")
        p.barrier()
    return p.finalize()


def build_L1b():
    p = Prog()
    T = 512
    NSUB = SQ // T

    ybin = p.din("ybf", [H, SQ], BF16)
    wliT = p.din("wliT", [H, H])
    wlgT = p.din("wlgT", [H, H])
    lbi = p.din("lbi", [128, 12])
    lbg = p.din("lbg", [128, 12])
    lam = p.din("lam", [128, 12])
    o_al = p.dout("alpha", [H, SQ])
    o_xb = p.dout("xbeta", [H, SQ])
    o_sm = p.dout("summ", [H, 4])

    s_wli = p.sbuf("s_wli", [128, 12, H], BF16)
    s_wlg = p.sbuf("s_wlg", [128, 12, H], BF16)
    s_lbi = p.sbuf("s_lbi", [128, 12])
    s_lbg = p.sbuf("s_lbg", [128, 12])
    s_msp = p.sbuf("s_msp", [128, 12])
    s_t0 = p.sbuf("s_t0", [128, 12])
    s_yb = p.sbuf("s_yb", [128, 12, T], BF16)
    s_ig = p.sbuf("s_ig", [128, 12, T])
    s_rg = p.sbuf("s_rg", [128, 12, T])
    s_al = p.sbuf("s_al", [128, 12, T])
    s_xb = p.sbuf("s_xb", [128, 12, T])
    s_sm = p.sbuf("s_sm", [128, 12, 4])
    ps0 = p.psum("ps0", [128, T])
    ps1 = p.psum("ps1", [128, T])

    twl1 = p.dma("gpsimd", s_wli[:, :, :], rearr(wliT, "(c p) h -> p c h", p=128), inc="dG")
    twl2 = p.dma("gpsimd", s_wlg[:, :, :], rearr(wlgT, "(c p) h -> p c h", p=128), inc="dG")
    tb1 = p.dma("sync", s_lbi[:, :], lbi[:, :], inc="dW")
    tb2 = p.dma("sync", s_lbg[:, :], lbg[:, :], inc="dW")
    tlam = p.dma("sync", s_msp[:, :], lam[:, :], inc="dW")
    tabs = p.op("scalar", lambda e: e.activation(out=s_t0[:, :], in_=s_msp[:, :],
                                                 func=AF.Abs), after=[tlam])
    p.op("scalar", lambda e: e.activation(out=s_t0[:, :], in_=s_t0[:, :],
                                          func=AF.Exp, scale=-1.0))
    ta = p.op("scalar", lambda e: e.activation(out=s_t0[:, :], in_=s_t0[:, :],
                                               func=AF.Ln, bias=1.0))
    p.op("vector", lambda e: e.tensor_single_scalar(
        out=s_msp[:, :], in_=s_msp[:, :], scalar=0.0, op=ALU.max),
        after=[tlam, tabs])
    p.op("vector", lambda e: e.scalar_tensor_tensor(
        out=s_msp[:, :], in0=s_msp[:, :], scalar=1.0, in1=s_t0[:, :],
        op0=ALU.mult, op1=ALU.add), after=[ta])
    tmsp = p.op("vector", lambda e: e.tensor_single_scalar(
        out=s_msp[:, :], in_=s_msp[:, :], scalar=-8.0, op=ALU.mult))
    p.op("vector", lambda e: e.memset(s_sm[:, :, 0:1], 1.0))
    p.op("vector", lambda e: e.memset(s_sm[:, :, 1:2], 0.0))

    for it in range(NSUB):
        o = it * T
        last = (it == NSUB - 1)
        ty = p.dma("sync", s_yb[:, :, :],
                   rearr(ybin, "(c p) t -> p c t", p=128)[:, :, o:o + T], inc="dL")
        pslast = {}
        for (wt, bt, dst, wtok) in ((s_wli, s_lbi, s_ig, twl1),
                                    (s_wlg, s_lbg, s_rg, twl2)):
            for m in range(12):
                pb = ps0 if (m % 2 == 0) else ps1
                aft = [ty, wtok] + ([pslast[id(pb)]] if id(pb) in pslast else [])
                tg = mm_group(p, pb[:, 0:T],
                              lambda k, m=m, wt=wt: wt[:, k, m * 128:(m + 1) * 128],
                              lambda k: s_yb[:, k, :], 12, after=aft)
                pslast[id(pb)] = p.op("scalar", lambda e, m=m, pb=pb, dst=dst, bt=bt:
                               e.activation(out=dst[:, m, :], in_=pb[:, 0:T],
                                            func=AF.Sigmoid, bias=bt[:, m:m + 1]),
                               after=[tg, tb1, tb2])
        tsig = p.tok("eA")
        for m in range(12):
            p.op("scalar", lambda e, m=m: e.activation(
                out=s_al[:, m, :], in_=s_rg[:, m, :], func=AF.Exp,
                scale=s_msp[:, m:m + 1]), after=[tsig, tmsp])
        tal = p.tok("eA")
        p.dma("sync", rearr(o_al, "(c p) t -> p c t", p=128)[:, :, o:o + T],
              s_al[:, :, :], after=[tal], inc="dS")
        for m in range(12):
            p.op("vector", lambda e, m=m: e.tensor_tensor(
                out=s_rg[:, m, :], in0=s_al[:, m, :], in1=s_al[:, m, :],
                op=ALU.mult), after=[tal])
            t1 = p.op("vector", lambda e, m=m: e.tensor_scalar(
                out=s_rg[:, m, :], in0=s_rg[:, m, :], scalar1=-1.0,
                scalar2=1.000001, op0=ALU.mult, op1=ALU.add))
            p.op("scalar", lambda e, m=m: e.activation(
                out=s_rg[:, m, :], in_=s_rg[:, m, :], func=AF.Sqrt), after=[t1])
        tbe = p.tok("eA")
        for m in range(12):
            p.op("vector", lambda e, m=m: e.tensor_tensor(
                out=s_ig[:, m, :], in0=s_ig[:, m, :], in1=s_yb[:, m, :],
                op=ALU.mult), after=[tbe])
            p.op("vector", lambda e, m=m: e.tensor_tensor(
                out=s_xb[:, m, :], in0=s_rg[:, m, :], in1=s_ig[:, m, :],
                op=ALU.mult))
        txb = p.tok("eV")
        p.dma("sync", rearr(o_xb, "(c p) t -> p c t", p=128)[:, :, o:o + T],
              s_xb[:, :, :], after=[txb], inc="dS")
        for m in range(12):
            p.op("vector", lambda e, m=m: e.tensor_tensor_scan(
                out=s_ig[:, m, :], data0=s_al[:, m, :], data1=s_al[:, m, :],
                initial=s_sm[:, m, 0:1], op0=ALU.mult, op1=ALU.bypass),
                after=[txb])
            if last:
                p.op("vector", lambda e, m=m: e.tensor_copy(
                    out=s_sm[:, m, 2:3], in_=s_ig[:, m, T - 2:T - 1]))
            p.op("vector", lambda e, m=m: e.tensor_copy(
                out=s_sm[:, m, 0:1], in_=s_ig[:, m, T - 1:T]))
            p.op("vector", lambda e, m=m: e.tensor_tensor_scan(
                out=s_ig[:, m, :], data0=s_al[:, m, :], data1=s_xb[:, m, :],
                initial=s_sm[:, m, 1:2], op0=ALU.mult, op1=ALU.add))
            if last:
                p.op("vector", lambda e, m=m: e.tensor_copy(
                    out=s_sm[:, m, 3:4], in_=s_ig[:, m, T - 2:T - 1]))
            p.op("vector", lambda e, m=m: e.tensor_copy(
                out=s_sm[:, m, 1:2], in_=s_ig[:, m, T - 1:T]))
        p.barrier()
    tsm = p.tok("eV")
    p.dma("sync", rearr(o_sm, "(c p) k -> p c k", p=128), s_sm[:, :, :],
          after=[tsm], inc="dS")
    return p.finalize()


def build_L2a():
    p = Prog()
    T = 512
    NSUB = SQ // T

    a_in = p.din("alpha", [H, SQ])
    xb_in = p.din("xbeta", [H, SQ])
    gg_in = p.din("ggate", [H, SQ], BF16)
    x_in = p.din("x", [D, SQ])
    pgam = p.din("pgam", [128, 12, 3])
    psnd = p.din("psend", [128, 12, 3])
    woT = p.din("woT", [H, D])
    o_r1 = p.dout("r1", [D, SQ])

    s_wo = p.sbuf("s_wo", [128, 12, D], BF16)
    s_pg = p.sbuf("s_pg", [128, 12, 3])
    s_ps = p.sbuf("s_ps", [128, 12, 3])
    s_cy = p.sbuf("s_cy", [128, 12])
    s_al = p.sbuf("s_al", [128, 12, T])
    s_xb = p.sbuf("s_xb", [128, 12, T])
    s_gg = p.sbuf("s_gg", [128, 2, 12, T], BF16)
    s_h = p.sbuf("s_h", [128, 12, T])
    s_a1 = p.sbuf("s_a1", [128, 2, 12, T], BF16)
    s_x = p.sbuf("s_x", [128, 2, 8, T])
    s_r1 = p.sbuf("s_r1", [128, 8, T])
    ps0 = p.psum("ps0", [128, T])
    ps1 = p.psum("ps1", [128, T])

    two = p.dma("gpsimd", s_wo[:, :, :], rearr(woT, "(c p) d -> p c d", p=128), inc="dG")
    tg1 = p.dma("sync", s_pg[:, :, :], pgam.ap(), inc="dW")
    tg2 = p.dma("sync", s_ps[:, :, :], psnd.ap(), inc="dW")
    tc = p.op("vector", lambda e: e.memset(s_cy[:, :], 0.0), after=[tg1, tg2])
    for j in range(3):
        p.op("vector", lambda e, j=j: e.tensor_tensor(
            out=s_cy[:, :], in0=s_cy[:, :], in1=s_pg[:, :, j], op=ALU.mult))
        p.op("vector", lambda e, j=j: e.tensor_tensor(
            out=s_cy[:, :], in0=s_cy[:, :], in1=s_ps[:, :, j], op=ALU.add))
    tcy = p.tok("eV")

    rd_albx = []
    rd_gx = {}
    rd_r1 = {}
    tld_albx = {}
    tld_gx = {}
    pslast = {}

    def emit_loads(it):
        q = it % 2
        o = it * T
        p.dma("sync", s_al[:, :, :],
              rearr(a_in, "(c p) t -> p c t", p=128)[:, :, o:o + T],
              after=list(rd_albx), inc="dL")
        tld_albx[it] = p.dma("sync", s_xb[:, :, :],
              rearr(xb_in, "(c p) t -> p c t", p=128)[:, :, o:o + T], inc="dL")
        p.dma("sync", s_gg[:, q, :, :],
              rearr(gg_in, "(c p) t -> p c t", p=128)[:, :, o:o + T],
              after=rd_gx.get(q, []), inc="dL")
        tld_gx[it] = p.dma("sync", s_x[:, q, :, :],
              rearr(x_in, "(c p) t -> p c t", p=128)[:, :, o:o + T], inc="dL")

    emit_loads(0)
    for it in range(NSUB):
        q = it % 2
        o = it * T
        tldA = tld_albx.pop(it)
        tldG = tld_gx.pop(it)
        tsc = None
        for m in range(12):
            tsc = p.op("vector", lambda e, m=m: e.tensor_tensor_scan(
                out=s_h[:, m, :], data0=s_al[:, m, :], data1=s_xb[:, m, :],
                initial=s_cy[:, m:m + 1], op0=ALU.mult, op1=ALU.add),
                after=[tldA, tcy])
            p.op("vector", lambda e, m=m: e.tensor_copy(
                out=s_cy[:, m:m + 1], in_=s_h[:, m, T - 1:T]))
            p.op("vector", lambda e, m=m, q=q: e.tensor_tensor(
                out=s_a1[:, q, m, :], in0=s_gg[:, q, m, :], in1=s_h[:, m, :],
                op=ALU.mult), after=[tldG])
        ta1 = p.tok("eV")
        rd_albx[:] = [tsc]
        if it + 1 < NSUB:
            emit_loads(it + 1)
        for m in range(8):
            pb = ps0 if (m % 2 == 0) else ps1
            aft = [ta1, two] + pslast.get(id(pb), [])
            tmm = mm_group(p, pb[:, 0:T],
                           lambda k, m=m, q=q: s_wo[:, k, m * 128:(m + 1) * 128],
                           lambda k, q=q: s_a1[:, q, k, :], 12, after=aft)
            pslast[id(pb)] = [p.op("vector", lambda e, m=m, pb=pb, q=q:
                e.tensor_tensor(out=s_r1[:, m, :], in0=pb[:, 0:T],
                                in1=s_x[:, q, m, :], op=ALU.add),
                after=[tmm] + rd_r1.get(0, []))]
        tr1 = p.tok("eV")
        tst = p.dma("sync", rearr(o_r1, "(c p) t -> p c t", p=128)[:, :, o:o + T],
                    s_r1[:, :, :], after=[tr1], inc="dS")
        rd_r1[0] = [tst]
        rd_gx[q] = [tr1]
    return p.finalize()


def _emit_wquant(p, wT, qdst, nrc, ncol, wsc, c0, c1, tcst, s_stg, s_wt2,
                 psS, s_acc, s_m1, s_sc128, bcP, stgw):
    stages = []
    for c in range(nrc):
        for hcol in range(0, ncol, stgw):
            stages.append((c, hcol, min(stgw, ncol - hcol)))
    tz = p.op("vector", lambda e: e.memset(s_acc[:, :], 0.0))
    rd = {}
    tld_prev = None
    for si, (c, hcol, w) in enumerate(stages):
        half = si % 2
        stg = s_stg[:, half * stgw:half * stgw + w]
        tld = p.dma("sync", stg,
                    rearr(wT, "(c p) f -> p c f", p=128)[:, c, hcol:hcol + w],
                    after=rd.get(half, []), inc="dL")
        tred = p.op("vector", lambda e, stg=stg, half=half: e.tensor_reduce(
            out=s_m1[:, half:half + 1], in_=stg, axis=AX.X, op=ALU.add,
            apply_absolute_value=True), after=[tld, tz])
        rd[half] = [tred]
        tld_prev = p.op("vector", lambda e, half=half: e.tensor_tensor(
            out=s_acc[:, :], in0=s_acc[:, :], in1=s_m1[:, half:half + 1],
            op=ALU.add))
    tmm = p.op("tensor", lambda e: e.matmul(
        out=psS[0:1, 0:1], lhsT=p.ones_col[:, :], rhs=s_acc[:, :],
        start=True, stop=True), after=[tld_prev, tcst], inc="eT")
    n = nrc * 128 * ncol
    p.op("vector", lambda e, n=n, c1=c1: e.tensor_single_scalar(
        out=wsc[0:1, c1:c1 + 1], in_=psS[0:1, 0:1], scalar=1.0 / n,
        op=ALU.mult), after=[tmm])
    tm = p.op("vector", lambda e, c1=c1: e.tensor_single_scalar(
        out=wsc[0:1, c1:c1 + 1], in_=wsc[0:1, c1:c1 + 1], scalar=1e-5,
        op=ALU.max))
    ta = p.op("scalar", lambda e, c0=c0, c1=c1: e.activation(
        out=wsc[0:1, c0:c0 + 1], in_=wsc[0:1, c1:c1 + 1], func=AF.Ln),
        after=[tm])
    ta = p.op("scalar", lambda e, c0=c0: e.activation(
        out=wsc[0:1, c0:c0 + 1], in_=wsc[0:1, c0:c0 + 1], func=AF.Exp,
        scale=-1.0))
    tbc = p.op("tensor", lambda e, c0=c0: e.matmul(
        out=bcP[:, 0:1], lhsT=p.ones_row[0:1, :], rhs=wsc[0:1, c0:c0 + 1],
        start=True, stop=True), after=[ta], inc="eT")
    tsc = p.op("vector", lambda e: e.tensor_copy(
        out=s_sc128[:, :], in_=bcP[:, 0:1]), after=[tbc])
    tld_prev = None
    for si, (c, hcol, w) in enumerate(stages):
        half = si % 2
        stg = s_stg[:, half * stgw:half * stgw + w]
        wt2 = s_wt2[:, half * stgw:half * stgw + w]
        tld = p.dma("sync", stg,
                    rearr(wT, "(c p) f -> p c f", p=128)[:, c, hcol:hcol + w],
                    after=rd.get(half, []), inc="dL")
        p.op("vector", lambda e, stg=stg: e.tensor_scalar(
            out=stg, in0=stg, scalar1=s_sc128[:, 0:1],
            scalar2=None, op0=ALU.mult), after=[tld, tsc])
        p.op("scalar", lambda e, stg=stg, wt2=wt2: e.activation(
            out=wt2, in_=stg, func=AF.Abs),
            after=[p.tok("eV")])
        p.op("vector", lambda e, wt2=wt2: e.tensor_single_scalar(
            out=wt2, in_=wt2, scalar=0.5, op=ALU.is_ge),
            after=[p.tok("eA")])
        ts = p.op("scalar", lambda e, stg=stg: e.activation(
            out=stg, in_=stg, func=AF.Sign),
            after=[p.tok("eV")])
        tld_prev = p.op("vector", lambda e, c=c, hcol=hcol, w=w, qdst=qdst,
                        stg=stg, wt2=wt2:
                        e.tensor_tensor(out=qdst[:, c, hcol:hcol + w],
                                        in0=stg, in1=wt2,
                                        op=ALU.mult), after=[ts])
        rd[half] = [tld_prev]
    return tld_prev


def _emit_round_clip(p, s_v, s_int, s_cf, s_fg, dst_ap, lo, hi, T, after):
    t1 = p.op("vector", lambda e: e.tensor_copy(out=s_int[:, 0:T], in_=s_v[:, 0:T]),
              after=after)
    t2 = p.op("vector", lambda e: e.tensor_copy(out=s_cf[:, 0:T], in_=s_int[:, 0:T]))
    t3 = p.op("vector", lambda e: e.scalar_tensor_tensor(
        out=s_v[:, 0:T], in0=s_cf[:, 0:T], scalar=-1.0, in1=s_v[:, 0:T],
        op0=ALU.mult, op1=ALU.add), after=[t2])
    t45 = p.op("vector", lambda e: e.scalar_tensor_tensor(
        out=s_cf[:, 0:T], in0=s_v[:, 0:T], scalar=0.5, in1=s_cf[:, 0:T],
        op0=ALU.is_gt, op1=ALU.add), after=[t3])
    t6 = p.op("vector", lambda e: e.tensor_single_scalar(
        out=s_fg[:, 0:T], in_=s_v[:, 0:T], scalar=-0.5, op=ALU.is_lt),
        after=[t45])
    t7 = p.op("vector", lambda e: e.tensor_tensor(
        out=s_cf[:, 0:T], in0=s_cf[:, 0:T], in1=s_fg[:, 0:T], op=ALU.subtract),
        after=[t6])
    return p.op("vector", lambda e: e.tensor_scalar(
        out=dst_ap, in0=s_cf[:, 0:T], scalar1=float(hi), scalar2=float(lo),
        op0=ALU.min, op1=ALU.max))


def build_L2b1():
    p = Prog()
    T = 512
    NSUB = SQ // T

    r1_in = p.din("r1", [D, SQ])
    w1T = p.din("w1T", [D, FF])
    b1 = p.din("b1", [128, 32])
    ident = p.din("ident", [128, 128])
    o_h1 = p.dout("h1", [FF, SQ], BF16)

    s_w1q = p.sbuf("s_w1q", [128, 8, FF], BF16)
    s_stg = p.sbuf("s_stg", [128, 2048])
    s_wt2 = p.sbuf("s_wt2", [128, 2048])
    s_int = p.sbuf("s_int", [128, T], I32)
    s_id = p.sbuf("s_id", [128, 128])
    s_acc = p.sbuf("s_acc", [128, 1])
    s_wsc = p.sbuf("s_wsc", [1, 8])
    s_sc128 = p.sbuf("s_sc128", [128, 1])
    s_b1 = p.sbuf("s_b1", [128, 32])
    s_r1 = p.sbuf("s_r1", [128, 8, T])
    s_xm = p.sbuf("s_xm", [128, 8, T])
    s_q1 = p.sbuf("s_q1", [128, 8, T], BF16)
    s_h1 = p.sbuf("s_h1", [128, 32, T], BF16)
    s_m1 = p.sbuf("s_m1", [128, T])
    s_sq = p.sbuf("s_sq", [128, T])
    s_v = p.sbuf("s_v", [128, T])
    s_cf = p.sbuf("s_cf", [128, T])
    s_fx = p.sbuf("s_fx", [128, T])
    s_fg = p.sbuf("s_fg", [128, T])
    s_mxc = p.sbuf("s_mxc", [128, 1])
    s_bm = p.sbuf("s_bm", [128, T])
    s_bc = p.sbuf("s_bc", [128, T])
    s_be = p.sbuf("s_be", [128, T])
    ROW = p.sbuf("ROWS", [1, 10 * T])
    psS = p.psum("psS", [1, T])
    psQ = p.psum("psQ", [1, T])
    bcP = p.psum("bcP", [128, T])
    ps0 = p.psum("ps0", [128, T])
    ps1 = p.psum("ps1", [128, T])
    psT = p.psum("psT", [128, 128])

    r_mean = ROW[0:1, 0 * T:1 * T]
    r_rstd = ROW[0:1, 1 * T:2 * T]
    r_var = ROW[0:1, 2 * T:3 * T]
    r_r2a = ROW[0:1, 3 * T:4 * T]
    r_mx = ROW[0:1, 4 * T:4 * T + 128]
    r_mq = ROW[0:1, 5 * T:6 * T]
    r_sx = ROW[0:1, 6 * T:7 * T]
    r_esc = ROW[0:1, 7 * T:8 * T]
    r_csc = ROW[0:1, 8 * T:9 * T]

    tcst = common_consts(p)
    tid = p.dma("sync", s_id[:, :], ident.ap(), inc="dW")
    tb1 = p.dma("sync", s_b1[:, :], b1.ap(), inc="dW")
    twsc0 = p.op("vector", lambda e: e.memset(s_wsc[:, :], 0.0))
    tq = _emit_wquant(p, w1T, s_w1q, 8, FF, s_wsc, 0, 1, tcst, s_stg, s_wt2,
                      psS, s_acc, s_m1, s_sc128, bcP, 1024)
    p.barrier()

    for it in range(NSUB):
        o = it * T
        tld = p.dma("sync", s_r1[:, :, :],
                    rearr(r1_in, "(c p) t -> p c t", p=128)[:, :, o:o + T],
                    inc="dL")
        trs = emit_ln_stats(p, lambda i: s_r1[:, i, :], 8, T, psS, psQ, s_sq,
                            r_mean, r_rstd, [tld, tcst], D, EPS, row_var=r_var,
                            sq2=s_v)
        p.op("vector", lambda e: e.tensor_tensor(
            out=r_r2a, in0=r_var, in1=r_rstd, op=ALU.mult), after=[trs])
        p.op("vector", lambda e: e.tensor_tensor(
            out=r_r2a, in0=r_r2a, in1=r_rstd, op=ALU.mult))
        t1 = p.op("vector", lambda e: e.tensor_single_scalar(
            out=r_r2a, in_=r_r2a, scalar=EPS, op=ALU.add))
        p.op("scalar", lambda e: e.activation(out=r_r2a, in_=r_r2a, func=AF.Ln),
             after=[t1])
        tr2 = p.op("scalar", lambda e: e.activation(out=r_r2a, in_=r_r2a,
                                                    func=AF.Exp, scale=-0.5))
        tbc = emit_bcast(p, r_mean, bcP[:, 0:T], [trs])
        tbm = p.op("vector", lambda e: e.tensor_copy(out=s_bm[:, 0:T],
                                                     in_=bcP[:, 0:T]), after=[tbc])
        for i in range(8):
            p.op("vector", lambda e, i=i: e.tensor_tensor(
                out=s_xm[:, i, :], in0=s_r1[:, i, :], in1=s_bm[:, 0:T],
                op=ALU.subtract), after=[tbm])
        txm = p.tok("eV")
        ta0 = p.op("scalar", lambda e: e.activation(
            out=s_m1[:, 0:T], in_=s_xm[:, 0, :], func=AF.Abs), after=[txm])
        mx_by = {}
        for i in range(1, 8):
            buf = s_fx if i % 2 == 0 else s_cf
            tai = p.op("scalar", lambda e, i=i, buf=buf: e.activation(
                out=buf[:, 0:T], in_=s_xm[:, i, :], func=AF.Abs),
                after=[txm] + ([mx_by[id(buf)]] if id(buf) in mx_by else []))
            mx_by[id(buf)] = p.op("vector", lambda e, buf=buf: e.tensor_tensor(
                out=s_m1[:, 0:T], in0=s_m1[:, 0:T], in1=buf[:, 0:T],
                op=ALU.max), after=[tai, ta0])
        tmx = p.tok("eV")
        tmxr = None
        for tb_ in range(T // 128):
            ttr = p.op("tensor", lambda e, tb_=tb_: e.transpose(
                out=psT[:, :], in_=s_m1[:, tb_ * 128:(tb_ + 1) * 128],
                identity=s_id[:, :]),
                after=[tmx, tid] + ([tmxr] if tmxr else []), inc="eT")
            trd = p.op("vector", lambda e: e.tensor_reduce(
                out=s_mxc[:, 0:1], in_=psT[:, :], axis=AX.X, op=ALU.max),
                after=[ttr])
            ttr2 = p.op("tensor", lambda e: e.transpose(
                out=psT[0:1, 0:128], in_=s_mxc[:, 0:1], identity=s_id[:, :]),
                after=[trd], inc="eT")
            tmxr = p.op("vector", lambda e, tb_=tb_: e.tensor_copy(
                out=ROW[0:1, 5 * T + tb_ * 128:5 * T + (tb_ + 1) * 128],
                in_=psT[0:1, 0:128]), after=[ttr2])
        p.op("vector", lambda e: e.tensor_tensor(
            out=r_mq, in0=r_mq, in1=r_rstd, op=ALU.mult), after=[tmxr, tr2])
        p.op("vector", lambda e: e.tensor_tensor(
            out=r_mq, in0=r_mq, in1=r_r2a, op=ALU.mult))
        tq1 = p.op("vector", lambda e: e.tensor_single_scalar(
            out=r_mq, in_=r_mq, scalar=1e-5, op=ALU.max))
        p.op("scalar", lambda e: e.activation(out=r_sx, in_=r_mq, func=AF.Ln),
             after=[tq1])
        ta = p.op("scalar", lambda e: e.activation(out=r_sx, in_=r_sx,
                                                   func=AF.Exp, scale=-1.0))
        p.op("vector", lambda e: e.tensor_single_scalar(
            out=r_sx, in_=r_sx, scalar=127.0, op=ALU.mult), after=[ta])
        p.op("vector", lambda e: e.tensor_single_scalar(
            out=r_esc, in_=r_mq, scalar=1.0 / 127.0, op=ALU.mult))
        p.op("vector", lambda e: e.tensor_scalar(
            out=r_esc, in0=r_esc, scalar1=s_wsc[0:1, 1:2], scalar2=None,
            op0=ALU.mult))
        p.op("vector", lambda e: e.tensor_tensor(
            out=r_csc, in0=r_rstd, in1=r_r2a, op=ALU.mult))
        tcsc = p.op("vector", lambda e: e.tensor_tensor(
            out=r_csc, in0=r_csc, in1=r_sx, op=ALU.mult))
        tbc1 = emit_bcast(p, r_csc, bcP[:, 0:T], [tcsc, tbm])
        tb_c = p.op("vector", lambda e: e.tensor_copy(out=s_bc[:, 0:T],
                                                      in_=bcP[:, 0:T]), after=[tbc1])
        tbc2 = emit_bcast(p, r_esc, bcP[:, 0:T], [tb_c])
        tb_e = p.op("vector", lambda e: e.tensor_copy(out=s_be[:, 0:T],
                                                      in_=bcP[:, 0:T]), after=[tbc2])
        for i in range(8):
            tv = p.op("vector", lambda e, i=i: e.tensor_tensor(
                out=s_v[:, 0:T], in0=s_xm[:, i, :], in1=s_bc[:, 0:T],
                op=ALU.mult), after=[tb_c])
            _emit_round_clip(p, s_v, s_int, s_cf, s_fg, s_q1[:, i, :],
                             -128.0, 127.0, T, [tv])
        tq1b = p.tok("eV")
        pslast = {}
        for m in range(32):
            pb = ps0 if (m % 2 == 0) else ps1
            aft = [tq1b, tq] + ([pslast[id(pb)]] if id(pb) in pslast else [])
            tmm = mm_group(p, pb[:, 0:T],
                           lambda k, m=m: s_w1q[:, k, m * 128:(m + 1) * 128],
                           lambda k: s_q1[:, k, :], 8, after=aft)
            tt = p.op("vector", lambda e, m=m, pb=pb: e.tensor_tensor(
                out=s_sq[:, 0:T], in0=pb[:, 0:T], in1=s_be[:, 0:T],
                op=ALU.mult), after=[tmm, tb_e, ("eA", p.cnt["eA"])])
            pslast[id(pb)] = tt
            p.op("scalar", lambda e, m=m: e.activation(
                out=s_h1[:, m, :], in_=s_sq[:, 0:T], func=AF.Gelu,
                bias=s_b1[:, m:m + 1]), after=[tt, tb1])
        th1 = p.tok("eA")
        p.dma("sync", rearr(o_h1, "(c p) t -> p c t", p=128)[:, :, o:o + T],
              s_h1[:, :, :], after=[th1], inc="dS")
        p.barrier()
    return p.finalize()


def build_L2b2():
    p = Prog()
    T = 512
    NSUB = SQ // T

    h1_in = p.din("h1", [FF, SQ], BF16)
    r1_in = p.din("r1", [D, SQ])
    w2T = p.din("w2T", [FF, D])
    b2 = p.din("b2", [128, 8])
    ident = p.din("ident", [128, 128])
    o_r2 = p.dout("r2", [D, SQ])

    s_w2q = p.sbuf("s_w2q", [128, 32, D], BF16)
    s_stg = p.sbuf("s_stg", [128, 1024])
    s_wt2 = p.sbuf("s_wt2", [128, 1024])
    s_int = p.sbuf("s_int", [128, T], I32)
    s_id = p.sbuf("s_id", [128, 128])
    s_acc = p.sbuf("s_acc", [128, 1])
    s_wsc = p.sbuf("s_wsc", [1, 8])
    s_sc128 = p.sbuf("s_sc128", [128, 1])
    s_b2 = p.sbuf("s_b2", [128, 8])
    s_h1 = p.sbuf("s_h1", [128, 32, T], BF16)
    s_q2 = p.sbuf("s_q2", [128, 48, T], BF16)

    def q2slot(q, i):
        return q * 16 + i if i < 16 else 16 + i
    s_r1 = p.sbuf("s_r1", [128, 8, T])
    s_r2 = p.sbuf("s_r2", [128, 2, T])
    s_m1 = p.sbuf("s_m1", [128, T])
    s_v = p.sbuf("s_v", [128, T])
    s_cf = p.sbuf("s_cf", [128, T])
    s_fg = p.sbuf("s_fg", [128, T])
    s_mxc = p.sbuf("s_mxc", [128, 1])
    s_bc = p.sbuf("s_bc", [128, 2, T])
    s_be = p.sbuf("s_be", [128, 2, T])
    ROW = p.sbuf("ROWS", [1, 5 * T])
    psS = p.psum("psS", [1, T])
    psQ = p.psum("psQ", [1, T])
    bcP = p.psum("bcP", [128, T])
    ps0 = p.psum("ps0", [128, T])
    ps1 = p.psum("ps1", [128, T])
    psT = p.psum("psT", [128, 128])

    r_r2a = ROW[0:1, 0 * T:1 * T]
    r_mq = ROW[0:1, 1 * T:2 * T]
    r_sx = ROW[0:1, 2 * T:3 * T]
    r_esc = ROW[0:1, 3 * T:4 * T]
    r_csc = ROW[0:1, 4 * T:5 * T]

    tcst = common_consts(p)
    tid = p.dma("sync", s_id[:, :], ident.ap(), inc="dW")
    tb2 = p.dma("sync", s_b2[:, :], b2.ap(), inc="dW")
    twsc0 = p.op("vector", lambda e: e.memset(s_wsc[:, :], 0.0))
    tq = _emit_wquant(p, w2T, s_w2q, 32, D, s_wsc, 2, 3, tcst, s_stg, s_wt2,
                      psS, s_acc, s_m1, s_sc128, bcP, 512)
    p.barrier()

    abuf = s_stg[:, T:2 * T]
    abuf2 = s_stg[:, 0:T]
    sq_a = s_wt2[:, 0:T]
    sq_b = s_wt2[:, T:2 * T]

    def emit_stats(it, h1_free, t1_prev, trans_last):
        q = it % 2
        o = it * T
        tlds = []
        for i in range(32):
            tlds.append(p.dma("sync", s_h1[:, i, :],
                rearr(h1_in, "(c p) t -> p c t", p=128)[:, i, o:o + T],
                after=([h1_free[i]] if h1_free else []), inc="dL"))
        mm_by = {}
        tQ = None
        for i in range(32):
            buf = sq_a if i % 2 == 0 else sq_b
            aft = [tlds[i], tcst, tq] + ([mm_by[id(buf)]] if id(buf) in mm_by else [])
            tqq = p.op("scalar", lambda e, i=i, buf=buf: e.activation(
                out=buf, in_=s_h1[:, i, :], func=AF.Square), after=aft)
            maft = [tqq] + ([t1_prev] if (i == 0 and t1_prev) else [])
            tQ = p.op("tensor", lambda e, i=i, buf=buf: e.matmul(
                out=psQ[0:1, 0:T], lhsT=p.ones_col[:, :], rhs=buf,
                start=(i == 0), stop=(i == 31)), after=maft, inc="eT")
            mm_by[id(buf)] = tQ
        ta0 = p.op("scalar", lambda e: e.activation(
            out=s_m1[:, 0:T], in_=s_h1[:, 0, :], func=AF.Abs),
            after=[tlds[0], tq] + ([trans_last] if trans_last else []))
        mx_by = {}
        tmx = None
        for i in range(1, 32):
            buf = abuf if i % 2 == 0 else abuf2
            tai = p.op("scalar", lambda e, i=i, buf=buf: e.activation(
                out=buf, in_=s_h1[:, i, :], func=AF.Abs),
                after=[tlds[i]] + mx_by.get(id(buf), []))
            tmx = p.op("vector", lambda e, buf=buf: e.tensor_tensor(
                out=s_m1[:, 0:T], in0=s_m1[:, 0:T], in1=buf,
                op=ALU.max), after=[tai, ta0])
            mx_by[id(buf)] = [tmx]
        t1a = p.op("vector", lambda e: e.tensor_single_scalar(
            out=r_r2a, in_=psQ[0:1, 0:T], scalar=1.0 / FF, op=ALU.mult),
            after=[tQ])
        t1b = p.op("vector", lambda e: e.tensor_single_scalar(
            out=r_r2a, in_=r_r2a, scalar=EPS, op=ALU.add))
        p.op("scalar", lambda e: e.activation(out=r_r2a, in_=r_r2a, func=AF.Ln),
             after=[t1b])
        trh = p.op("scalar", lambda e: e.activation(out=r_r2a, in_=r_r2a,
                                                    func=AF.Exp, scale=-0.5))
        tmxr = None
        ttr_last = None
        for tb_ in range(T // 128):
            ttr = p.op("tensor", lambda e, tb_=tb_: e.transpose(
                out=psT[:, :], in_=s_m1[:, tb_ * 128:(tb_ + 1) * 128],
                identity=s_id[:, :]),
                after=[tmx, tid] + ([tmxr] if tmxr else []), inc="eT")
            trd = p.op("vector", lambda e: e.tensor_reduce(
                out=s_mxc[:, 0:1], in_=psT[:, :], axis=AX.X, op=ALU.max),
                after=[ttr])
            ttr2 = p.op("tensor", lambda e: e.transpose(
                out=psT[0:1, 0:128], in_=s_mxc[:, 0:1], identity=s_id[:, :]),
                after=[trd], inc="eT")
            ttr_last = ttr2
            tmxr = p.op("vector", lambda e, tb_=tb_: e.tensor_copy(
                out=ROW[0:1, 1 * T + tb_ * 128:1 * T + (tb_ + 1) * 128],
                in_=psT[0:1, 0:128]), after=[ttr2])
        p.op("vector", lambda e: e.tensor_tensor(
            out=r_mq, in0=r_mq, in1=r_r2a, op=ALU.mult), after=[tmxr, trh])
        tq1_ = p.op("vector", lambda e: e.tensor_single_scalar(
            out=r_mq, in_=r_mq, scalar=1e-5, op=ALU.max))
        p.op("scalar", lambda e: e.activation(out=r_sx, in_=r_mq, func=AF.Ln),
             after=[tq1_])
        ta = p.op("scalar", lambda e: e.activation(out=r_sx, in_=r_sx,
                                                   func=AF.Exp, scale=-1.0))
        p.op("vector", lambda e: e.tensor_single_scalar(
            out=r_sx, in_=r_sx, scalar=127.0, op=ALU.mult), after=[ta])
        p.op("vector", lambda e: e.tensor_single_scalar(
            out=r_esc, in_=r_mq, scalar=1.0 / 127.0, op=ALU.mult))
        p.op("vector", lambda e: e.tensor_scalar(
            out=r_esc, in0=r_esc, scalar1=s_wsc[0:1, 3:4], scalar2=None,
            op0=ALU.mult))
        tcsc = p.op("vector", lambda e: e.tensor_tensor(
            out=r_csc, in0=r_r2a, in1=r_sx, op=ALU.mult))
        tbc1 = emit_bcast(p, r_csc, bcP[:, 0:T], [tcsc])
        tb_c = p.op("vector", lambda e, q=q: e.tensor_copy(
            out=s_bc[:, q, :], in_=bcP[:, 0:T]), after=[tbc1])
        tbc2 = emit_bcast(p, r_esc, bcP[:, 0:T], [tb_c])
        tb_e = p.op("vector", lambda e, q=q: e.tensor_copy(
            out=s_be[:, q, :], in_=bcP[:, 0:T]), after=[tbc2])
        return {"tb_c": tb_c, "tb_e": tb_e, "t1": t1a, "ttr_last": ttr_last}

    tld_r1 = p.dma("sync", s_r1[:, :, :],
                   rearr(r1_in, "(c p) t -> p c t", p=128)[:, :, 0:T], inc="dL")
    sres = emit_stats(0, None, None, None)
    q2_free = {}
    r2st = {}
    for it in range(NSUB):
        q = it % 2
        o = it * T
        tvs = []
        trc = None
        for i in range(32):
            gate = q2_free.get(q) if i < 16 else q2_free.get("last")
            aft = [sres["tb_c"]] + ([gate] if gate else [])
            tv = p.op("vector", lambda e, i=i, q=q: e.tensor_tensor(
                out=s_v[:, 0:T], in0=s_h1[:, i, :], in1=s_bc[:, q, :],
                op=ALU.mult), after=aft)
            tvs.append(tv)
            trc = _emit_round_clip(p, s_v, s_int, s_cf, s_fg,
                                   s_q2[:, q2slot(q, i), :], -128.0, 127.0,
                                   T, [tv])
        tq2b = trc
        sres_next = None
        if it + 1 < NSUB:
            sres_next = emit_stats(it + 1, tvs, sres["t1"], sres["ttr_last"])
        pslast = {}
        tmm = None
        adds_g = None
        for m in range(8):
            pb = ps0 if (m % 2 == 0) else ps1
            h = m % 2
            aft = [tq2b, tq] + pslast.get(id(pb), [])
            tmm = mm_group(p, pb[:, 0:T],
                           lambda k, m=m: s_w2q[:, k, m * 128:(m + 1) * 128],
                           lambda k, q=q: s_q2[:, q2slot(q, k), :], 32,
                           after=aft)
            tcp = p.op("scalar", lambda e, pb=pb, h=h: e.copy(
                out=s_r2[:, h, :], in_=pb[:, 0:T]),
                after=[tmm] + r2st.get(h, []))
            pslast[id(pb)] = [tcp]
            tt = p.op("gpsimd", lambda e, q=q, h=h: e.tensor_tensor(
                out=s_r2[:, h, :], in0=s_r2[:, h, :], in1=s_be[:, q, :],
                op=ALU.mult), after=[tcp, sres["tb_e"]])
            tb_ = p.op("scalar", lambda e, m=m, h=h: e.activation(
                out=s_r2[:, h, :], in_=s_r2[:, h, :], func=AF.Identity,
                bias=s_b2[:, m:m + 1]), after=[tt, tb2])
            adds_g = p.op("gpsimd", lambda e, m=m, h=h: e.tensor_tensor(
                out=s_r2[:, h, :], in0=s_r2[:, h, :], in1=s_r1[:, m, :],
                op=ALU.add), after=[tb_, tld_r1])
            r2st[h] = [p.dma("sync",
                rearr(o_r2, "(c p) t -> p c t", p=128)[:, m, o:o + T],
                s_r2[:, h, :], after=[adds_g], inc="dS")]
        q2_free[q] = tmm
        q2_free["last"] = tmm
        if it + 1 < NSUB:
            tld_r1 = p.dma("sync", s_r1[:, :, :],
                           rearr(r1_in, "(c p) t -> p c t", p=128)
                           [:, :, o + T:o + 2 * T],
                           after=[adds_g], inc="dL")
        sres = sres_next
    return p.finalize()


def build_L3():
    p = Prog()
    T = 512
    NSUB = SQ // T

    r2h = p.din("r2h", [D, SQ + 3])
    ipjT = p.din("ipjT", [D, 2 * D])
    ipb = p.din("ipb", [128, 16])
    cw1 = p.din("cw1", [128, 32])
    cb1 = p.din("cb1", [128, 8])
    bcwT = p.din("bcwT", [D, 64])
    bcb = p.din("bcb", [64, 1])
    dtwT = p.din("dtwT", [D, D])
    dtb = p.din("dtb", [128, 8])
    hm = p.din("hmask", [128, 3])
    o_dt = p.dout("dt", [D, SQ])
    o_dx = p.dout("dtxs", [D, SQ])
    o_bc = p.dout("bc", [64, SQ])
    o_sz = p.dout("siluz", [D, SQ], BF16)

    s_ipj = p.sbuf("s_ipj", [128, 8, 2 * D], BF16)
    s_dtw = p.sbuf("s_dtw", [128, 8, D], BF16)
    s_bcw = p.sbuf("s_bcw", [128, 8, 64], BF16)
    s_ipb = p.sbuf("s_ipb", [128, 16])
    s_cw = p.sbuf("s_cw", [128, 8, 4])
    s_cb = p.sbuf("s_cb", [128, 8])
    s_dtb = p.sbuf("s_dtb", [128, 8])
    s_bcb = p.sbuf("s_bcb", [64, 1])
    s_hm = p.sbuf("s_hm", [128, 3])
    tcst = common_consts(p)
    s_x = p.sbuf("s_x", [128, 8, T])
    s_ln = p.sbuf("s_ln", [128, 8, T], BF16)
    s_sq = p.sbuf("s_sq", [128, T])
    s_row = p.sbuf("s_row2", [1, 2 * T])
    s_xc = p.sbuf("s_xc", [128, 8, T + 3])
    s_z = p.sbuf("s_z", [128, 8, T], BF16)
    s_xs = p.sbuf("s_xs", [128, 8, T])
    s_xsb = p.sbuf("s_xsb", [128, 8, T], BF16)
    s_dt = p.sbuf("s_dt", [128, 8, T])
    s_dx = p.sbuf("s_dx", [128, 8, T])
    s_sp = p.sbuf("s_sp", [128, T])
    s_bc64 = p.sbuf("s_bc64", [128, T])
    psS = p.psum("psS", [1, T])
    psQ = p.psum("psQ", [1, T])
    ps_bm = p.psum("ps_bm", [128, T])
    ps_br = p.psum("ps_br", [128, T])
    ps0 = p.psum("ps0", [128, T])
    ps1 = p.psum("ps1", [128, T])

    tw1 = p.dma("gpsimd", s_ipj[:, :, :], rearr(ipjT, "(c p) f -> p c f", p=128), inc="dG")
    tw2 = p.dma("gpsimd", s_dtw[:, :, :], rearr(dtwT, "(c p) f -> p c f", p=128), inc="dG")
    tw3 = p.dma("gpsimd", s_bcw[:, :, :], rearr(bcwT, "(c p) f -> p c f", p=128), inc="dG")
    tv = p.dma("sync", s_ipb[:, :], ipb.ap(), inc="dW")
    tv = p.dma("sync", s_cw[:, :, :], rearr(cw1, "p (c k) -> p c k", k=4), inc="dW")
    tv = p.dma("sync", s_cb[:, :], cb1.ap(), inc="dW")
    tv = p.dma("sync", s_dtb[:, :], dtb.ap(), inc="dW")
    tv = p.dma("sync", s_bcb[:, :], bcb.ap(), inc="dW")
    tv = p.dma("sync", s_hm[:, :], hm.ap(), inc="dW")
    twv = p.tok("dW")

    def ln_block(width, tx):
        trs = emit_ln_stats(p, lambda i: s_x[:, i, 0:width], 8, width, psS, psQ,
                            s_sq, s_row[0:1, 0:width], s_row[0:1, T:T + width],
                            [tx, tcst], D, EPS, sq2=s_sp)
        tb1 = emit_bcast(p, s_row[0:1, 0:width], ps_bm[:, 0:width], [trs])
        tb2 = emit_bcast(p, s_row[0:1, T:T + width], ps_br[:, 0:width], [trs])
        for i in range(8):
            p.op("vector", lambda e, i=i, width=width: e.tensor_tensor(
                out=s_sq[:, 0:width], in0=s_x[:, i, 0:width], in1=ps_bm[:, 0:width],
                op=ALU.subtract), after=[tb1, tb2])
            p.op("vector", lambda e, i=i, width=width: e.tensor_tensor(
                out=s_ln[:, i, 0:width], in0=s_sq[:, 0:width], in1=ps_br[:, 0:width],
                op=ALU.mult))
        return p.tok("eV")

    def inproj(width, tln, z_dst_w):
        pslast = {}
        for m in range(16):
            pb = ps0 if (m % 2 == 0) else ps1
            aft = [tln, tw1, twv] + ([pslast[id(pb)]] if id(pb) in pslast else [])
            tmm = mm_group(p, pb[:, 0:width],
                           lambda k, m=m: s_ipj[:, k, m * 128:(m + 1) * 128],
                           lambda k, width=width: s_ln[:, k, 0:width], 8, after=aft)
            if m < 8:
                pslast[id(pb)] = p.op("scalar", lambda e, m=m, pb=pb, width=width:
                                      e.activation(out=s_xc[:, m, 3:3 + width],
                                                   in_=pb[:, 0:width],
                                                   func=AF.Identity,
                                                   bias=s_ipb[:, m:m + 1]),
                                      after=[tmm])
            else:
                pslast[id(pb)] = p.op("scalar", lambda e, m=m, pb=pb, width=width:
                                      e.activation(out=s_z[:, m - 8, 0:width],
                                                   in_=pb[:, 0:width], func=AF.Silu,
                                                   bias=s_ipb[:, 8 + (m - 8):9 + (m - 8)]),
                                      after=[tmm])
        return p.tok("eA")

    tx = p.dma("sync", s_x[:, :, 0:3],
               rearr(r2h, "(c p) t -> p c t", p=128)[:, :, 0:3], inc="dL")
    tln = ln_block(3, tx)
    tz = inproj(3, tln, 0)
    for i in range(8):
        p.op("vector", lambda e, i=i: e.tensor_tensor(
            out=s_xc[:, i, 3:5], in0=s_xc[:, i, 3:5], in1=s_hm[:, 0:2],
            op=ALU.mult), after=[tz])
    for i in range(8):
        p.op("vector", lambda e, i=i: e.tensor_copy(
            out=s_xc[:, i, 0:3], in_=s_xc[:, i, 3:6]))
    tcar = p.tok("eV")
    p.dma("sync", rearr(o_sz, "(c p) t -> p c t", p=128)[:, :, 0:1],
          s_z[:, :, 2:3], after=[tz], inc="dS", slow_ok=True)
    p.barrier()

    for it in range(NSUB):
        o = it * T
        last = (it == NSUB - 1)
        tx = p.dma("sync", s_x[:, :, :],
                   rearr(r2h, "(c p) t -> p c t", p=128)[:, :, o + 3:o + T + 3],
                   inc="dL")
        tln = ln_block(T, tx)
        tz = inproj(T, tln, T)
        if last:
            for i in range(8):
                p.op("vector", lambda e, i=i: e.tensor_tensor(
                    out=s_xc[:, i, T + 2:T + 3], in0=s_xc[:, i, T + 2:T + 3],
                    in1=s_hm[:, 2:3], op=ALU.mult), after=[tz])
        tmask = p.tok("eV")
        w = T if not last else T - 1
        p.dma("sync", rearr(o_sz, "(c p) t -> p c t", p=128)[:, :, o + 1:o + 1 + w],
              s_z[:, :, 0:w], after=[tz], inc="dS")
        for i in range(8):
            p.op("vector", lambda e, i=i: e.tensor_scalar(
                out=s_xs[:, i, :], in0=s_xc[:, i, 0:T], scalar1=s_cw[:, i, 0:1],
                scalar2=None, op0=ALU.mult), after=[tz, tmask, twv])
            for j in (1, 2, 3):
                p.op("vector", lambda e, i=i, j=j: e.scalar_tensor_tensor(
                    out=s_xs[:, i, :], in0=s_xc[:, i, j:j + T],
                    scalar=s_cw[:, i, j:j + 1], in1=s_xs[:, i, :],
                    op0=ALU.mult, op1=ALU.add))
            p.op("scalar", lambda e, i=i: e.activation(
                out=s_xs[:, i, :], in_=s_xs[:, i, :], func=AF.Silu,
                bias=s_cb[:, i:i + 1]), after=[p.tok("eV")])
            p.op("vector", lambda e, i=i: e.tensor_copy(
                out=s_xsb[:, i, :], in_=s_xs[:, i, :]), after=[p.tok("eA")])
            p.op("vector", lambda e, i=i: e.tensor_copy(
                out=s_xc[:, i, 0:3], in_=s_xc[:, i, T:T + 3]))
        txs = p.tok("eV")
        tmm = mm_group(p, ps0[0:64, 0:T],
                       lambda k: s_bcw[:, k, :],
                       lambda k: s_xsb[:, k, :], 8, after=[txs, tw3])
        tbc_ = p.op("scalar", lambda e: e.activation(
            out=s_bc64[0:64, 0:T], in_=ps0[0:64, 0:T], func=AF.Identity,
            bias=s_bcb[0:64, 0:1]), after=[tmm])
        p.dma("sync", o_bc[:, o:o + T], s_bc64[0:64, 0:T],
              after=[tbc_], inc="dS")
        pslast = {id(ps0): tbc_}
        tsp_rd = None
        for m in range(8):
            pb = ps1 if (m % 2 == 0) else ps0
            aft = [txs, tw2] + ([pslast[id(pb)]] if id(pb) in pslast else [])
            tmm = mm_group(p, pb[:, 0:T],
                           lambda k, m=m: s_dtw[:, k, m * 128:(m + 1) * 128],
                           lambda k: s_xsb[:, k, :], 8, after=aft)
            ta = p.op("scalar", lambda e, m=m, pb=pb: e.activation(
                out=s_sp[:, 0:T], in_=pb[:, 0:T], func=AF.Abs,
                bias=s_dtb[:, m:m + 1]),
                after=[tmm] + ([tsp_rd] if tsp_rd else []))
            p.op("scalar", lambda e: e.activation(
                out=s_sp[:, 0:T], in_=s_sp[:, 0:T], func=AF.Exp, scale=-1.0))
            ta2 = p.op("scalar", lambda e: e.activation(
                out=s_sp[:, 0:T], in_=s_sp[:, 0:T], func=AF.Ln, bias=1.0))
            trl = p.op("scalar", lambda e, m=m, pb=pb: e.activation(
                out=s_dt[:, m, :], in_=pb[:, 0:T], func=AF.Relu,
                bias=s_dtb[:, m:m + 1]))
            pslast[id(pb)] = trl
            tsp_rd = p.op("vector", lambda e, m=m: e.tensor_tensor(
                out=s_dt[:, m, :], in0=s_dt[:, m, :], in1=s_sp[:, 0:T],
                op=ALU.add), after=[trl, ta2])
            p.op("vector", lambda e, m=m: e.tensor_scalar(
                out=s_dt[:, m, :], in0=s_dt[:, m, :], scalar1=0.2,
                scalar2=0.001, op0=ALU.min, op1=ALU.max))
            p.op("vector", lambda e, m=m: e.tensor_tensor(
                out=s_dx[:, m, :], in0=s_dt[:, m, :], in1=s_xs[:, m, :],
                op=ALU.mult))
        tdt = p.tok("eV")
        p.dma("sync", rearr(o_dt, "(c p) t -> p c t", p=128)[:, :, o:o + T],
              s_dt[:, :, :], after=[tdt], inc="dS")
        p.dma("sync", rearr(o_dx, "(c p) t -> p c t", p=128)[:, :, o:o + T],
              s_dx[:, :, :], after=[tdt], inc="dS")
        p.barrier()
    return p.finalize()


def build_L4():
    p = Prog()
    TC = 1024
    NCH = S // TC

    dtd = p.din("dtd", [DQ, S])
    dxd = p.din("dxd", [DQ, S])
    bnm = p.din("bnm", [NS, S])
    cnm = p.din("cnm", [NS, S])
    alog = p.din("alog", [DQ, NS])
    identb = p.din("identb", [128, 128], BF16)
    onesb = p.din("onesb", [1, 128], BF16)
    o_y = p.dout("ydm", [DQ, S])

    s_A = p.sbuf("s_A", [128, 2, NS])
    s_idb = p.sbuf("s_idb", [128, 128], BF16)
    s_onb = p.sbuf("s_onb", [1, 128], BF16)
    s_dt = p.sbuf("s_dt", [128, 2, TC])
    s_dx = p.sbuf("s_dx", [128, 2, TC])
    s_bn = p.sbuf("s_bn", [1, NS, TC], BF16)
    s_cn = p.sbuf("s_cn", [1, NS, TC], BF16)
    s_ab = p.sbuf("s_ab", [128, 2, 2, TC])
    s_u = p.sbuf("s_u", [128, 2, TC])
    s_s = p.sbuf("s_s", [128, 2, 2, TC])
    s_pr = p.sbuf("s_pr", [128, 2, 2, TC], BF16)
    s_cy = p.sbuf("s_cy", [128, 64])
    s_y = p.sbuf("s_y", [128, TC])
    sbC = p.sbuf("sbC", [128, 2, TC], BF16)
    psB = p.psum("psB", [128, TC])
    psC = p.psum("psC", [128, TC])
    psY0 = p.psum("psY0", [128, TC])
    psY1 = p.psum("psY1", [128, TC])
    psYs = (psY0, psY1)

    tcst = common_consts(p)
    tidb = p.dma("sync", s_idb[:, :], identb.ap(), inc="dW")
    tonb = p.dma("sync", s_onb[:, :], onesb.ap(), inc="dW")
    tA = p.dma("sync", s_A[:, :, :], rearr(alog, "(c p) n -> p c n", p=128), inc="dW")
    t1 = p.op("vector", lambda e: e.tensor_scalar(
        out=s_A[:, :, :], in0=s_A[:, :, :], scalar1=2.0, scalar2=-5.0,
        op0=ALU.min, op1=ALU.max), after=[tA])
    t2 = p.op("scalar", lambda e: e.activation(
        out=s_A[:, :, :], in_=s_A[:, :, :], func=AF.Exp), after=[t1])
    tAr = p.op("vector", lambda e: e.tensor_single_scalar(
        out=s_A[:, :, :], in_=s_A[:, :, :], scalar=-1.0, op=ALU.mult), after=[t2])
    tcy = p.op("vector", lambda e: e.memset(s_cy[:, :], 0.0))

    def bcastB(n, extra):
        t = None
        for h in range(2):
            t = p.op("tensor", lambda e, n=n, h=h: e.matmul(
                out=psB[:, h * 512:(h + 1) * 512], lhsT=s_onb[0:1, :],
                rhs=s_bn[0:1, n, h * 512:(h + 1) * 512],
                start=True, stop=True), after=extra, inc="eT")
        return t

    def bcastC(n, extra):
        t = None
        for h in range(2):
            t = p.op("tensor", lambda e, n=n, h=h: e.matmul(
                out=psC[:, h * 512:(h + 1) * 512], lhsT=s_onb[0:1, :],
                rhs=s_cn[0:1, n, h * 512:(h + 1) * 512],
                start=True, stop=True), after=extra, inc="eT")
        return t

    for it in range(NCH):
        o = it * TC
        p.dma("sync", s_dt[:, :, :],
              rearr(dtd, "(c p) t -> p c t", p=128)[:, :, o:o + TC], inc="dL")
        p.dma("sync", s_dx[:, :, :],
              rearr(dxd, "(c p) t -> p c t", p=128)[:, :, o:o + TC], inc="dL")
        p.dma("gpsimd", s_bn[0:1, :, :], bnm[0:NS, o:o + TC], inc="dG")
        tldG = p.dma("gpsimd", s_cn[0:1, :, :], cnm[0:NS, o:o + TC], inc="dG")
        tld = p.tok("dL")
        tbcB = bcastB(0, [tldG, tonb, tcst])
        tbcC = bcastC(0, [tldG])
        scan_rd = {}
        s_rd = {}
        pr_rd = {}
        sbc_rd = {}
        for n in range(NS):
            par = n % 2
            ceng = "vector" if par == 0 else "gpsimd"
            tabs = []
            for c in range(2):
                tabs.append(p.op("scalar", lambda e, c=c, n=n, par=par:
                    e.activation(out=s_ab[:, par, c, :], in_=s_dt[:, c, :],
                                 func=AF.Exp, scale=s_A[:, c, n:n + 1]),
                    after=[tld, tAr] + scan_rd.get((par, c), [])))
            tu = {}
            tcl = {}
            for c in range(2):
                tu[c] = p.op("vector", lambda e, c=c: e.tensor_tensor(
                    out=s_u[:, c, :], in0=s_dx[:, c, :], in1=psB[:, 0:TC],
                    op=ALU.mult),
                    after=[tbcB, tld] + scan_rd.get((par, c), []))
                ts = p.op("vector", lambda e, c=c, n=n, par=par: e.tensor_tensor_scan(
                    out=s_s[:, par, c, :], data0=s_ab[:, par, c, :],
                    data1=s_u[:, c, :],
                    initial=s_cy[:, c * NS + n:c * NS + n + 1],
                    op0=ALU.mult, op1=ALU.add),
                    after=[tabs[c], tu[c], tcy] + s_rd.get((par, c), []))
                scan_rd[(par, c)] = [ts]
                tcl[c] = p.op(ceng, lambda e, c=c, par=par: e.tensor_scalar(
                    out=s_s[:, par, c, :], in0=s_s[:, par, c, :], scalar1=10.0,
                    scalar2=-10.0, op0=ALU.min, op1=ALU.max), after=[ts])
            if n + 1 < NS:
                tbcB = bcastB(n + 1, [tu[0], tu[1]])
            tCcp = p.op("scalar", lambda e, par=par: e.copy(
                out=sbC[:, par, :], in_=psC[:, 0:TC]),
                after=[tbcC] + sbc_rd.get(par, []))
            if n + 1 < NS:
                tbcC = bcastC(n + 1, [tCcp])
            tprs = []
            for c in range(2):
                tcar = p.op("scalar", lambda e, c=c, n=n, par=par: e.copy(
                    out=s_cy[:, c * NS + n:c * NS + n + 1],
                    in_=s_s[:, par, c, TC - 1:TC]), after=[tcl[c]])
                tpr = p.op("gpsimd", lambda e, c=c, par=par: e.tensor_tensor(
                    out=s_pr[:, par, c, :], in0=s_s[:, par, c, :],
                    in1=sbC[:, par, :], op=ALU.mult),
                    after=[tcl[c], tCcp] + pr_rd.get((par, c), []))
                tprs.append(tpr)
                s_rd[(par, c)] = [tpr, tcar]
                tmm = None
                for h in range(2):
                    tmm = p.op("tensor", lambda e, c=c, n=n, h=h, par=par: e.matmul(
                        out=psYs[c][:, h * 512:(h + 1) * 512], lhsT=s_idb[:, :],
                        rhs=s_pr[:, par, c, h * 512:(h + 1) * 512],
                        start=(n == 0), stop=(n == NS - 1)),
                        after=[tpr, tidb], inc="eT")
                pr_rd[(par, c)] = [tmm]
            sbc_rd[par] = tprs
        mm_done = pr_rd[(0, 0)] + pr_rd[(0, 1)] + pr_rd[(1, 0)] + pr_rd[(1, 1)]
        tst = None
        for c in range(2):
            tev = p.op("vector", lambda e, c=c: e.tensor_copy(
                out=s_y[:, :], in_=psYs[c][:, 0:TC]),
                after=mm_done + ([tst] if tst else []))
            tst = p.dma("sync",
                        rearr(o_y, "(c p) t -> p c t", p=128)[:, c, o:o + TC],
                        s_y[:, :], after=[tev], inc="dS")
        p.barrier()
    return p.finalize()


def build_L5():
    p = Prog()
    T = 512
    NSUB = SQ // T

    y_in = p.din("yfm", [D, SQ])
    r2_in = p.din("r2", [D, SQ])
    sz_in = p.din("siluz", [D, SQ], BF16)
    owT = p.din("owT", [D, D])
    ob = p.din("ob", [128, 8])
    lng = p.din("lng", [128, 8])
    lnb = p.din("lnb", [128, 8])
    fwT = p.din("fwT", [D, D])
    o_out = p.dout("outfm", [D, SQ])

    s_ow = p.sbuf("s_ow", [128, 8, D], BF16)
    s_fw = p.sbuf("s_fw", [128, 8, D], BF16)
    s_stg = p.sbuf("s_stg", [128, D])
    s_ob = p.sbuf("s_ob", [128, 8])
    s_g = p.sbuf("s_g", [128, 8])
    s_bt = p.sbuf("s_bt", [128, 8])
    tcst = common_consts(p)
    s_y = p.sbuf("s_y", [128, 8, T])
    s_r2 = p.sbuf("s_r2", [128, 8, T])
    s_sz = p.sbuf("s_sz", [128, 8, T], BF16)
    s_g1 = p.sbuf("s_g1", [128, 8, T], BF16)
    s_o0 = p.sbuf("s_o0", [128, 8, T])
    s_l2 = p.sbuf("s_l2", [128, 8, T])
    s_r3 = p.sbuf("s_r3", [128, 8, T])
    s_l3 = p.sbuf("s_l3", [128, 8, T], BF16)
    s_sq = p.sbuf("s_sq", [128, T])
    s_sq2 = p.sbuf("s_sq2L5", [128, T])
    s_row = p.sbuf("s_rowL5", [1, 2 * T])
    psS = p.psum("psS", [1, T])
    psQ = p.psum("psQ", [1, T])
    ps_bm = p.psum("ps_bm", [128, T])
    ps_br = p.psum("ps_br", [128, T])
    ps0 = p.psum("ps0", [128, T])
    ps1 = p.psum("ps1", [128, T])

    tow = p.dma("gpsimd", s_ow[:, :, :], rearr(owT, "(c p) f -> p c f", p=128), inc="dG")
    tv = p.dma("sync", s_ob[:, :], ob.ap(), inc="dW")
    tv = p.dma("sync", s_g[:, :], lng.ap(), inc="dW")
    tv = p.dma("sync", s_bt[:, :], lnb.ap(), inc="dW")
    twv = p.tok("dW")
    tsg_prev = None
    for c in range(8):
        tld = p.dma("sync", s_stg[:, :],
                    rearr(fwT, "(c p) f -> p c f", p=128)[:, c, :],
                    after=([tsg_prev] if tsg_prev else []), inc="dL")
        tsg_prev = p.op("scalar", lambda e, c=c: e.activation(
            out=s_fw[:, c, :], in_=s_stg[:, :], func=AF.Sign), after=[tld])
    tfw = tsg_prev
    p.barrier()

    def ln_stats_apply(src, dst, tafter, dst_bf=False, gain=False):
        trs = emit_ln_stats(p, lambda i: src[:, i, :], 8, T, psS, psQ, s_sq,
                            s_row[0:1, 0:T], s_row[0:1, T:2 * T],
                            tafter + [tcst], D, EPS, sq2=s_sq2)
        tb1 = emit_bcast(p, s_row[0:1, 0:T], ps_bm[:, 0:T], [trs])
        tb2 = emit_bcast(p, s_row[0:1, T:2 * T], ps_br[:, 0:T], [trs])
        for i in range(8):
            p.op("vector", lambda e, i=i: e.tensor_tensor(
                out=s_sq[:, 0:T], in0=src[:, i, :], in1=ps_bm[:, 0:T],
                op=ALU.subtract), after=[tb1, tb2, ("eA", p.cnt["eA"])])
            if gain:
                p.op("vector", lambda e, i=i: e.tensor_tensor(
                    out=s_sq[:, 0:T], in0=s_sq[:, 0:T], in1=ps_br[:, 0:T],
                    op=ALU.mult))
                p.op("scalar", lambda e, i=i: e.activation(
                    out=dst[:, i, :], in_=s_sq[:, 0:T], func=AF.Identity,
                    scale=s_g[:, i:i + 1], bias=s_bt[:, i:i + 1]),
                    after=[p.tok("eV"), twv])
            else:
                p.op("vector", lambda e, i=i: e.tensor_tensor(
                    out=dst[:, i, :], in0=s_sq[:, 0:T], in1=ps_br[:, 0:T],
                    op=ALU.mult))
        return p.tok("eA") if gain else p.tok("eV")

    for it in range(NSUB):
        o = it * T
        p.dma("sync", s_y[:, :, :],
              rearr(y_in, "(c p) t -> p c t", p=128)[:, :, o:o + T], inc="dL")
        p.dma("sync", s_r2[:, :, :],
              rearr(r2_in, "(c p) t -> p c t", p=128)[:, :, o:o + T], inc="dL")
        tld = p.dma("sync", s_sz[:, :, :],
                    rearr(sz_in, "(c p) t -> p c t", p=128)[:, :, o:o + T], inc="dL")
        for i in range(8):
            p.op("vector", lambda e, i=i: e.tensor_tensor(
                out=s_g1[:, i, :], in0=s_y[:, i, :], in1=s_sz[:, i, :],
                op=ALU.mult), after=[tld])
        tg1 = p.tok("eV")
        tl2 = ln_stats_apply(s_r2, s_l2, [tld])
        pslast = {}
        for m in range(8):
            pb = ps0 if (m % 2 == 0) else ps1
            aft = [tg1, tow] + ([pslast[id(pb)]] if id(pb) in pslast else [])
            tmm = mm_group(p, pb[:, 0:T],
                           lambda k, m=m: s_ow[:, k, m * 128:(m + 1) * 128],
                           lambda k: s_g1[:, k, :], 8, after=aft)
            ta = p.op("scalar", lambda e, m=m, pb=pb: e.activation(
                out=s_o0[:, m, :], in_=pb[:, 0:T], func=AF.Identity,
                bias=s_ob[:, m:m + 1]), after=[tmm, twv])
            pslast[id(pb)] = ta
            p.op("vector", lambda e, m=m: e.tensor_tensor(
                out=s_o0[:, m, :], in0=s_o0[:, m, :], in1=s_l2[:, m, :],
                op=ALU.add), after=[ta, tl2])
        to0 = p.tok("eV")
        tat = ln_stats_apply(s_o0, s_l2, [to0], gain=True)
        for i in range(8):
            p.op("vector", lambda e, i=i: e.tensor_tensor(
                out=s_r3[:, i, :], in0=s_r2[:, i, :], in1=s_l2[:, i, :],
                op=ALU.add), after=[tat])
        tr3 = p.tok("eV")
        tl3 = ln_stats_apply(s_r3, s_l3, [tr3])
        pslast = {}
        for m in range(8):
            pb = ps0 if (m % 2 == 0) else ps1
            aft = [tl3, tfw] + ([pslast[id(pb)]] if id(pb) in pslast else [])
            tmm = mm_group(p, pb[:, 0:T],
                           lambda k, m=m: s_fw[:, k, m * 128:(m + 1) * 128],
                           lambda k: s_l3[:, k, :], 8, after=aft)
            pslast[id(pb)] = p.op("vector", lambda e, m=m, pb=pb: e.tensor_tensor(
                out=s_o0[:, m, :], in0=pb[:, 0:T], in1=s_r3[:, m, :],
                op=ALU.add), after=[tmm])
        tout = p.tok("eV")
        p.dma("sync", rearr(o_out, "(c p) t -> p c t", p=128)[:, :, o:o + T],
              s_o0[:, :, :], after=[tout], inc="dS")
        p.barrier()
    return p.finalize()


def vec_pc(v, nch):
    return np.ascontiguousarray(np.asarray(v, np.float32).reshape(nch, 128).T)


def _core_slices():
    return [(c // 4, (c % 4) * SQ, (c % 4 + 1) * SQ) for c in range(NCORES)]


_PROG_CACHE = {}


def _prog(name, builder):
    if name not in _PROG_CACHE:
        _PROG_CACHE[name] = builder()
    return _PROG_CACHE[name]


def _run(name, builder, maps):
    nc = _prog(name, builder)
    last = None
    for attempt in range(3):
        try:
            return run_bass_kernel_spmd(nc, maps,
                                        core_ids=list(range(NCORES))).results
        except Exception as ex:
            last = ex
            if "UNAVAILABLE" not in str(ex) and "UNRECOVERABLE" not in str(ex):
                raise
            import time as _t
            _t.sleep(10)
    raise last


def kernel(**inputs):
    f = lambda k: np.asarray(inputs[k], np.float32)
    cc = np.ascontiguousarray

    x = f("L0_attn_x")
    x_fm = [cc(x[b].T) for b in range(B)]

    wgT = cc(f("L0_attn_Wg").T)
    wiT = cc(f("L0_attn_Wi").T)
    cw0 = cc(f("L0_attn_conv_w")[:, 0, :].reshape(12, 128, 4)
             .transpose(1, 0, 2).reshape(128, 48))
    cb0 = vec_pc(f("L0_attn_conv_b"), 12)
    wliT = cc(f("L0_attn_lru_Wi").T)
    wlgT = cc(f("L0_attn_lru_Wg").T)
    lbi = vec_pc(f("L0_attn_lru_bi"), 12)
    lbg = vec_pc(f("L0_attn_lru_bg"), 12)
    lam = vec_pc(f("L0_attn_lam"), 12)
    woT = cc(f("L0_attn_Wo").T)
    w1T = cc(f("L0_ffn_w1").T)
    b1 = vec_pc(f("L0_ffn_b1"), 32)
    w2T = cc(f("L0_ffn_w2").T)
    b2 = vec_pc(f("L0_ffn_b2"), 8)
    ipjT = cc(f("L1_attn_in_proj_w").T)
    ipb = vec_pc(f("L1_attn_in_proj_b"), 16)
    cw1 = cc(f("L1_attn_conv_w")[:, 0, :].reshape(8, 128, 4)
             .transpose(1, 0, 2).reshape(128, 32))
    cb1 = vec_pc(f("L1_attn_conv_b"), 8)
    alog = f("L1_attn_A_log")
    bcwT = cc(np.concatenate([f("L1_attn_B_w"), f("L1_attn_C_w")], 0).T)
    bcb = cc(np.concatenate([f("L1_attn_B_b"), f("L1_attn_C_b")])[:, None])
    dtwT = cc(f("L1_attn_dt_w").T)
    dtb = vec_pc(f("L1_attn_dt_b"), 8)
    owT = cc(f("L1_attn_out_w").T)
    ob = vec_pc(f("L1_attn_out_b"), 8)
    lng = vec_pc(f("L1_attn_ln_g"), 8)
    lnb = vec_pc(f("L1_attn_ln_beta"), 8)
    fwT = cc(f("L1_ffn_weight").T)
    ident = np.eye(128, dtype=np.float32)
    identb = np.eye(128, dtype=BFNP)

    cs = _core_slices()

    maps = []
    for (b, lo, hi) in cs:
        xs = np.zeros((D, SQ + 3), np.float32)
        xs[:, 3:] = x_fm[b][:, lo:hi]
        if lo > 0:
            xs[:, 0:3] = x_fm[b][:, lo - 3:lo]
        maps.append({"x": xs, "wgT": wgT, "wiT": wiT, "cw": cw0, "cb": cb0})
    rA = _run("L1a", build_L1a, maps)

    maps = [{"ybf": rA[c]["ybf"], "wliT": wliT, "wlgT": wlgT,
             "lbi": lbi, "lbg": lbg, "lam": lam} for c in range(NCORES)]
    rB = _run("L1b", build_L1b, maps)

    maps = []
    for c, (b, lo, hi) in enumerate(cs):
        q = c % 4
        pg = np.zeros((128, 12, 3), np.float32)
        ps = np.zeros((128, 12, 3), np.float32)
        for idx, j in enumerate(range(b * 4, b * 4 + q)):
            slot = 3 - q + idx
            pg[:, :, slot] = rB[j]["summ"][:, 0].reshape(12, 128).T
            ps[:, :, slot] = rB[j]["summ"][:, 1].reshape(12, 128).T
        maps.append({"alpha": rB[c]["alpha"], "xbeta": rB[c]["xbeta"],
                     "ggate": rA[c]["ggate"], "x": cc(x_fm[b][:, lo:hi]),
                     "pgam": pg, "psend": ps, "woT": woT})
    r2a = _run("L2a", build_L2a, maps)

    maps = [{"r1": r2a[c]["r1"], "w1T": w1T, "b1": b1, "ident": ident}
            for c in range(NCORES)]
    rb1 = _run("L2b1", build_L2b1, maps)
    maps = [{"h1": rb1[c]["h1"], "r1": r2a[c]["r1"], "w2T": w2T, "b2": b2,
             "ident": ident} for c in range(NCORES)]
    r2b = _run("L2b2", build_L2b2, maps)

    maps = []
    for c, (b, lo, hi) in enumerate(cs):
        q = c % 4
        r2h = np.zeros((D, SQ + 3), np.float32)
        r2h[:, 2:2 + SQ] = r2b[c]["r2"]
        hm = np.ones((128, 3), np.float32)
        if q > 0:
            r2h[:, 0:2] = r2b[c - 1]["r2"][:, SQ - 2:SQ]
        else:
            hm[:, 0:2] = 0.0
        if q < 3:
            r2h[:, SQ + 2:SQ + 3] = r2b[c + 1]["r2"][:, 0:1]
        else:
            hm[:, 2] = 0.0
        maps.append({"r2h": r2h, "ipjT": ipjT, "ipb": ipb, "cw1": cw1,
                     "cb1": cb1, "bcwT": bcwT, "bcb": bcb, "dtwT": dtwT,
                     "dtb": dtb, "hmask": hm})
    r3 = _run("L3", build_L3, maps)

    maps = []
    for c in range(NCORES):
        b, dq = c // 4, c % 4
        dsl = slice(dq * DQ, (dq + 1) * DQ)
        dt_b = np.concatenate([r3[b * 4 + j]["dt"] for j in range(4)], axis=1)
        dx_b = np.concatenate([r3[b * 4 + j]["dtxs"] for j in range(4)], axis=1)
        bc_b = np.concatenate([r3[b * 4 + j]["bc"] for j in range(4)], axis=1)
        maps.append({"dtd": cc(dt_b[dsl, :]), "dxd": cc(dx_b[dsl, :]),
                     "bnm": cc(bc_b[0:NS, :]), "cnm": cc(bc_b[NS:2 * NS, :]),
                     "alog": cc(alog[dsl, :]), "identb": identb,
                     "onesb": np.ones((1, 128), BFNP)})
    r4 = _run("L4", build_L4, maps)

    maps = []
    for c, (b, lo, hi) in enumerate(cs):
        y_b = np.concatenate([r4[b * 4 + dq]["ydm"] for dq in range(4)], axis=0)
        maps.append({"yfm": cc(y_b[:, lo:hi]), "r2": r2b[c]["r2"],
                     "siluz": r3[c]["siluz"], "owT": owT, "ob": ob,
                     "lng": lng, "lnb": lnb, "fwT": fwT})
    r5 = _run("L5", build_L5, maps)

    out = np.empty((B, S, D), np.float32)
    for c, (b, lo, hi) in enumerate(cs):
        out[b, lo:hi, :] = r5[c]["outfm"].T
    return out
